# version 02
# baseline (speedup 1.0000x reference)
import numpy as np
from contextlib import ExitStack
import concourse.bass as bass
import concourse.mybir as mybir
from concourse.bass_utils import run_bass_kernel_spmd

F32 = mybir.dt.float32
BF16 = mybir.dt.bfloat16
I32 = mybir.dt.int32
AF = mybir.ActivationFunctionType
ALU = mybir.AluOpType
AX = mybir.AxisListType

D = 1024
T = 2048
DCH = 8
NTT = 4
H = 8
KV = 4096
CD = 768
NCORE = 8


class Sem:
    def __init__(self, ctx, nc, name):
        self.h = ctx.enter_context(nc.semaphore(name))
        self.v = 0

    def inc(self, instr, n=1):
        instr.then_inc(self.h, n)
        self.v += n
        return self.v


class Prog:
    def __init__(self, nc, ctx, xln_cols=DCH * T, with_trimask=True):
        self.nc = nc
        self.ctx = ctx
        self.block = ctx.enter_context(nc.Block())
        for name in ["dma", "dmg", "dmw", "dmk", "cc", "tm", "sc", "ve", "gp", "od"]:
            setattr(self, name, Sem(ctx, nc, "s_" + name))
        self.PA = ctx.enter_context(nc.psum_tensor("PA", [128, 1024], F32))
        self.PB = ctx.enter_context(nc.psum_tensor("PB", [128, 512], F32))
        self.PC = ctx.enter_context(nc.psum_tensor("PC", [128, 512], F32))
        self.PD = ctx.enter_context(nc.psum_tensor("PD", [128, 512], F32))
        self.PT1 = ctx.enter_context(nc.psum_tensor("PT1", [128, 1024], BF16))
        self.PT2 = ctx.enter_context(nc.psum_tensor("PT2", [128, 1024], BF16))
        self.PF = ctx.enter_context(nc.psum_tensor("PF", [128, 512], F32))
        self.ident_b = ctx.enter_context(nc.sbuf_tensor("ident_b", [128, 128], BF16))
        self.ident_f = ctx.enter_context(nc.sbuf_tensor("ident_f", [128, 128], F32))
        self.ones_bcol = ctx.enter_context(nc.sbuf_tensor("ones_bcol", [128, 1], BF16))
        self.ones_brow = ctx.enter_context(nc.sbuf_tensor("ones_brow", [1, 128], BF16))
        self.ones_frow = ctx.enter_context(nc.sbuf_tensor("ones_frow", [1, 128], F32))
        self.ones_f11 = ctx.enter_context(nc.sbuf_tensor("ones_f11", [1, 1], F32))
        self.eps_t = ctx.enter_context(nc.sbuf_tensor("eps_t", [1, 1], F32))
        self.neg_half = ctx.enter_context(nc.sbuf_tensor("neg_half", [128, 1], F32))
        self.srow512 = ctx.enter_context(nc.sbuf_tensor("srow512", [1, 512], F32))
        self.srow32 = ctx.enter_context(nc.sbuf_tensor("srow32", [1, 128], F32))
        self.s11 = ctx.enter_context(nc.sbuf_tensor("s11", [1, 1], F32))
        self.s12 = ctx.enter_context(nc.sbuf_tensor("s12", [1, 1], F32))
        self.s13 = ctx.enter_context(nc.sbuf_tensor("s13", [1, 1], F32))
        self.s14 = ctx.enter_context(nc.sbuf_tensor("s14", [1, 2], F32))
        self.zrow8 = ctx.enter_context(nc.sbuf_tensor("zrow8", [1, 8], F32))
        self.trimask = (ctx.enter_context(nc.sbuf_tensor("trimask", [128, 128], BF16))
                        if with_trimask else None)
        self.onecol = ctx.enter_context(nc.sbuf_tensor("onecol", [128, 1], F32))
        self.eps9 = ctx.enter_context(nc.sbuf_tensor("eps9", [1, 1], F32))
        self.xT = ctx.enter_context(nc.sbuf_tensor("xT", [128, DCH * T], F32))
        self.xln = ctx.enter_context(nc.sbuf_tensor("xln", [128, xln_cols], BF16))
        self.smu = ctx.enter_context(nc.sbuf_tensor("smu", [1, 512], F32))
        self.svar = ctx.enter_context(nc.sbuf_tensor("svar", [1, 512], F32))
        self.sr = ctx.enter_context(nc.sbuf_tensor("sr", [1, 512], F32))
        self.smur = ctx.enter_context(nc.sbuf_tensor("smur", [1, 512], F32))
        self.rb = ctx.enter_context(nc.sbuf_tensor("rb", [128, 512], F32))
        self.mrb = ctx.enter_context(nc.sbuf_tensor("mrb", [128, 512], F32))
        self.tmpf = ctx.enter_context(nc.sbuf_tensor("tmpf", [128, 512], F32))

    def emit_consts(self):
        pr = self
        nc = self.nc
        it1 = self.ctx.enter_context(nc.sbuf_tensor("c_it1", [128, 128], F32))
        it2 = self.ctx.enter_context(nc.sbuf_tensor("c_it2", [128, 128], F32))

        @pr.block.gpsimd
        def _(g):
            g.iota(it1[:], pattern=[[1, 128]], base=0, channel_multiplier=0,
                   allow_small_or_imprecise_dtypes=True)
            g.iota(it2[:], pattern=[[0, 128]], base=0, channel_multiplier=1,
                   allow_small_or_imprecise_dtypes=True)
            g.memset(pr.ones_bcol[:], 1.0)
            g.memset(pr.ones_brow[:], 1.0)
            g.memset(pr.ones_frow[:], 1.0)
            g.memset(pr.eps_t[:], 1e-5)
            g.memset(pr.neg_half[:], -0.5)
            g.memset(pr.onecol[:], 1.0)
            g.memset(pr.eps9[:], 1e-9)
            pr.gp.inc(g.memset(pr.ones_f11[:], 1.0))
        gp0 = pr.gp.v

        @pr.block.vector
        def _(v):
            v.wait_ge(pr.gp.h, gp0)
            v.tensor_tensor(pr.ident_f[:], it1[:], it2[:], ALU.is_equal)
            v.drain()
            if pr.trimask is not None:
                v.tensor_copy(pr.ident_b[:], pr.ident_f[:])
                v.drain()
                pr.ve.inc(v.tensor_tensor(pr.trimask[:], it2[:], it1[:], ALU.is_le))
            else:
                pr.ve.inc(v.tensor_copy(pr.ident_b[:], pr.ident_f[:]))

    def ln(self, src, dst, sq_scratch, waits, tts=None, dst_stride=None, dst_f32=None):
        pr = self
        eps = 1e-5
        xbf = dst
        sq = sq_scratch
        if tts is None:
            tts = list(range(NTT))
        if dst_stride is None:
            dst_stride = 512 * len(tts)
        NT_ = len(tts)
        tt0 = tts[0]
        def dsl(c, tt):
            return slice(c * dst_stride + (tt - tt0) * 512,
                         c * dst_stride + (tt - tt0) * 512 + 512)
        def ssl(c, tt):
            return slice(c * T + tt * 512, c * T + tt * 512 + 512)

        @pr.block.scalar
        def _(s):
            for sem, val in waits:
                s.wait_ge(sem.h, val)
            for c in range(DCH):
                for tt in tts:
                    s.copy(xbf[:, dsl(c, tt)], src[:, ssl(c, tt)])
                    i = s.activation(sq[:, dsl(c, tt)], src[:, ssl(c, tt)], AF.Square)
            pr.sc.inc(i)
        cast_done = pr.sc.v

        tm0, sc0, ve0 = pr.tm.v, pr.sc.v, pr.ve.v

        @pr.block.tensor
        def _(t):
            t.wait_ge(pr.sc.h, cast_done)
            t.wait_ge(pr.ve.h, ve0)
            for j, tt in enumerate(tts):
                if j >= 1:
                    t.wait_ge(pr.ve.h, ve0 + 4 * j - 3)
                for c in range(DCH):
                    t.matmul(pr.PC[0:1, 0:512], pr.ones_bcol[:],
                             xbf[:, dsl(c, tt)], start=(c == 0), stop=(c == DCH - 1))
                for c in range(DCH):
                    i2 = t.matmul(pr.PD[0:1, 0:512], pr.ones_bcol[:],
                                  sq[:, dsl(c, tt)], start=(c == 0), stop=(c == DCH - 1))
                pr.tm.inc(i2)
                t.wait_ge(pr.ve.h, ve0 + 4 * j + 2)
                pr.tm.inc(t.matmul(pr.PF[:], pr.ones_frow[:], pr.sr[:], start=True, stop=True))
                t.wait_ge(pr.ve.h, ve0 + 4 * j + 3)
                pr.tm.inc(t.matmul(pr.PF[:], pr.ones_frow[:], pr.smur[:], start=True, stop=True))

        @pr.block.vector
        def _(v):
            for j, tt in enumerate(tts):
                v.wait_ge(pr.tm.h, tm0 + 3 * j + 1)
                v.tensor_scalar(pr.smu[:], pr.PC[0:1, 0:512], 1.0 / D, None, ALU.mult)
                v.tensor_scalar(pr.svar[:], pr.PD[0:1, 0:512], 1.0 / D, None, ALU.mult)
                v.drain()
                v.tensor_tensor(pr.tmpf[0:1, :], pr.smu[:], pr.smu[:], ALU.mult)
                v.drain()
                pr.ve.inc(v.tensor_tensor(pr.svar[:], pr.svar[:], pr.tmpf[0:1, :], ALU.subtract))
                v.wait_ge(pr.sc.h, sc0 + j + 1)
                v.drain()
                pr.ve.inc(v.tensor_tensor(pr.smur[:], pr.smu[:], pr.sr[:], ALU.mult))
                v.wait_ge(pr.tm.h, tm0 + 3 * j + 2)
                pr.ve.inc(v.tensor_copy(pr.rb[:], pr.PF[:]))
                v.wait_ge(pr.tm.h, tm0 + 3 * j + 3)
                v.tensor_copy(pr.mrb[:], pr.PF[:])
                v.drain()
                for c in range(DCH):
                    od = dst_f32 if dst_f32 is not None else dst
                    v.tensor_tensor(pr.tmpf[:], src[:, ssl(c, tt)], pr.rb[:], ALU.mult)
                    v.drain()
                    i = v.tensor_tensor(od[:, dsl(c, tt)], pr.tmpf[:], pr.mrb[:], ALU.subtract)
                    v.drain()
                pr.ve.inc(i)

        @pr.block.scalar
        def _(s):
            for j, tt in enumerate(tts):
                s.wait_ge(pr.ve.h, ve0 + 4 * j + 1)
                s.activation(pr.sr[:], pr.svar[:], AF.Ln, bias=pr.eps_t[:])
                s.drain()
                pr.sc.inc(s.activation(pr.sr[:], pr.sr[:], AF.Exp, scale=-0.5))

        return pr.ve.v



def emit_block0(pr, io, have_x_natural=True, upto=99):
    nc = pr.nc
    SCALE = (128 ** -0.5) * 1.5

    kvloc = nc.dram_tensor("kvloc", [2 * 512 * 1024], BF16, kind="Internal")
    kvall = nc.dram_tensor("kvall", [NCORE * 2 * 512 * 1024], BF16, kind="Internal")

    phA = ExitStack()
    cnat = phA.enter_context(nc.sbuf_tensor("b0_cnat", [128, 4 * CD], BF16))
    ctxT = phA.enter_context(nc.sbuf_tensor("b0_ctxT", [128, 6 * 512], BF16))
    wkb = phA.enter_context(nc.sbuf_tensor("b0_wkb", [128, 6 * 1024], BF16))
    wvb = phA.enter_context(nc.sbuf_tensor("b0_wvb", [128, 6 * 1024], BF16))
    khsl = phA.enter_context(nc.sbuf_tensor("b0_khsl", [128, 8 * 512], BF16))
    vhsl = phA.enter_context(nc.sbuf_tensor("b0_vhsl", [128, 4 * 1024], BF16))

    @pr.block.gpsimd
    def _(g):
        c_r = io["b0_ctx"].ap().rearrange("(c p) d -> p c d", p=128)
        pr.dmg.inc(g.dma_start(out=cnat[:].rearrange("p (c d) -> p c d", c=4), in_=c_r), 16)
        pr.dmw.inc(g.dma_start(out=wkb[:].rearrange("p (c m) -> p c m", c=6),
                               in_=io["b0_wk"].ap().rearrange("(c p) m -> p c m", p=128)), 16)
        pr.dmw.inc(g.dma_start(out=wvb[:].rearrange("p (c m) -> p c m", c=6),
                               in_=io["b0_wv"].ap().rearrange("(c p) m -> p c m", p=128)), 16)
    dma_cnat = pr.dmg.v
    dma_wkv = pr.dmw.v
    ve_id = pr.ve.v
    tm0, sc0 = pr.tm.v, pr.sc.v

    @pr.block.tensor
    def _(t):
        t.wait_ge(pr.dmg.h, dma_cnat)
        t.wait_ge(pr.ve.h, ve_id)
        for idx in range(24):
            rc, cc = divmod(idx, 6)
            if idx >= 2:
                t.wait_ge(pr.sc.h, sc0 + idx - 1)
            pt = pr.PT1 if idx % 2 == 0 else pr.PT2
            pr.tm.inc(t.transpose(pt[:, 0:128],
                                  cnat[:, rc * CD + cc * 128: rc * CD + cc * 128 + 128],
                                  pr.ident_b[:]))

    @pr.block.scalar
    def _(s):
        for idx in range(24):
            rc, cc = divmod(idx, 6)
            s.wait_ge(pr.tm.h, tm0 + idx + 1)
            pt = pr.PT1 if idx % 2 == 0 else pr.PT2
            pr.sc.inc(s.copy(ctxT[:, cc * 512 + rc * 128: cc * 512 + rc * 128 + 128],
                             pt[:, 0:128]))
    ctxT_done = pr.sc.v

    tm1, sc1 = pr.tm.v, pr.sc.v

    @pr.block.tensor
    def _(t):
        t.wait_ge(pr.sc.h, ctxT_done)
        t.wait_ge(pr.dmw.h, dma_wkv)
        for dc in range(8):
            if dc >= 2:
                t.wait_ge(pr.sc.h, sc1 + dc - 1)
            ps = pr.PB if dc % 2 == 0 else pr.PC
            for c in range(6):
                i = t.matmul(ps[:], wkb[:, c * 1024 + dc * 128: c * 1024 + dc * 128 + 128],
                             ctxT[:, c * 512:(c + 1) * 512], start=(c == 0), stop=(c == 5))
            pr.tm.inc(i)
        for idx in range(8):
            kc, dt_ = divmod(idx, 2)
            t.wait_ge(pr.sc.h, sc1 + idx + 7)
            ps = pr.PB if idx % 2 == 0 else pr.PC
            for c in range(6):
                i = t.matmul(ps[:], ctxT[:, c * 512 + kc * 128: c * 512 + kc * 128 + 128],
                             wvb[:, c * 1024 + dt_ * 512: c * 1024 + dt_ * 512 + 512],
                             start=(c == 0), stop=(c == 5))
            pr.tm.inc(i)

    @pr.block.scalar
    def _(s):
        for dc in range(8):
            s.wait_ge(pr.tm.h, tm1 + dc + 1)
            ps = pr.PB if dc % 2 == 0 else pr.PC
            pr.sc.inc(s.copy(khsl[:, dc * 512:(dc + 1) * 512], ps[:]))
        for idx in range(8):
            kc, dt_ = divmod(idx, 2)
            s.wait_ge(pr.tm.h, tm1 + 8 + idx + 1)
            ps = pr.PB if idx % 2 == 0 else pr.PC
            pr.sc.inc(s.copy(vhsl[:, kc * 1024 + dt_ * 512: kc * 1024 + dt_ * 512 + 512], ps[:]))
    kv_evac = pr.sc.v

    @pr.block.sync
    def _(sy):
        sy.wait_ge(pr.sc.h, kv_evac)
        kv2 = kvloc.ap().rearrange("(a p j) -> p a j", p=128, a=2)
        pr.dma.inc(sy.dma_start(out=kv2[:, 0, :], in_=khsl[:]), 16)
        pr.dma.inc(sy.dma_start(out=kv2[:, 1, :], in_=vhsl[:]), 16)
    kv_stored = pr.dma.v

    @pr.block.gpsimd
    def _(g):
        g.wait_ge(pr.dma.h, kv_stored)
        pr.cc.inc(g.collective_compute(
            "AllGather", ALU.bypass,
            replica_groups=[list(range(NCORE))],
            ins=[kvloc.ap().rearrange("(r c) -> r c", r=128).opt()],
            outs=[kvall.ap().rearrange("(r c) -> r c", r=8 * 128).opt()],
        ))
    cc_done = pr.cc.v
    phA.close()
    if upto <= 0:
        return pr.ve.v

    if have_x_natural:
        phX = ExitStack()
        xin = phX.enter_context(nc.sbuf_tensor("b0_xin", [128, 16 * 1024], F32))

        @pr.block.sync
        def _(sy):
            sy.wait_ge(pr.dma.h, kv_stored)
            x_r = io["xin"].ap().rearrange("(c p) d -> p c d", p=128)
            pr.dma.inc(sy.dma_start(out=xin[:].rearrange("p (c d) -> p c d", c=16), in_=x_r), 16)
        x_loaded = pr.dma.v

        tm2, ve2 = pr.tm.v, pr.ve.v

        @pr.block.tensor
        def _(t):
            t.wait_ge(pr.dma.h, x_loaded)
            t.wait_ge(pr.ve.h, ve_id)
            for idx in range(128):
                tc, dc = divmod(idx, DCH)
                if idx >= 2:
                    t.wait_ge(pr.ve.h, ve2 + idx - 1)
                pr.tm.inc(t.transpose(pr.PA[:, 512 * (idx % 2):512 * (idx % 2) + 128],
                                      xin[:, tc * 1024 + dc * 128: tc * 1024 + dc * 128 + 128],
                                      pr.ident_f[:]))

        @pr.block.vector
        def _(v):
            for idx in range(128):
                tc, dc = divmod(idx, DCH)
                v.wait_ge(pr.tm.h, tm2 + idx + 1)
                pr.ve.inc(v.tensor_copy(pr.xT[:, dc * T + tc * 128: dc * T + tc * 128 + 128],
                                        pr.PA[:, 512 * (idx % 2):512 * (idx % 2) + 128]))
        xT_done = pr.ve.v
        phX.close()
    else:
        @pr.block.sync
        def _(sy):
            xi = io["xin"].ap().rearrange("(c p) t -> p c t", p=128)
            pr.dma.inc(sy.dma_start(out=pr.xT[:].rearrange("p (c t) -> p c t", c=DCH), in_=xi), 16)
        x_loaded = pr.dma.v

        @pr.block.vector
        def _(v):
            v.wait_ge(pr.dma.h, x_loaded)
            pr.ve.inc(v.engine_nop())
        xT_done = pr.ve.v

    if upto <= 1:
        return pr.ve.v
    phQ = ExitStack()
    qT = phQ.enter_context(nc.sbuf_tensor("b0_qT", [128, DCH * T], BF16))
    wA = phQ.enter_context(nc.sbuf_tensor("b0_wA", [128, DCH * 1024], BF16))

    ln1_done = pr.ln(pr.xT, pr.xln, qT[:], [(pr.ve, xT_done)])

    if upto <= 2:
        return pr.ve.v
    @pr.block.gpsimd
    def _(g):
        g.wait_ge(pr.ve.h, xT_done)
        g.wait_ge(pr.dmw.h, pr.dmw.v)
        pr.dmw.inc(g.dma_start(out=wA[:].rearrange("p (c m) -> p c m", c=DCH),
                               in_=io["b0_wq"].ap().rearrange("(c p) m -> p c m", p=128)), 16)
    wq_loaded = pr.dmw.v

    tm3, sc3 = pr.tm.v, pr.sc.v

    @pr.block.tensor
    def _(t):
        t.wait_ge(pr.dmw.h, wq_loaded)
        t.wait_ge(pr.ve.h, ln1_done)
        for idx in range(DCH * NTT):
            dc, tt = divmod(idx, NTT)
            if idx >= 2:
                t.wait_ge(pr.sc.h, sc3 + idx - 1)
            for k in range(DCH):
                i = t.matmul(pr.PA[:, 512 * (idx % 2):512 * (idx % 2) + 512],
                             wA[:, k * 1024 + dc * 128: k * 1024 + dc * 128 + 128],
                             pr.xln[:, k * T + tt * 512: k * T + tt * 512 + 512],
                             start=(k == 0), stop=(k == DCH - 1))
            pr.tm.inc(i)

    @pr.block.scalar
    def _(s):
        for idx in range(DCH * NTT):
            dc, tt = divmod(idx, NTT)
            s.wait_ge(pr.tm.h, tm3 + idx + 1)
            pr.sc.inc(s.copy(qT[:, dc * T + tt * 512: dc * T + tt * 512 + 512],
                             pr.PA[:, 512 * (idx % 2):512 * (idx % 2) + 512]))
    q_done = pr.sc.v

    if upto <= 3:
        return pr.ve.v
    khb = [phQ.enter_context(nc.sbuf_tensor(f"b0_khb{i}", [128, KV], BF16)) for i in range(2)]
    vhb = [phQ.enter_context(nc.sbuf_tensor(f"b0_vhb{i}", [128, KV], BF16)) for i in range(2)]
    Ebuf = phQ.enter_context(nc.sbuf_tensor("b0_E", [128, 4 * 512], BF16))
    rsum = phQ.enter_context(nc.sbuf_tensor("b0_rsum", [1, 512], F32))
    oT = pr.xln
    rb32 = pr.rb

    kva = kvall.ap().rearrange("(r a p dc j) -> a r dc p j", r=NCORE, a=2, p=128, dc=8)
    kva_v = kvall.ap().rearrange("(r a p kc dv) -> a r p kc dv", r=NCORE, a=2, p=128, kc=4)

    NIT = H * NTT
    tm_b, sc_b, ve_b = pr.tm.v, pr.sc.v, pr.ve.v
    TPI, SPI, VPI = 33, 17, 2

    @pr.block.gpsimd
    def _(g):
        g.wait_ge(pr.cc.h, cc_done)
        for h in range(H):
            if h >= 2:
                g.wait_ge(pr.tm.h, tm_b + TPI * NTT * (h - 1))
            if h >= 1:
                g.wait_ge(pr.dmk.h, 144 * h)
            pr.dmk.inc(g.dma_start(out=khb[h % 2][:].rearrange("p (r j) -> p r j", r=8),
                                   in_=kva[0, :, h, :, :].rearrange("r p j -> p r j")), 16)
            for r in range(NCORE):
                pr.dmk.inc(g.dma_start(
                    out=vhb[h % 2][:, r * 512:(r + 1) * 512].rearrange("p (kc dv) -> p kc dv", kc=4),
                    in_=kva_v[1, r, :, :, h * 128:(h + 1) * 128]), 16)

    for j in range(NIT):
        h, qt = divmod(j, NTT)
        tmj = tm_b + TPI * j
        scj = sc_b + SPI * j
        vej = ve_b + VPI * j
        kh, vh = khb[h % 2], vhb[h % 2]

        @pr.block.tensor
        def _(t, h=h, qt=qt, tmj=tmj, scj=scj, vej=vej, kh=kh, vh=vh, j=j):
            if qt == 0:
                t.wait_ge(pr.dmk.h, 144 * (h + 1))
                if h == 0:
                    t.wait_ge(pr.sc.h, q_done)
            qsl = qT[:, h * T + qt * 512: h * T + qt * 512 + 512]
            for p in range(16):
                if p >= 1:
                    t.wait_ge(pr.sc.h, scj + p)
                elif j >= 1:
                    t.wait_ge(pr.sc.h, scj - 1)
                t.matmul(pr.PA[:, 0:512], kh[:, (2 * p) * 128:(2 * p) * 128 + 128],
                         qsl, start=True, stop=True)
                pr.tm.inc(t.matmul(pr.PA[:, 512:1024],
                                   kh[:, (2 * p + 1) * 128:(2 * p + 1) * 128 + 128],
                                   qsl, start=True, stop=True))
                t.wait_ge(pr.sc.h, scj + p + 1)
                if p == 0 and j >= 1:
                    t.wait_ge(pr.ve.h, vej)
                for cc_ in (2 * p, 2 * p + 1):
                    t.matmul(pr.PB[:], vh[:, cc_ * 128:cc_ * 128 + 128],
                             Ebuf[:, (cc_ % 4) * 512:(cc_ % 4) * 512 + 512],
                             start=(cc_ == 0), stop=(cc_ == 31))
                    i = t.matmul(pr.PC[0:1, 0:512], pr.ones_bcol[:],
                                 Ebuf[:, (cc_ % 4) * 512:(cc_ % 4) * 512 + 512],
                                 start=(cc_ == 0), stop=(cc_ == 31))
                pr.tm.inc(i)
            t.wait_ge(pr.ve.h, vej + 1)
            pr.tm.inc(t.matmul(pr.PD[:], pr.ones_frow[:], rsum[:], start=True, stop=True))

        @pr.block.scalar
        def _(s, tmj=tmj, scj=scj):
            for p in range(16):
                s.wait_ge(pr.tm.h, tmj + 2 * p + 1)
                pr.sc.inc(s.activation(Ebuf[:, (2 * p % 4) * 512:(2 * p % 4) * 512 + 1024],
                                       pr.PA[:], AF.Exp, scale=SCALE))
            s.wait_ge(pr.tm.h, tmj + TPI)
            pr.sc.inc(s.copy(rb32[:], pr.PD[:]))

        @pr.block.vector
        def _(v, h=h, qt=qt, tmj=tmj, scj=scj):
            v.wait_ge(pr.tm.h, tmj + 32)
            pr.ve.inc(v.reciprocal(rsum[:], pr.PC[0:1, 0:512]))
            v.wait_ge(pr.sc.h, scj + SPI)
            pr.ve.inc(v.tensor_tensor(oT[:, h * T + qt * 512: h * T + qt * 512 + 512],
                                      pr.PB[:], rb32[:], ALU.mult))

    attn_done = pr.ve.v
    assert attn_done == ve_b + VPI * NIT and pr.tm.v == tm_b + TPI * NIT and pr.sc.v == sc_b + SPI * NIT

    if upto <= 4:
        return pr.ve.v
    bob = phQ.enter_context(nc.sbuf_tensor("b0_bob", [128, DCH], F32))
    brow = phQ.enter_context(nc.sbuf_tensor("b0_brow", [1, D], F32))

    @pr.block.sync
    def _(sy):
        sy.wait_ge(pr.dma.h, pr.dma.v)
        pr.dma.inc(sy.dma_start(out=brow[:], in_=io["b0_bo"].ap().rearrange("(a d) -> a d", a=1)), 16)
    brow_loaded = pr.dma.v

    @pr.block.gpsimd
    def _(g):
        g.wait_ge(pr.tm.h, tm_b)
        g.wait_ge(pr.dmw.h, pr.dmw.v)
        pr.dmw.inc(g.dma_start(out=wA[:].rearrange("p (c m) -> p c m", c=DCH),
                               in_=io["b0_wo"].ap().rearrange("(c p) m -> p c m", p=128)), 16)
    wo_loaded = pr.dmw.v

    @pr.block.tensor
    def _(t):
        t.wait_ge(pr.dma.h, brow_loaded)
        t.wait_ge(pr.sc.h, sc_b + SPI * NIT)
        for c in range(DCH):
            i = t.matmul(pr.PD[0:128, 64 + c:64 + c + 1],
                         brow[:, c * 128:(c + 1) * 128], pr.ones_f11[:], start=True, stop=True)
        pr.tm.inc(i)
    bob_mm = pr.tm.v

    @pr.block.vector
    def _(v):
        v.wait_ge(pr.tm.h, bob_mm)
        pr.ve.inc(v.tensor_copy(bob[:], pr.PD[:, 64:64 + DCH]))
    bob_done = pr.ve.v

    tm4, ve4 = pr.tm.v, pr.ve.v

    @pr.block.tensor
    def _(t):
        t.wait_ge(pr.dmw.h, wo_loaded)
        t.wait_ge(pr.ve.h, attn_done)
        for idx in range(DCH * NTT):
            dc, tt = divmod(idx, NTT)
            if idx >= 2:
                t.wait_ge(pr.ve.h, ve4 + idx)
            for k in range(DCH):
                i = t.matmul(pr.PA[:, 512 * (idx % 2):512 * (idx % 2) + 512],
                             wA[:, k * 1024 + dc * 128: k * 1024 + dc * 128 + 128],
                             oT[:, k * T + tt * 512: k * T + tt * 512 + 512],
                             start=(k == 0), stop=(k == DCH - 1))
            pr.tm.inc(i)

    @pr.block.vector
    def _(v):
        v.wait_ge(pr.ve.h, bob_done)
        for idx in range(DCH * NTT):
            dc, tt = divmod(idx, NTT)
            v.wait_ge(pr.tm.h, tm4 + idx + 1)
            sl = slice(dc * T + tt * 512, dc * T + tt * 512 + 512)
            pr.ve.inc(v.scalar_tensor_tensor(
                pr.xT[:, sl], pr.PA[:, 512 * (idx % 2):512 * (idx % 2) + 512],
                bob[:, dc:dc + 1], pr.xT[:, sl], ALU.add, ALU.add))
    phQ.close()
    return pr.ve.v




def emit_load_xt(pr, xt_in):
    @pr.block.sync
    def _(sy):
        sy.wait_ge(pr.dma.h, pr.dma.v)
        xi = xt_in.ap().rearrange("(c p) t -> p c t", p=128)
        pr.dma.inc(sy.dma_start(out=pr.xT[:].rearrange("p (c t) -> p c t", c=DCH), in_=xi), 16)
    x_loaded = pr.dma.v

    @pr.block.vector
    def _(v):
        v.wait_ge(pr.dma.h, x_loaded)
        pr.ve.inc(v.engine_nop())
    return pr.ve.v


def columnize_vec(pr, rows_and_slices, dst, dst_col0=0):
    cols = []
    col = dst_col0
    for row in rows_and_slices:
        m = row.shape[-1]
        nch = m // 128
        cols.append((row, col, nch))
        col += nch
    ncol_total = col - dst_col0

    @pr.block.tensor
    def _(t):
        t.wait_ge(pr.dma.h, pr.dma.v)
        t.wait_ge(pr.dmg.h, pr.dmg.v)
        t.wait_ge(pr.sc.h, pr.sc.v)
        t.wait_ge(pr.ve.h, pr.ve.v)
        for row, c0, nch in cols:
            for c in range(nch):
                i = t.matmul(pr.PD[0:128, c0 - dst_col0 + c:c0 - dst_col0 + c + 1],
                             row[:, c * 128:(c + 1) * 128], pr.ones_f11[:],
                             start=True, stop=True)
        pr.tm.inc(i)
    mm = pr.tm.v

    @pr.block.vector
    def _(v):
        v.wait_ge(pr.tm.h, mm)
        pr.ve.inc(v.tensor_copy(dst[:, dst_col0:dst_col0 + ncol_total],
                                pr.PD[:, 0:ncol_total]))
    return pr.ve.v



def emit_block1(pr, io, xt_ready, upto=99):
    nc = pr.nc
    ph = ExitStack()
    A = ph.enter_context(nc.sbuf_tensor("b1_A", [128, 32 * 512], BF16))
    B = ph.enter_context(nc.sbuf_tensor("b1_B", [128, 32 * 512], BF16))
    w12 = [ph.enter_context(nc.sbuf_tensor(f"b1_w12_{i}", [128, 2 * 8 * 128], BF16)) for i in range(2)]
    wh = [ph.enter_context(nc.sbuf_tensor(f"b1_wh_{i}", [128, 32 * 128], BF16)) for i in range(2)]
    scl = ph.enter_context(nc.sbuf_tensor("b1_scl", [128, 4 * 32 + 8], F32))
    t1f = ph.enter_context(nc.sbuf_tensor("b1_t1", [128, 1024], F32))
    xlq = pr.xln[:, 0:DCH * 512]
    sqq = pr.xln[:, DCH * 512:DCH * 1024]

    phs = ExitStack()
    scrow = phs.enter_context(nc.sbuf_tensor("b1_scrow", [1, 512], F32))
    svecs = [("b1_s1", 4096), ("b1_s2", 4096), ("b1_sh1", 4096), ("b1_sh2", 4096), ("b1_s3", 1024)]
    col0 = 0
    for name, m in svecs:
        for part in range(0, m, 512):
            w_ = min(512, m - part)
            @pr.block.sync
            def _(sy, name=name, part=part, w_=w_):
                sy.wait_ge(pr.dma.h, pr.dma.v)
                sy.wait_ge(pr.ve.h, pr.ve.v)
                pr.dma.inc(sy.dma_start(out=scrow[:, 0:w_],
                                        in_=io[name].ap().rearrange("(a d) -> a d", a=1)[:, part:part + w_]), 16)
            columnize_vec(pr, [scrow[:, 0:w_]], scl, col0)
            col0 += w_ // 128
    phs.close()
    scl_done = pr.ve.v
    S1C, S2C, SH1C, SH2C, S3C = 0, 32, 64, 96, 128

    NQ = 4
    prev_q_done = xt_ready
    for q in range(NQ):
        ln_done = pr.ln(pr.xT, xlq, sqq, [(pr.ve, prev_q_done)], tts=[q], dst_stride=512)

        dmw0 = pr.dmw.v
        tm0 = pr.tm.v
        sc0 = pr.sc.v
        ve0 = pr.ve.v

        @pr.block.gpsimd
        def _(g, q=q, tm0=tm0):
            for oc in range(32):
                g.wait_ge(pr.dmw.h, pr.dmw.v)
                if oc >= 2:
                    g.wait_ge(pr.tm.h, tm0 + oc - 1)
                w1v = io["b1_w1t"].ap()[:, oc * 128:(oc + 1) * 128].rearrange("(kc p) j -> p kc j", p=128)
                w2v = io["b1_w2t"].ap()[:, oc * 128:(oc + 1) * 128].rearrange("(kc p) j -> p kc j", p=128)
                buf = w12[oc % 2]
                pr.dmw.inc(g.dma_start(out=buf[:, 0:1024].rearrange("p (kc j) -> p kc j", kc=8), in_=w1v), 16)
                g.wait_ge(pr.dmw.h, pr.dmw.v)
                pr.dmw.inc(g.dma_start(out=buf[:, 1024:2048].rearrange("p (kc j) -> p kc j", kc=8), in_=w2v), 16)

        @pr.block.tensor
        def _(t, q=q, tm0=tm0, sc0=sc0, ve0=ve0, dmw0=dmw0):
            t.wait_ge(pr.ve.h, ln_done)
            t.wait_ge(pr.ve.h, scl_done)
            t.wait_ge(pr.ve.h, ve0)
            for oc in range(32):
                t.wait_ge(pr.dmw.h, dmw0 + 32 * (oc + 1))
                if oc >= 2:
                    t.wait_ge(pr.sc.h, sc0 + oc - 1)
                    t.wait_ge(pr.ve.h, ve0 + oc - 1)
                buf = w12[oc % 2]
                psA = pr.PA[:, 512 * (oc % 2):512 * (oc % 2) + 512]
                psB = pr.PB[:] if oc % 2 == 0 else pr.PC[:]
                for k in range(8):
                    t.matmul(psA, buf[:, k * 128:(k + 1) * 128],
                             xlq[:, k * 512:(k + 1) * 512],
                             start=(k == 0), stop=(k == 7))
                for k in range(8):
                    i = t.matmul(psB, buf[:, 1024 + k * 128:1024 + k * 128 + 128],
                                 xlq[:, k * 512:(k + 1) * 512],
                                 start=(k == 0), stop=(k == 7))
                pr.tm.inc(i)

        @pr.block.scalar
        def _(s, tm0=tm0, ve0=ve0):
            for oc in range(32):
                s.wait_ge(pr.tm.h, tm0 + oc + 1)
                if oc >= 2:
                    s.wait_ge(pr.ve.h, ve0 + oc - 1)
                psA = pr.PA[:, 512 * (oc % 2):512 * (oc % 2) + 512]
                psB = pr.PB[:] if oc % 2 == 0 else pr.PC[:]
                h_ = slice(512 * (oc % 2), 512 * (oc % 2) + 512)
                pr.sc.inc(s.activation(t1f[:, h_], psA, AF.Silu, scale=scl[:, S1C + oc:S1C + oc + 1]))

        @pr.block.vector
        def _(v, sc0=sc0, ve0=ve0, q=q):
            for oc in range(32):
                v.wait_ge(pr.sc.h, sc0 + oc + 1)
                h_ = slice(512 * (oc % 2), 512 * (oc % 2) + 512)
                psB = pr.PB[:] if oc % 2 == 0 else pr.PC[:]
                pr.ve.inc(v.scalar_tensor_tensor(A[:, oc * 512:(oc + 1) * 512], psB,
                                                 scl[:, S2C + oc:S2C + oc + 1], t1f[:, h_],
                                                 ALU.mult, ALU.mult))
        st1_ve = pr.ve.v

        prev_guard = (pr.ve, st1_ve)
        for stage, (wname, scol, src, dstb) in enumerate(
                [("b1_h1t", SH1C, A, B), ("b1_h2t", SH2C, B, A)]):
            dmw1 = pr.dmw.v
            tm1 = pr.tm.v
            sc1 = pr.sc.v

            @pr.block.gpsimd
            def _(g, wname=wname, tm1=tm1):
                for oc in range(32):
                    wv = io[wname].ap()[:, oc * 128:(oc + 1) * 128]
                    buf = wh[oc % 2]
                    if oc >= 2:
                        g.wait_ge(pr.tm.h, tm1 + oc - 1)
                    else:
                        g.wait_ge(pr.tm.h, tm1)
                    for half in range(2):
                        g.wait_ge(pr.dmw.h, pr.dmw.v)
                        pr.dmw.inc(g.dma_start(
                            out=buf[:, half * 2048:(half + 1) * 2048].rearrange("p (kc j) -> p kc j", kc=16),
                            in_=wv[half * 2048:(half + 1) * 2048, :].rearrange("(kc p) j -> p kc j", p=128)), 16)

            @pr.block.tensor
            def _(t, tm1=tm1, sc1=sc1, dmw1=dmw1, src=src, prev_guard=prev_guard, stage=stage):
                t.wait_ge(prev_guard[0].h, prev_guard[1])
                for oc in range(32):
                    t.wait_ge(pr.dmw.h, dmw1 + 32 * (oc + 1))
                    if oc >= 2:
                        t.wait_ge(pr.sc.h, sc1 + oc - 1)
                    buf = wh[oc % 2]
                    ps = pr.PA[:, 512 * (oc % 2):512 * (oc % 2) + 512]
                    for k in range(32):
                        i = t.matmul(ps, buf[:, k * 128:(k + 1) * 128],
                                     src[:, k * 512:(k + 1) * 512],
                                     start=(k == 0), stop=(k == 31))
                    pr.tm.inc(i)

            @pr.block.scalar
            def _(s, tm1=tm1, scol=scol, dstb=dstb):
                for oc in range(32):
                    s.wait_ge(pr.tm.h, tm1 + oc + 1)
                    ps = pr.PA[:, 512 * (oc % 2):512 * (oc % 2) + 512]
                    pr.sc.inc(s.activation(dstb[:, oc * 512:(oc + 1) * 512], ps, AF.Gelu,
                                           scale=scl[:, scol + oc:scol + oc + 1]))
            prev_guard = (pr.sc, pr.sc.v)

        st3_sc = pr.sc.v

        dmw2 = pr.dmw.v
        tm2g = pr.tm.v
        @pr.block.gpsimd
        def _(g, tm2g=tm2g):
            for oc in range(8):
                wv = io["b1_w3t"].ap()[:, oc * 128:(oc + 1) * 128]
                buf = wh[oc % 2]
                if oc >= 2:
                    g.wait_ge(pr.tm.h, tm2g + oc - 1)
                else:
                    g.wait_ge(pr.tm.h, tm2g)
                for half in range(2):
                    g.wait_ge(pr.dmw.h, pr.dmw.v)
                    pr.dmw.inc(g.dma_start(
                        out=buf[:, half * 2048:(half + 1) * 2048].rearrange("p (kc j) -> p kc j", kc=16),
                        in_=wv[half * 2048:(half + 1) * 2048, :].rearrange("(kc p) j -> p kc j", p=128)), 16)

        tm2 = pr.tm.v
        ve2 = pr.ve.v

        @pr.block.tensor
        def _(t, tm2=tm2, dmw2=dmw2, ve2=ve2, st3_sc=st3_sc):
            t.wait_ge(pr.sc.h, st3_sc)
            for oc in range(8):
                t.wait_ge(pr.dmw.h, dmw2 + 32 * (oc + 1))
                if oc >= 2:
                    t.wait_ge(pr.ve.h, ve2 + oc - 1)
                buf = wh[oc % 2]
                ps = pr.PA[:, 512 * (oc % 2):512 * (oc % 2) + 512]
                for k in range(32):
                    i = t.matmul(ps, buf[:, k * 128:(k + 1) * 128],
                                 A[:, k * 512:(k + 1) * 512],
                                 start=(k == 0), stop=(k == 31))
                pr.tm.inc(i)

        @pr.block.vector
        def _(v, tm2=tm2, q=q):
            for oc in range(8):
                v.wait_ge(pr.tm.h, tm2 + oc + 1)
                ps = pr.PA[:, 512 * (oc % 2):512 * (oc % 2) + 512]
                sl = slice(oc * T + q * 512, oc * T + q * 512 + 512)
                pr.ve.inc(v.scalar_tensor_tensor(pr.xT[:, sl], ps,
                                                 scl[:, S3C + oc:S3C + oc + 1],
                                                 pr.xT[:, sl], ALU.mult, ALU.add))
        prev_q_done = pr.ve.v
    ph.close()
    return pr.ve.v


def build_b1():
    nc = bass.Bass("TRN2", target_bir_lowering=False, debug=False)
    io = {
        "xin": nc.dram_tensor("xin", [D, T], F32, kind="ExternalInput"),
        "b1_w1t": nc.dram_tensor("b1_w1t", [1024, 4096], F32, kind="ExternalInput"),
        "b1_w2t": nc.dram_tensor("b1_w2t", [1024, 4096], F32, kind="ExternalInput"),
        "b1_h1t": nc.dram_tensor("b1_h1t", [4096, 4096], F32, kind="ExternalInput"),
        "b1_h2t": nc.dram_tensor("b1_h2t", [4096, 4096], F32, kind="ExternalInput"),
        "b1_w3t": nc.dram_tensor("b1_w3t", [4096, 1024], F32, kind="ExternalInput"),
        "b1_s1": nc.dram_tensor("b1_s1", [4096], F32, kind="ExternalInput"),
        "b1_s2": nc.dram_tensor("b1_s2", [4096], F32, kind="ExternalInput"),
        "b1_sh1": nc.dram_tensor("b1_sh1", [4096], F32, kind="ExternalInput"),
        "b1_sh2": nc.dram_tensor("b1_sh2", [4096], F32, kind="ExternalInput"),
        "b1_s3": nc.dram_tensor("b1_s3", [1024], F32, kind="ExternalInput"),
    }
    xt_out = nc.dram_tensor("xt_out", [D, T], F32, kind="ExternalOutput")
    with ExitStack() as ctx:
        pr = Prog(nc, ctx, xln_cols=DCH * 1024)
        pr.emit_consts()
        xt_ready = emit_load_xt(pr, io["xin"])
        emit_block1(pr, io, xt_ready)
        emit_store_xt(pr, xt_out)
    return nc







def emit_quant_tiles(pr, tiles, scale_ap, inv_ap, zt, it_, dt_, s1t, s2t,
                     width, pre_wait_v=None):
    ve0 = pr.ve.v
    sc0 = pr.sc.v
    n = len(tiles)

    @pr.block.vector
    def _(v):
        if pre_wait_v:
            for sem, val in pre_wait_v:
                v.wait_ge(sem.h, val)
        for j, (src, dst, waits) in enumerate(tiles):
            for sem, val in waits:
                v.wait_ge(sem.h, val)
            v.tensor_scalar(zt[:, 0:width], src, scale_ap, None, ALU.mult)
            v.drain()
            v.tensor_copy(it_[:, 0:width], zt[:, 0:width])
            v.drain()
            pr.ve.inc(v.tensor_tensor(dt_[:, 0:width], zt[:, 0:width], it_[:, 0:width], ALU.subtract))
            v.wait_ge(pr.sc.h, sc0 + j + 1)
            v.scalar_tensor_tensor(zt[:, 0:width], s1t[:, 0:width], 0.5, it_[:, 0:width], ALU.mult, ALU.add)
            v.drain()
            v.scalar_tensor_tensor(zt[:, 0:width], s2t[:, 0:width], -0.5, zt[:, 0:width], ALU.mult, ALU.add)
            v.drain()
            pr.ve.inc(v.tensor_scalar(dst, zt[:, 0:width], inv_ap, None, ALU.mult))
            v.drain()

    @pr.block.scalar
    def _(s):
        for j in range(n):
            s.wait_ge(pr.ve.h, ve0 + 2 * j + 1)
            s.activation(s1t[:, 0:width], dt_[:, 0:width], AF.Sign, bias=pr.neg_half[:])
            s.drain()
            pr.sc.inc(s.activation(s2t[:, 0:width], dt_[:, 0:width], AF.Sign, scale=-1.0, bias=pr.neg_half[:]))


def emit_wq_stream(pr, pr_io_name, wst, dst_writer, zt, it_, dt_, s1t, s2t, qsc,
                   gp_guard, io, wqb=None, transpose_dst=None):
    dmw0 = pr.dmw.v
    ve0 = pr.ve.v
    sc0 = pr.sc.v
    tmT0 = pr.tm.v
    TR = transpose_dst is not None
    VPC = 10 if TR else 2

    @pr.block.gpsimd
    def _(g):
        for kc in range(32):
            g.wait_ge(pr.dmw.h, pr.dmw.v)
            for gg in gp_guard(kc) or []:
                g.wait_ge(gg[0].h, gg[1])
            if TR and kc >= 2:
                g.wait_ge(pr.tm.h, tmT0 + 8 * (kc - 1))
            pr.dmw.inc(g.dma_start(out=wst[kc % 2][:],
                                   in_=io[pr_io_name].ap()[kc * 128:(kc + 1) * 128, :]), 16)

    def quant_d(kc):
        if not TR or kc == 0:
            return ve0 + VPC * kc + 1 if not TR else ve0 + 1
        return ve0 + 10 * kc - 7

    def quant_f(kc):
        if not TR or kc == 0:
            return ve0 + VPC * kc + 2 if not TR else ve0 + 2
        return ve0 + 10 * kc - 6

    def evac_val(j, dc):
        if j <= 30:
            return ve0 + 10 * j + 5 + dc
        return ve0 + 313 + dc

    @pr.block.vector
    def _(v):
        for kc in range(32):
            v.wait_ge(pr.dmw.h, dmw0 + 16 * (kc + 1))
            w = wst[kc % 2]
            v.tensor_reduce(qsc[:, 0:1], w[:], AX.X, ALU.max, apply_absolute_value=True)
            v.drain()
            v.tensor_scalar(qsc[:, 0:1], qsc[:, 0:1], 1e-8, None, ALU.max)
            v.drain()
            v.reciprocal(qsc[:, 1:2], qsc[:, 0:1])
            v.drain()
            v.tensor_scalar(qsc[:, 2:3], qsc[:, 1:2], 127.0, None, ALU.mult)
            v.drain()
            v.reciprocal(qsc[:, 3:4], qsc[:, 2:3])
            v.drain()
            v.tensor_scalar(zt[:], w[:], qsc[:, 2:3], None, ALU.mult)
            v.drain()
            v.tensor_copy(it_[:], zt[:])
            v.drain()
            pr.ve.inc(v.tensor_tensor(dt_[:], zt[:], it_[:], ALU.subtract))
            v.wait_ge(pr.sc.h, sc0 + kc + 1)
            v.scalar_tensor_tensor(zt[:], s1t[:], 0.5, it_[:], ALU.mult, ALU.add)
            v.drain()
            v.scalar_tensor_tensor(zt[:], s2t[:], -0.5, zt[:], ALU.mult, ALU.add)
            v.drain()
            dst = wqb[kc % 2][:] if TR else dst_writer(kc)
            pr.ve.inc(v.tensor_scalar(dst, zt[:], qsc[:, 3:4], None, ALU.mult))
            v.drain()
            if TR and kc >= 1:
                for dc in range(8):
                    v.wait_ge(pr.tm.h, tmT0 + (kc - 1) * 8 + dc + 1)
                    pt = pr.PT1 if ((kc - 1) * 8 + dc) % 2 == 0 else pr.PT2
                    pr.ve.inc(v.tensor_copy(transpose_dst(kc - 1, dc), pt[:, 0:128]))
        if TR:
            for dc in range(8):
                v.wait_ge(pr.tm.h, tmT0 + 31 * 8 + dc + 1)
                pt = pr.PT1 if (31 * 8 + dc) % 2 == 0 else pr.PT2
                pr.ve.inc(v.tensor_copy(transpose_dst(31, dc), pt[:, 0:128]))

    @pr.block.scalar
    def _(s):
        for kc in range(32):
            s.wait_ge(pr.ve.h, quant_d(kc))
            s.activation(s1t[:], dt_[:], AF.Sign, bias=pr.neg_half[:])
            s.drain()
            pr.sc.inc(s.activation(s2t[:], dt_[:], AF.Sign, scale=-1.0, bias=pr.neg_half[:]))

    if TR:
        @pr.block.tensor
        def _(t):
            for kc in range(32):
                t.wait_ge(pr.ve.h, quant_f(kc))
                for dc in range(8):
                    idx = kc * 8 + dc
                    if idx >= 2:
                        j2, dc2 = divmod(idx - 2, 8)
                        t.wait_ge(pr.ve.h, evac_val(j2, dc2))
                    pt = pr.PT1 if idx % 2 == 0 else pr.PT2
                    pr.tm.inc(t.transpose(pt[:, 0:128],
                                          wqb[kc % 2][:, dc * 128:(dc + 1) * 128],
                                          pr.ident_b[:]))
    return pr.ve.v


def emit_global_amax(pr, tile, cc_in, cc_out, dst_col, dst_inv_col, qlevels, bscr):
    ve0 = pr.ve.v

    @pr.block.vector
    def _(v):
        pr.ve.inc(v.engine_nop())
    tree_done = pr.ve.v

    @pr.block.sync
    def _(sy):
        sy.wait_ge(pr.ve.h, tree_done)
        sy.wait_ge(pr.dma.h, pr.dma.v)
        pr.dma.inc(sy.dma_start(out=bscr.ap().rearrange("(a b) -> a b", a=128)[:, 0:1],
                                in_=tile[:, 0:1]), 16)
        sy.wait_ge(pr.dma.h, pr.dma.v)
        pr.dma.inc(sy.dma_start(out=pr.srow32[:],
                                in_=bscr.ap().rearrange("(a b) -> a b", a=1)[:, 0:128]), 16)
    ld1 = pr.dma.v

    @pr.block.vector
    def _(v):
        v.wait_ge(pr.dma.h, ld1)
        v.tensor_reduce(pr.s11[:], pr.srow32[:], AX.X, ALU.max)
        v.drain()
        pr.ve.inc(v.tensor_scalar(pr.s11[:], pr.s11[:], 1e-8, None, ALU.max))
    amax_local = pr.ve.v

    @pr.block.gpsimd
    def _(g):
        pr.gp.inc(g.memset(pr.zrow8[:], 0.0))
    zr = pr.gp.v

    @pr.block.sync
    def _(sy):
        sy.wait_ge(pr.ve.h, amax_local)
        sy.wait_ge(pr.gp.h, zr)
        sy.wait_ge(pr.dma.h, pr.dma.v)
        pr.dma.inc(sy.dma_start(out=cc_in.ap(), in_=pr.zrow8[:]), 16)
        sy.wait_ge(pr.dma.h, pr.dma.v)
        pr.dma.inc(sy.dma_start(out=cc_in.ap()[0:1, 0:1], in_=pr.s11[:]), 16)
    ccst = pr.dma.v

    @pr.block.gpsimd
    def _(g):
        g.wait_ge(pr.dma.h, ccst)
        pr.cc.inc(g.collective_compute(
            "AllReduce", ALU.max, replica_groups=[list(range(NCORE))],
            ins=[cc_in.ap().opt()], outs=[cc_out.ap().opt()]))
    ccd = pr.cc.v

    @pr.block.sync
    def _(sy):
        sy.wait_ge(pr.cc.h, ccd)
        sy.wait_ge(pr.dma.h, pr.dma.v)
        pr.dma.inc(sy.dma_start(out=pr.s11[:], in_=cc_out.ap()[0:1, 0:1]), 16)
    ld2 = pr.dma.v

    @pr.block.vector
    def _(v):
        v.wait_ge(pr.dma.h, ld2)
        v.reciprocal(pr.s12[:], pr.s11[:])
        v.drain()
        v.tensor_scalar(pr.s12[:], pr.s12[:], qlevels, None, ALU.mult)
        v.drain()
        v.reciprocal(pr.s13[:], pr.s12[:])
        v.drain()
        v.tensor_copy(pr.s14[:, 0:1], pr.s12[:])
        v.drain()
        pr.ve.inc(v.tensor_copy(pr.s14[:, 1:2], pr.s13[:]))
    row2 = pr.ve.v

    @pr.block.tensor
    def _(t):
        t.wait_ge(pr.ve.h, row2)
        pr.tm.inc(t.matmul(pr.PD[0:128, 0:2], pr.ones_frow[:], pr.s14[:, 0:2],
                           start=True, stop=True))
    bmm = pr.tm.v

    @pr.block.vector
    def _(v):
        v.wait_ge(pr.tm.h, bmm)
        v.tensor_copy(dst_col, pr.PD[:, 0:1])
        v.drain()
        pr.ve.inc(v.tensor_copy(dst_inv_col, pr.PD[:, 1:2]))
    return pr.ve.v


def emit_block3(pr, io, xt_ready):
    nc = pr.nc
    ph = ExitStack()
    W_q = ph.enter_context(nc.sbuf_tensor("b3_wq", [128, 32 * 1024], BF16))
    zt = ph.enter_context(nc.sbuf_tensor("b3_zt", [128, 1024], F32))
    it_ = ph.enter_context(nc.sbuf_tensor("b3_it", [128, 1024], I32))
    dt_ = ph.enter_context(nc.sbuf_tensor("b3_dt", [128, 1024], F32))
    s1t = ph.enter_context(nc.sbuf_tensor("b3_s1", [128, 1024], BF16))
    s2t = ph.enter_context(nc.sbuf_tensor("b3_s2", [128, 1024], BF16))
    binc = ph.enter_context(nc.sbuf_tensor("b3_binc", [128, 44], F32))
    qsc = ph.enter_context(nc.sbuf_tensor("b3_qsc", [128, 4], F32))
    xmax = ph.enter_context(nc.sbuf_tensor("b3_xmax", [128, 4], F32))
    hmax = ph.enter_context(nc.sbuf_tensor("b3_hmax", [128, 4], F32))

    hid = nc.dram_tensor("b3_hid", [32, 128, T], F32, kind="Internal")
    bscr = nc.dram_tensor("b3_bscr", [128], F32, kind="Internal")
    ccxi = nc.dram_tensor("b3_ccxi", [1, 8], F32, kind="Internal")
    ccxo = nc.dram_tensor("b3_ccxo", [1, 8], F32, kind="Internal")
    cchi = nc.dram_tensor("b3_cchi", [1, 8], F32, kind="Internal")
    ccho = nc.dram_tensor("b3_ccho", [1, 8], F32, kind="Internal")
    SXC, SIXC, SHC, SIHC = 40, 41, 42, 43

    @pr.block.gpsimd
    def _(g):
        g.memset(xmax[:], 0.0)
        pr.gp.inc(g.memset(hmax[:], 0.0))
    mm0 = pr.gp.v

    phW1 = ExitStack()
    wst = [phW1.enter_context(nc.sbuf_tensor(f"b3_wst{i}", [128, 1024], F32)) for i in range(2)]
    wqb = [phW1.enter_context(nc.sbuf_tensor(f"b3_wqb{i}", [128, 1024], BF16)) for i in range(2)]

    winq_done = emit_wq_stream(
        pr, "b3_wint", wst, None, zt, it_, dt_, s1t, s2t, qsc,
        lambda kc: [(pr.ve, xt_ready)] if kc == 0 else None, io,
        wqb=wqb,
        transpose_dst=lambda kc, dc: W_q[:, dc * 4096 + kc * 128: dc * 4096 + kc * 128 + 128])
    phW1.close()
    phI = ExitStack()
    xf32 = phI.enter_context(nc.sbuf_tensor("b3_xf32", [128, DCH * 512], F32))
    xqb = phI.enter_context(nc.sbuf_tensor("b3_xqb", [128, DCH * 512], BF16))
    stg = phI.enter_context(nc.sbuf_tensor("b3_stg", [128, 2 * 512], F32))

    for nm, c0, m in [("b3_bin", 0, 4096), ("b3_bout", 32, 1024)]:
        for part in range(0, m, 512):
            @pr.block.sync
            def _(sy, nm=nm, part=part):
                sy.wait_ge(pr.dma.h, pr.dma.v)
                sy.wait_ge(pr.ve.h, pr.ve.v)
                pr.dma.inc(sy.dma_start(out=pr.srow512[:],
                                        in_=io[nm].ap().rearrange("(a d) -> a d", a=1)[:, part:part + 512]), 16)
            columnize_vec(pr, [pr.srow512[:]], binc, c0 + part // 128)

    for q in range(NTT):
        lnd = pr.ln(pr.xT, pr.xln[:, 0:DCH * 512], pr.xln[:, DCH * 512:DCH * 1024],
                    [(pr.ve, xt_ready if q == 0 else pr.ve.v)], tts=[q], dst_stride=512,
                    dst_f32=xf32)

        @pr.block.vector
        def _(v, lnd=lnd):
            v.wait_ge(pr.ve.h, lnd)
            v.wait_ge(pr.gp.h, mm0)
            v.tensor_reduce(xmax[:, 1:2], xf32[:], AX.X, ALU.max, apply_absolute_value=True)
            v.drain()
            pr.ve.inc(v.tensor_tensor(xmax[:, 0:1], xmax[:, 0:1], xmax[:, 1:2], ALU.max))
            v.drain()

    emit_global_amax(pr, xmax[:, 0:4], ccxi, ccxo, binc[:, SXC:SXC + 1],
                     binc[:, SIXC:SIXC + 1], 32767.0, bscr)
    sx_ready = pr.ve.v

    for q in range(NTT):
        lnd = pr.ln(pr.xT, pr.xln[:, 0:DCH * 512], pr.xln[:, DCH * 512:DCH * 1024],
                    [(pr.ve, pr.ve.v)], tts=[q], dst_stride=512, dst_f32=xf32)
        tiles = [(xf32[:, sl * 512:(sl + 1) * 512], xqb[:, sl * 512:(sl + 1) * 512], [])
                 for sl in range(8)]
        emit_quant_tiles(pr, tiles, binc[:, SXC:SXC + 1], binc[:, SIXC:SIXC + 1],
                         zt, it_, dt_, s1t, s2t, 512,
                         pre_wait_v=[(pr.ve, lnd), (pr.ve, sx_ready)])
        xq_done = pr.ve.v

        tmS = pr.tm.v
        scS = pr.sc.v
        veS = pr.ve.v
        dmaS = pr.dma.v

        @pr.block.tensor
        def _(t, tmS=tmS, scS=scS, xq_done=xq_done):
            t.wait_ge(pr.ve.h, xq_done)
            t.wait_ge(pr.ve.h, winq_done)
            for oc in range(32):
                if oc >= 2:
                    t.wait_ge(pr.sc.h, scS + oc - 1)
                ps = pr.PA[:, 512 * (oc % 2):512 * (oc % 2) + 512]
                for k in range(8):
                    i = t.matmul(ps, W_q[:, k * 4096 + oc * 128: k * 4096 + oc * 128 + 128],
                                 xqb[:, k * 512:(k + 1) * 512], start=(k == 0), stop=(k == 7))
                pr.tm.inc(i)

        @pr.block.scalar
        def _(s, tmS=tmS, veS=veS, dmaS=dmaS):
            for oc in range(32):
                s.wait_ge(pr.tm.h, tmS + oc + 1)
                if oc >= 2:
                    s.wait_ge(pr.ve.h, veS + 2 * (oc - 1))
                    s.wait_ge(pr.dma.h, dmaS + 16 * (oc - 1))
                else:
                    s.wait_ge(pr.dma.h, dmaS)
                ps = pr.PA[:, 512 * (oc % 2):512 * (oc % 2) + 512]
                pr.sc.inc(s.activation(stg[:, 512 * (oc % 2):512 * (oc % 2) + 512], ps,
                                       AF.Gelu, bias=binc[:, oc:oc + 1]))

        @pr.block.vector
        def _(v, scS=scS):
            for oc in range(32):
                v.wait_ge(pr.sc.h, scS + oc + 1)
                sgl = stg[:, 512 * (oc % 2):512 * (oc % 2) + 512]
                v.tensor_reduce(hmax[:, 1:2], sgl, AX.X, ALU.max, apply_absolute_value=True)
                v.drain()
                v.tensor_tensor(hmax[:, 0:1], hmax[:, 0:1], hmax[:, 1:2], ALU.max)
                v.drain()
                pr.ve.inc(v.engine_nop(), 2)

        @pr.block.sync
        def _(sy, q=q, veS=veS, dmaS=dmaS):
            for oc in range(32):
                sy.wait_ge(pr.ve.h, veS + 2 * (oc + 1))
                sy.wait_ge(pr.dma.h, pr.dma.v)
                pr.dma.inc(sy.dma_start(out=hid.ap()[oc, :, q * 512:(q + 1) * 512],
                                        in_=stg[:, 512 * (oc % 2):512 * (oc % 2) + 512]), 16)
    stageI_tm = pr.tm.v
    stageI_dma = pr.dma.v
    phI.close()
    phW2 = ExitStack()
    wst = [phW2.enter_context(nc.sbuf_tensor(f"b3_wst2{i}", [128, 1024], F32)) for i in range(2)]

    emit_global_amax(pr, hmax[:, 0:4], cchi, ccho, binc[:, SHC:SHC + 1],
                     binc[:, SIHC:SIHC + 1], 32767.0, bscr)
    sh_ready = pr.ve.v

    ve_w1 = pr.ve.v
    emit_wq_stream(
        pr, "b3_wout", wst, lambda kc: W_q[:, kc * 1024:(kc + 1) * 1024],
        zt, it_, dt_, s1t, s2t, qsc,
        lambda kc: ([(pr.tm, stageI_tm), (pr.dma, stageI_dma)] if kc < 2
                    else [(pr.ve, ve_w1 + 2 * (kc - 1))]), io)
    woutq_done = pr.ve.v
    phW2.close()

    phII = ExitStack()
    hld = [phII.enter_context(nc.sbuf_tensor(f"b3_hld{i}", [128, 512], F32)) for i in range(2)]
    hqb = [phII.enter_context(nc.sbuf_tensor(f"b3_hqb{i}", [128, 512], BF16)) for i in range(2)]

    for q in range(NTT):
        dmaQ = pr.dma.v
        veQ = pr.ve.v
        tmQ = pr.tm.v

        @pr.block.sync
        def _(sy, q=q, veQ=veQ):
            if q == 0:
                sy.wait_ge(pr.ve.h, woutq_done)
            else:
                sy.wait_ge(pr.ve.h, veQ)
            for k in range(32):
                sy.wait_ge(pr.dma.h, pr.dma.v)
                if k >= 2:
                    sy.wait_ge(pr.ve.h, veQ + 2 * (k - 1))
                pr.dma.inc(sy.dma_start(out=hld[k % 2][:],
                                        in_=hid.ap()[k, :, q * 512:(q + 1) * 512]), 16)

        tiles = [(hld[k % 2][:], hqb[k % 2][:],
                  [(pr.dma, dmaQ + 16 * (k + 1))] + ([(pr.tm, tmQ + k - 1)] if k >= 2 else []))
                 for k in range(32)]
        emit_quant_tiles(pr, tiles, binc[:, SHC:SHC + 1], binc[:, SIHC:SIHC + 1],
                         zt, it_, dt_, s1t, s2t, 512,
                         pre_wait_v=([(pr.ve, sh_ready), (pr.ve, woutq_done)] if q == 0 else None))

        banks = [pr.PA[:, 0:512], pr.PA[:, 512:1024], pr.PB[:], pr.PC[:], pr.PD[:],
                 pr.PF[:], pr.PT1.bitcast(F32)[:, 0:512], pr.PT2.bitcast(F32)[:, 0:512]]

        @pr.block.tensor
        def _(t, q=q, veQ=veQ, banks=banks):
            if q == 0:
                t.wait_ge(pr.ve.h, woutq_done)
            else:
                t.wait_ge(pr.ve.h, veQ)
            for k in range(32):
                t.wait_ge(pr.ve.h, veQ + 2 * (k + 1))
                for dc in range(8):
                    i = t.matmul(banks[dc], W_q[:, k * 1024 + dc * 128: k * 1024 + dc * 128 + 128],
                                 hqb[k % 2][:], start=(k == 0), stop=(k == 31))
                pr.tm.inc(i)

        tmE = pr.tm.v

        @pr.block.vector
        def _(v, q=q, tmE=tmE, banks=banks):
            v.wait_ge(pr.tm.h, tmE)
            for dc in range(8):
                sl = slice(dc * T + q * 512, dc * T + q * 512 + 512)
                i = v.scalar_tensor_tensor(pr.xT[:, sl], banks[dc], binc[:, 32 + dc:32 + dc + 1],
                                           pr.xT[:, sl], ALU.add, ALU.add)
                v.drain()
            pr.ve.inc(i)
    phII.close()
    ph.close()
    return pr.ve.v


def build_b3():
    nc = bass.Bass("TRN2", target_bir_lowering=False, debug=False)
    io = {
        "xin": nc.dram_tensor("xin", [D, T], F32, kind="ExternalInput"),
        "b3_wint": nc.dram_tensor("b3_wint", [4096, 1024], F32, kind="ExternalInput"),
        "b3_wout": nc.dram_tensor("b3_wout", [4096, 1024], F32, kind="ExternalInput"),
        "b3_bin": nc.dram_tensor("b3_bin", [4096], F32, kind="ExternalInput"),
        "b3_bout": nc.dram_tensor("b3_bout", [1024], F32, kind="ExternalInput"),
    }
    xt_out = nc.dram_tensor("xt_out", [D, T], F32, kind="ExternalOutput")
    with ExitStack() as ctx:
        pr = Prog(nc, ctx, xln_cols=DCH * 1024, with_trimask=False)
        pr.emit_consts()
        xt_ready = emit_load_xt(pr, io["xin"])
        emit_block3(pr, io, xt_ready)
        emit_store_xt(pr, xt_out)
    return nc





def emit_block2(pr, io, xt_ready):
    nc = pr.nc
    NIT_BIS = 33
    ph = ExitStack()
    qn = ph.enter_context(nc.sbuf_tensor("b2_qn", [128, 4096], BF16))
    kn = ph.enter_context(nc.sbuf_tensor("b2_kn", [128, 4096], BF16))
    vs = ph.enter_context(nc.sbuf_tensor("b2_vs", [128, 4096], BF16))
    rS2 = ph.enter_context(nc.sbuf_tensor("b2_rs2", [128, 4096], F32))
    wbuf = [ph.enter_context(nc.sbuf_tensor("b2_w0", [128, 8192], BF16))] * 2
    oT = ph.enter_context(nc.sbuf_tensor("b2_oT", [128, 4096], BF16))
    sqs = ph.enter_context(nc.sbuf_tensor("b2_sqs", [128, 4096], BF16))
    msk = ph.enter_context(nc.sbuf_tensor("b2_msk", [128, 1024], BF16))
    mmt = ph.enter_context(nc.sbuf_tensor("b2_mmt", [128, 4096], BF16))
    sgm = mmt
    bis = ph.enter_context(nc.sbuf_tensor("b2_bis", [128, 256], F32))
    selu = ph.enter_context(nc.sbuf_tensor("b2_selu", [128, 64], mybir.dt.uint8))
    S32 = ph.enter_context(nc.sbuf_tensor("b2_S32", [128, 1024], F32))
    Sbf = ph.enter_context(nc.sbuf_tensor("b2_Sbf", [128, 1024], BF16))
    fh8 = ph.enter_context(nc.sbuf_tensor("b2_fh8", [8, 512], F32))
    rows = ph.enter_context(nc.sbuf_tensor("b2_rows", [1, 1024], F32))
    wab = ph.enter_context(nc.sbuf_tensor("b2_wab", [128, 64], BF16))
    nrm = ph.enter_context(nc.sbuf_tensor("b2_nrm", [128, 10], F32))
    ab8 = ph.enter_context(nc.sbuf_tensor("b2_ab8", [1, 16], F32))
    dtb = ph.enter_context(nc.sbuf_tensor("b2_dtbc", [128, 2], F32))
    chs = ph.enter_context(nc.sbuf_tensor("b2_chs", [128, 1024], F32))
    chb = ph.enter_context(nc.sbuf_tensor("b2_chb", [128, 2048], BF16))
    fts = ph.enter_context(nc.sbuf_tensor("b2_fts", [128, 512], F32))
    rbc = ph.enter_context(nc.sbuf_tensor("b2_rbc", [128, 512], F32))
    vtt = ph.enter_context(nc.sbuf_tensor("b2_vtt", [128, 128], BF16))
    fscr = nc.dram_tensor("b2_fscr", [8, 512], F32, kind="Internal")

    LO, HI, MID, CNT, T1 = 0, 32, 64, 96, 128

    @pr.block.gpsimd
    def _(g):
        g.wait_ge(pr.dmw.h, pr.dmw.v)
        pr.dmw.inc(g.dma_start(out=wab[:].rearrange("p (c m) -> p c m", c=8),
                               in_=io["b2_wa"].ap().rearrange("(c p) m -> p c m", p=128)), 16)
    wa_ld = pr.dmw.v

    @pr.block.sync
    def _(sy):
        sy.wait_ge(pr.dma.h, pr.dma.v)
        pr.dma.inc(sy.dma_start(out=ab8[:, 0:8], in_=io["b2_alog"].ap().rearrange("(a d) -> a d", a=1)), 16)
        sy.wait_ge(pr.dma.h, pr.dma.v)
        pr.dma.inc(sy.dma_start(out=ab8[:, 8:16], in_=io["b2_dtb"].ap().rearrange("(a d) -> a d", a=1)), 16)
        sy.wait_ge(pr.dma.h, pr.dma.v)
        pr.dma.inc(sy.dma_start(out=pr.srow512[:, 0:128], in_=io["b2_qnw"].ap().rearrange("(a d) -> a d", a=1)), 16)
        sy.wait_ge(pr.dma.h, pr.dma.v)
        pr.dma.inc(sy.dma_start(out=pr.srow512[:, 128:256], in_=io["b2_knw"].ap().rearrange("(a d) -> a d", a=1)), 16)
    nw_ld = pr.dma.v

    @pr.block.scalar
    def _(s):
        s.wait_ge(pr.dma.h, nw_ld)
        pr.sc.inc(s.activation(ab8[:, 0:8], ab8[:, 0:8], AF.Exp, scale=1.0))
    negea = pr.sc.v

    @pr.block.tensor
    def _(t):
        t.wait_ge(pr.sc.h, negea)
        t.wait_ge(pr.ve.h, pr.ve.v)
        t.wait_ge(pr.tm.h, pr.tm.v)
        t.matmul(pr.PD[0:128, 0:1], pr.srow512[:, 0:128], pr.ones_f11[:], start=True, stop=True)
        t.matmul(pr.PD[0:128, 1:2], pr.srow512[:, 128:256], pr.ones_f11[:], start=True, stop=True)
        t.matmul(pr.PD[0:8, 2:3], ab8[:, 8:16], pr.ones_f11[:], start=True, stop=True)
        pr.tm.inc(t.matmul(pr.PD[0:8, 3:4], ab8[:, 0:8], pr.ones_f11[:], start=True, stop=True))
    cmm = pr.tm.v

    @pr.block.vector
    def _(v):
        v.wait_ge(pr.tm.h, cmm)
        v.tensor_copy(nrm[:, 0:2], pr.PD[:, 0:2])
        v.drain()
        v.tensor_copy(dtb[0:8, 0:2], pr.PD[0:8, 2:4])
        v.drain()
        pr.ve.inc(v.tensor_scalar(dtb[0:8, 1:2], dtb[0:8, 1:2], -1.0, None, ALU.mult))
    small_done = pr.ve.v

    for part in range(0, 1024, 512):
        @pr.block.sync
        def _(sy, part=part):
            sy.wait_ge(pr.dma.h, pr.dma.v)
            sy.wait_ge(pr.ve.h, pr.ve.v)
            pr.dma.inc(sy.dma_start(out=pr.srow512[:],
                                    in_=io["b2_gnw"].ap().rearrange("(a d) -> a d", a=1)[:, part:part + 512]), 16)
        columnize_vec(pr, [pr.srow512[:]], nrm, 2 + part // 128)
    gnw_done = pr.ve.v

    @pr.block.gpsimd
    def _(g):
        pr.gp.inc(g.memset(S32[:], 0.0))
        pr.gp.inc(g.memset(Sbf[:], 0.0))
    s0done = pr.gp.v

    prev_tt = xt_ready
    for tt in range(NTT):
        lnd = pr.ln(pr.xT, pr.xln[:, 0:4096], pr.xln[:, 4096:8192],
                    [(pr.ve, prev_tt)], tts=[tt], dst_stride=512)
        xlq = pr.xln[:, 0:4096]

        def proj(wname, evac_fn, n_evac_inc, prev_marker):
            pass

        projs = [("b2_wq", "q"), ("b2_wk", "k"), ("b2_wv", "v"), ("b2_wr", "r")]
        ev_markers = {}
        for pi, (wname, kind) in enumerate(projs):
            dmw0 = pr.dmw.v

            @pr.block.gpsimd
            def _(g, wname=wname, pi=pi, tt=tt):
                g.wait_ge(pr.dmw.h, pr.dmw.v)
                if not (tt == 0 and pi == 0):
                    g.wait_ge(pr.tm.h, pr.tm.v)
                pr.dmw.inc(g.dma_start(out=wbuf[pi % 2][:].rearrange("p (c m) -> p c m", c=8),
                                       in_=io[wname].ap().rearrange("(c p) m -> p c m", p=128)), 16)
            wld = pr.dmw.v

            tmP = pr.tm.v
            scP = pr.sc.v
            veP = pr.ve.v

            if kind != "r":
                @pr.block.tensor
                def _(t, wld=wld, lnd=lnd, tmP=tmP, scP=scP, pi=pi, tt=tt):
                    t.wait_ge(pr.dmw.h, wld)
                    t.wait_ge(pr.ve.h, lnd)
                    t.wait_ge(pr.sc.h, scP)
                    if tt == 0 and pi == 0:
                        t.wait_ge(pr.gp.h, s0done)
                        t.wait_ge(pr.ve.h, gnw_done)
                    for dc in range(8):
                        if dc >= 2:
                            t.wait_ge(pr.sc.h, scP + dc - 1)
                        ps = pr.PA[:, 512 * (dc % 2):512 * (dc % 2) + 512]
                        for k in range(8):
                            i = t.matmul(ps, wbuf[pi % 2][:, k * 1024 + dc * 128: k * 1024 + dc * 128 + 128],
                                         xlq[:, k * 512:(k + 1) * 512], start=(k == 0), stop=(k == 7))
                        pr.tm.inc(i)

            if kind in ("q", "k", "v"):
                dstb = {"q": qn, "k": kn, "v": vs}[kind]

                @pr.block.scalar
                def _(s, tmP=tmP, dstb=dstb):
                    for dc in range(8):
                        s.wait_ge(pr.tm.h, tmP + dc + 1)
                        ps = pr.PA[:, 512 * (dc % 2):512 * (dc % 2) + 512]
                        pr.sc.inc(s.activation(dstb[:, dc * 512:(dc + 1) * 512], ps, AF.Silu))
                ev_markers[kind] = pr.sc.v
            else:
                veR = pr.ve.v

                def tm_mm(dc):
                    return tmP + dc + 1 + 4 * max(0, dc - 1)

                def tm_tr(j, ts):
                    if j <= 6:
                        return tmP + (j + 2) + 4 * j + ts + 1
                    return tmP + 36 + ts + 1

                @pr.block.tensor
                def _(t, wld=wld, lnd=lnd, tmP=tmP, scP=scP, pi=pi, veR=veR):
                    t.wait_ge(pr.dmw.h, wld)
                    t.wait_ge(pr.ve.h, lnd)
                    t.wait_ge(pr.sc.h, scP)
                    def emit_tr(j):
                        t.wait_ge(pr.sc.h, scP + j + 1)
                        for ts in range(4):
                            idx = j * 4 + ts
                            if idx >= 2:
                                t.wait_ge(pr.ve.h, veR + idx - 1)
                            pt = pr.PT1 if idx % 2 == 0 else pr.PT2
                            pr.tm.inc(t.transpose(pt.bitcast(F32)[:, 0:128],
                                                  chs[:, 512 * (j % 2) + ts * 128: 512 * (j % 2) + ts * 128 + 128],
                                                  pr.ident_f[:]))
                    for dc in range(8):
                        if dc >= 2:
                            t.wait_ge(pr.sc.h, scP + dc - 1)
                        ps = pr.PA[:, 512 * (dc % 2):512 * (dc % 2) + 512]
                        for k in range(8):
                            i = t.matmul(ps, wbuf[pi % 2][:, k * 1024 + dc * 128: k * 1024 + dc * 128 + 128],
                                         xlq[:, k * 512:(k + 1) * 512], start=(k == 0), stop=(k == 7))
                        pr.tm.inc(i)
                        if dc >= 1:
                            emit_tr(dc - 1)
                    emit_tr(7)

                @pr.block.scalar
                def _(s, tmP=tmP):
                    for dc in range(8):
                        s.wait_ge(pr.tm.h, tm_mm(dc))
                        ps = pr.PA[:, 512 * (dc % 2):512 * (dc % 2) + 512]
                        pr.sc.inc(s.copy(chs[:, 512 * (dc % 2):512 * (dc % 2) + 512], ps))

                @pr.block.vector
                def _(v):
                    for dc in range(8):
                        for ts in range(4):
                            idx = dc * 4 + ts
                            v.wait_ge(pr.tm.h, tm_tr(dc, ts))
                            pt = pr.PT1 if idx % 2 == 0 else pr.PT2
                            pr.ve.inc(v.tensor_copy(rS2[:, ts * 1024 + dc * 128: ts * 1024 + dc * 128 + 128],
                                                    pt.bitcast(F32)[:, 0:128]))
                ev_markers["r"] = pr.ve.v

        dmw0 = pr.dmw.v
        tmA = pr.tm.v

        @pr.block.tensor
        def _(t, tmA=tmA, wa_ld=wa_ld, lnd=lnd):
            t.wait_ge(pr.dmw.h, wa_ld)
            t.wait_ge(pr.ve.h, lnd)
            t.wait_ge(pr.ve.h, pr.ve.v)
            for k in range(8):
                i = t.matmul(pr.PD[0:8, 0:512], wab[:, k * 8:(k + 1) * 8],
                             xlq[:, k * 512:(k + 1) * 512], start=(k == 0), stop=(k == 7))
            pr.tm.inc(i)
        fhmm = pr.tm.v

        @pr.block.scalar
        def _(s, fhmm=fhmm, dmav=pr.dma.v):
            s.wait_ge(pr.tm.h, fhmm)
            s.wait_ge(pr.ve.h, small_done)
            s.wait_ge(pr.dma.h, dmav)
            s.activation(fh8[:, 0:512], pr.PD[0:8, 0:512], AF.Exp, bias=dtb[0:8, 0:1])
            s.drain()
            pr.sc.inc(s.activation(fh8[:, 0:512], fh8[:, 0:512], AF.Ln, bias=pr.onecol[0:8, 0:1]))
        sp_done = pr.sc.v

        @pr.block.vector
        def _(v, sp_done=sp_done):
            v.wait_ge(pr.sc.h, sp_done)
            pr.ve.inc(v.tensor_scalar(fh8[:, 0:512], fh8[:, 0:512], dtb[0:8, 1:2], None, ALU.mult))
        fh_done = pr.ve.v

        @pr.block.sync
        def _(sy, fh_done=fh_done):
            sy.wait_ge(pr.ve.h, fh_done)
            sy.wait_ge(pr.dma.h, pr.dma.v)
            pr.dma.inc(sy.dma_start(out=fscr.ap(), in_=fh8[:, 0:512]), 16)
        fhr_ld = pr.dma.v


        for kind, buf, wcol in [("q", qn, 0), ("k", kn, 1)]:
            sqm = pr.sc.v

            @pr.block.scalar
            def _(s, buf=buf, kind=kind, tmguard=pr.tm.v, evm=ev_markers[kind]):
                s.wait_ge(pr.sc.h, evm)
                s.wait_ge(pr.tm.h, tmguard)
                pr.sc.inc(s.activation(sqs[:], buf[:], AF.Square))
            sq_d = pr.sc.v

            for h in range(8):
                tmN = pr.tm.v

                @pr.block.tensor
                def _(t, h=h, sq_d=sq_d, ve_pf=pr.ve.v):
                    t.wait_ge(pr.sc.h, sq_d)
                    t.wait_ge(pr.ve.h, ve_pf)
                    pr.tm.inc(t.matmul(pr.PF[0:1, 0:512], pr.ones_bcol[:],
                                       sqs[:, h * 512:(h + 1) * 512], start=True, stop=True))
                nmm = pr.tm.v

                @pr.block.scalar
                def _(s, nmm=nmm):
                    s.wait_ge(pr.tm.h, nmm)
                    s.activation(rows[:, 0:512], pr.PF[0:1, 0:512], AF.Ln, scale=1.0 / 128.0,
                                 bias=pr.eps_t[:])
                    s.drain()
                    pr.sc.inc(s.activation(rows[:, 0:512], rows[:, 0:512], AF.Exp, scale=-0.5))
                nr_d = pr.sc.v

                @pr.block.tensor
                def _(t, nr_d=nr_d):
                    t.wait_ge(pr.sc.h, nr_d)
                    pr.tm.inc(t.matmul(pr.PF[:, 0:512], pr.ones_frow[:], rows[:, 0:512],
                                       start=True, stop=True))
                nb_mm = pr.tm.v

                @pr.block.vector
                def _(v, nb_mm=nb_mm, h=h, buf=buf, wcol=wcol):
                    v.wait_ge(pr.tm.h, nb_mm)
                    v.tensor_copy(rbc[:], pr.PF[:, 0:512])
                    v.drain()
                    pr.ve.inc(v.scalar_tensor_tensor(buf[:, h * 512:(h + 1) * 512],
                                                     buf[:, h * 512:(h + 1) * 512],
                                                     nrm[:, wcol:wcol + 1], rbc[:],
                                                     ALU.mult, ALU.mult))
                    v.drain()
        qknorm_done = pr.ve.v

        veB = pr.ve.v

        @pr.block.vector
        def _(v, veB=veB, tmguard=pr.tm.v, rmark=ev_markers["r"]):
            v.wait_ge(pr.ve.h, rmark)
            v.wait_ge(pr.tm.h, tmguard)
            v.drain()
            v.memset(bis[:, LO:LO + 32], -40.0)
            v.memset(bis[:, HI:HI + 32], 40.0)
            v.memset(bis[:, MID:MID + 32], 0.0)
            v.drain()
            for it in range(NIT_BIS):
                for u in range(32):
                    ts, h = divmod(u, 8)
                    if u > 0 and u % 8 == 0:
                        v.drain()
                    v.tensor_scalar(msk[:, (u % 8) * 128:(u % 8) * 128 + 128],
                                    rS2[:, ts * 1024 + h * 128: ts * 1024 + h * 128 + 128],
                                    bis[:, MID + u:MID + u + 1], 0.0, ALU.is_ge, ALU.add,
                                    accum_out=bis[:, CNT + u:CNT + u + 1])
                v.drain()
                v.tensor_scalar(selu[:, 0:32], bis[:, CNT:CNT + 32], 31.5, None, ALU.is_ge)
                v.drain()
                v.tensor_scalar(selu[:, 32:64], bis[:, CNT:CNT + 32], 31.5, None, ALU.is_lt)
                v.drain()
                v.select(bis[:, LO:LO + 32], selu[:, 0:32], bis[:, MID:MID + 32], bis[:, LO:LO + 32], add_drain=True)
                v.drain()
                v.select(bis[:, HI:HI + 32], selu[:, 32:64], bis[:, MID:MID + 32], bis[:, HI:HI + 32], add_drain=True)
                v.drain()
                v.tensor_tensor(bis[:, T1:T1 + 32], bis[:, LO:LO + 32], bis[:, HI:HI + 32], ALU.add)
                v.drain()
                v.tensor_scalar(bis[:, MID:MID + 32], bis[:, T1:T1 + 32], 0.5, None, ALU.mult)
                v.drain()
            for u in range(32):
                ts, h = divmod(u, 8)
                i = v.tensor_scalar(sqs[:, ts * 1024 + h * 128: ts * 1024 + h * 128 + 128],
                                    rS2[:, ts * 1024 + h * 128: ts * 1024 + h * 128 + 128],
                                    bis[:, LO + u:LO + u + 1], None, ALU.is_ge)
            pr.ve.inc(i)
        bis_done = pr.ve.v

        @pr.block.scalar
        def _(s, bis_done=bis_done):
            s.wait_ge(pr.ve.h, ev_markers["r"])
            pr.sc.inc(s.activation(sgm[:], rS2[:], AF.Sigmoid))
        sgm_done = pr.sc.v

        @pr.block.vector
        def _(v, sgm_done=sgm_done, bis_done=bis_done):
            v.wait_ge(pr.sc.h, sgm_done)
            v.wait_ge(pr.ve.h, bis_done)
            pr.ve.inc(v.tensor_tensor(mmt[:], sgm[:], sqs[:], ALU.mult))
        mskd_done = pr.ve.v

        tmM = pr.tm.v
        veM = pr.ve.v

        @pr.block.tensor
        def _(t, tmM=tmM, veM=veM, mskd_done=mskd_done):
            t.wait_ge(pr.ve.h, mskd_done)
            for h in range(8):
                for ts in range(4):
                    idx = h * 4 + ts
                    if idx >= 2:
                        t.wait_ge(pr.ve.h, veM + idx - 1)
                    pt = pr.PT1 if idx % 2 == 0 else pr.PT2
                    pr.tm.inc(t.transpose(pt[:, 0:128],
                                          mmt[:, ts * 1024 + h * 128: ts * 1024 + h * 128 + 128],
                                          pr.ident_b[:]))

        @pr.block.vector
        def _(v, tmM=tmM):
            for h in range(8):
                for ts in range(4):
                    idx = h * 4 + ts
                    v.wait_ge(pr.tm.h, tmM + idx + 1)
                    pt = pr.PT1 if idx % 2 == 0 else pr.PT2
                    pr.ve.inc(v.tensor_copy(sqs[:, h * 512 + ts * 128: h * 512 + ts * 128 + 128],
                                            pt[:, 0:128]))
        mmt_done = pr.ve.v

        for h in range(8):
            tmH = pr.tm.v
            scH = pr.sc.v
            veH = pr.ve.v

            @pr.block.tensor
            def _(t, h=h, mmt_done=mmt_done, veH=veH, tmH=tmH):
                t.wait_ge(pr.ve.h, mmt_done)
                if h >= 1:
                    t.wait_ge(pr.ve.h, veH)
                pr.tm.inc(t.matmul(pr.PF[0:1, 0:512], pr.ones_bcol[:],
                                   sqs[:, h * 512:(h + 1) * 512], start=True, stop=True))
            ts_mm = pr.tm.v

            @pr.block.scalar
            def _(s, ts_mm=ts_mm, h=h):
                s.wait_ge(pr.tm.h, ts_mm)
                s.activation(rows[:, 0:512], pr.PF[0:1, 0:512], AF.Ln, bias=pr.eps9[:])
                s.drain()
                pr.sc.inc(s.activation(rows[:, 0:512], rows[:, 0:512], AF.Exp, scale=-1.0))
            rt_done = pr.sc.v

            @pr.block.sync
            def _(sy, h=h, fhr_ld=fhr_ld):
                sy.wait_ge(pr.dma.h, pr.dma.v)
                sy.wait_ge(pr.tm.h, pr.tm.v)
                pr.dma.inc(sy.dma_start(out=rows[:, 512:1024], in_=fscr.ap()[h, :]), 16)
            fh_row = pr.dma.v

            @pr.block.tensor
            def _(t, rt_done=rt_done, h=h, fh_row=fh_row):
                t.wait_ge(pr.sc.h, rt_done)
                t.wait_ge(pr.dma.h, fh_row)
                t.matmul(pr.PF[:, 0:512], pr.ones_frow[:], rows[:, 0:512], start=True, stop=True)
                pr.tm.inc(t.matmul(pr.PB[:, 0:512], pr.ones_frow[:], rows[:, 512:1024],
                                   start=True, stop=True))
            bc_mm = pr.tm.v

            @pr.block.vector
            def _(v, bc_mm=bc_mm, h=h):
                v.wait_ge(pr.tm.h, bc_mm)
                v.tensor_copy(rbc[:], pr.PF[:, 0:512])
                v.drain()
                v.tensor_tensor(fts[:], sqs[:, h * 512:(h + 1) * 512], rbc[:], ALU.mult)
                v.drain()
                pr.ve.inc(v.tensor_tensor(fts[:], fts[:], pr.PB[:, 0:512], ALU.mult))
                v.drain()
            f_done = pr.ve.v

            prev_o_ev = pr.sc.v
            for ck in range(4):
                csl = slice(ck * 128, ck * 128 + 128)
                first = (tt == 0 and ck == 0)
                tmC = pr.tm.v
                scC = pr.sc.v
                veC = pr.ve.v

                @pr.block.vector
                def _(v, h=h, ck=ck, csl=csl, f_done=f_done, veC=veC):
                    v.wait_ge(pr.ve.h, f_done)
                    pr.ve.inc(v.tensor_tensor_scan(chs[:, 0:128], fts[:, csl], fts[:, csl],
                                                   0.0, ALU.add, ALU.bypass))
                    v.drain()
                L_done = pr.ve.v

                @pr.block.scalar
                def _(s, L_done=L_done, csl=csl):
                    s.wait_ge(pr.ve.h, L_done)
                    s.activation(chb[:, 0:128], chs[:, 0:128], AF.Exp)
                    s.drain()
                    s.activation(chb[:, 128:256], chs[:, 0:128], AF.Exp, scale=-1.0)
                    s.drain()
                    s.activation(chs[:, 128:256], fts[:, csl], AF.Exp)
                    s.drain()
                    pr.sc.inc(s.copy(chs[:, 384:385], chs[:, 127:128]))
                exp_done = pr.sc.v

                @pr.block.scalar
                def _(s, exp_done=exp_done):
                    s.wait_ge(pr.sc.h, exp_done)
                    pr.sc.inc(s.activation(chs[:, 385:386], chs[:, 384:385], AF.Exp))
                eL_done = pr.sc.v

                @pr.block.vector
                def _(v, h=h, csl=csl, eL_done=eL_done):
                    v.wait_ge(pr.sc.h, eL_done)
                    v.tensor_scalar(chs[:, 256:384], chs[:, 128:256], -1.0, 1.0, ALU.mult, ALU.add)
                    v.drain()
                    v.tensor_tensor(chb[:, 256:384], qn[:, h * 512:h * 512 + 512][:, csl],
                                    chb[:, 0:128], ALU.mult)
                    v.drain()
                    v.tensor_tensor(chb[:, 384:512], kn[:, h * 512:h * 512 + 512][:, csl],
                                    chs[:, 256:384], ALU.mult)
                    v.drain()
                    v.tensor_tensor(chb[:, 512:640], chb[:, 384:512], chb[:, 128:256], ALU.mult)
                    v.drain()
                    pr.ve.inc(v.tensor_scalar(chb[:, 640:768], chb[:, 512:640],
                                              chs[:, 385:386], None, ALU.mult))
                    v.drain()
                qk_done = pr.ve.v

                tmC2 = pr.tm.v

                @pr.block.tensor
                def _(t, h=h, ck=ck, qk_done=qk_done, tmC2=tmC2, first=first):
                    t.wait_ge(pr.ve.h, qk_done)
                    pr.tm.inc(t.matmul(pr.PB[:, 0:128], chb[:, 512:640], chb[:, 256:384],
                                       start=True, stop=True))
                    pr.tm.inc(t.transpose(pr.PT1[:, 0:128],
                                          vs[:, h * 512:h * 512 + 512][:, ck * 128:ck * 128 + 128],
                                          pr.ident_b[:]))
                    pr.tm.inc(t.transpose(pr.PT2[:, 0:128], chb[:, 640:768], pr.ident_b[:]))
                pt_mm = pr.tm.v

                @pr.block.vector
                def _(v, pt_mm=pt_mm):
                    v.wait_ge(pr.tm.h, pt_mm)
                    v.tensor_tensor(chb[:, 768:896], pr.PB[:, 0:128], pr.trimask[:], ALU.mult)
                    v.drain()
                    v.tensor_copy(vtt[:, 0:128], pr.PT1[:, 0:128])
                    v.drain()
                    pr.ve.inc(v.tensor_copy(chb[:, 896:1024], pr.PT2[:, 0:128]))
                    v.drain()
                pm_done = pr.ve.v

                @pr.block.tensor
                def _(t, h=h, pm_done=pm_done, first=first, prev_o_ev=prev_o_ev):
                    t.wait_ge(pr.ve.h, pm_done)
                    t.wait_ge(pr.sc.h, prev_o_ev)
                    t.matmul(pr.PC[:, 0:128], vtt[:, 0:128], chb[:, 768:896],
                             start=True, stop=first)
                    if not first:
                        t.matmul(pr.PC[:, 0:128], Sbf[:, h * 128:(h + 1) * 128],
                                 chb[:, 256:384], start=False, stop=True)
                    i = t.matmul(pr.PD[:, 0:128], chb[:, 896:1024], vtt[:, 0:128],
                                 start=True, stop=True)
                    pr.tm.inc(i)
                o_mm = pr.tm.v

                @pr.block.scalar
                def _(s, h=h, ck=ck, o_mm=o_mm, tt=tt):
                    s.wait_ge(pr.tm.h, o_mm)
                    pr.sc.inc(s.copy(oT[:, h * 512 + ck * 128: h * 512 + ck * 128 + 128],
                                     pr.PC[:, 0:128]))
                o_ev = pr.sc.v
                prev_o_ev = o_ev

                @pr.block.vector
                def _(v, h=h, o_mm=o_mm, o_ev=o_ev):
                    v.wait_ge(pr.tm.h, o_mm)
                    v.scalar_tensor_tensor(S32[:, h * 128:(h + 1) * 128], S32[:, h * 128:(h + 1) * 128],
                                           chs[:, 385:386], pr.PD[:, 0:128], ALU.mult, ALU.add)
                    v.drain()
                    pr.ve.inc(v.tensor_copy(Sbf[:, h * 128:(h + 1) * 128], S32[:, h * 128:(h + 1) * 128]))
                    v.drain()

        last_o_sc = pr.sc.v
        last_ve = pr.ve.v

        @pr.block.scalar
        def _(s, last_o_sc=last_o_sc, last_ve=last_ve, tmg=pr.tm.v):
            s.wait_ge(pr.sc.h, last_o_sc)
            s.wait_ge(pr.ve.h, last_ve)
            s.wait_ge(pr.tm.h, tmg)
            pr.sc.inc(s.activation(sqs[:], oT[:], AF.Square))
        sq_done = pr.sc.v

        tmG = pr.tm.v

        @pr.block.tensor
        def _(t, sq_done=sq_done, tmG=tmG):
            t.wait_ge(pr.sc.h, sq_done)
            for h in range(8):
                i = t.matmul(pr.PF[0:1, 0:512], pr.ones_bcol[:], sqs[:, h * 512:(h + 1) * 512],
                             start=(h == 0), stop=(h == 7))
            pr.tm.inc(i)
        gs_mm = pr.tm.v

        @pr.block.scalar
        def _(s, gs_mm=gs_mm):
            s.wait_ge(pr.tm.h, gs_mm)
            s.activation(rows[:, 0:512], pr.PF[0:1, 0:512], AF.Ln, scale=1.0 / 1024.0,
                         bias=pr.eps_t[:])
            s.drain()
            pr.sc.inc(s.activation(rows[:, 0:512], rows[:, 0:512], AF.Exp, scale=-0.5))
        gr_done = pr.sc.v

        @pr.block.tensor
        def _(t, gr_done=gr_done):
            t.wait_ge(pr.sc.h, gr_done)
            pr.tm.inc(t.matmul(pr.PF[:, 0:512], pr.ones_frow[:], rows[:, 0:512],
                               start=True, stop=True))
        grb_mm = pr.tm.v

        @pr.block.vector
        def _(v, grb_mm=grb_mm):
            v.wait_ge(pr.tm.h, grb_mm)
            v.tensor_copy(rbc[:], pr.PF[:, 0:512])
            v.drain()
            for h in range(8):
                i = v.scalar_tensor_tensor(oT[:, h * 512:(h + 1) * 512], oT[:, h * 512:(h + 1) * 512],
                                           nrm[:, 2 + h:3 + h], rbc[:], ALU.mult, ALU.mult)
                v.drain()
            pr.ve.inc(i)
        on_done = pr.ve.v

        dmwO = pr.dmw.v

        @pr.block.gpsimd
        def _(g, tt=tt):
            g.wait_ge(pr.dmw.h, pr.dmw.v)
            g.wait_ge(pr.tm.h, pr.tm.v)
            pr.dmw.inc(g.dma_start(out=wbuf[0][:].rearrange("p (c m) -> p c m", c=8),
                                   in_=io["b2_wo"].ap().rearrange("(c p) m -> p c m", p=128)), 16)
        wo_ld = pr.dmw.v

        tmW = pr.tm.v
        veW = pr.ve.v

        @pr.block.tensor
        def _(t, wo_ld=wo_ld, on_done=on_done, tmW=tmW, veW=veW):
            t.wait_ge(pr.dmw.h, wo_ld)
            t.wait_ge(pr.ve.h, on_done)
            for dc in range(8):
                if dc >= 2:
                    t.wait_ge(pr.ve.h, veW + dc - 1)
                ps = pr.PA[:, 512 * (dc % 2):512 * (dc % 2) + 512]
                for k in range(8):
                    i = t.matmul(ps, wbuf[0][:, k * 1024 + dc * 128: k * 1024 + dc * 128 + 128],
                                 oT[:, k * 512:(k + 1) * 512], start=(k == 0), stop=(k == 7))
                pr.tm.inc(i)

        @pr.block.vector
        def _(v, tmW=tmW, tt=tt):
            for dc in range(8):
                v.wait_ge(pr.tm.h, tmW + dc + 1)
                sl = slice(dc * T + tt * 512, dc * T + tt * 512 + 512)
                ps = pr.PA[:, 512 * (dc % 2):512 * (dc % 2) + 512]
                pr.ve.inc(v.tensor_tensor(pr.xT[:, sl], pr.xT[:, sl], ps, ALU.add))
                v.drain()
        prev_tt = pr.ve.v
    ph.close()
    return pr.ve.v


def build_b2():
    nc = bass.Bass("TRN2", target_bir_lowering=False, debug=False)
    io = {
        "xin": nc.dram_tensor("xin", [D, T], F32, kind="ExternalInput"),
        "b2_wq": nc.dram_tensor("b2_wq", [D, D], F32, kind="ExternalInput"),
        "b2_wk": nc.dram_tensor("b2_wk", [D, D], F32, kind="ExternalInput"),
        "b2_wv": nc.dram_tensor("b2_wv", [D, D], F32, kind="ExternalInput"),
        "b2_wr": nc.dram_tensor("b2_wr", [D, D], F32, kind="ExternalInput"),
        "b2_wo": nc.dram_tensor("b2_wo", [D, D], F32, kind="ExternalInput"),
        "b2_wa": nc.dram_tensor("b2_wa", [D, 8], F32, kind="ExternalInput"),
        "b2_alog": nc.dram_tensor("b2_alog", [8], F32, kind="ExternalInput"),
        "b2_dtb": nc.dram_tensor("b2_dtb", [8], F32, kind="ExternalInput"),
        "b2_qnw": nc.dram_tensor("b2_qnw", [128], F32, kind="ExternalInput"),
        "b2_knw": nc.dram_tensor("b2_knw", [128], F32, kind="ExternalInput"),
        "b2_gnw": nc.dram_tensor("b2_gnw", [1024], F32, kind="ExternalInput"),
    }
    xt_out = nc.dram_tensor("xt_out", [D, T], F32, kind="ExternalOutput")
    with ExitStack() as ctx:
        pr = Prog(nc, ctx, xln_cols=DCH * 1024)
        pr.emit_consts()
        xt_ready = emit_load_xt(pr, io["xin"])
        emit_block2(pr, io, xt_ready)
        emit_store_xt(pr, xt_out)
    return nc


def emit_store_xt(pr, xt_out):
    @pr.block.sync
    def _(sy):
        sy.wait_ge(pr.ve.h, pr.ve.v)
        o = xt_out.ap().rearrange("(c p) t -> p c t", p=128)
        pr.od.inc(sy.dma_start(out=o, in_=pr.xT[:].rearrange("p (c t) -> p c t", c=DCH)), 16)
        sy.wait_ge(pr.od.h, pr.od.v)


def build_b0(x_natural=True, upto=99):
    nc = bass.Bass("TRN2", target_bir_lowering=False, debug=False)
    io = {
        "xin": nc.dram_tensor("xin", [T, D] if x_natural else [D, T], F32, kind="ExternalInput"),
        "b0_ctx": nc.dram_tensor("b0_ctx", [512, CD], F32, kind="ExternalInput"),
        "b0_wq": nc.dram_tensor("b0_wq", [D, D], F32, kind="ExternalInput"),
        "b0_wk": nc.dram_tensor("b0_wk", [CD, D], F32, kind="ExternalInput"),
        "b0_wv": nc.dram_tensor("b0_wv", [CD, D], F32, kind="ExternalInput"),
        "b0_wo": nc.dram_tensor("b0_wo", [D, D], F32, kind="ExternalInput"),
        "b0_bo": nc.dram_tensor("b0_bo", [D], F32, kind="ExternalInput"),
    }
    xt_out = nc.dram_tensor("xt_out", [D, T], F32, kind="ExternalOutput")
    with ExitStack() as ctx:
        pr = Prog(nc, ctx)
        pr.emit_consts()
        emit_block0(pr, io, have_x_natural=x_natural, upto=upto)
        emit_store_xt(pr, xt_out)
    return nc


def kernel(**inputs):
    from concourse.bass_utils import run_bass_kernel_spmd
    f32 = np.float32
    g = lambda k: np.ascontiguousarray(np.asarray(inputs[k], dtype=f32))
    cores = list(range(NCORE))

    x = g("L0_attn_x")
    ctxf = np.ascontiguousarray(g("L0_attn_context").reshape(4096, CD))

    nc0 = build_b0(x_natural=True)
    in0 = [dict(xin=np.ascontiguousarray(x[c]),
                b0_ctx=np.ascontiguousarray(ctxf[c * 512:(c + 1) * 512]),
                b0_wq=g("L0_attn_Wq"), b0_wk=g("L0_attn_Wk"),
                b0_wv=g("L0_attn_Wv"), b0_wo=g("L0_attn_Wout"),
                b0_bo=g("L0_attn_bout")) for c in cores]
    r0 = run_bass_kernel_spmd(nc0, in0, core_ids=cores)
    xt = [np.ascontiguousarray(r0.results[c]["xt_out"]) for c in cores]

    nc1 = build_b1()
    w1t = np.ascontiguousarray(g("L0_ffn_w1_w").T)
    w2t = np.ascontiguousarray(g("L0_ffn_w2_w").T)
    h1t = np.ascontiguousarray(g("L0_ffn_h1_w").T)
    h2t = np.ascontiguousarray(g("L0_ffn_h2_w").T)
    w3t = np.ascontiguousarray(g("L0_ffn_w3_w").T)
    in1 = [dict(xin=xt[c], b1_w1t=w1t, b1_w2t=w2t, b1_h1t=h1t, b1_h2t=h2t,
                b1_w3t=w3t, b1_s1=g("L0_ffn_w1_s"), b1_s2=g("L0_ffn_w2_s"),
                b1_sh1=g("L0_ffn_h1_s"), b1_sh2=g("L0_ffn_h2_s"),
                b1_s3=g("L0_ffn_w3_s")) for c in cores]
    r1 = run_bass_kernel_spmd(nc1, in1, core_ids=cores)
    xt = [np.ascontiguousarray(r1.results[c]["xt_out"]) for c in cores]

    nc2 = build_b2()
    in2 = [dict(xin=xt[c], b2_wq=g("L1_attn_Wq"), b2_wk=g("L1_attn_Wk"),
                b2_wv=g("L1_attn_Wv"), b2_wr=g("L1_attn_Wr"),
                b2_wo=g("L1_attn_Wo"), b2_wa=g("L1_attn_Wa"),
                b2_alog=g("L1_attn_A_log"), b2_dtb=g("L1_attn_dt_bias"),
                b2_qnw=g("L1_attn_q_norm_w"), b2_knw=g("L1_attn_k_norm_w"),
                b2_gnw=g("L1_attn_g_norm_w")) for c in cores]
    r2 = run_bass_kernel_spmd(nc2, in2, core_ids=cores)
    xt = [np.ascontiguousarray(r2.results[c]["xt_out"]) for c in cores]

    nc3 = build_b3()
    wint = np.ascontiguousarray(g("L1_ffn_W_in").T)
    in3 = [dict(xin=xt[c], b3_wint=wint, b3_wout=g("L1_ffn_W_out"),
                b3_bin=g("L1_ffn_b_in"), b3_bout=g("L1_ffn_b_out"))
           for c in cores]
    r3 = run_bass_kernel_spmd(nc3, in3, core_ids=cores)

    return np.stack([r3.results[c]["xt_out"].T for c in cores]).astype(f32)


# version 03
# speedup vs baseline: 1.0109x; 1.0109x over previous
import numpy as np
from contextlib import ExitStack
import concourse.bass as bass
import concourse.mybir as mybir
from concourse.bass_utils import run_bass_kernel_spmd

F32 = mybir.dt.float32
BF16 = mybir.dt.bfloat16
I32 = mybir.dt.int32
AF = mybir.ActivationFunctionType
ALU = mybir.AluOpType
AX = mybir.AxisListType

D = 1024
T = 2048
DCH = 8
NTT = 4
H = 8
KV = 4096
CD = 768
NCORE = 8


class Sem:
    def __init__(self, ctx, nc, name):
        self.h = ctx.enter_context(nc.semaphore(name))
        self.v = 0

    def inc(self, instr, n=1):
        instr.then_inc(self.h, n)
        self.v += n
        return self.v


class Prog:
    def __init__(self, nc, ctx, xln_cols=DCH * T, with_trimask=True):
        self.nc = nc
        self.ctx = ctx
        self.block = ctx.enter_context(nc.Block())
        for name in ["dma", "dmg", "dmw", "dmk", "cc", "tm", "sc", "ve", "gp", "od"]:
            setattr(self, name, Sem(ctx, nc, "s_" + name))
        self.PA = ctx.enter_context(nc.psum_tensor("PA", [128, 1024], F32))
        self.PB = ctx.enter_context(nc.psum_tensor("PB", [128, 512], F32))
        self.PC = ctx.enter_context(nc.psum_tensor("PC", [128, 512], F32))
        self.PD = ctx.enter_context(nc.psum_tensor("PD", [128, 512], F32))
        self.PT1 = ctx.enter_context(nc.psum_tensor("PT1", [128, 1024], BF16))
        self.PT2 = ctx.enter_context(nc.psum_tensor("PT2", [128, 1024], BF16))
        self.PF = ctx.enter_context(nc.psum_tensor("PF", [128, 512], F32))
        self.ident_b = ctx.enter_context(nc.sbuf_tensor("ident_b", [128, 128], BF16))
        self.ident_f = ctx.enter_context(nc.sbuf_tensor("ident_f", [128, 128], F32))
        self.ones_bcol = ctx.enter_context(nc.sbuf_tensor("ones_bcol", [128, 1], BF16))
        self.ones_brow = ctx.enter_context(nc.sbuf_tensor("ones_brow", [1, 128], BF16))
        self.ones_frow = ctx.enter_context(nc.sbuf_tensor("ones_frow", [1, 128], F32))
        self.ones_f11 = ctx.enter_context(nc.sbuf_tensor("ones_f11", [1, 1], F32))
        self.eps_t = ctx.enter_context(nc.sbuf_tensor("eps_t", [1, 1], F32))
        self.neg_half = ctx.enter_context(nc.sbuf_tensor("neg_half", [128, 1], F32))
        self.srow512 = ctx.enter_context(nc.sbuf_tensor("srow512", [1, 512], F32))
        self.srow32 = ctx.enter_context(nc.sbuf_tensor("srow32", [1, 128], F32))
        self.s11 = ctx.enter_context(nc.sbuf_tensor("s11", [1, 1], F32))
        self.s12 = ctx.enter_context(nc.sbuf_tensor("s12", [1, 1], F32))
        self.s13 = ctx.enter_context(nc.sbuf_tensor("s13", [1, 1], F32))
        self.s14 = ctx.enter_context(nc.sbuf_tensor("s14", [1, 2], F32))
        self.zrow8 = ctx.enter_context(nc.sbuf_tensor("zrow8", [1, 8], F32))
        self.trimask = (ctx.enter_context(nc.sbuf_tensor("trimask", [128, 128], BF16))
                        if with_trimask else None)
        self.onecol = ctx.enter_context(nc.sbuf_tensor("onecol", [128, 1], F32))
        self.eps9 = ctx.enter_context(nc.sbuf_tensor("eps9", [1, 1], F32))
        self.xT = ctx.enter_context(nc.sbuf_tensor("xT", [128, DCH * T], F32))
        self.xln = ctx.enter_context(nc.sbuf_tensor("xln", [128, xln_cols], BF16))
        self.smu = ctx.enter_context(nc.sbuf_tensor("smu", [1, 512], F32))
        self.svar = ctx.enter_context(nc.sbuf_tensor("svar", [1, 512], F32))
        self.sr = ctx.enter_context(nc.sbuf_tensor("sr", [1, 512], F32))
        self.smur = ctx.enter_context(nc.sbuf_tensor("smur", [1, 512], F32))
        self.rb = ctx.enter_context(nc.sbuf_tensor("rb", [128, 512], F32))
        self.mrb = ctx.enter_context(nc.sbuf_tensor("mrb", [128, 512], F32))
        self.tmpf = ctx.enter_context(nc.sbuf_tensor("tmpf", [128, 512], F32))

    def emit_consts(self):
        pr = self
        nc = self.nc
        cst = ExitStack()
        it1 = cst.enter_context(nc.sbuf_tensor("c_it1", [128, 128], F32))
        it2 = cst.enter_context(nc.sbuf_tensor("c_it2", [128, 128], F32))

        @pr.block.gpsimd
        def _(g):
            g.iota(it1[:], pattern=[[1, 128]], base=0, channel_multiplier=0,
                   allow_small_or_imprecise_dtypes=True)
            g.iota(it2[:], pattern=[[0, 128]], base=0, channel_multiplier=1,
                   allow_small_or_imprecise_dtypes=True)
            g.memset(pr.ones_bcol[:], 1.0)
            g.memset(pr.ones_brow[:], 1.0)
            g.memset(pr.ones_frow[:], 1.0)
            g.memset(pr.eps_t[:], 1e-5)
            g.memset(pr.neg_half[:], -0.5)
            g.memset(pr.onecol[:], 1.0)
            g.memset(pr.eps9[:], 1e-9)
            pr.gp.inc(g.memset(pr.ones_f11[:], 1.0))
        gp0 = pr.gp.v

        @pr.block.vector
        def _(v):
            v.wait_ge(pr.gp.h, gp0)
            v.tensor_tensor(pr.ident_f[:], it1[:], it2[:], ALU.is_equal)
            v.drain()
            if pr.trimask is not None:
                v.tensor_copy(pr.ident_b[:], pr.ident_f[:])
                v.drain()
                pr.ve.inc(v.tensor_tensor(pr.trimask[:], it2[:], it1[:], ALU.is_le))
            else:
                pr.ve.inc(v.tensor_copy(pr.ident_b[:], pr.ident_f[:]))

        ve0 = pr.ve.v
        for sec in (pr.block.tensor, pr.block.vector, pr.block.scalar,
                    pr.block.gpsimd, pr.block.sync):
            @sec
            def _(e):
                e.wait_ge(pr.ve.h, ve0)
        cst.close()

    def ln(self, src, dst, sq_scratch, waits, tts=None, dst_stride=None, dst_f32=None):
        pr = self
        eps = 1e-5
        xbf = dst
        sq = sq_scratch
        if tts is None:
            tts = list(range(NTT))
        if dst_stride is None:
            dst_stride = 512 * len(tts)
        NT_ = len(tts)
        tt0 = tts[0]
        def dsl(c, tt):
            return slice(c * dst_stride + (tt - tt0) * 512,
                         c * dst_stride + (tt - tt0) * 512 + 512)
        def ssl(c, tt):
            return slice(c * T + tt * 512, c * T + tt * 512 + 512)

        @pr.block.scalar
        def _(s):
            for sem, val in waits:
                s.wait_ge(sem.h, val)
            for c in range(DCH):
                for tt in tts:
                    s.copy(xbf[:, dsl(c, tt)], src[:, ssl(c, tt)])
                    i = s.activation(sq[:, dsl(c, tt)], src[:, ssl(c, tt)], AF.Square)
            pr.sc.inc(i)
        cast_done = pr.sc.v

        tm0, sc0, ve0 = pr.tm.v, pr.sc.v, pr.ve.v

        @pr.block.tensor
        def _(t):
            t.wait_ge(pr.sc.h, cast_done)
            t.wait_ge(pr.ve.h, ve0)
            for j, tt in enumerate(tts):
                if j >= 1:
                    t.wait_ge(pr.ve.h, ve0 + 4 * j - 3)
                for c in range(DCH):
                    t.matmul(pr.PC[0:1, 0:512], pr.ones_bcol[:],
                             xbf[:, dsl(c, tt)], start=(c == 0), stop=(c == DCH - 1))
                for c in range(DCH):
                    i2 = t.matmul(pr.PD[0:1, 0:512], pr.ones_bcol[:],
                                  sq[:, dsl(c, tt)], start=(c == 0), stop=(c == DCH - 1))
                pr.tm.inc(i2)
                t.wait_ge(pr.ve.h, ve0 + 4 * j + 2)
                pr.tm.inc(t.matmul(pr.PF[:], pr.ones_frow[:], pr.sr[:], start=True, stop=True))
                t.wait_ge(pr.ve.h, ve0 + 4 * j + 3)
                pr.tm.inc(t.matmul(pr.PF[:], pr.ones_frow[:], pr.smur[:], start=True, stop=True))

        @pr.block.vector
        def _(v):
            for j, tt in enumerate(tts):
                v.wait_ge(pr.tm.h, tm0 + 3 * j + 1)
                v.tensor_scalar(pr.smu[:], pr.PC[0:1, 0:512], 1.0 / D, None, ALU.mult)
                v.tensor_scalar(pr.svar[:], pr.PD[0:1, 0:512], 1.0 / D, None, ALU.mult)
                v.drain()
                v.tensor_tensor(pr.tmpf[0:1, :], pr.smu[:], pr.smu[:], ALU.mult)
                v.drain()
                pr.ve.inc(v.tensor_tensor(pr.svar[:], pr.svar[:], pr.tmpf[0:1, :], ALU.subtract))
                v.wait_ge(pr.sc.h, sc0 + j + 1)
                v.drain()
                pr.ve.inc(v.tensor_tensor(pr.smur[:], pr.smu[:], pr.sr[:], ALU.mult))
                v.wait_ge(pr.tm.h, tm0 + 3 * j + 2)
                pr.ve.inc(v.tensor_copy(pr.rb[:], pr.PF[:]))
                v.wait_ge(pr.tm.h, tm0 + 3 * j + 3)
                v.tensor_copy(pr.mrb[:], pr.PF[:])
                v.drain()
                for c in range(DCH):
                    od = dst_f32 if dst_f32 is not None else dst
                    v.tensor_tensor(pr.tmpf[:], src[:, ssl(c, tt)], pr.rb[:], ALU.mult)
                    v.drain()
                    i = v.tensor_tensor(od[:, dsl(c, tt)], pr.tmpf[:], pr.mrb[:], ALU.subtract)
                    v.drain()
                pr.ve.inc(i)

        @pr.block.scalar
        def _(s):
            for j, tt in enumerate(tts):
                s.wait_ge(pr.ve.h, ve0 + 4 * j + 1)
                s.activation(pr.sr[:], pr.svar[:], AF.Ln, bias=pr.eps_t[:])
                s.drain()
                pr.sc.inc(s.activation(pr.sr[:], pr.sr[:], AF.Exp, scale=-0.5))

        return pr.ve.v



def emit_block0(pr, io, have_x_natural=True, upto=99):
    nc = pr.nc
    SCALE = (128 ** -0.5) * 1.5

    kvloc = nc.dram_tensor("kvloc", [2 * 512 * 1024], BF16, kind="Internal")
    kvall = nc.dram_tensor("kvall", [NCORE * 2 * 512 * 1024], BF16, kind="Internal")

    phA = ExitStack()
    cnat = phA.enter_context(nc.sbuf_tensor("b0_cnat", [128, 4 * CD], BF16))
    ctxT = phA.enter_context(nc.sbuf_tensor("b0_ctxT", [128, 6 * 512], BF16))
    wkb = phA.enter_context(nc.sbuf_tensor("b0_wkb", [128, 6 * 1024], BF16))
    wvb = phA.enter_context(nc.sbuf_tensor("b0_wvb", [128, 6 * 1024], BF16))
    khsl = phA.enter_context(nc.sbuf_tensor("b0_khsl", [128, 8 * 512], BF16))
    vhsl = phA.enter_context(nc.sbuf_tensor("b0_vhsl", [128, 4 * 1024], BF16))

    @pr.block.gpsimd
    def _(g):
        c_r = io["b0_ctx"].ap().rearrange("(c p) d -> p c d", p=128)
        pr.dmg.inc(g.dma_start(out=cnat[:].rearrange("p (c d) -> p c d", c=4), in_=c_r), 16)
        pr.dmw.inc(g.dma_start(out=wkb[:].rearrange("p (c m) -> p c m", c=6),
                               in_=io["b0_wk"].ap().rearrange("(c p) m -> p c m", p=128)), 16)
        pr.dmw.inc(g.dma_start(out=wvb[:].rearrange("p (c m) -> p c m", c=6),
                               in_=io["b0_wv"].ap().rearrange("(c p) m -> p c m", p=128)), 16)
    dma_cnat = pr.dmg.v
    dma_wkv = pr.dmw.v
    ve_id = pr.ve.v
    tm0, sc0 = pr.tm.v, pr.sc.v

    @pr.block.tensor
    def _(t):
        t.wait_ge(pr.dmg.h, dma_cnat)
        t.wait_ge(pr.ve.h, ve_id)
        for idx in range(24):
            rc, cc = divmod(idx, 6)
            if idx >= 2:
                t.wait_ge(pr.sc.h, sc0 + idx - 1)
            pt = pr.PT1 if idx % 2 == 0 else pr.PT2
            pr.tm.inc(t.transpose(pt[:, 0:128],
                                  cnat[:, rc * CD + cc * 128: rc * CD + cc * 128 + 128],
                                  pr.ident_b[:]))

    @pr.block.scalar
    def _(s):
        for idx in range(24):
            rc, cc = divmod(idx, 6)
            s.wait_ge(pr.tm.h, tm0 + idx + 1)
            pt = pr.PT1 if idx % 2 == 0 else pr.PT2
            pr.sc.inc(s.copy(ctxT[:, cc * 512 + rc * 128: cc * 512 + rc * 128 + 128],
                             pt[:, 0:128]))
    ctxT_done = pr.sc.v

    tm1, sc1 = pr.tm.v, pr.sc.v

    @pr.block.tensor
    def _(t):
        t.wait_ge(pr.sc.h, ctxT_done)
        t.wait_ge(pr.dmw.h, dma_wkv)
        for dc in range(8):
            if dc >= 2:
                t.wait_ge(pr.sc.h, sc1 + dc - 1)
            ps = pr.PB if dc % 2 == 0 else pr.PC
            for c in range(6):
                i = t.matmul(ps[:], wkb[:, c * 1024 + dc * 128: c * 1024 + dc * 128 + 128],
                             ctxT[:, c * 512:(c + 1) * 512], start=(c == 0), stop=(c == 5))
            pr.tm.inc(i)
        for idx in range(8):
            kc, dt_ = divmod(idx, 2)
            t.wait_ge(pr.sc.h, sc1 + idx + 7)
            ps = pr.PB if idx % 2 == 0 else pr.PC
            for c in range(6):
                i = t.matmul(ps[:], ctxT[:, c * 512 + kc * 128: c * 512 + kc * 128 + 128],
                             wvb[:, c * 1024 + dt_ * 512: c * 1024 + dt_ * 512 + 512],
                             start=(c == 0), stop=(c == 5))
            pr.tm.inc(i)

    @pr.block.scalar
    def _(s):
        for dc in range(8):
            s.wait_ge(pr.tm.h, tm1 + dc + 1)
            ps = pr.PB if dc % 2 == 0 else pr.PC
            pr.sc.inc(s.copy(khsl[:, dc * 512:(dc + 1) * 512], ps[:]))
        for idx in range(8):
            kc, dt_ = divmod(idx, 2)
            s.wait_ge(pr.tm.h, tm1 + 8 + idx + 1)
            ps = pr.PB if idx % 2 == 0 else pr.PC
            pr.sc.inc(s.copy(vhsl[:, kc * 1024 + dt_ * 512: kc * 1024 + dt_ * 512 + 512], ps[:]))
    kv_evac = pr.sc.v

    @pr.block.sync
    def _(sy):
        sy.wait_ge(pr.sc.h, kv_evac)
        kv2 = kvloc.ap().rearrange("(a p j) -> p a j", p=128, a=2)
        pr.dma.inc(sy.dma_start(out=kv2[:, 0, :], in_=khsl[:]), 16)
        pr.dma.inc(sy.dma_start(out=kv2[:, 1, :], in_=vhsl[:]), 16)
    kv_stored = pr.dma.v

    @pr.block.gpsimd
    def _(g):
        g.wait_ge(pr.dma.h, kv_stored)
        pr.cc.inc(g.collective_compute(
            "AllGather", ALU.bypass,
            replica_groups=[list(range(NCORE))],
            ins=[kvloc.ap().rearrange("(r c) -> r c", r=128).opt()],
            outs=[kvall.ap().rearrange("(r c) -> r c", r=8 * 128).opt()],
        ))
    cc_done = pr.cc.v
    phA.close()
    if upto <= 0:
        return pr.ve.v

    if have_x_natural:
        phX = ExitStack()
        xin = phX.enter_context(nc.sbuf_tensor("b0_xin", [128, 16 * 1024], F32))

        @pr.block.sync
        def _(sy):
            sy.wait_ge(pr.dma.h, kv_stored)
            x_r = io["xin"].ap().rearrange("(c p) d -> p c d", p=128)
            pr.dma.inc(sy.dma_start(out=xin[:].rearrange("p (c d) -> p c d", c=16), in_=x_r), 16)
        x_loaded = pr.dma.v

        tm2, ve2 = pr.tm.v, pr.ve.v

        @pr.block.tensor
        def _(t):
            t.wait_ge(pr.dma.h, x_loaded)
            t.wait_ge(pr.ve.h, ve_id)
            for idx in range(128):
                tc, dc = divmod(idx, DCH)
                if idx >= 2:
                    t.wait_ge(pr.ve.h, ve2 + idx - 1)
                pr.tm.inc(t.transpose(pr.PA[:, 512 * (idx % 2):512 * (idx % 2) + 128],
                                      xin[:, tc * 1024 + dc * 128: tc * 1024 + dc * 128 + 128],
                                      pr.ident_f[:]))

        @pr.block.vector
        def _(v):
            for idx in range(128):
                tc, dc = divmod(idx, DCH)
                v.wait_ge(pr.tm.h, tm2 + idx + 1)
                pr.ve.inc(v.tensor_copy(pr.xT[:, dc * T + tc * 128: dc * T + tc * 128 + 128],
                                        pr.PA[:, 512 * (idx % 2):512 * (idx % 2) + 128]))
        xT_done = pr.ve.v
        phX.close()
    else:
        @pr.block.sync
        def _(sy):
            xi = io["xin"].ap().rearrange("(c p) t -> p c t", p=128)
            pr.dma.inc(sy.dma_start(out=pr.xT[:].rearrange("p (c t) -> p c t", c=DCH), in_=xi), 16)
        x_loaded = pr.dma.v

        @pr.block.vector
        def _(v):
            v.wait_ge(pr.dma.h, x_loaded)
            pr.ve.inc(v.engine_nop())
        xT_done = pr.ve.v

    if upto <= 1:
        return pr.ve.v
    phQ = ExitStack()
    qT = phQ.enter_context(nc.sbuf_tensor("b0_qT", [128, DCH * T], BF16))
    wA = phQ.enter_context(nc.sbuf_tensor("b0_wA", [128, DCH * 1024], BF16))

    ln1_done = pr.ln(pr.xT, pr.xln, qT[:], [(pr.ve, xT_done)])

    if upto <= 2:
        return pr.ve.v
    @pr.block.gpsimd
    def _(g):
        g.wait_ge(pr.ve.h, xT_done)
        g.wait_ge(pr.dmw.h, pr.dmw.v)
        pr.dmw.inc(g.dma_start(out=wA[:].rearrange("p (c m) -> p c m", c=DCH),
                               in_=io["b0_wq"].ap().rearrange("(c p) m -> p c m", p=128)), 16)
    wq_loaded = pr.dmw.v

    tm3, sc3 = pr.tm.v, pr.sc.v

    @pr.block.tensor
    def _(t):
        t.wait_ge(pr.dmw.h, wq_loaded)
        t.wait_ge(pr.ve.h, ln1_done)
        for idx in range(DCH * NTT):
            dc, tt = divmod(idx, NTT)
            if idx >= 2:
                t.wait_ge(pr.sc.h, sc3 + idx - 1)
            for k in range(DCH):
                i = t.matmul(pr.PA[:, 512 * (idx % 2):512 * (idx % 2) + 512],
                             wA[:, k * 1024 + dc * 128: k * 1024 + dc * 128 + 128],
                             pr.xln[:, k * T + tt * 512: k * T + tt * 512 + 512],
                             start=(k == 0), stop=(k == DCH - 1))
            pr.tm.inc(i)

    @pr.block.scalar
    def _(s):
        for idx in range(DCH * NTT):
            dc, tt = divmod(idx, NTT)
            s.wait_ge(pr.tm.h, tm3 + idx + 1)
            pr.sc.inc(s.copy(qT[:, dc * T + tt * 512: dc * T + tt * 512 + 512],
                             pr.PA[:, 512 * (idx % 2):512 * (idx % 2) + 512]))
    q_done = pr.sc.v

    if upto <= 3:
        return pr.ve.v
    khb = [phQ.enter_context(nc.sbuf_tensor(f"b0_khb{i}", [128, KV], BF16)) for i in range(2)]
    vhb = [phQ.enter_context(nc.sbuf_tensor(f"b0_vhb{i}", [128, KV], BF16)) for i in range(2)]
    Ebuf = phQ.enter_context(nc.sbuf_tensor("b0_E", [128, 4 * 512], BF16))
    rsum = phQ.enter_context(nc.sbuf_tensor("b0_rsum", [1, 512], F32))
    oT = pr.xln
    rb32 = pr.rb

    kva = kvall.ap().rearrange("(r a p dc j) -> a r dc p j", r=NCORE, a=2, p=128, dc=8)
    kva_v = kvall.ap().rearrange("(r a p kc dv) -> a r p kc dv", r=NCORE, a=2, p=128, kc=4)

    NIT = H * NTT
    tm_b, sc_b, ve_b = pr.tm.v, pr.sc.v, pr.ve.v
    TPI, SPI, VPI = 33, 17, 2

    @pr.block.gpsimd
    def _(g):
        g.wait_ge(pr.cc.h, cc_done)
        for h in range(H):
            if h >= 2:
                g.wait_ge(pr.tm.h, tm_b + TPI * NTT * (h - 1))
            if h >= 1:
                g.wait_ge(pr.dmk.h, 144 * h)
            pr.dmk.inc(g.dma_start(out=khb[h % 2][:].rearrange("p (r j) -> p r j", r=8),
                                   in_=kva[0, :, h, :, :].rearrange("r p j -> p r j")), 16)
            for r in range(NCORE):
                pr.dmk.inc(g.dma_start(
                    out=vhb[h % 2][:, r * 512:(r + 1) * 512].rearrange("p (kc dv) -> p kc dv", kc=4),
                    in_=kva_v[1, r, :, :, h * 128:(h + 1) * 128]), 16)

    for j in range(NIT):
        h, qt = divmod(j, NTT)
        tmj = tm_b + TPI * j
        scj = sc_b + SPI * j
        vej = ve_b + VPI * j
        kh, vh = khb[h % 2], vhb[h % 2]

        @pr.block.tensor
        def _(t, h=h, qt=qt, tmj=tmj, scj=scj, vej=vej, kh=kh, vh=vh, j=j):
            if qt == 0:
                t.wait_ge(pr.dmk.h, 144 * (h + 1))
                if h == 0:
                    t.wait_ge(pr.sc.h, q_done)
            qsl = qT[:, h * T + qt * 512: h * T + qt * 512 + 512]
            for p in range(16):
                if p >= 1:
                    t.wait_ge(pr.sc.h, scj + p)
                elif j >= 1:
                    t.wait_ge(pr.sc.h, scj - 1)
                t.matmul(pr.PA[:, 0:512], kh[:, (2 * p) * 128:(2 * p) * 128 + 128],
                         qsl, start=True, stop=True)
                pr.tm.inc(t.matmul(pr.PA[:, 512:1024],
                                   kh[:, (2 * p + 1) * 128:(2 * p + 1) * 128 + 128],
                                   qsl, start=True, stop=True))
                t.wait_ge(pr.sc.h, scj + p + 1)
                if p == 0 and j >= 1:
                    t.wait_ge(pr.ve.h, vej)
                for cc_ in (2 * p, 2 * p + 1):
                    t.matmul(pr.PB[:], vh[:, cc_ * 128:cc_ * 128 + 128],
                             Ebuf[:, (cc_ % 4) * 512:(cc_ % 4) * 512 + 512],
                             start=(cc_ == 0), stop=(cc_ == 31))
                    i = t.matmul(pr.PC[0:1, 0:512], pr.ones_bcol[:],
                                 Ebuf[:, (cc_ % 4) * 512:(cc_ % 4) * 512 + 512],
                                 start=(cc_ == 0), stop=(cc_ == 31))
                pr.tm.inc(i)
            t.wait_ge(pr.ve.h, vej + 1)
            pr.tm.inc(t.matmul(pr.PD[:], pr.ones_frow[:], rsum[:], start=True, stop=True))

        @pr.block.scalar
        def _(s, tmj=tmj, scj=scj):
            for p in range(16):
                s.wait_ge(pr.tm.h, tmj + 2 * p + 1)
                pr.sc.inc(s.activation(Ebuf[:, (2 * p % 4) * 512:(2 * p % 4) * 512 + 1024],
                                       pr.PA[:], AF.Exp, scale=SCALE))
            s.wait_ge(pr.tm.h, tmj + TPI)
            pr.sc.inc(s.copy(rb32[:], pr.PD[:]))

        @pr.block.vector
        def _(v, h=h, qt=qt, tmj=tmj, scj=scj):
            v.wait_ge(pr.tm.h, tmj + 32)
            pr.ve.inc(v.reciprocal(rsum[:], pr.PC[0:1, 0:512]))
            v.wait_ge(pr.sc.h, scj + SPI)
            pr.ve.inc(v.tensor_tensor(oT[:, h * T + qt * 512: h * T + qt * 512 + 512],
                                      pr.PB[:], rb32[:], ALU.mult))

    attn_done = pr.ve.v
    assert attn_done == ve_b + VPI * NIT and pr.tm.v == tm_b + TPI * NIT and pr.sc.v == sc_b + SPI * NIT

    if upto <= 4:
        return pr.ve.v
    bob = phQ.enter_context(nc.sbuf_tensor("b0_bob", [128, DCH], F32))
    brow = phQ.enter_context(nc.sbuf_tensor("b0_brow", [1, D], F32))

    @pr.block.sync
    def _(sy):
        sy.wait_ge(pr.dma.h, pr.dma.v)
        pr.dma.inc(sy.dma_start(out=brow[:], in_=io["b0_bo"].ap().rearrange("(a d) -> a d", a=1)), 16)
    brow_loaded = pr.dma.v

    @pr.block.gpsimd
    def _(g):
        g.wait_ge(pr.tm.h, tm_b)
        g.wait_ge(pr.dmw.h, pr.dmw.v)
        pr.dmw.inc(g.dma_start(out=wA[:].rearrange("p (c m) -> p c m", c=DCH),
                               in_=io["b0_wo"].ap().rearrange("(c p) m -> p c m", p=128)), 16)
    wo_loaded = pr.dmw.v

    @pr.block.tensor
    def _(t):
        t.wait_ge(pr.dma.h, brow_loaded)
        t.wait_ge(pr.sc.h, sc_b + SPI * NIT)
        for c in range(DCH):
            i = t.matmul(pr.PD[0:128, 64 + c:64 + c + 1],
                         brow[:, c * 128:(c + 1) * 128], pr.ones_f11[:], start=True, stop=True)
        pr.tm.inc(i)
    bob_mm = pr.tm.v

    @pr.block.vector
    def _(v):
        v.wait_ge(pr.tm.h, bob_mm)
        pr.ve.inc(v.tensor_copy(bob[:], pr.PD[:, 64:64 + DCH]))
    bob_done = pr.ve.v

    tm4, ve4 = pr.tm.v, pr.ve.v

    @pr.block.tensor
    def _(t):
        t.wait_ge(pr.dmw.h, wo_loaded)
        t.wait_ge(pr.ve.h, attn_done)
        for idx in range(DCH * NTT):
            dc, tt = divmod(idx, NTT)
            if idx >= 2:
                t.wait_ge(pr.ve.h, ve4 + idx)
            for k in range(DCH):
                i = t.matmul(pr.PA[:, 512 * (idx % 2):512 * (idx % 2) + 512],
                             wA[:, k * 1024 + dc * 128: k * 1024 + dc * 128 + 128],
                             oT[:, k * T + tt * 512: k * T + tt * 512 + 512],
                             start=(k == 0), stop=(k == DCH - 1))
            pr.tm.inc(i)

    @pr.block.vector
    def _(v):
        v.wait_ge(pr.ve.h, bob_done)
        for idx in range(DCH * NTT):
            dc, tt = divmod(idx, NTT)
            v.wait_ge(pr.tm.h, tm4 + idx + 1)
            sl = slice(dc * T + tt * 512, dc * T + tt * 512 + 512)
            pr.ve.inc(v.scalar_tensor_tensor(
                pr.xT[:, sl], pr.PA[:, 512 * (idx % 2):512 * (idx % 2) + 512],
                bob[:, dc:dc + 1], pr.xT[:, sl], ALU.add, ALU.add))
    phQ.close()
    return pr.ve.v




def emit_load_xt(pr, xt_in):
    @pr.block.sync
    def _(sy):
        sy.wait_ge(pr.dma.h, pr.dma.v)
        xi = xt_in.ap().rearrange("(c p) t -> p c t", p=128)
        pr.dma.inc(sy.dma_start(out=pr.xT[:].rearrange("p (c t) -> p c t", c=DCH), in_=xi), 16)
    x_loaded = pr.dma.v

    @pr.block.vector
    def _(v):
        v.wait_ge(pr.dma.h, x_loaded)
        pr.ve.inc(v.engine_nop())
    return pr.ve.v


def columnize_vec(pr, rows_and_slices, dst, dst_col0=0):
    cols = []
    col = dst_col0
    for row in rows_and_slices:
        m = row.shape[-1]
        nch = m // 128
        cols.append((row, col, nch))
        col += nch
    ncol_total = col - dst_col0

    @pr.block.tensor
    def _(t):
        t.wait_ge(pr.dma.h, pr.dma.v)
        t.wait_ge(pr.dmg.h, pr.dmg.v)
        t.wait_ge(pr.sc.h, pr.sc.v)
        t.wait_ge(pr.ve.h, pr.ve.v)
        for row, c0, nch in cols:
            for c in range(nch):
                i = t.matmul(pr.PD[0:128, c0 - dst_col0 + c:c0 - dst_col0 + c + 1],
                             row[:, c * 128:(c + 1) * 128], pr.ones_f11[:],
                             start=True, stop=True)
        pr.tm.inc(i)
    mm = pr.tm.v

    @pr.block.vector
    def _(v):
        v.wait_ge(pr.tm.h, mm)
        pr.ve.inc(v.tensor_copy(dst[:, dst_col0:dst_col0 + ncol_total],
                                pr.PD[:, 0:ncol_total]))
    return pr.ve.v



def emit_block1(pr, io, xt_ready, upto=99):
    nc = pr.nc
    ph = ExitStack()
    A = ph.enter_context(nc.sbuf_tensor("b1_A", [128, 32 * 512], BF16))
    B = ph.enter_context(nc.sbuf_tensor("b1_B", [128, 32 * 512], BF16))
    w12 = [ph.enter_context(nc.sbuf_tensor(f"b1_w12_{i}", [128, 2 * 8 * 128], BF16)) for i in range(2)]
    wh = [ph.enter_context(nc.sbuf_tensor(f"b1_wh_{i}", [128, 32 * 128], BF16)) for i in range(2)]
    scl = ph.enter_context(nc.sbuf_tensor("b1_scl", [128, 4 * 32 + 8], F32))
    t1f = ph.enter_context(nc.sbuf_tensor("b1_t1", [128, 1024], F32))
    xlq = pr.xln[:, 0:DCH * 512]
    sqq = pr.xln[:, DCH * 512:DCH * 1024]

    scrow = pr.srow512
    svecs = [("b1_s1", 4096), ("b1_s2", 4096), ("b1_sh1", 4096), ("b1_sh2", 4096), ("b1_s3", 1024)]
    col0 = 0
    for name, m in svecs:
        for part in range(0, m, 512):
            w_ = min(512, m - part)
            @pr.block.sync
            def _(sy, name=name, part=part, w_=w_):
                sy.wait_ge(pr.dma.h, pr.dma.v)
                sy.wait_ge(pr.ve.h, pr.ve.v)
                pr.dma.inc(sy.dma_start(out=scrow[:, 0:w_],
                                        in_=io[name].ap().rearrange("(a d) -> a d", a=1)[:, part:part + w_]), 16)
            columnize_vec(pr, [scrow[:, 0:w_]], scl, col0)
            col0 += w_ // 128
    scl_done = pr.ve.v
    S1C, S2C, SH1C, SH2C, S3C = 0, 32, 64, 96, 128

    NQ = 4
    prev_q_done = xt_ready
    for q in range(NQ):
        ln_done = pr.ln(pr.xT, xlq, sqq, [(pr.ve, prev_q_done)], tts=[q], dst_stride=512)

        dmw0 = pr.dmw.v
        tm0 = pr.tm.v
        sc0 = pr.sc.v
        ve0 = pr.ve.v

        @pr.block.gpsimd
        def _(g, q=q, tm0=tm0):
            for oc in range(32):
                g.wait_ge(pr.dmw.h, pr.dmw.v)
                if oc >= 2:
                    g.wait_ge(pr.tm.h, tm0 + oc - 1)
                w1v = io["b1_w1t"].ap()[:, oc * 128:(oc + 1) * 128].rearrange("(kc p) j -> p kc j", p=128)
                w2v = io["b1_w2t"].ap()[:, oc * 128:(oc + 1) * 128].rearrange("(kc p) j -> p kc j", p=128)
                buf = w12[oc % 2]
                pr.dmw.inc(g.dma_start(out=buf[:, 0:1024].rearrange("p (kc j) -> p kc j", kc=8), in_=w1v), 16)
                g.wait_ge(pr.dmw.h, pr.dmw.v)
                pr.dmw.inc(g.dma_start(out=buf[:, 1024:2048].rearrange("p (kc j) -> p kc j", kc=8), in_=w2v), 16)

        @pr.block.tensor
        def _(t, q=q, tm0=tm0, sc0=sc0, ve0=ve0, dmw0=dmw0):
            t.wait_ge(pr.ve.h, ln_done)
            t.wait_ge(pr.ve.h, scl_done)
            t.wait_ge(pr.ve.h, ve0)
            for oc in range(32):
                t.wait_ge(pr.dmw.h, dmw0 + 32 * (oc + 1))
                if oc >= 2:
                    t.wait_ge(pr.sc.h, sc0 + oc - 1)
                    t.wait_ge(pr.ve.h, ve0 + oc - 1)
                buf = w12[oc % 2]
                psA = pr.PA[:, 512 * (oc % 2):512 * (oc % 2) + 512]
                psB = pr.PB[:] if oc % 2 == 0 else pr.PC[:]
                for k in range(8):
                    t.matmul(psA, buf[:, k * 128:(k + 1) * 128],
                             xlq[:, k * 512:(k + 1) * 512],
                             start=(k == 0), stop=(k == 7))
                for k in range(8):
                    i = t.matmul(psB, buf[:, 1024 + k * 128:1024 + k * 128 + 128],
                                 xlq[:, k * 512:(k + 1) * 512],
                                 start=(k == 0), stop=(k == 7))
                pr.tm.inc(i)

        @pr.block.scalar
        def _(s, tm0=tm0, ve0=ve0):
            for oc in range(32):
                s.wait_ge(pr.tm.h, tm0 + oc + 1)
                if oc >= 2:
                    s.wait_ge(pr.ve.h, ve0 + oc - 1)
                psA = pr.PA[:, 512 * (oc % 2):512 * (oc % 2) + 512]
                psB = pr.PB[:] if oc % 2 == 0 else pr.PC[:]
                h_ = slice(512 * (oc % 2), 512 * (oc % 2) + 512)
                pr.sc.inc(s.activation(t1f[:, h_], psA, AF.Silu, scale=scl[:, S1C + oc:S1C + oc + 1]))

        @pr.block.vector
        def _(v, sc0=sc0, ve0=ve0, q=q):
            for oc in range(32):
                v.wait_ge(pr.sc.h, sc0 + oc + 1)
                h_ = slice(512 * (oc % 2), 512 * (oc % 2) + 512)
                psB = pr.PB[:] if oc % 2 == 0 else pr.PC[:]
                pr.ve.inc(v.scalar_tensor_tensor(A[:, oc * 512:(oc + 1) * 512], psB,
                                                 scl[:, S2C + oc:S2C + oc + 1], t1f[:, h_],
                                                 ALU.mult, ALU.mult))
        st1_ve = pr.ve.v

        prev_guard = (pr.ve, st1_ve)
        for stage, (wname, scol, src, dstb) in enumerate(
                [("b1_h1t", SH1C, A, B), ("b1_h2t", SH2C, B, A)]):
            dmw1 = pr.dmw.v
            tm1 = pr.tm.v
            sc1 = pr.sc.v

            @pr.block.gpsimd
            def _(g, wname=wname, tm1=tm1):
                for oc in range(32):
                    wv = io[wname].ap()[:, oc * 128:(oc + 1) * 128]
                    buf = wh[oc % 2]
                    if oc >= 2:
                        g.wait_ge(pr.tm.h, tm1 + oc - 1)
                    else:
                        g.wait_ge(pr.tm.h, tm1)
                    for half in range(2):
                        g.wait_ge(pr.dmw.h, pr.dmw.v)
                        pr.dmw.inc(g.dma_start(
                            out=buf[:, half * 2048:(half + 1) * 2048].rearrange("p (kc j) -> p kc j", kc=16),
                            in_=wv[half * 2048:(half + 1) * 2048, :].rearrange("(kc p) j -> p kc j", p=128)), 16)

            @pr.block.tensor
            def _(t, tm1=tm1, sc1=sc1, dmw1=dmw1, src=src, prev_guard=prev_guard, stage=stage):
                t.wait_ge(prev_guard[0].h, prev_guard[1])
                for oc in range(32):
                    t.wait_ge(pr.dmw.h, dmw1 + 32 * (oc + 1))
                    if oc >= 2:
                        t.wait_ge(pr.sc.h, sc1 + oc - 1)
                    buf = wh[oc % 2]
                    ps = pr.PA[:, 512 * (oc % 2):512 * (oc % 2) + 512]
                    for k in range(32):
                        i = t.matmul(ps, buf[:, k * 128:(k + 1) * 128],
                                     src[:, k * 512:(k + 1) * 512],
                                     start=(k == 0), stop=(k == 31))
                    pr.tm.inc(i)

            @pr.block.scalar
            def _(s, tm1=tm1, scol=scol, dstb=dstb):
                for oc in range(32):
                    s.wait_ge(pr.tm.h, tm1 + oc + 1)
                    ps = pr.PA[:, 512 * (oc % 2):512 * (oc % 2) + 512]
                    pr.sc.inc(s.activation(dstb[:, oc * 512:(oc + 1) * 512], ps, AF.Gelu,
                                           scale=scl[:, scol + oc:scol + oc + 1]))
            prev_guard = (pr.sc, pr.sc.v)

        st3_sc = pr.sc.v

        dmw2 = pr.dmw.v
        tm2g = pr.tm.v
        @pr.block.gpsimd
        def _(g, tm2g=tm2g):
            for oc in range(8):
                wv = io["b1_w3t"].ap()[:, oc * 128:(oc + 1) * 128]
                buf = wh[oc % 2]
                if oc >= 2:
                    g.wait_ge(pr.tm.h, tm2g + oc - 1)
                else:
                    g.wait_ge(pr.tm.h, tm2g)
                for half in range(2):
                    g.wait_ge(pr.dmw.h, pr.dmw.v)
                    pr.dmw.inc(g.dma_start(
                        out=buf[:, half * 2048:(half + 1) * 2048].rearrange("p (kc j) -> p kc j", kc=16),
                        in_=wv[half * 2048:(half + 1) * 2048, :].rearrange("(kc p) j -> p kc j", p=128)), 16)

        tm2 = pr.tm.v
        ve2 = pr.ve.v

        @pr.block.tensor
        def _(t, tm2=tm2, dmw2=dmw2, ve2=ve2, st3_sc=st3_sc):
            t.wait_ge(pr.sc.h, st3_sc)
            for oc in range(8):
                t.wait_ge(pr.dmw.h, dmw2 + 32 * (oc + 1))
                if oc >= 2:
                    t.wait_ge(pr.ve.h, ve2 + oc - 1)
                buf = wh[oc % 2]
                ps = pr.PA[:, 512 * (oc % 2):512 * (oc % 2) + 512]
                for k in range(32):
                    i = t.matmul(ps, buf[:, k * 128:(k + 1) * 128],
                                 A[:, k * 512:(k + 1) * 512],
                                 start=(k == 0), stop=(k == 31))
                pr.tm.inc(i)

        @pr.block.vector
        def _(v, tm2=tm2, q=q):
            for oc in range(8):
                v.wait_ge(pr.tm.h, tm2 + oc + 1)
                ps = pr.PA[:, 512 * (oc % 2):512 * (oc % 2) + 512]
                sl = slice(oc * T + q * 512, oc * T + q * 512 + 512)
                pr.ve.inc(v.scalar_tensor_tensor(pr.xT[:, sl], ps,
                                                 scl[:, S3C + oc:S3C + oc + 1],
                                                 pr.xT[:, sl], ALU.mult, ALU.add))
        prev_q_done = pr.ve.v
    ph.close()
    return pr.ve.v


def build_b1():
    nc = bass.Bass("TRN2", target_bir_lowering=False, debug=False)
    io = {
        "xin": nc.dram_tensor("xin", [D, T], F32, kind="ExternalInput"),
        "b1_w1t": nc.dram_tensor("b1_w1t", [1024, 4096], F32, kind="ExternalInput"),
        "b1_w2t": nc.dram_tensor("b1_w2t", [1024, 4096], F32, kind="ExternalInput"),
        "b1_h1t": nc.dram_tensor("b1_h1t", [4096, 4096], F32, kind="ExternalInput"),
        "b1_h2t": nc.dram_tensor("b1_h2t", [4096, 4096], F32, kind="ExternalInput"),
        "b1_w3t": nc.dram_tensor("b1_w3t", [4096, 1024], F32, kind="ExternalInput"),
        "b1_s1": nc.dram_tensor("b1_s1", [4096], F32, kind="ExternalInput"),
        "b1_s2": nc.dram_tensor("b1_s2", [4096], F32, kind="ExternalInput"),
        "b1_sh1": nc.dram_tensor("b1_sh1", [4096], F32, kind="ExternalInput"),
        "b1_sh2": nc.dram_tensor("b1_sh2", [4096], F32, kind="ExternalInput"),
        "b1_s3": nc.dram_tensor("b1_s3", [1024], F32, kind="ExternalInput"),
    }
    xt_out = nc.dram_tensor("xt_out", [D, T], F32, kind="ExternalOutput")
    with ExitStack() as ctx:
        pr = Prog(nc, ctx, xln_cols=DCH * 1024)
        pr.emit_consts()
        xt_ready = emit_load_xt(pr, io["xin"])
        emit_block1(pr, io, xt_ready)
        emit_store_xt(pr, xt_out)
    return nc







def emit_quant_tiles(pr, tiles, scale_ap, inv_ap, zt, it_, dt_, s1t, s2t,
                     width, pre_wait_v=None):
    ve0 = pr.ve.v
    sc0 = pr.sc.v
    n = len(tiles)

    @pr.block.vector
    def _(v):
        if pre_wait_v:
            for sem, val in pre_wait_v:
                v.wait_ge(sem.h, val)
        for j, (src, dst, waits) in enumerate(tiles):
            for sem, val in waits:
                v.wait_ge(sem.h, val)
            v.tensor_scalar(zt[:, 0:width], src, scale_ap, None, ALU.mult)
            v.drain()
            v.tensor_copy(it_[:, 0:width], zt[:, 0:width])
            v.drain()
            pr.ve.inc(v.tensor_tensor(dt_[:, 0:width], zt[:, 0:width], it_[:, 0:width], ALU.subtract))
            v.wait_ge(pr.sc.h, sc0 + j + 1)
            v.scalar_tensor_tensor(zt[:, 0:width], s1t[:, 0:width], 0.5, it_[:, 0:width], ALU.mult, ALU.add)
            v.drain()
            v.scalar_tensor_tensor(zt[:, 0:width], s2t[:, 0:width], -0.5, zt[:, 0:width], ALU.mult, ALU.add)
            v.drain()
            pr.ve.inc(v.tensor_scalar(dst, zt[:, 0:width], inv_ap, None, ALU.mult))
            v.drain()

    @pr.block.scalar
    def _(s):
        for j in range(n):
            s.wait_ge(pr.ve.h, ve0 + 2 * j + 1)
            s.activation(s1t[:, 0:width], dt_[:, 0:width], AF.Sign, bias=pr.neg_half[:])
            s.drain()
            pr.sc.inc(s.activation(s2t[:, 0:width], dt_[:, 0:width], AF.Sign, scale=-1.0, bias=pr.neg_half[:]))


def emit_wq_stream(pr, pr_io_name, wst, dst_writer, zt, it_, dt_, s1t, s2t, qsc,
                   gp_guard, io, wqb=None, transpose_dst=None):
    dmw0 = pr.dmw.v
    ve0 = pr.ve.v
    sc0 = pr.sc.v
    tmT0 = pr.tm.v
    TR = transpose_dst is not None
    VPC = 10 if TR else 2

    @pr.block.gpsimd
    def _(g):
        for kc in range(32):
            g.wait_ge(pr.dmw.h, pr.dmw.v)
            for gg in gp_guard(kc) or []:
                g.wait_ge(gg[0].h, gg[1])
            if TR and kc >= 2:
                g.wait_ge(pr.tm.h, tmT0 + 8 * (kc - 1))
            pr.dmw.inc(g.dma_start(out=wst[kc % 2][:],
                                   in_=io[pr_io_name].ap()[kc * 128:(kc + 1) * 128, :]), 16)

    def quant_d(kc):
        if not TR or kc == 0:
            return ve0 + VPC * kc + 1 if not TR else ve0 + 1
        return ve0 + 10 * kc - 7

    def quant_f(kc):
        if not TR or kc == 0:
            return ve0 + VPC * kc + 2 if not TR else ve0 + 2
        return ve0 + 10 * kc - 6

    def evac_val(j, dc):
        if j <= 30:
            return ve0 + 10 * j + 5 + dc
        return ve0 + 313 + dc

    @pr.block.vector
    def _(v):
        for kc in range(32):
            v.wait_ge(pr.dmw.h, dmw0 + 16 * (kc + 1))
            w = wst[kc % 2]
            v.tensor_reduce(qsc[:, 0:1], w[:], AX.X, ALU.max, apply_absolute_value=True)
            v.drain()
            v.tensor_scalar(qsc[:, 0:1], qsc[:, 0:1], 1e-8, None, ALU.max)
            v.drain()
            v.reciprocal(qsc[:, 1:2], qsc[:, 0:1])
            v.drain()
            v.tensor_scalar(qsc[:, 2:3], qsc[:, 1:2], 127.0, None, ALU.mult)
            v.drain()
            v.reciprocal(qsc[:, 3:4], qsc[:, 2:3])
            v.drain()
            v.tensor_scalar(zt[:], w[:], qsc[:, 2:3], None, ALU.mult)
            v.drain()
            v.tensor_copy(it_[:], zt[:])
            v.drain()
            pr.ve.inc(v.tensor_tensor(dt_[:], zt[:], it_[:], ALU.subtract))
            v.wait_ge(pr.sc.h, sc0 + kc + 1)
            v.scalar_tensor_tensor(zt[:], s1t[:], 0.5, it_[:], ALU.mult, ALU.add)
            v.drain()
            v.scalar_tensor_tensor(zt[:], s2t[:], -0.5, zt[:], ALU.mult, ALU.add)
            v.drain()
            dst = wqb[kc % 2][:] if TR else dst_writer(kc)
            pr.ve.inc(v.tensor_scalar(dst, zt[:], qsc[:, 3:4], None, ALU.mult))
            v.drain()
            if TR and kc >= 1:
                for dc in range(8):
                    v.wait_ge(pr.tm.h, tmT0 + (kc - 1) * 8 + dc + 1)
                    pt = pr.PT1 if ((kc - 1) * 8 + dc) % 2 == 0 else pr.PT2
                    pr.ve.inc(v.tensor_copy(transpose_dst(kc - 1, dc), pt[:, 0:128]))
        if TR:
            for dc in range(8):
                v.wait_ge(pr.tm.h, tmT0 + 31 * 8 + dc + 1)
                pt = pr.PT1 if (31 * 8 + dc) % 2 == 0 else pr.PT2
                pr.ve.inc(v.tensor_copy(transpose_dst(31, dc), pt[:, 0:128]))

    @pr.block.scalar
    def _(s):
        for kc in range(32):
            s.wait_ge(pr.ve.h, quant_d(kc))
            s.activation(s1t[:], dt_[:], AF.Sign, bias=pr.neg_half[:])
            s.drain()
            pr.sc.inc(s.activation(s2t[:], dt_[:], AF.Sign, scale=-1.0, bias=pr.neg_half[:]))

    if TR:
        @pr.block.tensor
        def _(t):
            for kc in range(32):
                t.wait_ge(pr.ve.h, quant_f(kc))
                for dc in range(8):
                    idx = kc * 8 + dc
                    if idx >= 2:
                        j2, dc2 = divmod(idx - 2, 8)
                        t.wait_ge(pr.ve.h, evac_val(j2, dc2))
                    pt = pr.PT1 if idx % 2 == 0 else pr.PT2
                    pr.tm.inc(t.transpose(pt[:, 0:128],
                                          wqb[kc % 2][:, dc * 128:(dc + 1) * 128],
                                          pr.ident_b[:]))
    return pr.ve.v


def emit_global_amax(pr, tile, cc_in, cc_out, dst_col, dst_inv_col, qlevels, bscr):
    ve0 = pr.ve.v

    @pr.block.vector
    def _(v):
        pr.ve.inc(v.engine_nop())
    tree_done = pr.ve.v

    @pr.block.sync
    def _(sy):
        sy.wait_ge(pr.ve.h, tree_done)
        sy.wait_ge(pr.dma.h, pr.dma.v)
        pr.dma.inc(sy.dma_start(out=bscr.ap().rearrange("(a b) -> a b", a=128)[:, 0:1],
                                in_=tile[:, 0:1]), 16)
        sy.wait_ge(pr.dma.h, pr.dma.v)
        pr.dma.inc(sy.dma_start(out=pr.srow32[:],
                                in_=bscr.ap().rearrange("(a b) -> a b", a=1)[:, 0:128]), 16)
    ld1 = pr.dma.v

    @pr.block.vector
    def _(v):
        v.wait_ge(pr.dma.h, ld1)
        v.tensor_reduce(pr.s11[:], pr.srow32[:], AX.X, ALU.max)
        v.drain()
        pr.ve.inc(v.tensor_scalar(pr.s11[:], pr.s11[:], 1e-8, None, ALU.max))
    amax_local = pr.ve.v

    @pr.block.gpsimd
    def _(g):
        pr.gp.inc(g.memset(pr.zrow8[:], 0.0))
    zr = pr.gp.v

    @pr.block.sync
    def _(sy):
        sy.wait_ge(pr.ve.h, amax_local)
        sy.wait_ge(pr.gp.h, zr)
        sy.wait_ge(pr.dma.h, pr.dma.v)
        pr.dma.inc(sy.dma_start(out=cc_in.ap(), in_=pr.zrow8[:]), 16)
        sy.wait_ge(pr.dma.h, pr.dma.v)
        pr.dma.inc(sy.dma_start(out=cc_in.ap()[0:1, 0:1], in_=pr.s11[:]), 16)
    ccst = pr.dma.v

    @pr.block.gpsimd
    def _(g):
        g.wait_ge(pr.dma.h, ccst)
        pr.cc.inc(g.collective_compute(
            "AllReduce", ALU.max, replica_groups=[list(range(NCORE))],
            ins=[cc_in.ap().opt()], outs=[cc_out.ap().opt()]))
    ccd = pr.cc.v

    @pr.block.sync
    def _(sy):
        sy.wait_ge(pr.cc.h, ccd)
        sy.wait_ge(pr.dma.h, pr.dma.v)
        pr.dma.inc(sy.dma_start(out=pr.s11[:], in_=cc_out.ap()[0:1, 0:1]), 16)
    ld2 = pr.dma.v

    @pr.block.vector
    def _(v):
        v.wait_ge(pr.dma.h, ld2)
        v.reciprocal(pr.s12[:], pr.s11[:])
        v.drain()
        v.tensor_scalar(pr.s12[:], pr.s12[:], qlevels, None, ALU.mult)
        v.drain()
        v.reciprocal(pr.s13[:], pr.s12[:])
        v.drain()
        v.tensor_copy(pr.s14[:, 0:1], pr.s12[:])
        v.drain()
        pr.ve.inc(v.tensor_copy(pr.s14[:, 1:2], pr.s13[:]))
    row2 = pr.ve.v

    @pr.block.tensor
    def _(t):
        t.wait_ge(pr.ve.h, row2)
        pr.tm.inc(t.matmul(pr.PD[0:128, 0:2], pr.ones_frow[:], pr.s14[:, 0:2],
                           start=True, stop=True))
    bmm = pr.tm.v

    @pr.block.vector
    def _(v):
        v.wait_ge(pr.tm.h, bmm)
        v.tensor_copy(dst_col, pr.PD[:, 0:1])
        v.drain()
        pr.ve.inc(v.tensor_copy(dst_inv_col, pr.PD[:, 1:2]))
    return pr.ve.v


def emit_block3(pr, io, xt_ready):
    nc = pr.nc
    ph = ExitStack()
    W_q = ph.enter_context(nc.sbuf_tensor("b3_wq", [128, 32 * 1024], BF16))
    zt = ph.enter_context(nc.sbuf_tensor("b3_zt", [128, 1024], F32))
    it_ = ph.enter_context(nc.sbuf_tensor("b3_it", [128, 1024], I32))
    dt_ = zt
    s1t = pr.xln[:, 12288:13312]
    s2t = pr.xln[:, 13312:14336]
    binc = ph.enter_context(nc.sbuf_tensor("b3_binc", [128, 44], F32))
    qsc = ph.enter_context(nc.sbuf_tensor("b3_qsc", [128, 4], F32))
    xmax = ph.enter_context(nc.sbuf_tensor("b3_xmax", [128, 4], F32))
    hmax = ph.enter_context(nc.sbuf_tensor("b3_hmax", [128, 4], F32))

    hid = nc.dram_tensor("b3_hid", [32, 128, T], F32, kind="Internal")
    bscr = nc.dram_tensor("b3_bscr", [128], F32, kind="Internal")
    ccxi = nc.dram_tensor("b3_ccxi", [1, 8], F32, kind="Internal")
    ccxo = nc.dram_tensor("b3_ccxo", [1, 8], F32, kind="Internal")
    cchi = nc.dram_tensor("b3_cchi", [1, 8], F32, kind="Internal")
    ccho = nc.dram_tensor("b3_ccho", [1, 8], F32, kind="Internal")
    SXC, SIXC, SHC, SIHC = 40, 41, 42, 43

    @pr.block.gpsimd
    def _(g):
        g.memset(xmax[:], 0.0)
        pr.gp.inc(g.memset(hmax[:], 0.0))
    mm0 = pr.gp.v

    phW1 = ExitStack()
    wst = [phW1.enter_context(nc.sbuf_tensor(f"b3_wst{i}", [128, 1024], F32)) for i in range(2)]
    wqb = [phW1.enter_context(nc.sbuf_tensor(f"b3_wqb{i}", [128, 1024], BF16)) for i in range(2)]

    winq_done = emit_wq_stream(
        pr, "b3_wint", wst, None, zt, it_, dt_, s1t, s2t, qsc,
        lambda kc: [(pr.ve, xt_ready)] if kc == 0 else None, io,
        wqb=wqb,
        transpose_dst=lambda kc, dc: W_q[:, dc * 4096 + kc * 128: dc * 4096 + kc * 128 + 128])
    phW1.close()
    phI = ExitStack()
    xf32 = phI.enter_context(nc.sbuf_tensor("b3_xf32", [128, DCH * 512], F32))
    xqb = pr.xln[:, 8192:12288]
    stg = phI.enter_context(nc.sbuf_tensor("b3_stg", [128, 2 * 512], F32))

    for nm, c0, m in [("b3_bin", 0, 4096), ("b3_bout", 32, 1024)]:
        for part in range(0, m, 512):
            @pr.block.sync
            def _(sy, nm=nm, part=part):
                sy.wait_ge(pr.dma.h, pr.dma.v)
                sy.wait_ge(pr.ve.h, pr.ve.v)
                pr.dma.inc(sy.dma_start(out=pr.srow512[:],
                                        in_=io[nm].ap().rearrange("(a d) -> a d", a=1)[:, part:part + 512]), 16)
            columnize_vec(pr, [pr.srow512[:]], binc, c0 + part // 128)

    for q in range(NTT):
        lnd = pr.ln(pr.xT, pr.xln[:, 0:DCH * 512], pr.xln[:, DCH * 512:DCH * 1024],
                    [(pr.ve, xt_ready if q == 0 else pr.ve.v)], tts=[q], dst_stride=512,
                    dst_f32=xf32)

        @pr.block.vector
        def _(v, lnd=lnd):
            v.wait_ge(pr.ve.h, lnd)
            v.wait_ge(pr.gp.h, mm0)
            v.tensor_reduce(xmax[:, 1:2], xf32[:], AX.X, ALU.max, apply_absolute_value=True)
            v.drain()
            pr.ve.inc(v.tensor_tensor(xmax[:, 0:1], xmax[:, 0:1], xmax[:, 1:2], ALU.max))
            v.drain()

    emit_global_amax(pr, xmax[:, 0:4], ccxi, ccxo, binc[:, SXC:SXC + 1],
                     binc[:, SIXC:SIXC + 1], 32767.0, bscr)
    sx_ready = pr.ve.v

    for q in range(NTT):
        lnd = pr.ln(pr.xT, pr.xln[:, 0:DCH * 512], pr.xln[:, DCH * 512:DCH * 1024],
                    [(pr.ve, pr.ve.v)], tts=[q], dst_stride=512, dst_f32=xf32)
        tiles = [(xf32[:, sl * 512:(sl + 1) * 512], xqb[:, sl * 512:(sl + 1) * 512], [])
                 for sl in range(8)]
        emit_quant_tiles(pr, tiles, binc[:, SXC:SXC + 1], binc[:, SIXC:SIXC + 1],
                         zt, it_, dt_, s1t, s2t, 512,
                         pre_wait_v=[(pr.ve, lnd), (pr.ve, sx_ready)])
        xq_done = pr.ve.v

        tmS = pr.tm.v
        scS = pr.sc.v
        veS = pr.ve.v
        dmaS = pr.dma.v

        @pr.block.tensor
        def _(t, tmS=tmS, scS=scS, xq_done=xq_done):
            t.wait_ge(pr.ve.h, xq_done)
            t.wait_ge(pr.ve.h, winq_done)
            for oc in range(32):
                if oc >= 2:
                    t.wait_ge(pr.sc.h, scS + oc - 1)
                ps = pr.PA[:, 512 * (oc % 2):512 * (oc % 2) + 512]
                for k in range(8):
                    i = t.matmul(ps, W_q[:, k * 4096 + oc * 128: k * 4096 + oc * 128 + 128],
                                 xqb[:, k * 512:(k + 1) * 512], start=(k == 0), stop=(k == 7))
                pr.tm.inc(i)

        @pr.block.scalar
        def _(s, tmS=tmS, veS=veS, dmaS=dmaS):
            for oc in range(32):
                s.wait_ge(pr.tm.h, tmS + oc + 1)
                if oc >= 2:
                    s.wait_ge(pr.ve.h, veS + 2 * (oc - 1))
                    s.wait_ge(pr.dma.h, dmaS + 16 * (oc - 1))
                else:
                    s.wait_ge(pr.dma.h, dmaS)
                ps = pr.PA[:, 512 * (oc % 2):512 * (oc % 2) + 512]
                pr.sc.inc(s.activation(stg[:, 512 * (oc % 2):512 * (oc % 2) + 512], ps,
                                       AF.Gelu, bias=binc[:, oc:oc + 1]))

        @pr.block.vector
        def _(v, scS=scS):
            for oc in range(32):
                v.wait_ge(pr.sc.h, scS + oc + 1)
                sgl = stg[:, 512 * (oc % 2):512 * (oc % 2) + 512]
                v.tensor_reduce(hmax[:, 1:2], sgl, AX.X, ALU.max, apply_absolute_value=True)
                v.drain()
                v.tensor_tensor(hmax[:, 0:1], hmax[:, 0:1], hmax[:, 1:2], ALU.max)
                v.drain()
                pr.ve.inc(v.engine_nop(), 2)

        @pr.block.sync
        def _(sy, q=q, veS=veS, dmaS=dmaS):
            for oc in range(32):
                sy.wait_ge(pr.ve.h, veS + 2 * (oc + 1))
                sy.wait_ge(pr.dma.h, pr.dma.v)
                pr.dma.inc(sy.dma_start(out=hid.ap()[oc, :, q * 512:(q + 1) * 512],
                                        in_=stg[:, 512 * (oc % 2):512 * (oc % 2) + 512]), 16)
    stageI_tm = pr.tm.v
    stageI_dma = pr.dma.v
    phI.close()
    phW2 = ExitStack()
    wst = [phW2.enter_context(nc.sbuf_tensor(f"b3_wst2{i}", [128, 1024], F32)) for i in range(2)]

    emit_global_amax(pr, hmax[:, 0:4], cchi, ccho, binc[:, SHC:SHC + 1],
                     binc[:, SIHC:SIHC + 1], 32767.0, bscr)
    sh_ready = pr.ve.v

    ve_w1 = pr.ve.v
    emit_wq_stream(
        pr, "b3_wout", wst, lambda kc: W_q[:, kc * 1024:(kc + 1) * 1024],
        zt, it_, dt_, s1t, s2t, qsc,
        lambda kc: ([(pr.tm, stageI_tm), (pr.dma, stageI_dma)] if kc < 2
                    else [(pr.ve, ve_w1 + 2 * (kc - 1))]), io)
    woutq_done = pr.ve.v
    phW2.close()

    phII = ExitStack()
    hld = [phII.enter_context(nc.sbuf_tensor(f"b3_hld{i}", [128, 512], F32)) for i in range(2)]
    hqb = [phII.enter_context(nc.sbuf_tensor(f"b3_hqb{i}", [128, 512], BF16)) for i in range(2)]

    for q in range(NTT):
        dmaQ = pr.dma.v
        veQ = pr.ve.v
        tmQ = pr.tm.v

        @pr.block.sync
        def _(sy, q=q, veQ=veQ):
            if q == 0:
                sy.wait_ge(pr.ve.h, woutq_done)
            else:
                sy.wait_ge(pr.ve.h, veQ)
            for k in range(32):
                sy.wait_ge(pr.dma.h, pr.dma.v)
                if k >= 2:
                    sy.wait_ge(pr.ve.h, veQ + 2 * (k - 1))
                pr.dma.inc(sy.dma_start(out=hld[k % 2][:],
                                        in_=hid.ap()[k, :, q * 512:(q + 1) * 512]), 16)

        tiles = [(hld[k % 2][:], hqb[k % 2][:],
                  [(pr.dma, dmaQ + 16 * (k + 1))] + ([(pr.tm, tmQ + k - 1)] if k >= 2 else []))
                 for k in range(32)]
        emit_quant_tiles(pr, tiles, binc[:, SHC:SHC + 1], binc[:, SIHC:SIHC + 1],
                         zt, it_, dt_, s1t, s2t, 512,
                         pre_wait_v=([(pr.ve, sh_ready), (pr.ve, woutq_done)] if q == 0 else None))

        banks = [pr.PA[:, 0:512], pr.PA[:, 512:1024], pr.PB[:], pr.PC[:], pr.PD[:],
                 pr.PF[:], pr.PT1.bitcast(F32)[:, 0:512], pr.PT2.bitcast(F32)[:, 0:512]]

        @pr.block.tensor
        def _(t, q=q, veQ=veQ, banks=banks):
            if q == 0:
                t.wait_ge(pr.ve.h, woutq_done)
            else:
                t.wait_ge(pr.ve.h, veQ)
            for k in range(32):
                t.wait_ge(pr.ve.h, veQ + 2 * (k + 1))
                for dc in range(8):
                    i = t.matmul(banks[dc], W_q[:, k * 1024 + dc * 128: k * 1024 + dc * 128 + 128],
                                 hqb[k % 2][:], start=(k == 0), stop=(k == 31))
                pr.tm.inc(i)

        tmE = pr.tm.v

        @pr.block.vector
        def _(v, q=q, tmE=tmE, banks=banks):
            v.wait_ge(pr.tm.h, tmE)
            for dc in range(8):
                sl = slice(dc * T + q * 512, dc * T + q * 512 + 512)
                i = v.scalar_tensor_tensor(pr.xT[:, sl], banks[dc], binc[:, 32 + dc:32 + dc + 1],
                                           pr.xT[:, sl], ALU.add, ALU.add)
                v.drain()
            pr.ve.inc(i)
    phII.close()
    ph.close()
    return pr.ve.v


def build_b3():
    nc = bass.Bass("TRN2", target_bir_lowering=False, debug=False)
    io = {
        "xin": nc.dram_tensor("xin", [D, T], F32, kind="ExternalInput"),
        "b3_wint": nc.dram_tensor("b3_wint", [4096, 1024], F32, kind="ExternalInput"),
        "b3_wout": nc.dram_tensor("b3_wout", [4096, 1024], F32, kind="ExternalInput"),
        "b3_bin": nc.dram_tensor("b3_bin", [4096], F32, kind="ExternalInput"),
        "b3_bout": nc.dram_tensor("b3_bout", [1024], F32, kind="ExternalInput"),
    }
    xt_out = nc.dram_tensor("xt_out", [D, T], F32, kind="ExternalOutput")
    with ExitStack() as ctx:
        pr = Prog(nc, ctx, xln_cols=DCH * 1024, with_trimask=False)
        pr.emit_consts()
        xt_ready = emit_load_xt(pr, io["xin"])
        emit_block3(pr, io, xt_ready)
        emit_store_xt(pr, xt_out)
    return nc





def emit_block2(pr, io, xt_ready):
    nc = pr.nc
    NIT_BIS = 33
    ph = ExitStack()
    qn = pr.xln[:, 8192:12288]
    kn = pr.xln[:, 12288:16384]
    vs = ph.enter_context(nc.sbuf_tensor("b2_vs", [128, 4096], BF16))
    rS2 = ph.enter_context(nc.sbuf_tensor("b2_rs2", [128, 4096], F32))
    wbuf = [ph.enter_context(nc.sbuf_tensor("b2_w0", [128, 8192], BF16))] * 2
    oT = ph.enter_context(nc.sbuf_tensor("b2_oT", [128, 4096], BF16))
    sqs = ph.enter_context(nc.sbuf_tensor("b2_sqs", [128, 4096], BF16))
    msk = ph.enter_context(nc.sbuf_tensor("b2_msk", [128, 1024], BF16))
    mmt = ph.enter_context(nc.sbuf_tensor("b2_mmt", [128, 4096], BF16))
    sgm = mmt
    bis = ph.enter_context(nc.sbuf_tensor("b2_bis", [128, 256], F32))
    selu = ph.enter_context(nc.sbuf_tensor("b2_selu", [128, 64], mybir.dt.uint8))
    S32 = ph.enter_context(nc.sbuf_tensor("b2_S32", [128, 1024], F32))
    Sbf = ph.enter_context(nc.sbuf_tensor("b2_Sbf", [128, 1024], BF16))
    fh8 = ph.enter_context(nc.sbuf_tensor("b2_fh8", [8, 512], F32))
    rows = ph.enter_context(nc.sbuf_tensor("b2_rows", [1, 1024], F32))
    wab = ph.enter_context(nc.sbuf_tensor("b2_wab", [128, 64], BF16))
    nrm = ph.enter_context(nc.sbuf_tensor("b2_nrm", [128, 10], F32))
    ab8 = ph.enter_context(nc.sbuf_tensor("b2_ab8", [1, 16], F32))
    dtb = ph.enter_context(nc.sbuf_tensor("b2_dtbc", [128, 2], F32))
    chs = ph.enter_context(nc.sbuf_tensor("b2_chs", [128, 1024], F32))
    chb = ph.enter_context(nc.sbuf_tensor("b2_chb", [128, 2048], BF16))
    fts = ph.enter_context(nc.sbuf_tensor("b2_fts", [128, 512], F32))
    rbc = ph.enter_context(nc.sbuf_tensor("b2_rbc", [128, 512], F32))
    vtt = ph.enter_context(nc.sbuf_tensor("b2_vtt", [128, 128], BF16))
    fscr = nc.dram_tensor("b2_fscr", [8, 512], F32, kind="Internal")

    LO, HI, MID, CNT, T1 = 0, 32, 64, 96, 128

    @pr.block.gpsimd
    def _(g):
        g.wait_ge(pr.dmw.h, pr.dmw.v)
        pr.dmw.inc(g.dma_start(out=wab[:].rearrange("p (c m) -> p c m", c=8),
                               in_=io["b2_wa"].ap().rearrange("(c p) m -> p c m", p=128)), 16)
    wa_ld = pr.dmw.v

    @pr.block.sync
    def _(sy):
        sy.wait_ge(pr.dma.h, pr.dma.v)
        pr.dma.inc(sy.dma_start(out=ab8[:, 0:8], in_=io["b2_alog"].ap().rearrange("(a d) -> a d", a=1)), 16)
        sy.wait_ge(pr.dma.h, pr.dma.v)
        pr.dma.inc(sy.dma_start(out=ab8[:, 8:16], in_=io["b2_dtb"].ap().rearrange("(a d) -> a d", a=1)), 16)
        sy.wait_ge(pr.dma.h, pr.dma.v)
        pr.dma.inc(sy.dma_start(out=pr.srow512[:, 0:128], in_=io["b2_qnw"].ap().rearrange("(a d) -> a d", a=1)), 16)
        sy.wait_ge(pr.dma.h, pr.dma.v)
        pr.dma.inc(sy.dma_start(out=pr.srow512[:, 128:256], in_=io["b2_knw"].ap().rearrange("(a d) -> a d", a=1)), 16)
    nw_ld = pr.dma.v

    @pr.block.scalar
    def _(s):
        s.wait_ge(pr.dma.h, nw_ld)
        pr.sc.inc(s.activation(ab8[:, 0:8], ab8[:, 0:8], AF.Exp, scale=1.0))
    negea = pr.sc.v

    @pr.block.tensor
    def _(t):
        t.wait_ge(pr.sc.h, negea)
        t.wait_ge(pr.ve.h, pr.ve.v)
        t.wait_ge(pr.tm.h, pr.tm.v)
        t.matmul(pr.PD[0:128, 0:1], pr.srow512[:, 0:128], pr.ones_f11[:], start=True, stop=True)
        t.matmul(pr.PD[0:128, 1:2], pr.srow512[:, 128:256], pr.ones_f11[:], start=True, stop=True)
        t.matmul(pr.PD[0:8, 2:3], ab8[:, 8:16], pr.ones_f11[:], start=True, stop=True)
        pr.tm.inc(t.matmul(pr.PD[0:8, 3:4], ab8[:, 0:8], pr.ones_f11[:], start=True, stop=True))
    cmm = pr.tm.v

    @pr.block.vector
    def _(v):
        v.wait_ge(pr.tm.h, cmm)
        v.tensor_copy(nrm[:, 0:2], pr.PD[:, 0:2])
        v.drain()
        v.tensor_copy(dtb[0:8, 0:2], pr.PD[0:8, 2:4])
        v.drain()
        pr.ve.inc(v.tensor_scalar(dtb[0:8, 1:2], dtb[0:8, 1:2], -1.0, None, ALU.mult))
    small_done = pr.ve.v

    for part in range(0, 1024, 512):
        @pr.block.sync
        def _(sy, part=part):
            sy.wait_ge(pr.dma.h, pr.dma.v)
            sy.wait_ge(pr.ve.h, pr.ve.v)
            pr.dma.inc(sy.dma_start(out=pr.srow512[:],
                                    in_=io["b2_gnw"].ap().rearrange("(a d) -> a d", a=1)[:, part:part + 512]), 16)
        columnize_vec(pr, [pr.srow512[:]], nrm, 2 + part // 128)
    gnw_done = pr.ve.v

    @pr.block.gpsimd
    def _(g):
        pr.gp.inc(g.memset(S32[:], 0.0))
        pr.gp.inc(g.memset(Sbf[:], 0.0))
    s0done = pr.gp.v

    prev_tt = xt_ready
    for tt in range(NTT):
        lnd = pr.ln(pr.xT, pr.xln[:, 0:4096], pr.xln[:, 4096:8192],
                    [(pr.ve, prev_tt)], tts=[tt], dst_stride=512)
        xlq = pr.xln[:, 0:4096]

        def proj(wname, evac_fn, n_evac_inc, prev_marker):
            pass

        projs = [("b2_wq", "q"), ("b2_wk", "k"), ("b2_wv", "v"), ("b2_wr", "r")]
        ev_markers = {}
        for pi, (wname, kind) in enumerate(projs):
            dmw0 = pr.dmw.v

            @pr.block.gpsimd
            def _(g, wname=wname, pi=pi, tt=tt):
                g.wait_ge(pr.dmw.h, pr.dmw.v)
                if not (tt == 0 and pi == 0):
                    g.wait_ge(pr.tm.h, pr.tm.v)
                pr.dmw.inc(g.dma_start(out=wbuf[pi % 2][:].rearrange("p (c m) -> p c m", c=8),
                                       in_=io[wname].ap().rearrange("(c p) m -> p c m", p=128)), 16)
            wld = pr.dmw.v

            tmP = pr.tm.v
            scP = pr.sc.v
            veP = pr.ve.v

            if kind != "r":
                @pr.block.tensor
                def _(t, wld=wld, lnd=lnd, tmP=tmP, scP=scP, pi=pi, tt=tt):
                    t.wait_ge(pr.dmw.h, wld)
                    t.wait_ge(pr.ve.h, lnd)
                    t.wait_ge(pr.sc.h, scP)
                    if tt == 0 and pi == 0:
                        t.wait_ge(pr.gp.h, s0done)
                        t.wait_ge(pr.ve.h, gnw_done)
                    for dc in range(8):
                        if dc >= 2:
                            t.wait_ge(pr.sc.h, scP + dc - 1)
                        ps = pr.PA[:, 512 * (dc % 2):512 * (dc % 2) + 512]
                        for k in range(8):
                            i = t.matmul(ps, wbuf[pi % 2][:, k * 1024 + dc * 128: k * 1024 + dc * 128 + 128],
                                         xlq[:, k * 512:(k + 1) * 512], start=(k == 0), stop=(k == 7))
                        pr.tm.inc(i)

            if kind in ("q", "k", "v"):
                dstb = {"q": qn, "k": kn, "v": vs}[kind]

                @pr.block.scalar
                def _(s, tmP=tmP, dstb=dstb):
                    for dc in range(8):
                        s.wait_ge(pr.tm.h, tmP + dc + 1)
                        ps = pr.PA[:, 512 * (dc % 2):512 * (dc % 2) + 512]
                        pr.sc.inc(s.activation(dstb[:, dc * 512:(dc + 1) * 512], ps, AF.Silu))
                ev_markers[kind] = pr.sc.v
            else:
                veR = pr.ve.v

                def tm_mm(dc):
                    return tmP + dc + 1 + 4 * max(0, dc - 1)

                def tm_tr(j, ts):
                    if j <= 6:
                        return tmP + (j + 2) + 4 * j + ts + 1
                    return tmP + 36 + ts + 1

                @pr.block.tensor
                def _(t, wld=wld, lnd=lnd, tmP=tmP, scP=scP, pi=pi, veR=veR):
                    t.wait_ge(pr.dmw.h, wld)
                    t.wait_ge(pr.ve.h, lnd)
                    t.wait_ge(pr.sc.h, scP)
                    def emit_tr(j):
                        t.wait_ge(pr.sc.h, scP + j + 1)
                        for ts in range(4):
                            idx = j * 4 + ts
                            if idx >= 2:
                                t.wait_ge(pr.ve.h, veR + idx - 1)
                            pt = pr.PT1 if idx % 2 == 0 else pr.PT2
                            pr.tm.inc(t.transpose(pt.bitcast(F32)[:, 0:128],
                                                  chs[:, 512 * (j % 2) + ts * 128: 512 * (j % 2) + ts * 128 + 128],
                                                  pr.ident_f[:]))
                    for dc in range(8):
                        if dc >= 2:
                            t.wait_ge(pr.sc.h, scP + dc - 1)
                        ps = pr.PA[:, 512 * (dc % 2):512 * (dc % 2) + 512]
                        for k in range(8):
                            i = t.matmul(ps, wbuf[pi % 2][:, k * 1024 + dc * 128: k * 1024 + dc * 128 + 128],
                                         xlq[:, k * 512:(k + 1) * 512], start=(k == 0), stop=(k == 7))
                        pr.tm.inc(i)
                        if dc >= 1:
                            emit_tr(dc - 1)
                    emit_tr(7)

                @pr.block.scalar
                def _(s, tmP=tmP):
                    for dc in range(8):
                        s.wait_ge(pr.tm.h, tm_mm(dc))
                        ps = pr.PA[:, 512 * (dc % 2):512 * (dc % 2) + 512]
                        pr.sc.inc(s.copy(chs[:, 512 * (dc % 2):512 * (dc % 2) + 512], ps))

                @pr.block.vector
                def _(v):
                    for dc in range(8):
                        for ts in range(4):
                            idx = dc * 4 + ts
                            v.wait_ge(pr.tm.h, tm_tr(dc, ts))
                            pt = pr.PT1 if idx % 2 == 0 else pr.PT2
                            pr.ve.inc(v.tensor_copy(rS2[:, ts * 1024 + dc * 128: ts * 1024 + dc * 128 + 128],
                                                    pt.bitcast(F32)[:, 0:128]))
                ev_markers["r"] = pr.ve.v

        dmw0 = pr.dmw.v
        tmA = pr.tm.v

        @pr.block.tensor
        def _(t, tmA=tmA, wa_ld=wa_ld, lnd=lnd):
            t.wait_ge(pr.dmw.h, wa_ld)
            t.wait_ge(pr.ve.h, lnd)
            t.wait_ge(pr.ve.h, pr.ve.v)
            for k in range(8):
                i = t.matmul(pr.PD[0:8, 0:512], wab[:, k * 8:(k + 1) * 8],
                             xlq[:, k * 512:(k + 1) * 512], start=(k == 0), stop=(k == 7))
            pr.tm.inc(i)
        fhmm = pr.tm.v

        @pr.block.scalar
        def _(s, fhmm=fhmm, dmav=pr.dma.v):
            s.wait_ge(pr.tm.h, fhmm)
            s.wait_ge(pr.ve.h, small_done)
            s.wait_ge(pr.dma.h, dmav)
            s.activation(fh8[:, 0:512], pr.PD[0:8, 0:512], AF.Exp, bias=dtb[0:8, 0:1])
            s.drain()
            pr.sc.inc(s.activation(fh8[:, 0:512], fh8[:, 0:512], AF.Ln, bias=pr.onecol[0:8, 0:1]))
        sp_done = pr.sc.v

        @pr.block.vector
        def _(v, sp_done=sp_done):
            v.wait_ge(pr.sc.h, sp_done)
            pr.ve.inc(v.tensor_scalar(fh8[:, 0:512], fh8[:, 0:512], dtb[0:8, 1:2], None, ALU.mult))
        fh_done = pr.ve.v

        @pr.block.sync
        def _(sy, fh_done=fh_done):
            sy.wait_ge(pr.ve.h, fh_done)
            sy.wait_ge(pr.dma.h, pr.dma.v)
            pr.dma.inc(sy.dma_start(out=fscr.ap(), in_=fh8[:, 0:512]), 16)
        fhr_ld = pr.dma.v


        for kind, buf, wcol in [("q", qn, 0), ("k", kn, 1)]:
            sqm = pr.sc.v

            @pr.block.scalar
            def _(s, buf=buf, kind=kind, tmguard=pr.tm.v, evm=ev_markers[kind]):
                s.wait_ge(pr.sc.h, evm)
                s.wait_ge(pr.tm.h, tmguard)
                pr.sc.inc(s.activation(sqs[:], buf[:], AF.Square))
            sq_d = pr.sc.v

            for h in range(8):
                tmN = pr.tm.v

                @pr.block.tensor
                def _(t, h=h, sq_d=sq_d, ve_pf=pr.ve.v):
                    t.wait_ge(pr.sc.h, sq_d)
                    t.wait_ge(pr.ve.h, ve_pf)
                    pr.tm.inc(t.matmul(pr.PF[0:1, 0:512], pr.ones_bcol[:],
                                       sqs[:, h * 512:(h + 1) * 512], start=True, stop=True))
                nmm = pr.tm.v

                @pr.block.scalar
                def _(s, nmm=nmm):
                    s.wait_ge(pr.tm.h, nmm)
                    s.activation(rows[:, 0:512], pr.PF[0:1, 0:512], AF.Ln, scale=1.0 / 128.0,
                                 bias=pr.eps_t[:])
                    s.drain()
                    pr.sc.inc(s.activation(rows[:, 0:512], rows[:, 0:512], AF.Exp, scale=-0.5))
                nr_d = pr.sc.v

                @pr.block.tensor
                def _(t, nr_d=nr_d):
                    t.wait_ge(pr.sc.h, nr_d)
                    pr.tm.inc(t.matmul(pr.PF[:, 0:512], pr.ones_frow[:], rows[:, 0:512],
                                       start=True, stop=True))
                nb_mm = pr.tm.v

                @pr.block.vector
                def _(v, nb_mm=nb_mm, h=h, buf=buf, wcol=wcol):
                    v.wait_ge(pr.tm.h, nb_mm)
                    v.tensor_copy(rbc[:], pr.PF[:, 0:512])
                    v.drain()
                    pr.ve.inc(v.scalar_tensor_tensor(buf[:, h * 512:(h + 1) * 512],
                                                     buf[:, h * 512:(h + 1) * 512],
                                                     nrm[:, wcol:wcol + 1], rbc[:],
                                                     ALU.mult, ALU.mult))
                    v.drain()
        qknorm_done = pr.ve.v

        veB = pr.ve.v

        @pr.block.vector
        def _(v, veB=veB, tmguard=pr.tm.v, rmark=ev_markers["r"]):
            v.wait_ge(pr.ve.h, rmark)
            v.wait_ge(pr.tm.h, tmguard)
            v.drain()
            v.memset(bis[:, LO:LO + 32], -40.0)
            v.memset(bis[:, HI:HI + 32], 40.0)
            v.memset(bis[:, MID:MID + 32], 0.0)
            v.drain()
            for it in range(NIT_BIS):
                for u in range(32):
                    ts, h = divmod(u, 8)
                    if u > 0 and u % 8 == 0:
                        v.drain()
                    v.tensor_scalar(msk[:, (u % 8) * 128:(u % 8) * 128 + 128],
                                    rS2[:, ts * 1024 + h * 128: ts * 1024 + h * 128 + 128],
                                    bis[:, MID + u:MID + u + 1], 0.0, ALU.is_ge, ALU.add,
                                    accum_out=bis[:, CNT + u:CNT + u + 1])
                v.drain()
                v.tensor_scalar(selu[:, 0:32], bis[:, CNT:CNT + 32], 31.5, None, ALU.is_ge)
                v.drain()
                v.tensor_scalar(selu[:, 32:64], bis[:, CNT:CNT + 32], 31.5, None, ALU.is_lt)
                v.drain()
                v.select(bis[:, LO:LO + 32], selu[:, 0:32], bis[:, MID:MID + 32], bis[:, LO:LO + 32], add_drain=True)
                v.drain()
                v.select(bis[:, HI:HI + 32], selu[:, 32:64], bis[:, MID:MID + 32], bis[:, HI:HI + 32], add_drain=True)
                v.drain()
                v.tensor_tensor(bis[:, T1:T1 + 32], bis[:, LO:LO + 32], bis[:, HI:HI + 32], ALU.add)
                v.drain()
                v.tensor_scalar(bis[:, MID:MID + 32], bis[:, T1:T1 + 32], 0.5, None, ALU.mult)
                v.drain()
            for u in range(32):
                ts, h = divmod(u, 8)
                i = v.tensor_scalar(sqs[:, ts * 1024 + h * 128: ts * 1024 + h * 128 + 128],
                                    rS2[:, ts * 1024 + h * 128: ts * 1024 + h * 128 + 128],
                                    bis[:, LO + u:LO + u + 1], None, ALU.is_ge)
            pr.ve.inc(i)
        bis_done = pr.ve.v

        @pr.block.scalar
        def _(s, bis_done=bis_done):
            s.wait_ge(pr.ve.h, ev_markers["r"])
            pr.sc.inc(s.activation(sgm[:], rS2[:, 0:4096], AF.Sigmoid))
        sgm_done = pr.sc.v

        @pr.block.vector
        def _(v, sgm_done=sgm_done, bis_done=bis_done):
            v.wait_ge(pr.sc.h, sgm_done)
            v.wait_ge(pr.ve.h, bis_done)
            pr.ve.inc(v.tensor_tensor(mmt[:], sgm[:], sqs[:], ALU.mult))
        mskd_done = pr.ve.v

        tmM = pr.tm.v
        veM = pr.ve.v

        @pr.block.tensor
        def _(t, tmM=tmM, veM=veM, mskd_done=mskd_done):
            t.wait_ge(pr.ve.h, mskd_done)
            for h in range(8):
                for ts in range(4):
                    idx = h * 4 + ts
                    if idx >= 2:
                        t.wait_ge(pr.ve.h, veM + idx - 1)
                    pt = pr.PT1 if idx % 2 == 0 else pr.PT2
                    pr.tm.inc(t.transpose(pt[:, 0:128],
                                          mmt[:, ts * 1024 + h * 128: ts * 1024 + h * 128 + 128],
                                          pr.ident_b[:]))

        @pr.block.vector
        def _(v, tmM=tmM):
            for h in range(8):
                for ts in range(4):
                    idx = h * 4 + ts
                    v.wait_ge(pr.tm.h, tmM + idx + 1)
                    pt = pr.PT1 if idx % 2 == 0 else pr.PT2
                    pr.ve.inc(v.tensor_copy(sqs[:, h * 512 + ts * 128: h * 512 + ts * 128 + 128],
                                            pt[:, 0:128]))
        mmt_done = pr.ve.v

        for h in range(8):
            tmH = pr.tm.v
            scH = pr.sc.v
            veH = pr.ve.v

            @pr.block.tensor
            def _(t, h=h, mmt_done=mmt_done, veH=veH, tmH=tmH):
                t.wait_ge(pr.ve.h, mmt_done)
                if h >= 1:
                    t.wait_ge(pr.ve.h, veH)
                pr.tm.inc(t.matmul(pr.PF[0:1, 0:512], pr.ones_bcol[:],
                                   sqs[:, h * 512:(h + 1) * 512], start=True, stop=True))
            ts_mm = pr.tm.v

            @pr.block.scalar
            def _(s, ts_mm=ts_mm, h=h):
                s.wait_ge(pr.tm.h, ts_mm)
                s.activation(rows[:, 0:512], pr.PF[0:1, 0:512], AF.Ln, bias=pr.eps9[:])
                s.drain()
                pr.sc.inc(s.activation(rows[:, 0:512], rows[:, 0:512], AF.Exp, scale=-1.0))
            rt_done = pr.sc.v

            @pr.block.sync
            def _(sy, h=h, fhr_ld=fhr_ld):
                sy.wait_ge(pr.dma.h, pr.dma.v)
                sy.wait_ge(pr.tm.h, pr.tm.v)
                pr.dma.inc(sy.dma_start(out=rows[:, 512:1024], in_=fscr.ap()[h, :]), 16)
            fh_row = pr.dma.v

            @pr.block.tensor
            def _(t, rt_done=rt_done, h=h, fh_row=fh_row):
                t.wait_ge(pr.sc.h, rt_done)
                t.wait_ge(pr.dma.h, fh_row)
                t.matmul(pr.PF[:, 0:512], pr.ones_frow[:], rows[:, 0:512], start=True, stop=True)
                pr.tm.inc(t.matmul(pr.PB[:, 0:512], pr.ones_frow[:], rows[:, 512:1024],
                                   start=True, stop=True))
            bc_mm = pr.tm.v

            @pr.block.vector
            def _(v, bc_mm=bc_mm, h=h):
                v.wait_ge(pr.tm.h, bc_mm)
                v.tensor_copy(rbc[:], pr.PF[:, 0:512])
                v.drain()
                v.tensor_tensor(fts[:], sqs[:, h * 512:(h + 1) * 512], rbc[:], ALU.mult)
                v.drain()
                pr.ve.inc(v.tensor_tensor(fts[:], fts[:], pr.PB[:, 0:512], ALU.mult))
                v.drain()
            f_done = pr.ve.v

            prev_o_ev = pr.sc.v
            for ck in range(4):
                csl = slice(ck * 128, ck * 128 + 128)
                first = (tt == 0 and ck == 0)
                tmC = pr.tm.v
                scC = pr.sc.v
                veC = pr.ve.v

                @pr.block.vector
                def _(v, h=h, ck=ck, csl=csl, f_done=f_done, veC=veC):
                    v.wait_ge(pr.ve.h, f_done)
                    pr.ve.inc(v.tensor_tensor_scan(chs[:, 0:128], fts[:, csl], fts[:, csl],
                                                   0.0, ALU.add, ALU.bypass))
                    v.drain()
                L_done = pr.ve.v

                @pr.block.scalar
                def _(s, L_done=L_done, csl=csl):
                    s.wait_ge(pr.ve.h, L_done)
                    s.activation(chb[:, 0:128], chs[:, 0:128], AF.Exp)
                    s.drain()
                    s.activation(chb[:, 128:256], chs[:, 0:128], AF.Exp, scale=-1.0)
                    s.drain()
                    s.activation(chs[:, 128:256], fts[:, csl], AF.Exp)
                    s.drain()
                    pr.sc.inc(s.copy(chs[:, 384:385], chs[:, 127:128]))
                exp_done = pr.sc.v

                @pr.block.scalar
                def _(s, exp_done=exp_done):
                    s.wait_ge(pr.sc.h, exp_done)
                    pr.sc.inc(s.activation(chs[:, 385:386], chs[:, 384:385], AF.Exp))
                eL_done = pr.sc.v

                @pr.block.vector
                def _(v, h=h, csl=csl, eL_done=eL_done):
                    v.wait_ge(pr.sc.h, eL_done)
                    v.tensor_scalar(chs[:, 256:384], chs[:, 128:256], -1.0, 1.0, ALU.mult, ALU.add)
                    v.drain()
                    v.tensor_tensor(chb[:, 256:384], qn[:, h * 512:h * 512 + 512][:, csl],
                                    chb[:, 0:128], ALU.mult)
                    v.drain()
                    v.tensor_tensor(chb[:, 384:512], kn[:, h * 512:h * 512 + 512][:, csl],
                                    chs[:, 256:384], ALU.mult)
                    v.drain()
                    v.tensor_tensor(chb[:, 512:640], chb[:, 384:512], chb[:, 128:256], ALU.mult)
                    v.drain()
                    pr.ve.inc(v.tensor_scalar(chb[:, 640:768], chb[:, 512:640],
                                              chs[:, 385:386], None, ALU.mult))
                    v.drain()
                qk_done = pr.ve.v

                tmC2 = pr.tm.v

                @pr.block.tensor
                def _(t, h=h, ck=ck, qk_done=qk_done, tmC2=tmC2, first=first):
                    t.wait_ge(pr.ve.h, qk_done)
                    pr.tm.inc(t.matmul(pr.PB[:, 0:128], chb[:, 512:640], chb[:, 256:384],
                                       start=True, stop=True))
                    pr.tm.inc(t.transpose(pr.PT1[:, 0:128],
                                          vs[:, h * 512:h * 512 + 512][:, ck * 128:ck * 128 + 128],
                                          pr.ident_b[:]))
                    pr.tm.inc(t.transpose(pr.PT2[:, 0:128], chb[:, 640:768], pr.ident_b[:]))
                pt_mm = pr.tm.v

                @pr.block.vector
                def _(v, pt_mm=pt_mm):
                    v.wait_ge(pr.tm.h, pt_mm)
                    v.tensor_tensor(chb[:, 768:896], pr.PB[:, 0:128], pr.trimask[:], ALU.mult)
                    v.drain()
                    v.tensor_copy(vtt[:, 0:128], pr.PT1[:, 0:128])
                    v.drain()
                    pr.ve.inc(v.tensor_copy(chb[:, 896:1024], pr.PT2[:, 0:128]))
                    v.drain()
                pm_done = pr.ve.v

                @pr.block.tensor
                def _(t, h=h, pm_done=pm_done, first=first, prev_o_ev=prev_o_ev):
                    t.wait_ge(pr.ve.h, pm_done)
                    t.wait_ge(pr.sc.h, prev_o_ev)
                    t.matmul(pr.PC[:, 0:128], vtt[:, 0:128], chb[:, 768:896],
                             start=True, stop=first)
                    if not first:
                        t.matmul(pr.PC[:, 0:128], Sbf[:, h * 128:(h + 1) * 128],
                                 chb[:, 256:384], start=False, stop=True)
                    i = t.matmul(pr.PD[:, 0:128], chb[:, 896:1024], vtt[:, 0:128],
                                 start=True, stop=True)
                    pr.tm.inc(i)
                o_mm = pr.tm.v

                @pr.block.scalar
                def _(s, h=h, ck=ck, o_mm=o_mm, tt=tt):
                    s.wait_ge(pr.tm.h, o_mm)
                    pr.sc.inc(s.copy(oT[:, h * 512 + ck * 128: h * 512 + ck * 128 + 128],
                                     pr.PC[:, 0:128]))
                o_ev = pr.sc.v
                prev_o_ev = o_ev

                @pr.block.vector
                def _(v, h=h, o_mm=o_mm, o_ev=o_ev):
                    v.wait_ge(pr.tm.h, o_mm)
                    v.scalar_tensor_tensor(S32[:, h * 128:(h + 1) * 128], S32[:, h * 128:(h + 1) * 128],
                                           chs[:, 385:386], pr.PD[:, 0:128], ALU.mult, ALU.add)
                    v.drain()
                    pr.ve.inc(v.tensor_copy(Sbf[:, h * 128:(h + 1) * 128], S32[:, h * 128:(h + 1) * 128]))
                    v.drain()

        last_o_sc = pr.sc.v
        last_ve = pr.ve.v

        @pr.block.scalar
        def _(s, last_o_sc=last_o_sc, last_ve=last_ve, tmg=pr.tm.v):
            s.wait_ge(pr.sc.h, last_o_sc)
            s.wait_ge(pr.ve.h, last_ve)
            s.wait_ge(pr.tm.h, tmg)
            pr.sc.inc(s.activation(sqs[:], oT[:], AF.Square))
        sq_done = pr.sc.v

        tmG = pr.tm.v

        @pr.block.tensor
        def _(t, sq_done=sq_done, tmG=tmG):
            t.wait_ge(pr.sc.h, sq_done)
            for h in range(8):
                i = t.matmul(pr.PF[0:1, 0:512], pr.ones_bcol[:], sqs[:, h * 512:(h + 1) * 512],
                             start=(h == 0), stop=(h == 7))
            pr.tm.inc(i)
        gs_mm = pr.tm.v

        @pr.block.scalar
        def _(s, gs_mm=gs_mm):
            s.wait_ge(pr.tm.h, gs_mm)
            s.activation(rows[:, 0:512], pr.PF[0:1, 0:512], AF.Ln, scale=1.0 / 1024.0,
                         bias=pr.eps_t[:])
            s.drain()
            pr.sc.inc(s.activation(rows[:, 0:512], rows[:, 0:512], AF.Exp, scale=-0.5))
        gr_done = pr.sc.v

        @pr.block.tensor
        def _(t, gr_done=gr_done):
            t.wait_ge(pr.sc.h, gr_done)
            pr.tm.inc(t.matmul(pr.PF[:, 0:512], pr.ones_frow[:], rows[:, 0:512],
                               start=True, stop=True))
        grb_mm = pr.tm.v

        @pr.block.vector
        def _(v, grb_mm=grb_mm):
            v.wait_ge(pr.tm.h, grb_mm)
            v.tensor_copy(rbc[:], pr.PF[:, 0:512])
            v.drain()
            for h in range(8):
                i = v.scalar_tensor_tensor(oT[:, h * 512:(h + 1) * 512], oT[:, h * 512:(h + 1) * 512],
                                           nrm[:, 2 + h:3 + h], rbc[:], ALU.mult, ALU.mult)
                v.drain()
            pr.ve.inc(i)
        on_done = pr.ve.v

        dmwO = pr.dmw.v

        @pr.block.gpsimd
        def _(g, tt=tt):
            g.wait_ge(pr.dmw.h, pr.dmw.v)
            g.wait_ge(pr.tm.h, pr.tm.v)
            pr.dmw.inc(g.dma_start(out=wbuf[0][:].rearrange("p (c m) -> p c m", c=8),
                                   in_=io["b2_wo"].ap().rearrange("(c p) m -> p c m", p=128)), 16)
        wo_ld = pr.dmw.v

        tmW = pr.tm.v
        veW = pr.ve.v

        @pr.block.tensor
        def _(t, wo_ld=wo_ld, on_done=on_done, tmW=tmW, veW=veW):
            t.wait_ge(pr.dmw.h, wo_ld)
            t.wait_ge(pr.ve.h, on_done)
            for dc in range(8):
                if dc >= 2:
                    t.wait_ge(pr.ve.h, veW + dc - 1)
                ps = pr.PA[:, 512 * (dc % 2):512 * (dc % 2) + 512]
                for k in range(8):
                    i = t.matmul(ps, wbuf[0][:, k * 1024 + dc * 128: k * 1024 + dc * 128 + 128],
                                 oT[:, k * 512:(k + 1) * 512], start=(k == 0), stop=(k == 7))
                pr.tm.inc(i)

        @pr.block.vector
        def _(v, tmW=tmW, tt=tt):
            for dc in range(8):
                v.wait_ge(pr.tm.h, tmW + dc + 1)
                sl = slice(dc * T + tt * 512, dc * T + tt * 512 + 512)
                ps = pr.PA[:, 512 * (dc % 2):512 * (dc % 2) + 512]
                pr.ve.inc(v.tensor_tensor(pr.xT[:, sl], pr.xT[:, sl], ps, ALU.add))
                v.drain()
        prev_tt = pr.ve.v
    ph.close()
    return pr.ve.v


def build_b2():
    nc = bass.Bass("TRN2", target_bir_lowering=False, debug=False)
    io = {
        "xin": nc.dram_tensor("xin", [D, T], F32, kind="ExternalInput"),
        "b2_wq": nc.dram_tensor("b2_wq", [D, D], F32, kind="ExternalInput"),
        "b2_wk": nc.dram_tensor("b2_wk", [D, D], F32, kind="ExternalInput"),
        "b2_wv": nc.dram_tensor("b2_wv", [D, D], F32, kind="ExternalInput"),
        "b2_wr": nc.dram_tensor("b2_wr", [D, D], F32, kind="ExternalInput"),
        "b2_wo": nc.dram_tensor("b2_wo", [D, D], F32, kind="ExternalInput"),
        "b2_wa": nc.dram_tensor("b2_wa", [D, 8], F32, kind="ExternalInput"),
        "b2_alog": nc.dram_tensor("b2_alog", [8], F32, kind="ExternalInput"),
        "b2_dtb": nc.dram_tensor("b2_dtb", [8], F32, kind="ExternalInput"),
        "b2_qnw": nc.dram_tensor("b2_qnw", [128], F32, kind="ExternalInput"),
        "b2_knw": nc.dram_tensor("b2_knw", [128], F32, kind="ExternalInput"),
        "b2_gnw": nc.dram_tensor("b2_gnw", [1024], F32, kind="ExternalInput"),
    }
    xt_out = nc.dram_tensor("xt_out", [D, T], F32, kind="ExternalOutput")
    with ExitStack() as ctx:
        pr = Prog(nc, ctx, xln_cols=DCH * 1024)
        pr.emit_consts()
        xt_ready = emit_load_xt(pr, io["xin"])
        emit_block2(pr, io, xt_ready)
        emit_store_xt(pr, xt_out)
    return nc


def emit_store_xt(pr, xt_out):
    @pr.block.sync
    def _(sy):
        sy.wait_ge(pr.ve.h, pr.ve.v)
        o = xt_out.ap().rearrange("(c p) t -> p c t", p=128)
        pr.od.inc(sy.dma_start(out=o, in_=pr.xT[:].rearrange("p (c t) -> p c t", c=DCH)), 16)
        sy.wait_ge(pr.od.h, pr.od.v)


def build_b0(x_natural=True, upto=99):
    nc = bass.Bass("TRN2", target_bir_lowering=False, debug=False)
    io = {
        "xin": nc.dram_tensor("xin", [T, D] if x_natural else [D, T], F32, kind="ExternalInput"),
        "b0_ctx": nc.dram_tensor("b0_ctx", [512, CD], F32, kind="ExternalInput"),
        "b0_wq": nc.dram_tensor("b0_wq", [D, D], F32, kind="ExternalInput"),
        "b0_wk": nc.dram_tensor("b0_wk", [CD, D], F32, kind="ExternalInput"),
        "b0_wv": nc.dram_tensor("b0_wv", [CD, D], F32, kind="ExternalInput"),
        "b0_wo": nc.dram_tensor("b0_wo", [D, D], F32, kind="ExternalInput"),
        "b0_bo": nc.dram_tensor("b0_bo", [D], F32, kind="ExternalInput"),
    }
    xt_out = nc.dram_tensor("xt_out", [D, T], F32, kind="ExternalOutput")
    with ExitStack() as ctx:
        pr = Prog(nc, ctx)
        pr.emit_consts()
        emit_block0(pr, io, have_x_natural=x_natural, upto=upto)
        emit_store_xt(pr, xt_out)
    return nc


def emit_barrier(pr):

    @pr.block.tensor
    def _(t):
        pr.tm.inc(t.drain())

    @pr.block.vector
    def _(v):
        pr.ve.inc(v.drain())

    @pr.block.scalar
    def _(s):
        pr.sc.inc(s.drain())

    @pr.block.gpsimd
    def _(g):
        pr.gp.inc(g.drain())

    vals = [(s, s.v) for s in (pr.dma, pr.dmg, pr.dmw, pr.dmk, pr.cc,
                               pr.tm, pr.sc, pr.ve, pr.gp, pr.od)]
    for sec in (pr.block.tensor, pr.block.vector, pr.block.scalar,
                pr.block.gpsimd, pr.block.sync):
        @sec
        def _(e, vals=vals):
            for s, v in vals:
                e.wait_ge(s.h, v)
    return pr.ve.v


def build_all():
    nc = bass.Bass("TRN2", target_bir_lowering=False, debug=False)
    io = {
        "xin": nc.dram_tensor("xin", [T, D], F32, kind="ExternalInput"),
        "b0_ctx": nc.dram_tensor("b0_ctx", [512, CD], F32, kind="ExternalInput"),
        "b0_wq": nc.dram_tensor("b0_wq", [D, D], F32, kind="ExternalInput"),
        "b0_wk": nc.dram_tensor("b0_wk", [CD, D], F32, kind="ExternalInput"),
        "b0_wv": nc.dram_tensor("b0_wv", [CD, D], F32, kind="ExternalInput"),
        "b0_wo": nc.dram_tensor("b0_wo", [D, D], F32, kind="ExternalInput"),
        "b0_bo": nc.dram_tensor("b0_bo", [D], F32, kind="ExternalInput"),
        "b1_w1t": nc.dram_tensor("b1_w1t", [1024, 4096], F32, kind="ExternalInput"),
        "b1_w2t": nc.dram_tensor("b1_w2t", [1024, 4096], F32, kind="ExternalInput"),
        "b1_h1t": nc.dram_tensor("b1_h1t", [4096, 4096], F32, kind="ExternalInput"),
        "b1_h2t": nc.dram_tensor("b1_h2t", [4096, 4096], F32, kind="ExternalInput"),
        "b1_w3t": nc.dram_tensor("b1_w3t", [4096, 1024], F32, kind="ExternalInput"),
        "b1_s1": nc.dram_tensor("b1_s1", [4096], F32, kind="ExternalInput"),
        "b1_s2": nc.dram_tensor("b1_s2", [4096], F32, kind="ExternalInput"),
        "b1_sh1": nc.dram_tensor("b1_sh1", [4096], F32, kind="ExternalInput"),
        "b1_sh2": nc.dram_tensor("b1_sh2", [4096], F32, kind="ExternalInput"),
        "b1_s3": nc.dram_tensor("b1_s3", [1024], F32, kind="ExternalInput"),
        "b2_wq": nc.dram_tensor("b2_wq", [D, D], F32, kind="ExternalInput"),
        "b2_wk": nc.dram_tensor("b2_wk", [D, D], F32, kind="ExternalInput"),
        "b2_wv": nc.dram_tensor("b2_wv", [D, D], F32, kind="ExternalInput"),
        "b2_wr": nc.dram_tensor("b2_wr", [D, D], F32, kind="ExternalInput"),
        "b2_wo": nc.dram_tensor("b2_wo", [D, D], F32, kind="ExternalInput"),
        "b2_wa": nc.dram_tensor("b2_wa", [D, 8], F32, kind="ExternalInput"),
        "b2_alog": nc.dram_tensor("b2_alog", [8], F32, kind="ExternalInput"),
        "b2_dtb": nc.dram_tensor("b2_dtb", [8], F32, kind="ExternalInput"),
        "b2_qnw": nc.dram_tensor("b2_qnw", [128], F32, kind="ExternalInput"),
        "b2_knw": nc.dram_tensor("b2_knw", [128], F32, kind="ExternalInput"),
        "b2_gnw": nc.dram_tensor("b2_gnw", [1024], F32, kind="ExternalInput"),
        "b3_wint": nc.dram_tensor("b3_wint", [4096, 1024], F32, kind="ExternalInput"),
        "b3_wout": nc.dram_tensor("b3_wout", [4096, 1024], F32, kind="ExternalInput"),
        "b3_bin": nc.dram_tensor("b3_bin", [4096], F32, kind="ExternalInput"),
        "b3_bout": nc.dram_tensor("b3_bout", [1024], F32, kind="ExternalInput"),
    }
    xt_out = nc.dram_tensor("xt_out", [D, T], F32, kind="ExternalOutput")
    with ExitStack() as ctx:
        pr = Prog(nc, ctx, xln_cols=DCH * T)
        pr.emit_consts()
        emit_block0(pr, io, have_x_natural=True)
        emit_barrier(pr)
        emit_block1(pr, io, pr.ve.v)
        emit_barrier(pr)
        emit_block2(pr, io, pr.ve.v)
        emit_barrier(pr)
        emit_block3(pr, io, pr.ve.v)
        emit_store_xt(pr, xt_out)
    return nc


def kernel(**inputs):
    from concourse.bass_utils import run_bass_kernel_spmd
    f32 = np.float32
    g = lambda k: np.ascontiguousarray(np.asarray(inputs[k], dtype=f32))
    cores = list(range(NCORE))

    x = g("L0_attn_x")
    ctxf = np.ascontiguousarray(g("L0_attn_context").reshape(4096, CD))
    nc = build_all()
    w1t = np.ascontiguousarray(g("L0_ffn_w1_w").T)
    w2t = np.ascontiguousarray(g("L0_ffn_w2_w").T)
    h1t = np.ascontiguousarray(g("L0_ffn_h1_w").T)
    h2t = np.ascontiguousarray(g("L0_ffn_h2_w").T)
    w3t = np.ascontiguousarray(g("L0_ffn_w3_w").T)
    wint = np.ascontiguousarray(g("L1_ffn_W_in").T)
    in_maps = [dict(
        xin=np.ascontiguousarray(x[c]),
        b0_ctx=np.ascontiguousarray(ctxf[c * 512:(c + 1) * 512]),
        b0_wq=g("L0_attn_Wq"), b0_wk=g("L0_attn_Wk"), b0_wv=g("L0_attn_Wv"),
        b0_wo=g("L0_attn_Wout"), b0_bo=g("L0_attn_bout"),
        b1_w1t=w1t, b1_w2t=w2t, b1_h1t=h1t, b1_h2t=h2t, b1_w3t=w3t,
        b1_s1=g("L0_ffn_w1_s"), b1_s2=g("L0_ffn_w2_s"),
        b1_sh1=g("L0_ffn_h1_s"), b1_sh2=g("L0_ffn_h2_s"), b1_s3=g("L0_ffn_w3_s"),
        b2_wq=g("L1_attn_Wq"), b2_wk=g("L1_attn_Wk"), b2_wv=g("L1_attn_Wv"),
        b2_wr=g("L1_attn_Wr"), b2_wo=g("L1_attn_Wo"), b2_wa=g("L1_attn_Wa"),
        b2_alog=g("L1_attn_A_log"), b2_dtb=g("L1_attn_dt_bias"),
        b2_qnw=g("L1_attn_q_norm_w"), b2_knw=g("L1_attn_k_norm_w"),
        b2_gnw=g("L1_attn_g_norm_w"),
        b3_wint=wint, b3_wout=g("L1_ffn_W_out"),
        b3_bin=g("L1_ffn_b_in"), b3_bout=g("L1_ffn_b_out")) for c in cores]
    res = run_bass_kernel_spmd(nc, in_maps, core_ids=cores)
    return np.stack([res.results[c]["xt_out"].T for c in cores]).astype(f32)


def kernel_unfused(**inputs):
    from concourse.bass_utils import run_bass_kernel_spmd
    f32 = np.float32
    g = lambda k: np.ascontiguousarray(np.asarray(inputs[k], dtype=f32))
    cores = list(range(NCORE))

    x = g("L0_attn_x")
    ctxf = np.ascontiguousarray(g("L0_attn_context").reshape(4096, CD))

    nc0 = build_b0(x_natural=True)
    in0 = [dict(xin=np.ascontiguousarray(x[c]),
                b0_ctx=np.ascontiguousarray(ctxf[c * 512:(c + 1) * 512]),
                b0_wq=g("L0_attn_Wq"), b0_wk=g("L0_attn_Wk"),
                b0_wv=g("L0_attn_Wv"), b0_wo=g("L0_attn_Wout"),
                b0_bo=g("L0_attn_bout")) for c in cores]
    r0 = run_bass_kernel_spmd(nc0, in0, core_ids=cores)
    xt = [np.ascontiguousarray(r0.results[c]["xt_out"]) for c in cores]

    nc1 = build_b1()
    w1t = np.ascontiguousarray(g("L0_ffn_w1_w").T)
    w2t = np.ascontiguousarray(g("L0_ffn_w2_w").T)
    h1t = np.ascontiguousarray(g("L0_ffn_h1_w").T)
    h2t = np.ascontiguousarray(g("L0_ffn_h2_w").T)
    w3t = np.ascontiguousarray(g("L0_ffn_w3_w").T)
    in1 = [dict(xin=xt[c], b1_w1t=w1t, b1_w2t=w2t, b1_h1t=h1t, b1_h2t=h2t,
                b1_w3t=w3t, b1_s1=g("L0_ffn_w1_s"), b1_s2=g("L0_ffn_w2_s"),
                b1_sh1=g("L0_ffn_h1_s"), b1_sh2=g("L0_ffn_h2_s"),
                b1_s3=g("L0_ffn_w3_s")) for c in cores]
    r1 = run_bass_kernel_spmd(nc1, in1, core_ids=cores)
    xt = [np.ascontiguousarray(r1.results[c]["xt_out"]) for c in cores]

    nc2 = build_b2()
    in2 = [dict(xin=xt[c], b2_wq=g("L1_attn_Wq"), b2_wk=g("L1_attn_Wk"),
                b2_wv=g("L1_attn_Wv"), b2_wr=g("L1_attn_Wr"),
                b2_wo=g("L1_attn_Wo"), b2_wa=g("L1_attn_Wa"),
                b2_alog=g("L1_attn_A_log"), b2_dtb=g("L1_attn_dt_bias"),
                b2_qnw=g("L1_attn_q_norm_w"), b2_knw=g("L1_attn_k_norm_w"),
                b2_gnw=g("L1_attn_g_norm_w")) for c in cores]
    r2 = run_bass_kernel_spmd(nc2, in2, core_ids=cores)
    xt = [np.ascontiguousarray(r2.results[c]["xt_out"]) for c in cores]

    nc3 = build_b3()
    wint = np.ascontiguousarray(g("L1_ffn_W_in").T)
    in3 = [dict(xin=xt[c], b3_wint=wint, b3_wout=g("L1_ffn_W_out"),
                b3_bin=g("L1_ffn_b_in"), b3_bout=g("L1_ffn_b_out"))
           for c in cores]
    r3 = run_bass_kernel_spmd(nc3, in3, core_ids=cores)

    return np.stack([r3.results[c]["xt_out"].T for c in cores]).astype(f32)


# version 04
# speedup vs baseline: 1.0200x; 1.0090x over previous
import numpy as np
from contextlib import ExitStack
import concourse.bass as bass
import concourse.mybir as mybir
from concourse.bass_utils import run_bass_kernel_spmd

F32 = mybir.dt.float32
BF16 = mybir.dt.bfloat16
I32 = mybir.dt.int32
AF = mybir.ActivationFunctionType
ALU = mybir.AluOpType
AX = mybir.AxisListType

D = 1024
T = 2048
DCH = 8
NTT = 4
H = 8
KV = 4096
CD = 768
NCORE = 8


class Sem:
    def __init__(self, ctx, nc, name):
        self.h = ctx.enter_context(nc.semaphore(name))
        self.v = 0

    def inc(self, instr, n=1):
        instr.then_inc(self.h, n)
        self.v += n
        return self.v


class Prog:
    def __init__(self, nc, ctx, xln_cols=DCH * T, with_trimask=True):
        self.nc = nc
        self.ctx = ctx
        self.block = ctx.enter_context(nc.Block())
        for name in ["dma", "dmg", "dmw", "dmk", "cc", "tm", "sc", "ve", "gp", "od"]:
            setattr(self, name, Sem(ctx, nc, "s_" + name))
        self.PA = ctx.enter_context(nc.psum_tensor("PA", [128, 1024], F32))
        self.PB = ctx.enter_context(nc.psum_tensor("PB", [128, 512], F32))
        self.PC = ctx.enter_context(nc.psum_tensor("PC", [128, 512], F32))
        self.PD = ctx.enter_context(nc.psum_tensor("PD", [128, 512], F32))
        self.PT1 = ctx.enter_context(nc.psum_tensor("PT1", [128, 1024], BF16))
        self.PT2 = ctx.enter_context(nc.psum_tensor("PT2", [128, 1024], BF16))
        self.PF = ctx.enter_context(nc.psum_tensor("PF", [128, 512], F32))
        self.ident_b = ctx.enter_context(nc.sbuf_tensor("ident_b", [128, 128], BF16))
        self.ident_f = ctx.enter_context(nc.sbuf_tensor("ident_f", [128, 128], F32))
        self.ones_bcol = ctx.enter_context(nc.sbuf_tensor("ones_bcol", [128, 1], BF16))
        self.ones_brow = ctx.enter_context(nc.sbuf_tensor("ones_brow", [1, 128], BF16))
        self.ones_frow = ctx.enter_context(nc.sbuf_tensor("ones_frow", [1, 128], F32))
        self.ones_f11 = ctx.enter_context(nc.sbuf_tensor("ones_f11", [1, 1], F32))
        self.eps_t = ctx.enter_context(nc.sbuf_tensor("eps_t", [1, 1], F32))
        self.neg_half = ctx.enter_context(nc.sbuf_tensor("neg_half", [128, 1], F32))
        self.srow512 = ctx.enter_context(nc.sbuf_tensor("srow512", [1, 512], F32))
        self.srow32 = ctx.enter_context(nc.sbuf_tensor("srow32", [1, 128], F32))
        self.s11 = ctx.enter_context(nc.sbuf_tensor("s11", [1, 1], F32))
        self.s12 = ctx.enter_context(nc.sbuf_tensor("s12", [1, 1], F32))
        self.s13 = ctx.enter_context(nc.sbuf_tensor("s13", [1, 1], F32))
        self.s14 = ctx.enter_context(nc.sbuf_tensor("s14", [1, 2], F32))
        self.zrow8 = ctx.enter_context(nc.sbuf_tensor("zrow8", [1, 8], F32))
        self.trimask = (ctx.enter_context(nc.sbuf_tensor("trimask", [128, 128], BF16))
                        if with_trimask else None)
        self.onecol = ctx.enter_context(nc.sbuf_tensor("onecol", [128, 1], F32))
        self.eps9 = ctx.enter_context(nc.sbuf_tensor("eps9", [1, 1], F32))
        self.xT = ctx.enter_context(nc.sbuf_tensor("xT", [128, DCH * T], F32))
        self.xln = ctx.enter_context(nc.sbuf_tensor("xln", [128, xln_cols], BF16))
        self.smu = ctx.enter_context(nc.sbuf_tensor("smu", [1, 512], F32))
        self.svar = ctx.enter_context(nc.sbuf_tensor("svar", [1, 512], F32))
        self.sr = ctx.enter_context(nc.sbuf_tensor("sr", [1, 512], F32))
        self.smur = ctx.enter_context(nc.sbuf_tensor("smur", [1, 512], F32))
        self.rb = ctx.enter_context(nc.sbuf_tensor("rb", [128, 512], F32))
        self.mrb = ctx.enter_context(nc.sbuf_tensor("mrb", [128, 512], F32))
        self.tmpf = ctx.enter_context(nc.sbuf_tensor("tmpf", [128, 512], F32))

    def emit_consts(self):
        pr = self
        nc = self.nc
        cst = ExitStack()
        it1 = cst.enter_context(nc.sbuf_tensor("c_it1", [128, 128], F32))
        it2 = cst.enter_context(nc.sbuf_tensor("c_it2", [128, 128], F32))

        @pr.block.gpsimd
        def _(g):
            g.iota(it1[:], pattern=[[1, 128]], base=0, channel_multiplier=0,
                   allow_small_or_imprecise_dtypes=True)
            g.iota(it2[:], pattern=[[0, 128]], base=0, channel_multiplier=1,
                   allow_small_or_imprecise_dtypes=True)
            g.memset(pr.ones_bcol[:], 1.0)
            g.memset(pr.ones_brow[:], 1.0)
            g.memset(pr.ones_frow[:], 1.0)
            g.memset(pr.eps_t[:], 1e-5)
            g.memset(pr.neg_half[:], -0.5)
            g.memset(pr.onecol[:], 1.0)
            g.memset(pr.eps9[:], 1e-9)
            pr.gp.inc(g.memset(pr.ones_f11[:], 1.0))
        gp0 = pr.gp.v

        @pr.block.vector
        def _(v):
            v.wait_ge(pr.gp.h, gp0)
            v.tensor_tensor(pr.ident_f[:], it1[:], it2[:], ALU.is_equal)
            v.drain()
            if pr.trimask is not None:
                v.tensor_copy(pr.ident_b[:], pr.ident_f[:])
                v.drain()
                pr.ve.inc(v.tensor_tensor(pr.trimask[:], it2[:], it1[:], ALU.is_le))
            else:
                pr.ve.inc(v.tensor_copy(pr.ident_b[:], pr.ident_f[:]))

        ve0 = pr.ve.v
        for sec in (pr.block.tensor, pr.block.vector, pr.block.scalar,
                    pr.block.gpsimd, pr.block.sync):
            @sec
            def _(e):
                e.wait_ge(pr.ve.h, ve0)
        cst.close()

    def ln(self, src, dst, sq_scratch, waits, tts=None, dst_stride=None, dst_f32=None):
        pr = self
        eps = 1e-5
        xbf = dst
        sq = sq_scratch
        if tts is None:
            tts = list(range(NTT))
        if dst_stride is None:
            dst_stride = 512 * len(tts)
        NT_ = len(tts)
        tt0 = tts[0]
        def dsl(c, tt):
            return slice(c * dst_stride + (tt - tt0) * 512,
                         c * dst_stride + (tt - tt0) * 512 + 512)
        def ssl(c, tt):
            return slice(c * T + tt * 512, c * T + tt * 512 + 512)

        @pr.block.scalar
        def _(s):
            for sem, val in waits:
                s.wait_ge(sem.h, val)
            for c in range(DCH):
                for tt in tts:
                    s.copy(xbf[:, dsl(c, tt)], src[:, ssl(c, tt)])
                    i = s.activation(sq[:, dsl(c, tt)], src[:, ssl(c, tt)], AF.Square)
            pr.sc.inc(i)
        cast_done = pr.sc.v

        tm0, sc0, ve0 = pr.tm.v, pr.sc.v, pr.ve.v

        @pr.block.tensor
        def _(t):
            t.wait_ge(pr.sc.h, cast_done)
            t.wait_ge(pr.ve.h, ve0)
            for j, tt in enumerate(tts):
                if j >= 1:
                    t.wait_ge(pr.ve.h, ve0 + 4 * j - 3)
                for c in range(DCH):
                    t.matmul(pr.PC[0:1, 0:512], pr.ones_bcol[:],
                             xbf[:, dsl(c, tt)], start=(c == 0), stop=(c == DCH - 1))
                for c in range(DCH):
                    i2 = t.matmul(pr.PD[0:1, 0:512], pr.ones_bcol[:],
                                  sq[:, dsl(c, tt)], start=(c == 0), stop=(c == DCH - 1))
                pr.tm.inc(i2)
                t.wait_ge(pr.ve.h, ve0 + 4 * j + 2)
                pr.tm.inc(t.matmul(pr.PF[:], pr.ones_frow[:], pr.sr[:], start=True, stop=True))
                t.wait_ge(pr.ve.h, ve0 + 4 * j + 3)
                pr.tm.inc(t.matmul(pr.PF[:], pr.ones_frow[:], pr.smur[:], start=True, stop=True))

        @pr.block.vector
        def _(v):
            for j, tt in enumerate(tts):
                v.wait_ge(pr.tm.h, tm0 + 3 * j + 1)
                v.tensor_scalar(pr.smu[:], pr.PC[0:1, 0:512], 1.0 / D, None, ALU.mult)
                v.tensor_scalar(pr.svar[:], pr.PD[0:1, 0:512], 1.0 / D, None, ALU.mult)
                v.drain()
                v.tensor_tensor(pr.tmpf[0:1, :], pr.smu[:], pr.smu[:], ALU.mult)
                v.drain()
                pr.ve.inc(v.tensor_tensor(pr.svar[:], pr.svar[:], pr.tmpf[0:1, :], ALU.subtract))
                v.wait_ge(pr.sc.h, sc0 + j + 1)
                v.drain()
                pr.ve.inc(v.tensor_tensor(pr.smur[:], pr.smu[:], pr.sr[:], ALU.mult))
                v.wait_ge(pr.tm.h, tm0 + 3 * j + 2)
                pr.ve.inc(v.tensor_copy(pr.rb[:], pr.PF[:]))
                v.wait_ge(pr.tm.h, tm0 + 3 * j + 3)
                v.tensor_copy(pr.mrb[:], pr.PF[:])
                v.drain()
                for c in range(DCH):
                    od = dst_f32 if dst_f32 is not None else dst
                    v.tensor_tensor(pr.tmpf[:], src[:, ssl(c, tt)], pr.rb[:], ALU.mult)
                    v.drain()
                    i = v.tensor_tensor(od[:, dsl(c, tt)], pr.tmpf[:], pr.mrb[:], ALU.subtract)
                    v.drain()
                pr.ve.inc(i)

        @pr.block.scalar
        def _(s):
            for j, tt in enumerate(tts):
                s.wait_ge(pr.ve.h, ve0 + 4 * j + 1)
                s.activation(pr.sr[:], pr.svar[:], AF.Ln, bias=pr.eps_t[:])
                s.drain()
                pr.sc.inc(s.activation(pr.sr[:], pr.sr[:], AF.Exp, scale=-0.5))

        return pr.ve.v



def emit_block0(pr, io, have_x_natural=True, upto=99):
    nc = pr.nc
    SCALE = (128 ** -0.5) * 1.5

    kvloc = nc.dram_tensor("kvloc", [2 * 512 * 1024], BF16, kind="Internal")
    kvall = nc.dram_tensor("kvall", [NCORE * 2 * 512 * 1024], BF16, kind="Internal")

    phA = ExitStack()
    cnat = phA.enter_context(nc.sbuf_tensor("b0_cnat", [128, 4 * CD], BF16))
    ctxT = phA.enter_context(nc.sbuf_tensor("b0_ctxT", [128, 6 * 512], BF16))
    wkb = phA.enter_context(nc.sbuf_tensor("b0_wkb", [128, 6 * 1024], BF16))
    wvb = phA.enter_context(nc.sbuf_tensor("b0_wvb", [128, 6 * 1024], BF16))
    khsl = phA.enter_context(nc.sbuf_tensor("b0_khsl", [128, 8 * 512], BF16))
    vhsl = phA.enter_context(nc.sbuf_tensor("b0_vhsl", [128, 4 * 1024], BF16))

    @pr.block.gpsimd
    def _(g):
        c_r = io["b0_ctx"].ap().rearrange("(c p) d -> p c d", p=128)
        pr.dmg.inc(g.dma_start(out=cnat[:].rearrange("p (c d) -> p c d", c=4), in_=c_r), 16)
        pr.dmw.inc(g.dma_start(out=wkb[:].rearrange("p (c m) -> p c m", c=6),
                               in_=io["b0_wk"].ap().rearrange("(c p) m -> p c m", p=128)), 16)
        pr.dmw.inc(g.dma_start(out=wvb[:].rearrange("p (c m) -> p c m", c=6),
                               in_=io["b0_wv"].ap().rearrange("(c p) m -> p c m", p=128)), 16)
    dma_cnat = pr.dmg.v
    dma_wkv = pr.dmw.v
    ve_id = pr.ve.v
    tm0, sc0 = pr.tm.v, pr.sc.v

    @pr.block.tensor
    def _(t):
        t.wait_ge(pr.dmg.h, dma_cnat)
        t.wait_ge(pr.ve.h, ve_id)
        for idx in range(24):
            rc, cc = divmod(idx, 6)
            if idx >= 2:
                t.wait_ge(pr.sc.h, sc0 + idx - 1)
            pt = pr.PT1 if idx % 2 == 0 else pr.PT2
            pr.tm.inc(t.transpose(pt[:, 0:128],
                                  cnat[:, rc * CD + cc * 128: rc * CD + cc * 128 + 128],
                                  pr.ident_b[:]))

    @pr.block.scalar
    def _(s):
        for idx in range(24):
            rc, cc = divmod(idx, 6)
            s.wait_ge(pr.tm.h, tm0 + idx + 1)
            pt = pr.PT1 if idx % 2 == 0 else pr.PT2
            pr.sc.inc(s.copy(ctxT[:, cc * 512 + rc * 128: cc * 512 + rc * 128 + 128],
                             pt[:, 0:128]))
    ctxT_done = pr.sc.v

    tm1, sc1 = pr.tm.v, pr.sc.v

    @pr.block.tensor
    def _(t):
        t.wait_ge(pr.sc.h, ctxT_done)
        t.wait_ge(pr.dmw.h, dma_wkv)
        for dc in range(8):
            if dc >= 2:
                t.wait_ge(pr.sc.h, sc1 + dc - 1)
            ps = pr.PB if dc % 2 == 0 else pr.PC
            for c in range(6):
                i = t.matmul(ps[:], wkb[:, c * 1024 + dc * 128: c * 1024 + dc * 128 + 128],
                             ctxT[:, c * 512:(c + 1) * 512], start=(c == 0), stop=(c == 5))
            pr.tm.inc(i)
        for idx in range(8):
            kc, dt_ = divmod(idx, 2)
            t.wait_ge(pr.sc.h, sc1 + idx + 7)
            ps = pr.PB if idx % 2 == 0 else pr.PC
            for c in range(6):
                i = t.matmul(ps[:], ctxT[:, c * 512 + kc * 128: c * 512 + kc * 128 + 128],
                             wvb[:, c * 1024 + dt_ * 512: c * 1024 + dt_ * 512 + 512],
                             start=(c == 0), stop=(c == 5))
            pr.tm.inc(i)

    @pr.block.scalar
    def _(s):
        for dc in range(8):
            s.wait_ge(pr.tm.h, tm1 + dc + 1)
            ps = pr.PB if dc % 2 == 0 else pr.PC
            pr.sc.inc(s.copy(khsl[:, dc * 512:(dc + 1) * 512], ps[:]))
        for idx in range(8):
            kc, dt_ = divmod(idx, 2)
            s.wait_ge(pr.tm.h, tm1 + 8 + idx + 1)
            ps = pr.PB if idx % 2 == 0 else pr.PC
            pr.sc.inc(s.copy(vhsl[:, kc * 1024 + dt_ * 512: kc * 1024 + dt_ * 512 + 512], ps[:]))
    kv_evac = pr.sc.v

    @pr.block.sync
    def _(sy):
        sy.wait_ge(pr.sc.h, kv_evac)
        kv2 = kvloc.ap().rearrange("(a p j) -> p a j", p=128, a=2)
        pr.dma.inc(sy.dma_start(out=kv2[:, 0, :], in_=khsl[:]), 16)
        pr.dma.inc(sy.dma_start(out=kv2[:, 1, :], in_=vhsl[:]), 16)
    kv_stored = pr.dma.v

    @pr.block.gpsimd
    def _(g):
        g.wait_ge(pr.dma.h, kv_stored)
        pr.cc.inc(g.collective_compute(
            "AllGather", ALU.bypass,
            replica_groups=[list(range(NCORE))],
            ins=[kvloc.ap().rearrange("(r c) -> r c", r=128).opt()],
            outs=[kvall.ap().rearrange("(r c) -> r c", r=8 * 128).opt()],
        ))
    cc_done = pr.cc.v
    phA.close()
    if upto <= 0:
        return pr.ve.v

    if have_x_natural:
        phX = ExitStack()
        xin = phX.enter_context(nc.sbuf_tensor("b0_xin", [128, 16 * 1024], F32))

        @pr.block.sync
        def _(sy):
            sy.wait_ge(pr.dma.h, kv_stored)
            x_r = io["xin"].ap().rearrange("(c p) d -> p c d", p=128)
            pr.dma.inc(sy.dma_start(out=xin[:].rearrange("p (c d) -> p c d", c=16), in_=x_r), 16)
        x_loaded = pr.dma.v

        tm2, ve2 = pr.tm.v, pr.ve.v

        @pr.block.tensor
        def _(t):
            t.wait_ge(pr.dma.h, x_loaded)
            t.wait_ge(pr.ve.h, ve_id)
            for idx in range(128):
                tc, dc = divmod(idx, DCH)
                if idx >= 2:
                    t.wait_ge(pr.ve.h, ve2 + idx - 1)
                pr.tm.inc(t.transpose(pr.PA[:, 512 * (idx % 2):512 * (idx % 2) + 128],
                                      xin[:, tc * 1024 + dc * 128: tc * 1024 + dc * 128 + 128],
                                      pr.ident_f[:]))

        @pr.block.vector
        def _(v):
            for idx in range(128):
                tc, dc = divmod(idx, DCH)
                v.wait_ge(pr.tm.h, tm2 + idx + 1)
                pr.ve.inc(v.tensor_copy(pr.xT[:, dc * T + tc * 128: dc * T + tc * 128 + 128],
                                        pr.PA[:, 512 * (idx % 2):512 * (idx % 2) + 128]))
        xT_done = pr.ve.v
        phX.close()
    else:
        @pr.block.sync
        def _(sy):
            xi = io["xin"].ap().rearrange("(c p) t -> p c t", p=128)
            pr.dma.inc(sy.dma_start(out=pr.xT[:].rearrange("p (c t) -> p c t", c=DCH), in_=xi), 16)
        x_loaded = pr.dma.v

        @pr.block.vector
        def _(v):
            v.wait_ge(pr.dma.h, x_loaded)
            pr.ve.inc(v.engine_nop())
        xT_done = pr.ve.v

    if upto <= 1:
        return pr.ve.v
    phQ = ExitStack()
    qT = phQ.enter_context(nc.sbuf_tensor("b0_qT", [128, DCH * T], BF16))
    wA = phQ.enter_context(nc.sbuf_tensor("b0_wA", [128, DCH * 1024], BF16))

    ln1_done = pr.ln(pr.xT, pr.xln, qT[:], [(pr.ve, xT_done)])

    if upto <= 2:
        return pr.ve.v
    @pr.block.gpsimd
    def _(g):
        g.wait_ge(pr.ve.h, xT_done)
        g.wait_ge(pr.dmw.h, pr.dmw.v)
        pr.dmw.inc(g.dma_start(out=wA[:].rearrange("p (c m) -> p c m", c=DCH),
                               in_=io["b0_wq"].ap().rearrange("(c p) m -> p c m", p=128)), 16)
    wq_loaded = pr.dmw.v

    tm3, sc3 = pr.tm.v, pr.sc.v

    @pr.block.tensor
    def _(t):
        t.wait_ge(pr.dmw.h, wq_loaded)
        t.wait_ge(pr.ve.h, ln1_done)
        for idx in range(DCH * NTT):
            dc, tt = divmod(idx, NTT)
            if idx >= 2:
                t.wait_ge(pr.sc.h, sc3 + idx - 1)
            for k in range(DCH):
                i = t.matmul(pr.PA[:, 512 * (idx % 2):512 * (idx % 2) + 512],
                             wA[:, k * 1024 + dc * 128: k * 1024 + dc * 128 + 128],
                             pr.xln[:, k * T + tt * 512: k * T + tt * 512 + 512],
                             start=(k == 0), stop=(k == DCH - 1))
            pr.tm.inc(i)

    @pr.block.scalar
    def _(s):
        for idx in range(DCH * NTT):
            dc, tt = divmod(idx, NTT)
            s.wait_ge(pr.tm.h, tm3 + idx + 1)
            pr.sc.inc(s.copy(qT[:, dc * T + tt * 512: dc * T + tt * 512 + 512],
                             pr.PA[:, 512 * (idx % 2):512 * (idx % 2) + 512]))
    q_done = pr.sc.v

    if upto <= 3:
        return pr.ve.v
    khb = [phQ.enter_context(nc.sbuf_tensor(f"b0_khb{i}", [128, KV], BF16)) for i in range(2)]
    vhb = [phQ.enter_context(nc.sbuf_tensor(f"b0_vhb{i}", [128, KV], BF16)) for i in range(2)]
    Ebuf = phQ.enter_context(nc.sbuf_tensor("b0_E", [128, 4 * 512], BF16))
    rsum = phQ.enter_context(nc.sbuf_tensor("b0_rsum", [1, 512], F32))
    oT = pr.xln
    rb32 = pr.rb

    kva = kvall.ap().rearrange("(r a p dc j) -> a r dc p j", r=NCORE, a=2, p=128, dc=8)
    kva_v = kvall.ap().rearrange("(r a p kc dv) -> a r p kc dv", r=NCORE, a=2, p=128, kc=4)

    NIT = H * NTT
    tm_b, sc_b, ve_b = pr.tm.v, pr.sc.v, pr.ve.v
    TPI, SPI, VPI = 33, 17, 2

    @pr.block.gpsimd
    def _(g):
        g.wait_ge(pr.cc.h, cc_done)
        for h in range(H):
            if h >= 2:
                g.wait_ge(pr.tm.h, tm_b + TPI * NTT * (h - 1))
            if h >= 1:
                g.wait_ge(pr.dmk.h, 144 * h)
            pr.dmk.inc(g.dma_start(out=khb[h % 2][:].rearrange("p (r j) -> p r j", r=8),
                                   in_=kva[0, :, h, :, :].rearrange("r p j -> p r j")), 16)
            for r in range(NCORE):
                pr.dmk.inc(g.dma_start(
                    out=vhb[h % 2][:, r * 512:(r + 1) * 512].rearrange("p (kc dv) -> p kc dv", kc=4),
                    in_=kva_v[1, r, :, :, h * 128:(h + 1) * 128]), 16)

    for j in range(NIT):
        h, qt = divmod(j, NTT)
        tmj = tm_b + TPI * j
        scj = sc_b + SPI * j
        vej = ve_b + VPI * j
        kh, vh = khb[h % 2], vhb[h % 2]

        @pr.block.tensor
        def _(t, h=h, qt=qt, tmj=tmj, scj=scj, vej=vej, kh=kh, vh=vh, j=j):
            if qt == 0:
                t.wait_ge(pr.dmk.h, 144 * (h + 1))
                if h == 0:
                    t.wait_ge(pr.sc.h, q_done)
            qsl = qT[:, h * T + qt * 512: h * T + qt * 512 + 512]
            for p in range(16):
                if p >= 1:
                    t.wait_ge(pr.sc.h, scj + p)
                elif j >= 1:
                    t.wait_ge(pr.sc.h, scj - 1)
                t.matmul(pr.PA[:, 0:512], kh[:, (2 * p) * 128:(2 * p) * 128 + 128],
                         qsl, start=True, stop=True)
                pr.tm.inc(t.matmul(pr.PA[:, 512:1024],
                                   kh[:, (2 * p + 1) * 128:(2 * p + 1) * 128 + 128],
                                   qsl, start=True, stop=True))
                t.wait_ge(pr.sc.h, scj + p + 1)
                if p == 0 and j >= 1:
                    t.wait_ge(pr.ve.h, vej)
                for cc_ in (2 * p, 2 * p + 1):
                    t.matmul(pr.PB[:], vh[:, cc_ * 128:cc_ * 128 + 128],
                             Ebuf[:, (cc_ % 4) * 512:(cc_ % 4) * 512 + 512],
                             start=(cc_ == 0), stop=(cc_ == 31))
                    i = t.matmul(pr.PC[0:1, 0:512], pr.ones_bcol[:],
                                 Ebuf[:, (cc_ % 4) * 512:(cc_ % 4) * 512 + 512],
                                 start=(cc_ == 0), stop=(cc_ == 31))
                pr.tm.inc(i)
            t.wait_ge(pr.ve.h, vej + 1)
            pr.tm.inc(t.matmul(pr.PD[:], pr.ones_frow[:], rsum[:], start=True, stop=True))

        @pr.block.scalar
        def _(s, tmj=tmj, scj=scj):
            for p in range(16):
                s.wait_ge(pr.tm.h, tmj + 2 * p + 1)
                pr.sc.inc(s.activation(Ebuf[:, (2 * p % 4) * 512:(2 * p % 4) * 512 + 1024],
                                       pr.PA[:], AF.Exp, scale=SCALE))
            s.wait_ge(pr.tm.h, tmj + TPI)
            pr.sc.inc(s.copy(rb32[:], pr.PD[:]))

        @pr.block.vector
        def _(v, h=h, qt=qt, tmj=tmj, scj=scj):
            v.wait_ge(pr.tm.h, tmj + 32)
            pr.ve.inc(v.reciprocal(rsum[:], pr.PC[0:1, 0:512]))
            v.wait_ge(pr.sc.h, scj + SPI)
            pr.ve.inc(v.tensor_tensor(oT[:, h * T + qt * 512: h * T + qt * 512 + 512],
                                      pr.PB[:], rb32[:], ALU.mult))

    attn_done = pr.ve.v
    assert attn_done == ve_b + VPI * NIT and pr.tm.v == tm_b + TPI * NIT and pr.sc.v == sc_b + SPI * NIT

    if upto <= 4:
        return pr.ve.v
    bob = phQ.enter_context(nc.sbuf_tensor("b0_bob", [128, DCH], F32))
    brow = phQ.enter_context(nc.sbuf_tensor("b0_brow", [1, D], F32))

    @pr.block.sync
    def _(sy):
        sy.wait_ge(pr.dma.h, pr.dma.v)
        pr.dma.inc(sy.dma_start(out=brow[:], in_=io["b0_bo"].ap().rearrange("(a d) -> a d", a=1)), 16)
    brow_loaded = pr.dma.v

    @pr.block.gpsimd
    def _(g):
        g.wait_ge(pr.tm.h, tm_b)
        g.wait_ge(pr.dmw.h, pr.dmw.v)
        pr.dmw.inc(g.dma_start(out=wA[:].rearrange("p (c m) -> p c m", c=DCH),
                               in_=io["b0_wo"].ap().rearrange("(c p) m -> p c m", p=128)), 16)
    wo_loaded = pr.dmw.v

    @pr.block.tensor
    def _(t):
        t.wait_ge(pr.dma.h, brow_loaded)
        t.wait_ge(pr.sc.h, sc_b + SPI * NIT)
        for c in range(DCH):
            i = t.matmul(pr.PD[0:128, 64 + c:64 + c + 1],
                         brow[:, c * 128:(c + 1) * 128], pr.ones_f11[:], start=True, stop=True)
        pr.tm.inc(i)
    bob_mm = pr.tm.v

    @pr.block.vector
    def _(v):
        v.wait_ge(pr.tm.h, bob_mm)
        pr.ve.inc(v.tensor_copy(bob[:], pr.PD[:, 64:64 + DCH]))
    bob_done = pr.ve.v

    tm4, ve4 = pr.tm.v, pr.ve.v

    @pr.block.tensor
    def _(t):
        t.wait_ge(pr.dmw.h, wo_loaded)
        t.wait_ge(pr.ve.h, attn_done)
        for idx in range(DCH * NTT):
            dc, tt = divmod(idx, NTT)
            if idx >= 2:
                t.wait_ge(pr.ve.h, ve4 + idx)
            for k in range(DCH):
                i = t.matmul(pr.PA[:, 512 * (idx % 2):512 * (idx % 2) + 512],
                             wA[:, k * 1024 + dc * 128: k * 1024 + dc * 128 + 128],
                             oT[:, k * T + tt * 512: k * T + tt * 512 + 512],
                             start=(k == 0), stop=(k == DCH - 1))
            pr.tm.inc(i)

    @pr.block.vector
    def _(v):
        v.wait_ge(pr.ve.h, bob_done)
        for idx in range(DCH * NTT):
            dc, tt = divmod(idx, NTT)
            v.wait_ge(pr.tm.h, tm4 + idx + 1)
            sl = slice(dc * T + tt * 512, dc * T + tt * 512 + 512)
            pr.ve.inc(v.scalar_tensor_tensor(
                pr.xT[:, sl], pr.PA[:, 512 * (idx % 2):512 * (idx % 2) + 512],
                bob[:, dc:dc + 1], pr.xT[:, sl], ALU.add, ALU.add))
    phQ.close()
    return pr.ve.v




def emit_load_xt(pr, xt_in):
    @pr.block.sync
    def _(sy):
        sy.wait_ge(pr.dma.h, pr.dma.v)
        xi = xt_in.ap().rearrange("(c p) t -> p c t", p=128)
        pr.dma.inc(sy.dma_start(out=pr.xT[:].rearrange("p (c t) -> p c t", c=DCH), in_=xi), 16)
    x_loaded = pr.dma.v

    @pr.block.vector
    def _(v):
        v.wait_ge(pr.dma.h, x_loaded)
        pr.ve.inc(v.engine_nop())
    return pr.ve.v


def columnize_vec(pr, rows_and_slices, dst, dst_col0=0):
    cols = []
    col = dst_col0
    for row in rows_and_slices:
        m = row.shape[-1]
        nch = m // 128
        cols.append((row, col, nch))
        col += nch
    ncol_total = col - dst_col0

    @pr.block.tensor
    def _(t):
        t.wait_ge(pr.dma.h, pr.dma.v)
        t.wait_ge(pr.dmg.h, pr.dmg.v)
        t.wait_ge(pr.sc.h, pr.sc.v)
        t.wait_ge(pr.ve.h, pr.ve.v)
        for row, c0, nch in cols:
            for c in range(nch):
                i = t.matmul(pr.PD[0:128, c0 - dst_col0 + c:c0 - dst_col0 + c + 1],
                             row[:, c * 128:(c + 1) * 128], pr.ones_f11[:],
                             start=True, stop=True)
        pr.tm.inc(i)
    mm = pr.tm.v

    @pr.block.vector
    def _(v):
        v.wait_ge(pr.tm.h, mm)
        pr.ve.inc(v.tensor_copy(dst[:, dst_col0:dst_col0 + ncol_total],
                                pr.PD[:, 0:ncol_total]))
    return pr.ve.v



def emit_block1(pr, io, xt_ready, upto=99):
    nc = pr.nc
    ph = ExitStack()
    A = ph.enter_context(nc.sbuf_tensor("b1_A", [128, 32 * 512], BF16))
    B = ph.enter_context(nc.sbuf_tensor("b1_B", [128, 32 * 512], BF16))
    w12 = [ph.enter_context(nc.sbuf_tensor(f"b1_w12_{i}", [128, 2 * 8 * 128], BF16)) for i in range(2)]
    wh = [ph.enter_context(nc.sbuf_tensor(f"b1_wh_{i}", [128, 32 * 128], BF16)) for i in range(2)]
    scl = ph.enter_context(nc.sbuf_tensor("b1_scl", [128, 4 * 32 + 8], F32))
    t1f = ph.enter_context(nc.sbuf_tensor("b1_t1", [128, 1024], F32))
    xlq = pr.xln[:, 0:DCH * 512]
    sqq = pr.xln[:, DCH * 512:DCH * 1024]

    scrow = pr.srow512
    svecs = [("b1_s1", 4096), ("b1_s2", 4096), ("b1_sh1", 4096), ("b1_sh2", 4096), ("b1_s3", 1024)]
    col0 = 0
    for name, m in svecs:
        for part in range(0, m, 512):
            w_ = min(512, m - part)
            @pr.block.sync
            def _(sy, name=name, part=part, w_=w_):
                sy.wait_ge(pr.dma.h, pr.dma.v)
                sy.wait_ge(pr.ve.h, pr.ve.v)
                pr.dma.inc(sy.dma_start(out=scrow[:, 0:w_],
                                        in_=io[name].ap().rearrange("(a d) -> a d", a=1)[:, part:part + w_]), 16)
            columnize_vec(pr, [scrow[:, 0:w_]], scl, col0)
            col0 += w_ // 128
    scl_done = pr.ve.v
    S1C, S2C, SH1C, SH2C, S3C = 0, 32, 64, 96, 128

    NQ = 4
    prev_q_done = xt_ready
    for q in range(NQ):
        ln_done = pr.ln(pr.xT, xlq, sqq, [(pr.ve, prev_q_done)], tts=[q], dst_stride=512)

        dmw0 = pr.dmw.v
        tm0 = pr.tm.v
        sc0 = pr.sc.v
        ve0 = pr.ve.v

        @pr.block.gpsimd
        def _(g, q=q, tm0=tm0):
            for oc in range(32):
                g.wait_ge(pr.dmw.h, pr.dmw.v)
                if oc >= 2:
                    g.wait_ge(pr.tm.h, tm0 + oc - 1)
                w1v = io["b1_w1t"].ap()[:, oc * 128:(oc + 1) * 128].rearrange("(kc p) j -> p kc j", p=128)
                w2v = io["b1_w2t"].ap()[:, oc * 128:(oc + 1) * 128].rearrange("(kc p) j -> p kc j", p=128)
                buf = w12[oc % 2]
                pr.dmw.inc(g.dma_start(out=buf[:, 0:1024].rearrange("p (kc j) -> p kc j", kc=8), in_=w1v), 16)
                g.wait_ge(pr.dmw.h, pr.dmw.v)
                pr.dmw.inc(g.dma_start(out=buf[:, 1024:2048].rearrange("p (kc j) -> p kc j", kc=8), in_=w2v), 16)

        @pr.block.tensor
        def _(t, q=q, tm0=tm0, sc0=sc0, ve0=ve0, dmw0=dmw0):
            t.wait_ge(pr.ve.h, ln_done)
            t.wait_ge(pr.ve.h, scl_done)
            t.wait_ge(pr.ve.h, ve0)
            for oc in range(32):
                t.wait_ge(pr.dmw.h, dmw0 + 32 * (oc + 1))
                if oc >= 2:
                    t.wait_ge(pr.sc.h, sc0 + oc - 1)
                    t.wait_ge(pr.ve.h, ve0 + oc - 1)
                buf = w12[oc % 2]
                psA = pr.PA[:, 512 * (oc % 2):512 * (oc % 2) + 512]
                psB = pr.PB[:] if oc % 2 == 0 else pr.PC[:]
                for k in range(8):
                    t.matmul(psA, buf[:, k * 128:(k + 1) * 128],
                             xlq[:, k * 512:(k + 1) * 512],
                             start=(k == 0), stop=(k == 7))
                for k in range(8):
                    i = t.matmul(psB, buf[:, 1024 + k * 128:1024 + k * 128 + 128],
                                 xlq[:, k * 512:(k + 1) * 512],
                                 start=(k == 0), stop=(k == 7))
                pr.tm.inc(i)

        @pr.block.scalar
        def _(s, tm0=tm0, ve0=ve0):
            for oc in range(32):
                s.wait_ge(pr.tm.h, tm0 + oc + 1)
                if oc >= 2:
                    s.wait_ge(pr.ve.h, ve0 + oc - 1)
                psA = pr.PA[:, 512 * (oc % 2):512 * (oc % 2) + 512]
                psB = pr.PB[:] if oc % 2 == 0 else pr.PC[:]
                h_ = slice(512 * (oc % 2), 512 * (oc % 2) + 512)
                pr.sc.inc(s.activation(t1f[:, h_], psA, AF.Silu, scale=scl[:, S1C + oc:S1C + oc + 1]))

        @pr.block.vector
        def _(v, sc0=sc0, ve0=ve0, q=q):
            for oc in range(32):
                v.wait_ge(pr.sc.h, sc0 + oc + 1)
                h_ = slice(512 * (oc % 2), 512 * (oc % 2) + 512)
                psB = pr.PB[:] if oc % 2 == 0 else pr.PC[:]
                pr.ve.inc(v.scalar_tensor_tensor(A[:, oc * 512:(oc + 1) * 512], psB,
                                                 scl[:, S2C + oc:S2C + oc + 1], t1f[:, h_],
                                                 ALU.mult, ALU.mult))
        st1_ve = pr.ve.v

        prev_guard = (pr.ve, st1_ve)
        for stage, (wname, scol, src, dstb) in enumerate(
                [("b1_h1t", SH1C, A, B), ("b1_h2t", SH2C, B, A)]):
            dmw1 = pr.dmw.v
            tm1 = pr.tm.v
            sc1 = pr.sc.v

            @pr.block.gpsimd
            def _(g, wname=wname, tm1=tm1):
                for oc in range(32):
                    wv = io[wname].ap()[:, oc * 128:(oc + 1) * 128]
                    buf = wh[oc % 2]
                    if oc >= 2:
                        g.wait_ge(pr.tm.h, tm1 + oc - 1)
                    else:
                        g.wait_ge(pr.tm.h, tm1)
                    for half in range(2):
                        g.wait_ge(pr.dmw.h, pr.dmw.v)
                        pr.dmw.inc(g.dma_start(
                            out=buf[:, half * 2048:(half + 1) * 2048].rearrange("p (kc j) -> p kc j", kc=16),
                            in_=wv[half * 2048:(half + 1) * 2048, :].rearrange("(kc p) j -> p kc j", p=128)), 16)

            @pr.block.tensor
            def _(t, tm1=tm1, sc1=sc1, dmw1=dmw1, src=src, prev_guard=prev_guard, stage=stage):
                t.wait_ge(prev_guard[0].h, prev_guard[1])
                for oc in range(32):
                    t.wait_ge(pr.dmw.h, dmw1 + 32 * (oc + 1))
                    if oc >= 2:
                        t.wait_ge(pr.sc.h, sc1 + oc - 1)
                    buf = wh[oc % 2]
                    ps = pr.PA[:, 512 * (oc % 2):512 * (oc % 2) + 512]
                    for k in range(32):
                        i = t.matmul(ps, buf[:, k * 128:(k + 1) * 128],
                                     src[:, k * 512:(k + 1) * 512],
                                     start=(k == 0), stop=(k == 31))
                    pr.tm.inc(i)

            @pr.block.scalar
            def _(s, tm1=tm1, scol=scol, dstb=dstb):
                for oc in range(32):
                    s.wait_ge(pr.tm.h, tm1 + oc + 1)
                    ps = pr.PA[:, 512 * (oc % 2):512 * (oc % 2) + 512]
                    pr.sc.inc(s.activation(dstb[:, oc * 512:(oc + 1) * 512], ps, AF.Gelu,
                                           scale=scl[:, scol + oc:scol + oc + 1]))
            prev_guard = (pr.sc, pr.sc.v)

        st3_sc = pr.sc.v

        dmw2 = pr.dmw.v
        tm2g = pr.tm.v
        @pr.block.gpsimd
        def _(g, tm2g=tm2g):
            for oc in range(8):
                wv = io["b1_w3t"].ap()[:, oc * 128:(oc + 1) * 128]
                buf = wh[oc % 2]
                if oc >= 2:
                    g.wait_ge(pr.tm.h, tm2g + oc - 1)
                else:
                    g.wait_ge(pr.tm.h, tm2g)
                for half in range(2):
                    g.wait_ge(pr.dmw.h, pr.dmw.v)
                    pr.dmw.inc(g.dma_start(
                        out=buf[:, half * 2048:(half + 1) * 2048].rearrange("p (kc j) -> p kc j", kc=16),
                        in_=wv[half * 2048:(half + 1) * 2048, :].rearrange("(kc p) j -> p kc j", p=128)), 16)

        tm2 = pr.tm.v
        ve2 = pr.ve.v

        @pr.block.tensor
        def _(t, tm2=tm2, dmw2=dmw2, ve2=ve2, st3_sc=st3_sc):
            t.wait_ge(pr.sc.h, st3_sc)
            for oc in range(8):
                t.wait_ge(pr.dmw.h, dmw2 + 32 * (oc + 1))
                if oc >= 2:
                    t.wait_ge(pr.ve.h, ve2 + oc - 1)
                buf = wh[oc % 2]
                ps = pr.PA[:, 512 * (oc % 2):512 * (oc % 2) + 512]
                for k in range(32):
                    i = t.matmul(ps, buf[:, k * 128:(k + 1) * 128],
                                 A[:, k * 512:(k + 1) * 512],
                                 start=(k == 0), stop=(k == 31))
                pr.tm.inc(i)

        @pr.block.vector
        def _(v, tm2=tm2, q=q):
            for oc in range(8):
                v.wait_ge(pr.tm.h, tm2 + oc + 1)
                ps = pr.PA[:, 512 * (oc % 2):512 * (oc % 2) + 512]
                sl = slice(oc * T + q * 512, oc * T + q * 512 + 512)
                pr.ve.inc(v.scalar_tensor_tensor(pr.xT[:, sl], ps,
                                                 scl[:, S3C + oc:S3C + oc + 1],
                                                 pr.xT[:, sl], ALU.mult, ALU.add))
        prev_q_done = pr.ve.v
    ph.close()
    return pr.ve.v


def build_b1():
    nc = bass.Bass("TRN2", target_bir_lowering=False, debug=False)
    io = {
        "xin": nc.dram_tensor("xin", [D, T], F32, kind="ExternalInput"),
        "b1_w1t": nc.dram_tensor("b1_w1t", [1024, 4096], F32, kind="ExternalInput"),
        "b1_w2t": nc.dram_tensor("b1_w2t", [1024, 4096], F32, kind="ExternalInput"),
        "b1_h1t": nc.dram_tensor("b1_h1t", [4096, 4096], F32, kind="ExternalInput"),
        "b1_h2t": nc.dram_tensor("b1_h2t", [4096, 4096], F32, kind="ExternalInput"),
        "b1_w3t": nc.dram_tensor("b1_w3t", [4096, 1024], F32, kind="ExternalInput"),
        "b1_s1": nc.dram_tensor("b1_s1", [4096], F32, kind="ExternalInput"),
        "b1_s2": nc.dram_tensor("b1_s2", [4096], F32, kind="ExternalInput"),
        "b1_sh1": nc.dram_tensor("b1_sh1", [4096], F32, kind="ExternalInput"),
        "b1_sh2": nc.dram_tensor("b1_sh2", [4096], F32, kind="ExternalInput"),
        "b1_s3": nc.dram_tensor("b1_s3", [1024], F32, kind="ExternalInput"),
    }
    xt_out = nc.dram_tensor("xt_out", [D, T], F32, kind="ExternalOutput")
    with ExitStack() as ctx:
        pr = Prog(nc, ctx, xln_cols=DCH * 1024)
        pr.emit_consts()
        xt_ready = emit_load_xt(pr, io["xin"])
        emit_block1(pr, io, xt_ready)
        emit_store_xt(pr, xt_out)
    return nc







def emit_quant_tiles(pr, tiles, scale_ap, inv_ap, zt, it_, dt_, s1t, s2t,
                     width, pre_wait_v=None):
    ve0 = pr.ve.v
    sc0 = pr.sc.v
    n = len(tiles)

    @pr.block.vector
    def _(v):
        if pre_wait_v:
            for sem, val in pre_wait_v:
                v.wait_ge(sem.h, val)
        for j, (src, dst, waits) in enumerate(tiles):
            for sem, val in waits:
                v.wait_ge(sem.h, val)
            v.tensor_scalar(zt[:, 0:width], src, scale_ap, None, ALU.mult)
            v.drain()
            v.tensor_copy(it_[:, 0:width], zt[:, 0:width])
            v.drain()
            pr.ve.inc(v.tensor_tensor(dt_[:, 0:width], zt[:, 0:width], it_[:, 0:width], ALU.subtract))
            v.wait_ge(pr.sc.h, sc0 + j + 1)
            v.scalar_tensor_tensor(zt[:, 0:width], s1t[:, 0:width], 0.5, it_[:, 0:width], ALU.mult, ALU.add)
            v.drain()
            v.scalar_tensor_tensor(zt[:, 0:width], s2t[:, 0:width], -0.5, zt[:, 0:width], ALU.mult, ALU.add)
            v.drain()
            pr.ve.inc(v.tensor_scalar(dst, zt[:, 0:width], inv_ap, None, ALU.mult))
            v.drain()

    @pr.block.scalar
    def _(s):
        for j in range(n):
            s.wait_ge(pr.ve.h, ve0 + 2 * j + 1)
            s.activation(s1t[:, 0:width], dt_[:, 0:width], AF.Sign, bias=pr.neg_half[:])
            s.drain()
            pr.sc.inc(s.activation(s2t[:, 0:width], dt_[:, 0:width], AF.Sign, scale=-1.0, bias=pr.neg_half[:]))


def emit_wq_stream(pr, pr_io_name, wst, dst_writer, zt, it_, dt_, s1t, s2t, qsc,
                   gp_guard, io, wqb=None, transpose_dst=None):
    dmw0 = pr.dmw.v
    ve0 = pr.ve.v
    sc0 = pr.sc.v
    tmT0 = pr.tm.v
    TR = transpose_dst is not None
    VPC = 10 if TR else 2

    @pr.block.gpsimd
    def _(g):
        for kc in range(32):
            g.wait_ge(pr.dmw.h, pr.dmw.v)
            for gg in gp_guard(kc) or []:
                g.wait_ge(gg[0].h, gg[1])
            if TR and kc >= 2:
                g.wait_ge(pr.tm.h, tmT0 + 8 * (kc - 1))
            pr.dmw.inc(g.dma_start(out=wst[kc % 2][:],
                                   in_=io[pr_io_name].ap()[kc * 128:(kc + 1) * 128, :]), 16)

    def quant_d(kc):
        if not TR or kc == 0:
            return ve0 + VPC * kc + 1 if not TR else ve0 + 1
        return ve0 + 10 * kc - 7

    def quant_f(kc):
        if not TR or kc == 0:
            return ve0 + VPC * kc + 2 if not TR else ve0 + 2
        return ve0 + 10 * kc - 6

    def evac_val(j, dc):
        if j <= 30:
            return ve0 + 10 * j + 5 + dc
        return ve0 + 313 + dc

    @pr.block.vector
    def _(v):
        for kc in range(32):
            v.wait_ge(pr.dmw.h, dmw0 + 16 * (kc + 1))
            w = wst[kc % 2]
            v.tensor_reduce(qsc[:, 0:1], w[:], AX.X, ALU.max, apply_absolute_value=True)
            v.drain()
            v.tensor_scalar(qsc[:, 0:1], qsc[:, 0:1], 1e-8, None, ALU.max)
            v.drain()
            v.reciprocal(qsc[:, 1:2], qsc[:, 0:1])
            v.drain()
            v.tensor_scalar(qsc[:, 2:3], qsc[:, 1:2], 127.0, None, ALU.mult)
            v.drain()
            v.reciprocal(qsc[:, 3:4], qsc[:, 2:3])
            v.drain()
            v.tensor_scalar(zt[:], w[:], qsc[:, 2:3], None, ALU.mult)
            v.drain()
            v.tensor_copy(it_[:], zt[:])
            v.drain()
            pr.ve.inc(v.tensor_tensor(dt_[:], zt[:], it_[:], ALU.subtract))
            v.wait_ge(pr.sc.h, sc0 + kc + 1)
            v.scalar_tensor_tensor(zt[:], s1t[:], 0.5, it_[:], ALU.mult, ALU.add)
            v.drain()
            v.scalar_tensor_tensor(zt[:], s2t[:], -0.5, zt[:], ALU.mult, ALU.add)
            v.drain()
            dst = wqb[kc % 2][:] if TR else dst_writer(kc)
            pr.ve.inc(v.tensor_scalar(dst, zt[:], qsc[:, 3:4], None, ALU.mult))
            v.drain()
            if TR and kc >= 1:
                for dc in range(8):
                    v.wait_ge(pr.tm.h, tmT0 + (kc - 1) * 8 + dc + 1)
                    pt = pr.PT1 if ((kc - 1) * 8 + dc) % 2 == 0 else pr.PT2
                    pr.ve.inc(v.tensor_copy(transpose_dst(kc - 1, dc), pt[:, 0:128]))
        if TR:
            for dc in range(8):
                v.wait_ge(pr.tm.h, tmT0 + 31 * 8 + dc + 1)
                pt = pr.PT1 if (31 * 8 + dc) % 2 == 0 else pr.PT2
                pr.ve.inc(v.tensor_copy(transpose_dst(31, dc), pt[:, 0:128]))

    @pr.block.scalar
    def _(s):
        for kc in range(32):
            s.wait_ge(pr.ve.h, quant_d(kc))
            s.activation(s1t[:], dt_[:], AF.Sign, bias=pr.neg_half[:])
            s.drain()
            pr.sc.inc(s.activation(s2t[:], dt_[:], AF.Sign, scale=-1.0, bias=pr.neg_half[:]))

    if TR:
        @pr.block.tensor
        def _(t):
            for kc in range(32):
                t.wait_ge(pr.ve.h, quant_f(kc))
                for dc in range(8):
                    idx = kc * 8 + dc
                    if idx >= 2:
                        j2, dc2 = divmod(idx - 2, 8)
                        t.wait_ge(pr.ve.h, evac_val(j2, dc2))
                    pt = pr.PT1 if idx % 2 == 0 else pr.PT2
                    pr.tm.inc(t.transpose(pt[:, 0:128],
                                          wqb[kc % 2][:, dc * 128:(dc + 1) * 128],
                                          pr.ident_b[:]))
    return pr.ve.v


def emit_global_amax(pr, tile, cc_in, cc_out, dst_col, dst_inv_col, qlevels, bscr):
    ve0 = pr.ve.v

    @pr.block.vector
    def _(v):
        pr.ve.inc(v.engine_nop())
    tree_done = pr.ve.v

    @pr.block.sync
    def _(sy):
        sy.wait_ge(pr.ve.h, tree_done)
        sy.wait_ge(pr.dma.h, pr.dma.v)
        pr.dma.inc(sy.dma_start(out=bscr.ap().rearrange("(a b) -> a b", a=128)[:, 0:1],
                                in_=tile[:, 0:1]), 16)
        sy.wait_ge(pr.dma.h, pr.dma.v)
        pr.dma.inc(sy.dma_start(out=pr.srow32[:],
                                in_=bscr.ap().rearrange("(a b) -> a b", a=1)[:, 0:128]), 16)
    ld1 = pr.dma.v

    @pr.block.vector
    def _(v):
        v.wait_ge(pr.dma.h, ld1)
        v.tensor_reduce(pr.s11[:], pr.srow32[:], AX.X, ALU.max)
        v.drain()
        pr.ve.inc(v.tensor_scalar(pr.s11[:], pr.s11[:], 1e-8, None, ALU.max))
    amax_local = pr.ve.v

    @pr.block.gpsimd
    def _(g):
        pr.gp.inc(g.memset(pr.zrow8[:], 0.0))
    zr = pr.gp.v

    @pr.block.sync
    def _(sy):
        sy.wait_ge(pr.ve.h, amax_local)
        sy.wait_ge(pr.gp.h, zr)
        sy.wait_ge(pr.dma.h, pr.dma.v)
        pr.dma.inc(sy.dma_start(out=cc_in.ap(), in_=pr.zrow8[:]), 16)
        sy.wait_ge(pr.dma.h, pr.dma.v)
        pr.dma.inc(sy.dma_start(out=cc_in.ap()[0:1, 0:1], in_=pr.s11[:]), 16)
    ccst = pr.dma.v

    @pr.block.gpsimd
    def _(g):
        g.wait_ge(pr.dma.h, ccst)
        pr.cc.inc(g.collective_compute(
            "AllReduce", ALU.max, replica_groups=[list(range(NCORE))],
            ins=[cc_in.ap().opt()], outs=[cc_out.ap().opt()]))
    ccd = pr.cc.v

    @pr.block.sync
    def _(sy):
        sy.wait_ge(pr.cc.h, ccd)
        sy.wait_ge(pr.dma.h, pr.dma.v)
        pr.dma.inc(sy.dma_start(out=pr.s11[:], in_=cc_out.ap()[0:1, 0:1]), 16)
    ld2 = pr.dma.v

    @pr.block.vector
    def _(v):
        v.wait_ge(pr.dma.h, ld2)
        v.reciprocal(pr.s12[:], pr.s11[:])
        v.drain()
        v.tensor_scalar(pr.s12[:], pr.s12[:], qlevels, None, ALU.mult)
        v.drain()
        v.reciprocal(pr.s13[:], pr.s12[:])
        v.drain()
        v.tensor_copy(pr.s14[:, 0:1], pr.s12[:])
        v.drain()
        pr.ve.inc(v.tensor_copy(pr.s14[:, 1:2], pr.s13[:]))
    row2 = pr.ve.v

    @pr.block.tensor
    def _(t):
        t.wait_ge(pr.ve.h, row2)
        pr.tm.inc(t.matmul(pr.PD[0:128, 0:2], pr.ones_frow[:], pr.s14[:, 0:2],
                           start=True, stop=True))
    bmm = pr.tm.v

    @pr.block.vector
    def _(v):
        v.wait_ge(pr.tm.h, bmm)
        v.tensor_copy(dst_col, pr.PD[:, 0:1])
        v.drain()
        pr.ve.inc(v.tensor_copy(dst_inv_col, pr.PD[:, 1:2]))
    return pr.ve.v


def emit_block3(pr, io, xt_ready):
    nc = pr.nc
    ph = ExitStack()
    W_q = ph.enter_context(nc.sbuf_tensor("b3_wq", [128, 32 * 1024], BF16))
    zt = ph.enter_context(nc.sbuf_tensor("b3_zt", [128, 1024], F32))
    it_ = ph.enter_context(nc.sbuf_tensor("b3_it", [128, 1024], I32))
    dt_ = zt
    s1t = pr.xln[:, 12288:13312]
    s2t = pr.xln[:, 13312:14336]
    binc = ph.enter_context(nc.sbuf_tensor("b3_binc", [128, 44], F32))
    qsc = ph.enter_context(nc.sbuf_tensor("b3_qsc", [128, 4], F32))
    xmax = ph.enter_context(nc.sbuf_tensor("b3_xmax", [128, 4], F32))
    hmax = ph.enter_context(nc.sbuf_tensor("b3_hmax", [128, 4], F32))

    hid = nc.dram_tensor("b3_hid", [32, 128, T], F32, kind="Internal")
    bscr = nc.dram_tensor("b3_bscr", [128], F32, kind="Internal")
    ccxi = nc.dram_tensor("b3_ccxi", [1, 8], F32, kind="Internal")
    ccxo = nc.dram_tensor("b3_ccxo", [1, 8], F32, kind="Internal")
    cchi = nc.dram_tensor("b3_cchi", [1, 8], F32, kind="Internal")
    ccho = nc.dram_tensor("b3_ccho", [1, 8], F32, kind="Internal")
    SXC, SIXC, SHC, SIHC = 40, 41, 42, 43

    @pr.block.gpsimd
    def _(g):
        g.memset(xmax[:], 0.0)
        pr.gp.inc(g.memset(hmax[:], 0.0))
    mm0 = pr.gp.v

    phW1 = ExitStack()
    wst = [phW1.enter_context(nc.sbuf_tensor(f"b3_wst{i}", [128, 1024], F32)) for i in range(2)]
    wqb = [phW1.enter_context(nc.sbuf_tensor(f"b3_wqb{i}", [128, 1024], BF16)) for i in range(2)]

    winq_done = emit_wq_stream(
        pr, "b3_wint", wst, None, zt, it_, dt_, s1t, s2t, qsc,
        lambda kc: [(pr.ve, xt_ready)] if kc == 0 else None, io,
        wqb=wqb,
        transpose_dst=lambda kc, dc: W_q[:, dc * 4096 + kc * 128: dc * 4096 + kc * 128 + 128])
    phW1.close()
    phI = ExitStack()
    xf32 = phI.enter_context(nc.sbuf_tensor("b3_xf32", [128, DCH * 512], F32))
    xqb = pr.xln[:, 8192:12288]
    stg = phI.enter_context(nc.sbuf_tensor("b3_stg", [128, 2 * 512], F32))

    for nm, c0, m in [("b3_bin", 0, 4096), ("b3_bout", 32, 1024)]:
        for part in range(0, m, 512):
            @pr.block.sync
            def _(sy, nm=nm, part=part):
                sy.wait_ge(pr.dma.h, pr.dma.v)
                sy.wait_ge(pr.ve.h, pr.ve.v)
                pr.dma.inc(sy.dma_start(out=pr.srow512[:],
                                        in_=io[nm].ap().rearrange("(a d) -> a d", a=1)[:, part:part + 512]), 16)
            columnize_vec(pr, [pr.srow512[:]], binc, c0 + part // 128)

    for q in range(NTT):
        lnd = pr.ln(pr.xT, pr.xln[:, 0:DCH * 512], pr.xln[:, DCH * 512:DCH * 1024],
                    [(pr.ve, xt_ready if q == 0 else pr.ve.v)], tts=[q], dst_stride=512,
                    dst_f32=xf32)

        @pr.block.vector
        def _(v, lnd=lnd):
            v.wait_ge(pr.ve.h, lnd)
            v.wait_ge(pr.gp.h, mm0)
            v.tensor_reduce(xmax[:, 1:2], xf32[:], AX.X, ALU.max, apply_absolute_value=True)
            v.drain()
            pr.ve.inc(v.tensor_tensor(xmax[:, 0:1], xmax[:, 0:1], xmax[:, 1:2], ALU.max))
            v.drain()

    emit_global_amax(pr, xmax[:, 0:4], ccxi, ccxo, binc[:, SXC:SXC + 1],
                     binc[:, SIXC:SIXC + 1], 32767.0, bscr)
    sx_ready = pr.ve.v

    for q in range(NTT):
        lnd = pr.ln(pr.xT, pr.xln[:, 0:DCH * 512], pr.xln[:, DCH * 512:DCH * 1024],
                    [(pr.ve, pr.ve.v)], tts=[q], dst_stride=512, dst_f32=xf32)
        tiles = [(xf32[:, sl * 512:(sl + 1) * 512], xqb[:, sl * 512:(sl + 1) * 512], [])
                 for sl in range(8)]
        emit_quant_tiles(pr, tiles, binc[:, SXC:SXC + 1], binc[:, SIXC:SIXC + 1],
                         zt, it_, dt_, s1t, s2t, 512,
                         pre_wait_v=[(pr.ve, lnd), (pr.ve, sx_ready)])
        xq_done = pr.ve.v

        tmS = pr.tm.v
        scS = pr.sc.v
        veS = pr.ve.v
        dmaS = pr.dma.v

        @pr.block.tensor
        def _(t, tmS=tmS, scS=scS, xq_done=xq_done):
            t.wait_ge(pr.ve.h, xq_done)
            t.wait_ge(pr.ve.h, winq_done)
            for oc in range(32):
                if oc >= 2:
                    t.wait_ge(pr.sc.h, scS + oc - 1)
                ps = pr.PA[:, 512 * (oc % 2):512 * (oc % 2) + 512]
                for k in range(8):
                    i = t.matmul(ps, W_q[:, k * 4096 + oc * 128: k * 4096 + oc * 128 + 128],
                                 xqb[:, k * 512:(k + 1) * 512], start=(k == 0), stop=(k == 7))
                pr.tm.inc(i)

        @pr.block.scalar
        def _(s, tmS=tmS, veS=veS, dmaS=dmaS):
            for oc in range(32):
                s.wait_ge(pr.tm.h, tmS + oc + 1)
                if oc >= 2:
                    s.wait_ge(pr.ve.h, veS + 2 * (oc - 1))
                    s.wait_ge(pr.dma.h, dmaS + 16 * (oc - 1))
                else:
                    s.wait_ge(pr.dma.h, dmaS)
                ps = pr.PA[:, 512 * (oc % 2):512 * (oc % 2) + 512]
                pr.sc.inc(s.activation(stg[:, 512 * (oc % 2):512 * (oc % 2) + 512], ps,
                                       AF.Gelu, bias=binc[:, oc:oc + 1]))

        @pr.block.vector
        def _(v, scS=scS):
            for oc in range(32):
                v.wait_ge(pr.sc.h, scS + oc + 1)
                sgl = stg[:, 512 * (oc % 2):512 * (oc % 2) + 512]
                v.tensor_reduce(hmax[:, 1:2], sgl, AX.X, ALU.max, apply_absolute_value=True)
                v.drain()
                v.tensor_tensor(hmax[:, 0:1], hmax[:, 0:1], hmax[:, 1:2], ALU.max)
                v.drain()
                pr.ve.inc(v.engine_nop(), 2)

        @pr.block.sync
        def _(sy, q=q, veS=veS, dmaS=dmaS):
            for oc in range(32):
                sy.wait_ge(pr.ve.h, veS + 2 * (oc + 1))
                sy.wait_ge(pr.dma.h, pr.dma.v)
                pr.dma.inc(sy.dma_start(out=hid.ap()[oc, :, q * 512:(q + 1) * 512],
                                        in_=stg[:, 512 * (oc % 2):512 * (oc % 2) + 512]), 16)
    stageI_tm = pr.tm.v
    stageI_dma = pr.dma.v
    phI.close()
    phW2 = ExitStack()
    wst = [phW2.enter_context(nc.sbuf_tensor(f"b3_wst2{i}", [128, 1024], F32)) for i in range(2)]

    emit_global_amax(pr, hmax[:, 0:4], cchi, ccho, binc[:, SHC:SHC + 1],
                     binc[:, SIHC:SIHC + 1], 32767.0, bscr)
    sh_ready = pr.ve.v

    ve_w1 = pr.ve.v
    emit_wq_stream(
        pr, "b3_wout", wst, lambda kc: W_q[:, kc * 1024:(kc + 1) * 1024],
        zt, it_, dt_, s1t, s2t, qsc,
        lambda kc: ([(pr.tm, stageI_tm), (pr.dma, stageI_dma)] if kc < 2
                    else [(pr.ve, ve_w1 + 2 * (kc - 1))]), io)
    woutq_done = pr.ve.v
    phW2.close()

    phII = ExitStack()
    hld = [phII.enter_context(nc.sbuf_tensor(f"b3_hld{i}", [128, 512], F32)) for i in range(2)]
    hqb = [phII.enter_context(nc.sbuf_tensor(f"b3_hqb{i}", [128, 512], BF16)) for i in range(2)]

    for q in range(NTT):
        dmaQ = pr.dma.v
        veQ = pr.ve.v
        tmQ = pr.tm.v

        @pr.block.sync
        def _(sy, q=q, veQ=veQ):
            if q == 0:
                sy.wait_ge(pr.ve.h, woutq_done)
            else:
                sy.wait_ge(pr.ve.h, veQ)
            for k in range(32):
                sy.wait_ge(pr.dma.h, pr.dma.v)
                if k >= 2:
                    sy.wait_ge(pr.ve.h, veQ + 2 * (k - 1))
                pr.dma.inc(sy.dma_start(out=hld[k % 2][:],
                                        in_=hid.ap()[k, :, q * 512:(q + 1) * 512]), 16)

        tiles = [(hld[k % 2][:], hqb[k % 2][:],
                  [(pr.dma, dmaQ + 16 * (k + 1))] + ([(pr.tm, tmQ + k - 1)] if k >= 2 else []))
                 for k in range(32)]
        emit_quant_tiles(pr, tiles, binc[:, SHC:SHC + 1], binc[:, SIHC:SIHC + 1],
                         zt, it_, dt_, s1t, s2t, 512,
                         pre_wait_v=([(pr.ve, sh_ready), (pr.ve, woutq_done)] if q == 0 else None))

        banks = [pr.PA[:, 0:512], pr.PA[:, 512:1024], pr.PB[:], pr.PC[:], pr.PD[:],
                 pr.PF[:], pr.PT1.bitcast(F32)[:, 0:512], pr.PT2.bitcast(F32)[:, 0:512]]

        @pr.block.tensor
        def _(t, q=q, veQ=veQ, banks=banks):
            if q == 0:
                t.wait_ge(pr.ve.h, woutq_done)
            else:
                t.wait_ge(pr.ve.h, veQ)
            for k in range(32):
                t.wait_ge(pr.ve.h, veQ + 2 * (k + 1))
                for dc in range(8):
                    i = t.matmul(banks[dc], W_q[:, k * 1024 + dc * 128: k * 1024 + dc * 128 + 128],
                                 hqb[k % 2][:], start=(k == 0), stop=(k == 31))
                pr.tm.inc(i)

        tmE = pr.tm.v

        @pr.block.vector
        def _(v, q=q, tmE=tmE, banks=banks):
            v.wait_ge(pr.tm.h, tmE)
            for dc in range(8):
                sl = slice(dc * T + q * 512, dc * T + q * 512 + 512)
                i = v.scalar_tensor_tensor(pr.xT[:, sl], banks[dc], binc[:, 32 + dc:32 + dc + 1],
                                           pr.xT[:, sl], ALU.add, ALU.add)
                v.drain()
            pr.ve.inc(i)
    phII.close()
    ph.close()
    return pr.ve.v


def build_b3():
    nc = bass.Bass("TRN2", target_bir_lowering=False, debug=False)
    io = {
        "xin": nc.dram_tensor("xin", [D, T], F32, kind="ExternalInput"),
        "b3_wint": nc.dram_tensor("b3_wint", [4096, 1024], F32, kind="ExternalInput"),
        "b3_wout": nc.dram_tensor("b3_wout", [4096, 1024], F32, kind="ExternalInput"),
        "b3_bin": nc.dram_tensor("b3_bin", [4096], F32, kind="ExternalInput"),
        "b3_bout": nc.dram_tensor("b3_bout", [1024], F32, kind="ExternalInput"),
    }
    xt_out = nc.dram_tensor("xt_out", [D, T], F32, kind="ExternalOutput")
    with ExitStack() as ctx:
        pr = Prog(nc, ctx, xln_cols=DCH * 1024, with_trimask=False)
        pr.emit_consts()
        xt_ready = emit_load_xt(pr, io["xin"])
        emit_block3(pr, io, xt_ready)
        emit_store_xt(pr, xt_out)
    return nc





def emit_block2(pr, io, xt_ready):
    nc = pr.nc
    NIT_BIS = 33
    ph = ExitStack()
    qn = pr.xln[:, 8192:12288]
    kn = pr.xln[:, 12288:16384]
    vs = ph.enter_context(nc.sbuf_tensor("b2_vs", [128, 4096], BF16))
    rS2 = ph.enter_context(nc.sbuf_tensor("b2_rs2", [128, 4096], F32))
    wbuf = [ph.enter_context(nc.sbuf_tensor("b2_w0", [128, 8192], BF16))] * 2
    oT = ph.enter_context(nc.sbuf_tensor("b2_oT", [128, 4096], BF16))
    sqs = ph.enter_context(nc.sbuf_tensor("b2_sqs", [128, 4096], BF16))
    msk = ph.enter_context(nc.sbuf_tensor("b2_msk", [128, 1024], BF16))
    mmt = ph.enter_context(nc.sbuf_tensor("b2_mmt", [128, 4096], BF16))
    sgm = mmt
    bis = ph.enter_context(nc.sbuf_tensor("b2_bis", [128, 256], F32))
    selu = ph.enter_context(nc.sbuf_tensor("b2_selu", [128, 64], mybir.dt.uint8))
    S32 = ph.enter_context(nc.sbuf_tensor("b2_S32", [128, 1024], F32))
    Sbf = ph.enter_context(nc.sbuf_tensor("b2_Sbf", [128, 1024], BF16))
    fh8 = ph.enter_context(nc.sbuf_tensor("b2_fh8", [8, 512], F32))
    rows = ph.enter_context(nc.sbuf_tensor("b2_rows", [1, 1024], F32))
    wab = ph.enter_context(nc.sbuf_tensor("b2_wab", [128, 64], BF16))
    nrm = ph.enter_context(nc.sbuf_tensor("b2_nrm", [128, 10], F32))
    ab8 = ph.enter_context(nc.sbuf_tensor("b2_ab8", [1, 16], F32))
    dtb = ph.enter_context(nc.sbuf_tensor("b2_dtbc", [128, 2], F32))
    chs = ph.enter_context(nc.sbuf_tensor("b2_chs", [128, 1024], F32))
    chb = ph.enter_context(nc.sbuf_tensor("b2_chb", [128, 2048], BF16))
    fts = ph.enter_context(nc.sbuf_tensor("b2_fts", [128, 512], F32))
    rbc = ph.enter_context(nc.sbuf_tensor("b2_rbc", [128, 512], F32))
    vtt = ph.enter_context(nc.sbuf_tensor("b2_vtt", [128, 128], BF16))
    fscr = nc.dram_tensor("b2_fscr", [8, 512], F32, kind="Internal")

    LO, HI, MID, CNT, T1 = 0, 32, 64, 96, 128

    @pr.block.gpsimd
    def _(g):
        g.wait_ge(pr.dmw.h, pr.dmw.v)
        pr.dmw.inc(g.dma_start(out=wab[:].rearrange("p (c m) -> p c m", c=8),
                               in_=io["b2_wa"].ap().rearrange("(c p) m -> p c m", p=128)), 16)
    wa_ld = pr.dmw.v

    @pr.block.sync
    def _(sy):
        sy.wait_ge(pr.dma.h, pr.dma.v)
        pr.dma.inc(sy.dma_start(out=ab8[:, 0:8], in_=io["b2_alog"].ap().rearrange("(a d) -> a d", a=1)), 16)
        sy.wait_ge(pr.dma.h, pr.dma.v)
        pr.dma.inc(sy.dma_start(out=ab8[:, 8:16], in_=io["b2_dtb"].ap().rearrange("(a d) -> a d", a=1)), 16)
        sy.wait_ge(pr.dma.h, pr.dma.v)
        pr.dma.inc(sy.dma_start(out=pr.srow512[:, 0:128], in_=io["b2_qnw"].ap().rearrange("(a d) -> a d", a=1)), 16)
        sy.wait_ge(pr.dma.h, pr.dma.v)
        pr.dma.inc(sy.dma_start(out=pr.srow512[:, 128:256], in_=io["b2_knw"].ap().rearrange("(a d) -> a d", a=1)), 16)
    nw_ld = pr.dma.v

    @pr.block.scalar
    def _(s):
        s.wait_ge(pr.dma.h, nw_ld)
        pr.sc.inc(s.activation(ab8[:, 0:8], ab8[:, 0:8], AF.Exp, scale=1.0))
    negea = pr.sc.v

    @pr.block.tensor
    def _(t):
        t.wait_ge(pr.sc.h, negea)
        t.wait_ge(pr.ve.h, pr.ve.v)
        t.wait_ge(pr.tm.h, pr.tm.v)
        t.matmul(pr.PD[0:128, 0:1], pr.srow512[:, 0:128], pr.ones_f11[:], start=True, stop=True)
        t.matmul(pr.PD[0:128, 1:2], pr.srow512[:, 128:256], pr.ones_f11[:], start=True, stop=True)
        t.matmul(pr.PD[0:8, 2:3], ab8[:, 8:16], pr.ones_f11[:], start=True, stop=True)
        pr.tm.inc(t.matmul(pr.PD[0:8, 3:4], ab8[:, 0:8], pr.ones_f11[:], start=True, stop=True))
    cmm = pr.tm.v

    @pr.block.vector
    def _(v):
        v.wait_ge(pr.tm.h, cmm)
        v.tensor_copy(nrm[:, 0:2], pr.PD[:, 0:2])
        v.drain()
        v.tensor_copy(dtb[0:8, 0:2], pr.PD[0:8, 2:4])
        v.drain()
        pr.ve.inc(v.tensor_scalar(dtb[0:8, 1:2], dtb[0:8, 1:2], -1.0, None, ALU.mult))
    small_done = pr.ve.v

    for part in range(0, 1024, 512):
        @pr.block.sync
        def _(sy, part=part):
            sy.wait_ge(pr.dma.h, pr.dma.v)
            sy.wait_ge(pr.ve.h, pr.ve.v)
            pr.dma.inc(sy.dma_start(out=pr.srow512[:],
                                    in_=io["b2_gnw"].ap().rearrange("(a d) -> a d", a=1)[:, part:part + 512]), 16)
        columnize_vec(pr, [pr.srow512[:]], nrm, 2 + part // 128)
    gnw_done = pr.ve.v

    @pr.block.gpsimd
    def _(g):
        pr.gp.inc(g.memset(S32[:], 0.0))
        pr.gp.inc(g.memset(Sbf[:], 0.0))
    s0done = pr.gp.v

    prev_tt = xt_ready
    for tt in range(NTT):
        lnd = pr.ln(pr.xT, pr.xln[:, 0:4096], pr.xln[:, 4096:8192],
                    [(pr.ve, prev_tt)], tts=[tt], dst_stride=512)
        xlq = pr.xln[:, 0:4096]

        def proj(wname, evac_fn, n_evac_inc, prev_marker):
            pass

        projs = [("b2_wq", "q"), ("b2_wk", "k"), ("b2_wv", "v"), ("b2_wr", "r")]
        ev_markers = {}
        for pi, (wname, kind) in enumerate(projs):
            dmw0 = pr.dmw.v

            @pr.block.gpsimd
            def _(g, wname=wname, pi=pi, tt=tt):
                g.wait_ge(pr.dmw.h, pr.dmw.v)
                if not (tt == 0 and pi == 0):
                    g.wait_ge(pr.tm.h, pr.tm.v)
                pr.dmw.inc(g.dma_start(out=wbuf[pi % 2][:].rearrange("p (c m) -> p c m", c=8),
                                       in_=io[wname].ap().rearrange("(c p) m -> p c m", p=128)), 16)
            wld = pr.dmw.v

            tmP = pr.tm.v
            scP = pr.sc.v
            veP = pr.ve.v

            if kind != "r":
                @pr.block.tensor
                def _(t, wld=wld, lnd=lnd, tmP=tmP, scP=scP, pi=pi, tt=tt):
                    t.wait_ge(pr.dmw.h, wld)
                    t.wait_ge(pr.ve.h, lnd)
                    t.wait_ge(pr.sc.h, scP)
                    if tt == 0 and pi == 0:
                        t.wait_ge(pr.gp.h, s0done)
                        t.wait_ge(pr.ve.h, gnw_done)
                    for dc in range(8):
                        if dc >= 2:
                            t.wait_ge(pr.sc.h, scP + dc - 1)
                        ps = pr.PA[:, 512 * (dc % 2):512 * (dc % 2) + 512]
                        for k in range(8):
                            i = t.matmul(ps, wbuf[pi % 2][:, k * 1024 + dc * 128: k * 1024 + dc * 128 + 128],
                                         xlq[:, k * 512:(k + 1) * 512], start=(k == 0), stop=(k == 7))
                        pr.tm.inc(i)

            if kind in ("q", "k", "v"):
                dstb = {"q": qn, "k": kn, "v": vs}[kind]

                @pr.block.scalar
                def _(s, tmP=tmP, dstb=dstb):
                    for dc in range(8):
                        s.wait_ge(pr.tm.h, tmP + dc + 1)
                        ps = pr.PA[:, 512 * (dc % 2):512 * (dc % 2) + 512]
                        pr.sc.inc(s.activation(dstb[:, dc * 512:(dc + 1) * 512], ps, AF.Silu))
                ev_markers[kind] = pr.sc.v
            else:
                veR = pr.ve.v

                def tm_mm(dc):
                    return tmP + dc + 1 + 4 * max(0, dc - 1)

                def tm_tr(j, ts):
                    if j <= 6:
                        return tmP + (j + 2) + 4 * j + ts + 1
                    return tmP + 36 + ts + 1

                @pr.block.tensor
                def _(t, wld=wld, lnd=lnd, tmP=tmP, scP=scP, pi=pi, veR=veR):
                    t.wait_ge(pr.dmw.h, wld)
                    t.wait_ge(pr.ve.h, lnd)
                    t.wait_ge(pr.sc.h, scP)
                    def emit_tr(j):
                        t.wait_ge(pr.sc.h, scP + j + 1)
                        for ts in range(4):
                            idx = j * 4 + ts
                            if idx >= 2:
                                t.wait_ge(pr.ve.h, veR + idx - 1)
                            pt = pr.PT1 if idx % 2 == 0 else pr.PT2
                            pr.tm.inc(t.transpose(pt.bitcast(F32)[:, 0:128],
                                                  chs[:, 512 * (j % 2) + ts * 128: 512 * (j % 2) + ts * 128 + 128],
                                                  pr.ident_f[:]))
                    for dc in range(8):
                        if dc >= 2:
                            t.wait_ge(pr.sc.h, scP + dc - 1)
                        ps = pr.PA[:, 512 * (dc % 2):512 * (dc % 2) + 512]
                        for k in range(8):
                            i = t.matmul(ps, wbuf[pi % 2][:, k * 1024 + dc * 128: k * 1024 + dc * 128 + 128],
                                         xlq[:, k * 512:(k + 1) * 512], start=(k == 0), stop=(k == 7))
                        pr.tm.inc(i)
                        if dc >= 1:
                            emit_tr(dc - 1)
                    emit_tr(7)

                @pr.block.scalar
                def _(s, tmP=tmP):
                    for dc in range(8):
                        s.wait_ge(pr.tm.h, tm_mm(dc))
                        ps = pr.PA[:, 512 * (dc % 2):512 * (dc % 2) + 512]
                        pr.sc.inc(s.copy(chs[:, 512 * (dc % 2):512 * (dc % 2) + 512], ps))

                @pr.block.vector
                def _(v):
                    for dc in range(8):
                        for ts in range(4):
                            idx = dc * 4 + ts
                            v.wait_ge(pr.tm.h, tm_tr(dc, ts))
                            pt = pr.PT1 if idx % 2 == 0 else pr.PT2
                            pr.ve.inc(v.tensor_copy(rS2[:, ts * 1024 + dc * 128: ts * 1024 + dc * 128 + 128],
                                                    pt.bitcast(F32)[:, 0:128]))
                ev_markers["r"] = pr.ve.v

        dmw0 = pr.dmw.v
        tmA = pr.tm.v

        @pr.block.tensor
        def _(t, tmA=tmA, wa_ld=wa_ld, lnd=lnd):
            t.wait_ge(pr.dmw.h, wa_ld)
            t.wait_ge(pr.ve.h, lnd)
            t.wait_ge(pr.ve.h, pr.ve.v)
            for k in range(8):
                i = t.matmul(pr.PD[0:8, 0:512], wab[:, k * 8:(k + 1) * 8],
                             xlq[:, k * 512:(k + 1) * 512], start=(k == 0), stop=(k == 7))
            pr.tm.inc(i)
        fhmm = pr.tm.v

        @pr.block.scalar
        def _(s, fhmm=fhmm, dmav=pr.dma.v):
            s.wait_ge(pr.tm.h, fhmm)
            s.wait_ge(pr.ve.h, small_done)
            s.wait_ge(pr.dma.h, dmav)
            s.activation(fh8[:, 0:512], pr.PD[0:8, 0:512], AF.Exp, bias=dtb[0:8, 0:1])
            s.drain()
            pr.sc.inc(s.activation(fh8[:, 0:512], fh8[:, 0:512], AF.Ln, bias=pr.onecol[0:8, 0:1]))
        sp_done = pr.sc.v

        @pr.block.vector
        def _(v, sp_done=sp_done):
            v.wait_ge(pr.sc.h, sp_done)
            pr.ve.inc(v.tensor_scalar(fh8[:, 0:512], fh8[:, 0:512], dtb[0:8, 1:2], None, ALU.mult))
        fh_done = pr.ve.v

        @pr.block.sync
        def _(sy, fh_done=fh_done):
            sy.wait_ge(pr.ve.h, fh_done)
            sy.wait_ge(pr.dma.h, pr.dma.v)
            pr.dma.inc(sy.dma_start(out=fscr.ap(), in_=fh8[:, 0:512]), 16)
        fhr_ld = pr.dma.v


        for kind, buf, wcol in [("q", qn, 0), ("k", kn, 1)]:
            sqm = pr.sc.v

            @pr.block.scalar
            def _(s, buf=buf, kind=kind, tmguard=pr.tm.v, evm=ev_markers[kind]):
                s.wait_ge(pr.sc.h, evm)
                s.wait_ge(pr.tm.h, tmguard)
                pr.sc.inc(s.activation(sqs[:], buf[:], AF.Square))
            sq_d = pr.sc.v

            for h in range(8):
                tmN = pr.tm.v

                @pr.block.tensor
                def _(t, h=h, sq_d=sq_d, ve_pf=pr.ve.v):
                    t.wait_ge(pr.sc.h, sq_d)
                    t.wait_ge(pr.ve.h, ve_pf)
                    pr.tm.inc(t.matmul(pr.PF[0:1, 0:512], pr.ones_bcol[:],
                                       sqs[:, h * 512:(h + 1) * 512], start=True, stop=True))
                nmm = pr.tm.v

                @pr.block.scalar
                def _(s, nmm=nmm):
                    s.wait_ge(pr.tm.h, nmm)
                    s.activation(rows[:, 0:512], pr.PF[0:1, 0:512], AF.Ln, scale=1.0 / 128.0,
                                 bias=pr.eps_t[:])
                    s.drain()
                    pr.sc.inc(s.activation(rows[:, 0:512], rows[:, 0:512], AF.Exp, scale=-0.5))
                nr_d = pr.sc.v

                @pr.block.tensor
                def _(t, nr_d=nr_d):
                    t.wait_ge(pr.sc.h, nr_d)
                    pr.tm.inc(t.matmul(pr.PF[:, 0:512], pr.ones_frow[:], rows[:, 0:512],
                                       start=True, stop=True))
                nb_mm = pr.tm.v

                @pr.block.vector
                def _(v, nb_mm=nb_mm, h=h, buf=buf, wcol=wcol):
                    v.wait_ge(pr.tm.h, nb_mm)
                    v.tensor_copy(rbc[:], pr.PF[:, 0:512])
                    v.drain()
                    pr.ve.inc(v.scalar_tensor_tensor(buf[:, h * 512:(h + 1) * 512],
                                                     buf[:, h * 512:(h + 1) * 512],
                                                     nrm[:, wcol:wcol + 1], rbc[:],
                                                     ALU.mult, ALU.mult))
                    v.drain()
        qknorm_done = pr.ve.v

        veB = pr.ve.v

        @pr.block.vector
        def _(v, veB=veB, tmguard=pr.tm.v, rmark=ev_markers["r"]):
            v.wait_ge(pr.ve.h, rmark)
            v.wait_ge(pr.tm.h, tmguard)
            v.drain()
            v.memset(bis[:, LO:LO + 32], -40.0)
            v.memset(bis[:, HI:HI + 32], 40.0)
            v.memset(bis[:, MID:MID + 32], 0.0)
            v.drain()
            for it in range(NIT_BIS):
                for u in range(32):
                    ts, h = divmod(u, 8)
                    v.tensor_scalar(mmt[:, u * 128:(u + 1) * 128],
                                    rS2[:, ts * 1024 + h * 128: ts * 1024 + h * 128 + 128],
                                    bis[:, MID + u:MID + u + 1], 0.0, ALU.is_ge, ALU.add,
                                    accum_out=bis[:, CNT + u:CNT + u + 1])
                v.drain()
                v.tensor_scalar(selu[:, 0:32], bis[:, CNT:CNT + 32], 31.5, None, ALU.is_ge)
                v.tensor_scalar(selu[:, 32:64], bis[:, CNT:CNT + 32], 31.5, None, ALU.is_lt)
                v.drain()
                v.select(bis[:, LO:LO + 32], selu[:, 0:32], bis[:, MID:MID + 32], bis[:, LO:LO + 32], add_drain=True)
                v.select(bis[:, HI:HI + 32], selu[:, 32:64], bis[:, MID:MID + 32], bis[:, HI:HI + 32], add_drain=True)
                v.drain()
                v.tensor_tensor(bis[:, T1:T1 + 32], bis[:, LO:LO + 32], bis[:, HI:HI + 32], ALU.add)
                v.drain()
                v.tensor_scalar(bis[:, MID:MID + 32], bis[:, T1:T1 + 32], 0.5, None, ALU.mult)
                v.drain()
            for u in range(32):
                ts, h = divmod(u, 8)
                i = v.tensor_scalar(sqs[:, ts * 1024 + h * 128: ts * 1024 + h * 128 + 128],
                                    rS2[:, ts * 1024 + h * 128: ts * 1024 + h * 128 + 128],
                                    bis[:, LO + u:LO + u + 1], None, ALU.is_ge)
            pr.ve.inc(i)
        bis_done = pr.ve.v

        @pr.block.scalar
        def _(s, bis_done=bis_done):
            s.wait_ge(pr.ve.h, bis_done)
            pr.sc.inc(s.activation(sgm[:], rS2[:, 0:4096], AF.Sigmoid))
        sgm_done = pr.sc.v

        @pr.block.vector
        def _(v, sgm_done=sgm_done, bis_done=bis_done):
            v.wait_ge(pr.sc.h, sgm_done)
            v.wait_ge(pr.ve.h, bis_done)
            pr.ve.inc(v.tensor_tensor(mmt[:], sgm[:], sqs[:], ALU.mult))
        mskd_done = pr.ve.v

        tmM = pr.tm.v
        veM = pr.ve.v

        @pr.block.tensor
        def _(t, tmM=tmM, veM=veM, mskd_done=mskd_done):
            t.wait_ge(pr.ve.h, mskd_done)
            for h in range(8):
                for ts in range(4):
                    idx = h * 4 + ts
                    if idx >= 2:
                        t.wait_ge(pr.ve.h, veM + idx - 1)
                    pt = pr.PT1 if idx % 2 == 0 else pr.PT2
                    pr.tm.inc(t.transpose(pt[:, 0:128],
                                          mmt[:, ts * 1024 + h * 128: ts * 1024 + h * 128 + 128],
                                          pr.ident_b[:]))

        @pr.block.vector
        def _(v, tmM=tmM):
            for h in range(8):
                for ts in range(4):
                    idx = h * 4 + ts
                    v.wait_ge(pr.tm.h, tmM + idx + 1)
                    pt = pr.PT1 if idx % 2 == 0 else pr.PT2
                    pr.ve.inc(v.tensor_copy(sqs[:, h * 512 + ts * 128: h * 512 + ts * 128 + 128],
                                            pt[:, 0:128]))
        mmt_done = pr.ve.v

        for h in range(8):
            tmH = pr.tm.v
            scH = pr.sc.v
            veH = pr.ve.v

            @pr.block.tensor
            def _(t, h=h, mmt_done=mmt_done, veH=veH, tmH=tmH):
                t.wait_ge(pr.ve.h, mmt_done)
                if h >= 1:
                    t.wait_ge(pr.ve.h, veH)
                pr.tm.inc(t.matmul(pr.PF[0:1, 0:512], pr.ones_bcol[:],
                                   sqs[:, h * 512:(h + 1) * 512], start=True, stop=True))
            ts_mm = pr.tm.v

            @pr.block.scalar
            def _(s, ts_mm=ts_mm, h=h):
                s.wait_ge(pr.tm.h, ts_mm)
                s.activation(rows[:, 0:512], pr.PF[0:1, 0:512], AF.Ln, bias=pr.eps9[:])
                s.drain()
                pr.sc.inc(s.activation(rows[:, 0:512], rows[:, 0:512], AF.Exp, scale=-1.0))
            rt_done = pr.sc.v

            @pr.block.sync
            def _(sy, h=h, fhr_ld=fhr_ld):
                sy.wait_ge(pr.dma.h, pr.dma.v)
                sy.wait_ge(pr.tm.h, pr.tm.v)
                pr.dma.inc(sy.dma_start(out=rows[:, 512:1024], in_=fscr.ap()[h, :]), 16)
            fh_row = pr.dma.v

            @pr.block.tensor
            def _(t, rt_done=rt_done, h=h, fh_row=fh_row):
                t.wait_ge(pr.sc.h, rt_done)
                t.wait_ge(pr.dma.h, fh_row)
                t.matmul(pr.PF[:, 0:512], pr.ones_frow[:], rows[:, 0:512], start=True, stop=True)
                pr.tm.inc(t.matmul(pr.PB[:, 0:512], pr.ones_frow[:], rows[:, 512:1024],
                                   start=True, stop=True))
            bc_mm = pr.tm.v

            @pr.block.vector
            def _(v, bc_mm=bc_mm, h=h):
                v.wait_ge(pr.tm.h, bc_mm)
                v.tensor_copy(rbc[:], pr.PF[:, 0:512])
                v.drain()
                v.tensor_tensor(fts[:], sqs[:, h * 512:(h + 1) * 512], rbc[:], ALU.mult)
                v.drain()
                pr.ve.inc(v.tensor_tensor(fts[:], fts[:], pr.PB[:, 0:512], ALU.mult))
                v.drain()
            f_done = pr.ve.v

            prev_o_ev = pr.sc.v
            for ck in range(4):
                csl = slice(ck * 128, ck * 128 + 128)
                first = (tt == 0 and ck == 0)
                tmC = pr.tm.v
                scC = pr.sc.v
                veC = pr.ve.v

                @pr.block.vector
                def _(v, h=h, ck=ck, csl=csl, f_done=f_done, veC=veC):
                    v.wait_ge(pr.ve.h, f_done)
                    pr.ve.inc(v.tensor_tensor_scan(chs[:, 0:128], fts[:, csl], fts[:, csl],
                                                   0.0, ALU.add, ALU.bypass))
                    v.drain()
                L_done = pr.ve.v

                @pr.block.scalar
                def _(s, L_done=L_done, csl=csl):
                    s.wait_ge(pr.ve.h, L_done)
                    s.activation(chb[:, 0:128], chs[:, 0:128], AF.Exp)
                    s.drain()
                    s.activation(chb[:, 128:256], chs[:, 0:128], AF.Exp, scale=-1.0)
                    s.drain()
                    s.activation(chs[:, 128:256], fts[:, csl], AF.Exp)
                    s.drain()
                    pr.sc.inc(s.copy(chs[:, 384:385], chs[:, 127:128]))
                exp_done = pr.sc.v

                @pr.block.scalar
                def _(s, exp_done=exp_done):
                    s.wait_ge(pr.sc.h, exp_done)
                    pr.sc.inc(s.activation(chs[:, 385:386], chs[:, 384:385], AF.Exp))
                eL_done = pr.sc.v

                @pr.block.vector
                def _(v, h=h, csl=csl, eL_done=eL_done):
                    v.wait_ge(pr.sc.h, eL_done)
                    v.tensor_scalar(chs[:, 256:384], chs[:, 128:256], -1.0, 1.0, ALU.mult, ALU.add)
                    v.tensor_tensor(chb[:, 256:384], qn[:, h * 512:h * 512 + 512][:, csl],
                                    chb[:, 0:128], ALU.mult)
                    v.drain()
                    v.tensor_tensor(chb[:, 384:512], kn[:, h * 512:h * 512 + 512][:, csl],
                                    chs[:, 256:384], ALU.mult)
                    v.drain()
                    v.tensor_tensor(chb[:, 512:640], chb[:, 384:512], chb[:, 128:256], ALU.mult)
                    v.drain()
                    pr.ve.inc(v.tensor_scalar(chb[:, 640:768], chb[:, 512:640],
                                              chs[:, 385:386], None, ALU.mult))
                    v.drain()
                qk_done = pr.ve.v

                tmC2 = pr.tm.v

                @pr.block.tensor
                def _(t, h=h, ck=ck, qk_done=qk_done, tmC2=tmC2, first=first):
                    t.wait_ge(pr.ve.h, qk_done)
                    pr.tm.inc(t.matmul(pr.PB[:, 0:128], chb[:, 512:640], chb[:, 256:384],
                                       start=True, stop=True))
                    pr.tm.inc(t.transpose(pr.PT1[:, 0:128],
                                          vs[:, h * 512:h * 512 + 512][:, ck * 128:ck * 128 + 128],
                                          pr.ident_b[:]))
                    pr.tm.inc(t.transpose(pr.PT2[:, 0:128], chb[:, 640:768], pr.ident_b[:]))
                pt_mm = pr.tm.v

                @pr.block.vector
                def _(v, pt_mm=pt_mm):
                    v.wait_ge(pr.tm.h, pt_mm)
                    v.tensor_tensor(chb[:, 768:896], pr.PB[:, 0:128], pr.trimask[:], ALU.mult)
                    v.tensor_copy(vtt[:, 0:128], pr.PT1[:, 0:128])
                    v.tensor_copy(chb[:, 896:1024], pr.PT2[:, 0:128])
                    pr.ve.inc(v.drain())
                pm_done = pr.ve.v

                @pr.block.tensor
                def _(t, h=h, pm_done=pm_done, first=first, prev_o_ev=prev_o_ev):
                    t.wait_ge(pr.ve.h, pm_done)
                    t.wait_ge(pr.sc.h, prev_o_ev)
                    t.matmul(pr.PC[:, 0:128], vtt[:, 0:128], chb[:, 768:896],
                             start=True, stop=first)
                    if not first:
                        t.matmul(pr.PC[:, 0:128], Sbf[:, h * 128:(h + 1) * 128],
                                 chb[:, 256:384], start=False, stop=True)
                    i = t.matmul(pr.PD[:, 0:128], chb[:, 896:1024], vtt[:, 0:128],
                                 start=True, stop=True)
                    pr.tm.inc(i)
                o_mm = pr.tm.v

                @pr.block.scalar
                def _(s, h=h, ck=ck, o_mm=o_mm, tt=tt):
                    s.wait_ge(pr.tm.h, o_mm)
                    pr.sc.inc(s.copy(oT[:, h * 512 + ck * 128: h * 512 + ck * 128 + 128],
                                     pr.PC[:, 0:128]))
                o_ev = pr.sc.v
                prev_o_ev = o_ev

                @pr.block.vector
                def _(v, h=h, o_mm=o_mm, o_ev=o_ev):
                    v.wait_ge(pr.tm.h, o_mm)
                    v.scalar_tensor_tensor(S32[:, h * 128:(h + 1) * 128], S32[:, h * 128:(h + 1) * 128],
                                           chs[:, 385:386], pr.PD[:, 0:128], ALU.mult, ALU.add)
                    v.drain()
                    pr.ve.inc(v.tensor_copy(Sbf[:, h * 128:(h + 1) * 128], S32[:, h * 128:(h + 1) * 128]))
                    v.drain()

        last_o_sc = pr.sc.v
        last_ve = pr.ve.v

        @pr.block.scalar
        def _(s, last_o_sc=last_o_sc, last_ve=last_ve, tmg=pr.tm.v):
            s.wait_ge(pr.sc.h, last_o_sc)
            s.wait_ge(pr.ve.h, last_ve)
            s.wait_ge(pr.tm.h, tmg)
            pr.sc.inc(s.activation(sqs[:], oT[:], AF.Square))
        sq_done = pr.sc.v

        tmG = pr.tm.v

        @pr.block.tensor
        def _(t, sq_done=sq_done, tmG=tmG):
            t.wait_ge(pr.sc.h, sq_done)
            for h in range(8):
                i = t.matmul(pr.PF[0:1, 0:512], pr.ones_bcol[:], sqs[:, h * 512:(h + 1) * 512],
                             start=(h == 0), stop=(h == 7))
            pr.tm.inc(i)
        gs_mm = pr.tm.v

        @pr.block.scalar
        def _(s, gs_mm=gs_mm):
            s.wait_ge(pr.tm.h, gs_mm)
            s.activation(rows[:, 0:512], pr.PF[0:1, 0:512], AF.Ln, scale=1.0 / 1024.0,
                         bias=pr.eps_t[:])
            s.drain()
            pr.sc.inc(s.activation(rows[:, 0:512], rows[:, 0:512], AF.Exp, scale=-0.5))
        gr_done = pr.sc.v

        @pr.block.tensor
        def _(t, gr_done=gr_done):
            t.wait_ge(pr.sc.h, gr_done)
            pr.tm.inc(t.matmul(pr.PF[:, 0:512], pr.ones_frow[:], rows[:, 0:512],
                               start=True, stop=True))
        grb_mm = pr.tm.v

        @pr.block.vector
        def _(v, grb_mm=grb_mm):
            v.wait_ge(pr.tm.h, grb_mm)
            v.tensor_copy(rbc[:], pr.PF[:, 0:512])
            v.drain()
            for h in range(8):
                v.scalar_tensor_tensor(oT[:, h * 512:(h + 1) * 512], oT[:, h * 512:(h + 1) * 512],
                                       nrm[:, 2 + h:3 + h], rbc[:], ALU.mult, ALU.mult)
            pr.ve.inc(v.drain())
        on_done = pr.ve.v

        dmwO = pr.dmw.v

        @pr.block.gpsimd
        def _(g, tt=tt):
            g.wait_ge(pr.dmw.h, pr.dmw.v)
            g.wait_ge(pr.tm.h, pr.tm.v)
            pr.dmw.inc(g.dma_start(out=wbuf[0][:].rearrange("p (c m) -> p c m", c=8),
                                   in_=io["b2_wo"].ap().rearrange("(c p) m -> p c m", p=128)), 16)
        wo_ld = pr.dmw.v

        tmW = pr.tm.v
        veW = pr.ve.v

        @pr.block.tensor
        def _(t, wo_ld=wo_ld, on_done=on_done, tmW=tmW, veW=veW):
            t.wait_ge(pr.dmw.h, wo_ld)
            t.wait_ge(pr.ve.h, on_done)
            for dc in range(8):
                if dc >= 2:
                    t.wait_ge(pr.ve.h, veW + dc - 1)
                ps = pr.PA[:, 512 * (dc % 2):512 * (dc % 2) + 512]
                for k in range(8):
                    i = t.matmul(ps, wbuf[0][:, k * 1024 + dc * 128: k * 1024 + dc * 128 + 128],
                                 oT[:, k * 512:(k + 1) * 512], start=(k == 0), stop=(k == 7))
                pr.tm.inc(i)

        @pr.block.vector
        def _(v, tmW=tmW, tt=tt):
            for dc in range(8):
                v.wait_ge(pr.tm.h, tmW + dc + 1)
                sl = slice(dc * T + tt * 512, dc * T + tt * 512 + 512)
                ps = pr.PA[:, 512 * (dc % 2):512 * (dc % 2) + 512]
                pr.ve.inc(v.tensor_tensor(pr.xT[:, sl], pr.xT[:, sl], ps, ALU.add))
                v.drain()
        prev_tt = pr.ve.v
    ph.close()
    return pr.ve.v


def build_b2():
    nc = bass.Bass("TRN2", target_bir_lowering=False, debug=False)
    io = {
        "xin": nc.dram_tensor("xin", [D, T], F32, kind="ExternalInput"),
        "b2_wq": nc.dram_tensor("b2_wq", [D, D], F32, kind="ExternalInput"),
        "b2_wk": nc.dram_tensor("b2_wk", [D, D], F32, kind="ExternalInput"),
        "b2_wv": nc.dram_tensor("b2_wv", [D, D], F32, kind="ExternalInput"),
        "b2_wr": nc.dram_tensor("b2_wr", [D, D], F32, kind="ExternalInput"),
        "b2_wo": nc.dram_tensor("b2_wo", [D, D], F32, kind="ExternalInput"),
        "b2_wa": nc.dram_tensor("b2_wa", [D, 8], F32, kind="ExternalInput"),
        "b2_alog": nc.dram_tensor("b2_alog", [8], F32, kind="ExternalInput"),
        "b2_dtb": nc.dram_tensor("b2_dtb", [8], F32, kind="ExternalInput"),
        "b2_qnw": nc.dram_tensor("b2_qnw", [128], F32, kind="ExternalInput"),
        "b2_knw": nc.dram_tensor("b2_knw", [128], F32, kind="ExternalInput"),
        "b2_gnw": nc.dram_tensor("b2_gnw", [1024], F32, kind="ExternalInput"),
    }
    xt_out = nc.dram_tensor("xt_out", [D, T], F32, kind="ExternalOutput")
    with ExitStack() as ctx:
        pr = Prog(nc, ctx, xln_cols=DCH * 1024)
        pr.emit_consts()
        xt_ready = emit_load_xt(pr, io["xin"])
        emit_block2(pr, io, xt_ready)
        emit_store_xt(pr, xt_out)
    return nc


def emit_store_xt(pr, xt_out):
    @pr.block.sync
    def _(sy):
        sy.wait_ge(pr.ve.h, pr.ve.v)
        o = xt_out.ap().rearrange("(c p) t -> p c t", p=128)
        pr.od.inc(sy.dma_start(out=o, in_=pr.xT[:].rearrange("p (c t) -> p c t", c=DCH)), 16)
        sy.wait_ge(pr.od.h, pr.od.v)


def build_b0(x_natural=True, upto=99):
    nc = bass.Bass("TRN2", target_bir_lowering=False, debug=False)
    io = {
        "xin": nc.dram_tensor("xin", [T, D] if x_natural else [D, T], F32, kind="ExternalInput"),
        "b0_ctx": nc.dram_tensor("b0_ctx", [512, CD], F32, kind="ExternalInput"),
        "b0_wq": nc.dram_tensor("b0_wq", [D, D], F32, kind="ExternalInput"),
        "b0_wk": nc.dram_tensor("b0_wk", [CD, D], F32, kind="ExternalInput"),
        "b0_wv": nc.dram_tensor("b0_wv", [CD, D], F32, kind="ExternalInput"),
        "b0_wo": nc.dram_tensor("b0_wo", [D, D], F32, kind="ExternalInput"),
        "b0_bo": nc.dram_tensor("b0_bo", [D], F32, kind="ExternalInput"),
    }
    xt_out = nc.dram_tensor("xt_out", [D, T], F32, kind="ExternalOutput")
    with ExitStack() as ctx:
        pr = Prog(nc, ctx)
        pr.emit_consts()
        emit_block0(pr, io, have_x_natural=x_natural, upto=upto)
        emit_store_xt(pr, xt_out)
    return nc


def emit_barrier(pr):

    @pr.block.tensor
    def _(t):
        pr.tm.inc(t.drain())

    @pr.block.vector
    def _(v):
        pr.ve.inc(v.drain())

    @pr.block.scalar
    def _(s):
        pr.sc.inc(s.drain())

    @pr.block.gpsimd
    def _(g):
        pr.gp.inc(g.drain())

    vals = [(s, s.v) for s in (pr.dma, pr.dmg, pr.dmw, pr.dmk, pr.cc,
                               pr.tm, pr.sc, pr.ve, pr.gp, pr.od)]
    for sec in (pr.block.tensor, pr.block.vector, pr.block.scalar,
                pr.block.gpsimd, pr.block.sync):
        @sec
        def _(e, vals=vals):
            for s, v in vals:
                e.wait_ge(s.h, v)
    return pr.ve.v


def build_all():
    nc = bass.Bass("TRN2", target_bir_lowering=False, debug=False)
    io = {
        "xin": nc.dram_tensor("xin", [T, D], F32, kind="ExternalInput"),
        "b0_ctx": nc.dram_tensor("b0_ctx", [512, CD], F32, kind="ExternalInput"),
        "b0_wq": nc.dram_tensor("b0_wq", [D, D], F32, kind="ExternalInput"),
        "b0_wk": nc.dram_tensor("b0_wk", [CD, D], F32, kind="ExternalInput"),
        "b0_wv": nc.dram_tensor("b0_wv", [CD, D], F32, kind="ExternalInput"),
        "b0_wo": nc.dram_tensor("b0_wo", [D, D], F32, kind="ExternalInput"),
        "b0_bo": nc.dram_tensor("b0_bo", [D], F32, kind="ExternalInput"),
        "b1_w1t": nc.dram_tensor("b1_w1t", [1024, 4096], F32, kind="ExternalInput"),
        "b1_w2t": nc.dram_tensor("b1_w2t", [1024, 4096], F32, kind="ExternalInput"),
        "b1_h1t": nc.dram_tensor("b1_h1t", [4096, 4096], F32, kind="ExternalInput"),
        "b1_h2t": nc.dram_tensor("b1_h2t", [4096, 4096], F32, kind="ExternalInput"),
        "b1_w3t": nc.dram_tensor("b1_w3t", [4096, 1024], F32, kind="ExternalInput"),
        "b1_s1": nc.dram_tensor("b1_s1", [4096], F32, kind="ExternalInput"),
        "b1_s2": nc.dram_tensor("b1_s2", [4096], F32, kind="ExternalInput"),
        "b1_sh1": nc.dram_tensor("b1_sh1", [4096], F32, kind="ExternalInput"),
        "b1_sh2": nc.dram_tensor("b1_sh2", [4096], F32, kind="ExternalInput"),
        "b1_s3": nc.dram_tensor("b1_s3", [1024], F32, kind="ExternalInput"),
        "b2_wq": nc.dram_tensor("b2_wq", [D, D], F32, kind="ExternalInput"),
        "b2_wk": nc.dram_tensor("b2_wk", [D, D], F32, kind="ExternalInput"),
        "b2_wv": nc.dram_tensor("b2_wv", [D, D], F32, kind="ExternalInput"),
        "b2_wr": nc.dram_tensor("b2_wr", [D, D], F32, kind="ExternalInput"),
        "b2_wo": nc.dram_tensor("b2_wo", [D, D], F32, kind="ExternalInput"),
        "b2_wa": nc.dram_tensor("b2_wa", [D, 8], F32, kind="ExternalInput"),
        "b2_alog": nc.dram_tensor("b2_alog", [8], F32, kind="ExternalInput"),
        "b2_dtb": nc.dram_tensor("b2_dtb", [8], F32, kind="ExternalInput"),
        "b2_qnw": nc.dram_tensor("b2_qnw", [128], F32, kind="ExternalInput"),
        "b2_knw": nc.dram_tensor("b2_knw", [128], F32, kind="ExternalInput"),
        "b2_gnw": nc.dram_tensor("b2_gnw", [1024], F32, kind="ExternalInput"),
        "b3_wint": nc.dram_tensor("b3_wint", [4096, 1024], F32, kind="ExternalInput"),
        "b3_wout": nc.dram_tensor("b3_wout", [4096, 1024], F32, kind="ExternalInput"),
        "b3_bin": nc.dram_tensor("b3_bin", [4096], F32, kind="ExternalInput"),
        "b3_bout": nc.dram_tensor("b3_bout", [1024], F32, kind="ExternalInput"),
    }
    xt_out = nc.dram_tensor("xt_out", [D, T], F32, kind="ExternalOutput")
    with ExitStack() as ctx:
        pr = Prog(nc, ctx, xln_cols=DCH * T)
        pr.emit_consts()
        emit_block0(pr, io, have_x_natural=True)
        emit_barrier(pr)
        emit_block1(pr, io, pr.ve.v)
        emit_barrier(pr)
        emit_block2(pr, io, pr.ve.v)
        emit_barrier(pr)
        emit_block3(pr, io, pr.ve.v)
        emit_store_xt(pr, xt_out)
    return nc


def kernel(**inputs):
    from concourse.bass_utils import run_bass_kernel_spmd
    f32 = np.float32
    g = lambda k: np.ascontiguousarray(np.asarray(inputs[k], dtype=f32))
    cores = list(range(NCORE))

    x = g("L0_attn_x")
    ctxf = np.ascontiguousarray(g("L0_attn_context").reshape(4096, CD))
    nc = build_all()
    w1t = np.ascontiguousarray(g("L0_ffn_w1_w").T)
    w2t = np.ascontiguousarray(g("L0_ffn_w2_w").T)
    h1t = np.ascontiguousarray(g("L0_ffn_h1_w").T)
    h2t = np.ascontiguousarray(g("L0_ffn_h2_w").T)
    w3t = np.ascontiguousarray(g("L0_ffn_w3_w").T)
    wint = np.ascontiguousarray(g("L1_ffn_W_in").T)
    in_maps = [dict(
        xin=np.ascontiguousarray(x[c]),
        b0_ctx=np.ascontiguousarray(ctxf[c * 512:(c + 1) * 512]),
        b0_wq=g("L0_attn_Wq"), b0_wk=g("L0_attn_Wk"), b0_wv=g("L0_attn_Wv"),
        b0_wo=g("L0_attn_Wout"), b0_bo=g("L0_attn_bout"),
        b1_w1t=w1t, b1_w2t=w2t, b1_h1t=h1t, b1_h2t=h2t, b1_w3t=w3t,
        b1_s1=g("L0_ffn_w1_s"), b1_s2=g("L0_ffn_w2_s"),
        b1_sh1=g("L0_ffn_h1_s"), b1_sh2=g("L0_ffn_h2_s"), b1_s3=g("L0_ffn_w3_s"),
        b2_wq=g("L1_attn_Wq"), b2_wk=g("L1_attn_Wk"), b2_wv=g("L1_attn_Wv"),
        b2_wr=g("L1_attn_Wr"), b2_wo=g("L1_attn_Wo"), b2_wa=g("L1_attn_Wa"),
        b2_alog=g("L1_attn_A_log"), b2_dtb=g("L1_attn_dt_bias"),
        b2_qnw=g("L1_attn_q_norm_w"), b2_knw=g("L1_attn_k_norm_w"),
        b2_gnw=g("L1_attn_g_norm_w"),
        b3_wint=wint, b3_wout=g("L1_ffn_W_out"),
        b3_bin=g("L1_ffn_b_in"), b3_bout=g("L1_ffn_b_out")) for c in cores]
    res = run_bass_kernel_spmd(nc, in_maps, core_ids=cores)
    return np.stack([res.results[c]["xt_out"].T for c in cores]).astype(f32)


def kernel_unfused(**inputs):
    from concourse.bass_utils import run_bass_kernel_spmd
    f32 = np.float32
    g = lambda k: np.ascontiguousarray(np.asarray(inputs[k], dtype=f32))
    cores = list(range(NCORE))

    x = g("L0_attn_x")
    ctxf = np.ascontiguousarray(g("L0_attn_context").reshape(4096, CD))

    nc0 = build_b0(x_natural=True)
    in0 = [dict(xin=np.ascontiguousarray(x[c]),
                b0_ctx=np.ascontiguousarray(ctxf[c * 512:(c + 1) * 512]),
                b0_wq=g("L0_attn_Wq"), b0_wk=g("L0_attn_Wk"),
                b0_wv=g("L0_attn_Wv"), b0_wo=g("L0_attn_Wout"),
                b0_bo=g("L0_attn_bout")) for c in cores]
    r0 = run_bass_kernel_spmd(nc0, in0, core_ids=cores)
    xt = [np.ascontiguousarray(r0.results[c]["xt_out"]) for c in cores]

    nc1 = build_b1()
    w1t = np.ascontiguousarray(g("L0_ffn_w1_w").T)
    w2t = np.ascontiguousarray(g("L0_ffn_w2_w").T)
    h1t = np.ascontiguousarray(g("L0_ffn_h1_w").T)
    h2t = np.ascontiguousarray(g("L0_ffn_h2_w").T)
    w3t = np.ascontiguousarray(g("L0_ffn_w3_w").T)
    in1 = [dict(xin=xt[c], b1_w1t=w1t, b1_w2t=w2t, b1_h1t=h1t, b1_h2t=h2t,
                b1_w3t=w3t, b1_s1=g("L0_ffn_w1_s"), b1_s2=g("L0_ffn_w2_s"),
                b1_sh1=g("L0_ffn_h1_s"), b1_sh2=g("L0_ffn_h2_s"),
                b1_s3=g("L0_ffn_w3_s")) for c in cores]
    r1 = run_bass_kernel_spmd(nc1, in1, core_ids=cores)
    xt = [np.ascontiguousarray(r1.results[c]["xt_out"]) for c in cores]

    nc2 = build_b2()
    in2 = [dict(xin=xt[c], b2_wq=g("L1_attn_Wq"), b2_wk=g("L1_attn_Wk"),
                b2_wv=g("L1_attn_Wv"), b2_wr=g("L1_attn_Wr"),
                b2_wo=g("L1_attn_Wo"), b2_wa=g("L1_attn_Wa"),
                b2_alog=g("L1_attn_A_log"), b2_dtb=g("L1_attn_dt_bias"),
                b2_qnw=g("L1_attn_q_norm_w"), b2_knw=g("L1_attn_k_norm_w"),
                b2_gnw=g("L1_attn_g_norm_w")) for c in cores]
    r2 = run_bass_kernel_spmd(nc2, in2, core_ids=cores)
    xt = [np.ascontiguousarray(r2.results[c]["xt_out"]) for c in cores]

    nc3 = build_b3()
    wint = np.ascontiguousarray(g("L1_ffn_W_in").T)
    in3 = [dict(xin=xt[c], b3_wint=wint, b3_wout=g("L1_ffn_W_out"),
                b3_bin=g("L1_ffn_b_in"), b3_bout=g("L1_ffn_b_out"))
           for c in cores]
    r3 = run_bass_kernel_spmd(nc3, in3, core_ids=cores)

    return np.stack([r3.results[c]["xt_out"].T for c in cores]).astype(f32)


# version 05
# speedup vs baseline: 1.0611x; 1.0402x over previous
import numpy as np
from contextlib import ExitStack
import concourse.bass as bass
import concourse.mybir as mybir
from concourse.bass_utils import run_bass_kernel_spmd

F32 = mybir.dt.float32
BF16 = mybir.dt.bfloat16
I32 = mybir.dt.int32
AF = mybir.ActivationFunctionType
ALU = mybir.AluOpType
AX = mybir.AxisListType

D = 1024
T = 2048
DCH = 8
NTT = 4
H = 8
KV = 4096
CD = 768
NCORE = 8


class Sem:
    def __init__(self, ctx, nc, name):
        self.h = ctx.enter_context(nc.semaphore(name))
        self.v = 0

    def inc(self, instr, n=1):
        instr.then_inc(self.h, n)
        self.v += n
        return self.v


class Prog:
    def __init__(self, nc, ctx, xln_cols=DCH * T, with_trimask=True):
        self.nc = nc
        self.ctx = ctx
        self.block = ctx.enter_context(nc.Block())
        for name in ["dma", "dmg", "dmw", "dmk", "cc", "tm", "sc", "ve", "gp", "od",
                     "dmc", "dmc2"]:
            setattr(self, name, Sem(ctx, nc, "s_" + name))
        self.PA = ctx.enter_context(nc.psum_tensor("PA", [128, 1024], F32))
        self.PB = ctx.enter_context(nc.psum_tensor("PB", [128, 512], F32))
        self.PC = ctx.enter_context(nc.psum_tensor("PC", [128, 512], F32))
        self.PD = ctx.enter_context(nc.psum_tensor("PD", [128, 512], F32))
        self.PT1 = ctx.enter_context(nc.psum_tensor("PT1", [128, 1024], BF16))
        self.PT2 = ctx.enter_context(nc.psum_tensor("PT2", [128, 1024], BF16))
        self.PF = ctx.enter_context(nc.psum_tensor("PF", [128, 512], F32))
        self.ident_b = ctx.enter_context(nc.sbuf_tensor("ident_b", [128, 128], BF16))
        self.ident_f = ctx.enter_context(nc.sbuf_tensor("ident_f", [128, 128], F32))
        self.ones_bcol = ctx.enter_context(nc.sbuf_tensor("ones_bcol", [128, 1], BF16))
        self.ones_brow = ctx.enter_context(nc.sbuf_tensor("ones_brow", [1, 128], BF16))
        self.ones_frow = ctx.enter_context(nc.sbuf_tensor("ones_frow", [1, 128], F32))
        self.ones_f11 = ctx.enter_context(nc.sbuf_tensor("ones_f11", [1, 1], F32))
        self.eps_t = ctx.enter_context(nc.sbuf_tensor("eps_t", [1, 1], F32))
        self.neg_half = ctx.enter_context(nc.sbuf_tensor("neg_half", [128, 1], F32))
        self.srow512 = ctx.enter_context(nc.sbuf_tensor("srow512", [1, 512], F32))
        self.srow32 = ctx.enter_context(nc.sbuf_tensor("srow32", [1, 128], F32))
        self.s11 = ctx.enter_context(nc.sbuf_tensor("s11", [1, 1], F32))
        self.s12 = ctx.enter_context(nc.sbuf_tensor("s12", [1, 1], F32))
        self.s13 = ctx.enter_context(nc.sbuf_tensor("s13", [1, 1], F32))
        self.s14 = ctx.enter_context(nc.sbuf_tensor("s14", [1, 2], F32))
        self.zrow8 = ctx.enter_context(nc.sbuf_tensor("zrow8", [1, 8], F32))
        self.trimask = (ctx.enter_context(nc.sbuf_tensor("trimask", [128, 128], BF16))
                        if with_trimask else None)
        self.onecol = ctx.enter_context(nc.sbuf_tensor("onecol", [128, 1], F32))
        self.eps9 = ctx.enter_context(nc.sbuf_tensor("eps9", [1, 1], F32))
        self.xT = ctx.enter_context(nc.sbuf_tensor("xT", [128, DCH * T], F32))
        self.xln = ctx.enter_context(nc.sbuf_tensor("xln", [128, xln_cols], BF16))
        self.smu = ctx.enter_context(nc.sbuf_tensor("smu", [1, 512], F32))
        self.svar = ctx.enter_context(nc.sbuf_tensor("svar", [1, 512], F32))
        self.sr = ctx.enter_context(nc.sbuf_tensor("sr", [1, 512], F32))
        self.smur = ctx.enter_context(nc.sbuf_tensor("smur", [1, 512], F32))
        self.rb = ctx.enter_context(nc.sbuf_tensor("rb", [128, 512], F32))
        self.mrb = ctx.enter_context(nc.sbuf_tensor("mrb", [128, 512], F32))
        self.tmpf = ctx.enter_context(nc.sbuf_tensor("tmpf", [128, 512], F32))

    def emit_consts(self):
        pr = self
        nc = self.nc
        cst = ExitStack()
        it1 = cst.enter_context(nc.sbuf_tensor("c_it1", [128, 128], F32))
        it2 = cst.enter_context(nc.sbuf_tensor("c_it2", [128, 128], F32))

        @pr.block.gpsimd
        def _(g):
            g.iota(it1[:], pattern=[[1, 128]], base=0, channel_multiplier=0,
                   allow_small_or_imprecise_dtypes=True)
            g.iota(it2[:], pattern=[[0, 128]], base=0, channel_multiplier=1,
                   allow_small_or_imprecise_dtypes=True)
            g.memset(pr.ones_bcol[:], 1.0)
            g.memset(pr.ones_brow[:], 1.0)
            g.memset(pr.ones_frow[:], 1.0)
            g.memset(pr.eps_t[:], 1e-5)
            g.memset(pr.neg_half[:], -0.5)
            g.memset(pr.onecol[:], 1.0)
            g.memset(pr.eps9[:], 1e-9)
            pr.gp.inc(g.memset(pr.ones_f11[:], 1.0))
        gp0 = pr.gp.v

        @pr.block.vector
        def _(v):
            v.wait_ge(pr.gp.h, gp0)
            v.tensor_tensor(pr.ident_f[:], it1[:], it2[:], ALU.is_equal)
            v.drain()
            if pr.trimask is not None:
                v.tensor_copy(pr.ident_b[:], pr.ident_f[:])
                v.drain()
                pr.ve.inc(v.tensor_tensor(pr.trimask[:], it2[:], it1[:], ALU.is_le))
            else:
                pr.ve.inc(v.tensor_copy(pr.ident_b[:], pr.ident_f[:]))

        ve0 = pr.ve.v
        for sec in (pr.block.tensor, pr.block.vector, pr.block.scalar,
                    pr.block.gpsimd, pr.block.sync):
            @sec
            def _(e):
                e.wait_ge(pr.ve.h, ve0)
        cst.close()

    def ln(self, src, dst, sq_scratch, waits, tts=None, dst_stride=None, dst_f32=None):
        pr = self
        eps = 1e-5
        xbf = dst
        sq = sq_scratch
        if tts is None:
            tts = list(range(NTT))
        if dst_stride is None:
            dst_stride = 512 * len(tts)
        NT_ = len(tts)
        tt0 = tts[0]
        def dsl(c, tt):
            return slice(c * dst_stride + (tt - tt0) * 512,
                         c * dst_stride + (tt - tt0) * 512 + 512)
        def ssl(c, tt):
            return slice(c * T + tt * 512, c * T + tt * 512 + 512)

        @pr.block.scalar
        def _(s):
            for sem, val in waits:
                s.wait_ge(sem.h, val)
            for c in range(DCH):
                for tt in tts:
                    s.copy(xbf[:, dsl(c, tt)], src[:, ssl(c, tt)])
                    i = s.activation(sq[:, dsl(c, tt)], src[:, ssl(c, tt)], AF.Square)
            pr.sc.inc(i)
        cast_done = pr.sc.v

        tm0, sc0, ve0 = pr.tm.v, pr.sc.v, pr.ve.v

        @pr.block.tensor
        def _(t):
            t.wait_ge(pr.sc.h, cast_done)
            t.wait_ge(pr.ve.h, ve0)
            for j, tt in enumerate(tts):
                if j >= 1:
                    t.wait_ge(pr.ve.h, ve0 + 4 * j - 3)
                for c in range(DCH):
                    t.matmul(pr.PC[0:1, 0:512], pr.ones_bcol[:],
                             xbf[:, dsl(c, tt)], start=(c == 0), stop=(c == DCH - 1))
                for c in range(DCH):
                    i2 = t.matmul(pr.PD[0:1, 0:512], pr.ones_bcol[:],
                                  sq[:, dsl(c, tt)], start=(c == 0), stop=(c == DCH - 1))
                pr.tm.inc(i2)
                t.wait_ge(pr.ve.h, ve0 + 4 * j + 2)
                pr.tm.inc(t.matmul(pr.PF[:], pr.ones_frow[:], pr.sr[:], start=True, stop=True))
                t.wait_ge(pr.ve.h, ve0 + 4 * j + 3)
                pr.tm.inc(t.matmul(pr.PF[:], pr.ones_frow[:], pr.smur[:], start=True, stop=True))

        @pr.block.vector
        def _(v):
            for j, tt in enumerate(tts):
                v.wait_ge(pr.tm.h, tm0 + 3 * j + 1)
                v.tensor_scalar(pr.smu[:], pr.PC[0:1, 0:512], 1.0 / D, None, ALU.mult)
                v.tensor_scalar(pr.svar[:], pr.PD[0:1, 0:512], 1.0 / D, None, ALU.mult)
                v.drain()
                v.tensor_tensor(pr.tmpf[0:1, :], pr.smu[:], pr.smu[:], ALU.mult)
                v.drain()
                pr.ve.inc(v.tensor_tensor(pr.svar[:], pr.svar[:], pr.tmpf[0:1, :], ALU.subtract))
                v.wait_ge(pr.sc.h, sc0 + j + 1)
                v.drain()
                pr.ve.inc(v.tensor_tensor(pr.smur[:], pr.smu[:], pr.sr[:], ALU.mult))
                v.wait_ge(pr.tm.h, tm0 + 3 * j + 2)
                pr.ve.inc(v.tensor_copy(pr.rb[:], pr.PF[:]))
                v.wait_ge(pr.tm.h, tm0 + 3 * j + 3)
                v.tensor_copy(pr.mrb[:], pr.PF[:])
                v.drain()
                for c in range(DCH):
                    od = dst_f32 if dst_f32 is not None else dst
                    v.tensor_tensor(pr.tmpf[:], src[:, ssl(c, tt)], pr.rb[:], ALU.mult)
                    v.drain()
                    i = v.tensor_tensor(od[:, dsl(c, tt)], pr.tmpf[:], pr.mrb[:], ALU.subtract)
                    v.drain()
                pr.ve.inc(i)

        @pr.block.scalar
        def _(s):
            for j, tt in enumerate(tts):
                s.wait_ge(pr.ve.h, ve0 + 4 * j + 1)
                s.activation(pr.sr[:], pr.svar[:], AF.Ln, bias=pr.eps_t[:])
                s.drain()
                pr.sc.inc(s.activation(pr.sr[:], pr.sr[:], AF.Exp, scale=-0.5))

        return pr.ve.v



def emit_block0(pr, io, have_x_natural=True, upto=99):
    nc = pr.nc
    SCALE = (128 ** -0.5) * 1.5

    kvloc = nc.dram_tensor("kvloc", [2 * 512 * 1024], BF16, kind="Internal")
    kvall = nc.dram_tensor("kvall", [NCORE * 2 * 512 * 1024], BF16, kind="Internal")

    phA = ExitStack()
    cnat = phA.enter_context(nc.sbuf_tensor("b0_cnat", [128, 4 * CD], BF16))
    ctxT = phA.enter_context(nc.sbuf_tensor("b0_ctxT", [128, 6 * 512], BF16))
    wkb = phA.enter_context(nc.sbuf_tensor("b0_wkb", [128, 6 * 1024], BF16))
    wvb = phA.enter_context(nc.sbuf_tensor("b0_wvb", [128, 6 * 1024], BF16))
    khsl = phA.enter_context(nc.sbuf_tensor("b0_khsl", [128, 8 * 512], BF16))
    vhsl = phA.enter_context(nc.sbuf_tensor("b0_vhsl", [128, 4 * 1024], BF16))

    @pr.block.gpsimd
    def _(g):
        c_r = io["b0_ctx"].ap().rearrange("(c p) d -> p c d", p=128)
        pr.dmg.inc(g.dma_start(out=cnat[:].rearrange("p (c d) -> p c d", c=4), in_=c_r), 16)
        pr.dmw.inc(g.dma_start(out=wkb[:].rearrange("p (c m) -> p c m", c=6),
                               in_=io["b0_wk"].ap().rearrange("(c p) m -> p c m", p=128)), 16)
        pr.dmw.inc(g.dma_start(out=wvb[:].rearrange("p (c m) -> p c m", c=6),
                               in_=io["b0_wv"].ap().rearrange("(c p) m -> p c m", p=128)), 16)
    dma_cnat = pr.dmg.v
    dma_wkv = pr.dmw.v
    ve_id = pr.ve.v
    tm0, sc0 = pr.tm.v, pr.sc.v

    @pr.block.tensor
    def _(t):
        t.wait_ge(pr.dmg.h, dma_cnat)
        t.wait_ge(pr.ve.h, ve_id)
        for idx in range(24):
            rc, cc = divmod(idx, 6)
            if idx >= 2:
                t.wait_ge(pr.sc.h, sc0 + idx - 1)
            pt = pr.PT1 if idx % 2 == 0 else pr.PT2
            pr.tm.inc(t.transpose(pt[:, 0:128],
                                  cnat[:, rc * CD + cc * 128: rc * CD + cc * 128 + 128],
                                  pr.ident_b[:]))

    @pr.block.scalar
    def _(s):
        for idx in range(24):
            rc, cc = divmod(idx, 6)
            s.wait_ge(pr.tm.h, tm0 + idx + 1)
            pt = pr.PT1 if idx % 2 == 0 else pr.PT2
            pr.sc.inc(s.copy(ctxT[:, cc * 512 + rc * 128: cc * 512 + rc * 128 + 128],
                             pt[:, 0:128]))
    ctxT_done = pr.sc.v

    tm1, sc1 = pr.tm.v, pr.sc.v

    @pr.block.tensor
    def _(t):
        t.wait_ge(pr.sc.h, ctxT_done)
        t.wait_ge(pr.dmw.h, dma_wkv)
        for dc in range(8):
            if dc >= 2:
                t.wait_ge(pr.sc.h, sc1 + dc - 1)
            ps = pr.PB if dc % 2 == 0 else pr.PC
            for c in range(6):
                i = t.matmul(ps[:], wkb[:, c * 1024 + dc * 128: c * 1024 + dc * 128 + 128],
                             ctxT[:, c * 512:(c + 1) * 512], start=(c == 0), stop=(c == 5))
            pr.tm.inc(i)
        for idx in range(8):
            kc, dt_ = divmod(idx, 2)
            t.wait_ge(pr.sc.h, sc1 + idx + 7)
            ps = pr.PB if idx % 2 == 0 else pr.PC
            for c in range(6):
                i = t.matmul(ps[:], ctxT[:, c * 512 + kc * 128: c * 512 + kc * 128 + 128],
                             wvb[:, c * 1024 + dt_ * 512: c * 1024 + dt_ * 512 + 512],
                             start=(c == 0), stop=(c == 5))
            pr.tm.inc(i)

    @pr.block.scalar
    def _(s):
        for dc in range(8):
            s.wait_ge(pr.tm.h, tm1 + dc + 1)
            ps = pr.PB if dc % 2 == 0 else pr.PC
            pr.sc.inc(s.copy(khsl[:, dc * 512:(dc + 1) * 512], ps[:]))
        for idx in range(8):
            kc, dt_ = divmod(idx, 2)
            s.wait_ge(pr.tm.h, tm1 + 8 + idx + 1)
            ps = pr.PB if idx % 2 == 0 else pr.PC
            pr.sc.inc(s.copy(vhsl[:, kc * 1024 + dt_ * 512: kc * 1024 + dt_ * 512 + 512], ps[:]))
    kv_evac = pr.sc.v

    @pr.block.sync
    def _(sy):
        sy.wait_ge(pr.sc.h, kv_evac)
        kv2 = kvloc.ap().rearrange("(a p j) -> p a j", p=128, a=2)
        pr.dma.inc(sy.dma_start(out=kv2[:, 0, :], in_=khsl[:]), 16)
        pr.dma.inc(sy.dma_start(out=kv2[:, 1, :], in_=vhsl[:]), 16)
    kv_stored = pr.dma.v

    @pr.block.gpsimd
    def _(g):
        g.wait_ge(pr.dma.h, kv_stored)
        pr.cc.inc(g.collective_compute(
            "AllGather", ALU.bypass,
            replica_groups=[list(range(NCORE))],
            ins=[kvloc.ap().rearrange("(r c) -> r c", r=128).opt()],
            outs=[kvall.ap().rearrange("(r c) -> r c", r=8 * 128).opt()],
        ))
    cc_done = pr.cc.v
    phA.close()
    if upto <= 0:
        return pr.ve.v

    if have_x_natural:
        phX = ExitStack()
        xin = phX.enter_context(nc.sbuf_tensor("b0_xin", [128, 16 * 1024], F32))

        @pr.block.sync
        def _(sy):
            sy.wait_ge(pr.dma.h, kv_stored)
            x_r = io["xin"].ap().rearrange("(c p) d -> p c d", p=128)
            pr.dma.inc(sy.dma_start(out=xin[:].rearrange("p (c d) -> p c d", c=16), in_=x_r), 16)
        x_loaded = pr.dma.v

        tm2, ve2 = pr.tm.v, pr.ve.v

        @pr.block.tensor
        def _(t):
            t.wait_ge(pr.dma.h, x_loaded)
            t.wait_ge(pr.ve.h, ve_id)
            for idx in range(128):
                tc, dc = divmod(idx, DCH)
                if idx >= 2:
                    t.wait_ge(pr.ve.h, ve2 + idx - 1)
                pr.tm.inc(t.transpose(pr.PA[:, 512 * (idx % 2):512 * (idx % 2) + 128],
                                      xin[:, tc * 1024 + dc * 128: tc * 1024 + dc * 128 + 128],
                                      pr.ident_f[:]))

        @pr.block.vector
        def _(v):
            for idx in range(128):
                tc, dc = divmod(idx, DCH)
                v.wait_ge(pr.tm.h, tm2 + idx + 1)
                pr.ve.inc(v.tensor_copy(pr.xT[:, dc * T + tc * 128: dc * T + tc * 128 + 128],
                                        pr.PA[:, 512 * (idx % 2):512 * (idx % 2) + 128]))
        xT_done = pr.ve.v
        phX.close()
    else:
        @pr.block.sync
        def _(sy):
            xi = io["xin"].ap().rearrange("(c p) t -> p c t", p=128)
            pr.dma.inc(sy.dma_start(out=pr.xT[:].rearrange("p (c t) -> p c t", c=DCH), in_=xi), 16)
        x_loaded = pr.dma.v

        @pr.block.vector
        def _(v):
            v.wait_ge(pr.dma.h, x_loaded)
            pr.ve.inc(v.engine_nop())
        xT_done = pr.ve.v

    if upto <= 1:
        return pr.ve.v
    phQ = ExitStack()
    qT = phQ.enter_context(nc.sbuf_tensor("b0_qT", [128, DCH * T], BF16))
    wA = phQ.enter_context(nc.sbuf_tensor("b0_wA", [128, DCH * 1024], BF16))

    ln1_done = pr.ln(pr.xT, pr.xln, qT[:], [(pr.ve, xT_done)])

    if upto <= 2:
        return pr.ve.v
    @pr.block.gpsimd
    def _(g):
        g.wait_ge(pr.ve.h, xT_done)
        g.wait_ge(pr.dmw.h, pr.dmw.v)
        pr.dmw.inc(g.dma_start(out=wA[:].rearrange("p (c m) -> p c m", c=DCH),
                               in_=io["b0_wq"].ap().rearrange("(c p) m -> p c m", p=128)), 16)
    wq_loaded = pr.dmw.v

    tm3, sc3 = pr.tm.v, pr.sc.v

    @pr.block.tensor
    def _(t):
        t.wait_ge(pr.dmw.h, wq_loaded)
        t.wait_ge(pr.ve.h, ln1_done)
        for idx in range(DCH * NTT):
            dc, tt = divmod(idx, NTT)
            if idx >= 2:
                t.wait_ge(pr.sc.h, sc3 + idx - 1)
            for k in range(DCH):
                i = t.matmul(pr.PA[:, 512 * (idx % 2):512 * (idx % 2) + 512],
                             wA[:, k * 1024 + dc * 128: k * 1024 + dc * 128 + 128],
                             pr.xln[:, k * T + tt * 512: k * T + tt * 512 + 512],
                             start=(k == 0), stop=(k == DCH - 1))
            pr.tm.inc(i)

    @pr.block.scalar
    def _(s):
        for idx in range(DCH * NTT):
            dc, tt = divmod(idx, NTT)
            s.wait_ge(pr.tm.h, tm3 + idx + 1)
            pr.sc.inc(s.copy(qT[:, dc * T + tt * 512: dc * T + tt * 512 + 512],
                             pr.PA[:, 512 * (idx % 2):512 * (idx % 2) + 512]))
    q_done = pr.sc.v

    if upto <= 3:
        return pr.ve.v
    khb = [phQ.enter_context(nc.sbuf_tensor(f"b0_khb{i}", [128, KV], BF16)) for i in range(2)]
    vhb = [phQ.enter_context(nc.sbuf_tensor(f"b0_vhb{i}", [128, KV], BF16)) for i in range(2)]
    Ebuf = phQ.enter_context(nc.sbuf_tensor("b0_E", [128, 4 * 512], BF16))
    rsum = phQ.enter_context(nc.sbuf_tensor("b0_rsum", [1, 512], F32))
    oT = pr.xln
    rb32 = pr.rb

    kva = kvall.ap().rearrange("(r a p dc j) -> a r dc p j", r=NCORE, a=2, p=128, dc=8)
    kva_v = kvall.ap().rearrange("(r a p kc dv) -> a r p kc dv", r=NCORE, a=2, p=128, kc=4)

    NIT = H * NTT
    tm_b, sc_b, ve_b = pr.tm.v, pr.sc.v, pr.ve.v
    TPI, SPI, VPI = 33, 17, 2

    @pr.block.gpsimd
    def _(g):
        g.wait_ge(pr.cc.h, cc_done)
        for h in range(H):
            if h >= 2:
                g.wait_ge(pr.tm.h, tm_b + TPI * NTT * (h - 1))
            if h >= 1:
                g.wait_ge(pr.dmk.h, 144 * h)
            pr.dmk.inc(g.dma_start(out=khb[h % 2][:].rearrange("p (r j) -> p r j", r=8),
                                   in_=kva[0, :, h, :, :].rearrange("r p j -> p r j")), 16)
            for r in range(NCORE):
                pr.dmk.inc(g.dma_start(
                    out=vhb[h % 2][:, r * 512:(r + 1) * 512].rearrange("p (kc dv) -> p kc dv", kc=4),
                    in_=kva_v[1, r, :, :, h * 128:(h + 1) * 128]), 16)

    for j in range(NIT):
        h, qt = divmod(j, NTT)
        tmj = tm_b + TPI * j
        scj = sc_b + SPI * j
        vej = ve_b + VPI * j
        kh, vh = khb[h % 2], vhb[h % 2]

        @pr.block.tensor
        def _(t, h=h, qt=qt, tmj=tmj, scj=scj, vej=vej, kh=kh, vh=vh, j=j):
            if qt == 0:
                t.wait_ge(pr.dmk.h, 144 * (h + 1))
                if h == 0:
                    t.wait_ge(pr.sc.h, q_done)
            qsl = qT[:, h * T + qt * 512: h * T + qt * 512 + 512]
            for p in range(16):
                if p >= 1:
                    t.wait_ge(pr.sc.h, scj + p)
                elif j >= 1:
                    t.wait_ge(pr.sc.h, scj - 1)
                t.matmul(pr.PA[:, 0:512], kh[:, (2 * p) * 128:(2 * p) * 128 + 128],
                         qsl, start=True, stop=True)
                pr.tm.inc(t.matmul(pr.PA[:, 512:1024],
                                   kh[:, (2 * p + 1) * 128:(2 * p + 1) * 128 + 128],
                                   qsl, start=True, stop=True))
                t.wait_ge(pr.sc.h, scj + p + 1)
                if p == 0 and j >= 1:
                    t.wait_ge(pr.ve.h, vej)
                for cc_ in (2 * p, 2 * p + 1):
                    t.matmul(pr.PB[:], vh[:, cc_ * 128:cc_ * 128 + 128],
                             Ebuf[:, (cc_ % 4) * 512:(cc_ % 4) * 512 + 512],
                             start=(cc_ == 0), stop=(cc_ == 31))
                    i = t.matmul(pr.PC[0:1, 0:512], pr.ones_bcol[:],
                                 Ebuf[:, (cc_ % 4) * 512:(cc_ % 4) * 512 + 512],
                                 start=(cc_ == 0), stop=(cc_ == 31))
                pr.tm.inc(i)
            t.wait_ge(pr.ve.h, vej + 1)
            pr.tm.inc(t.matmul(pr.PD[:], pr.ones_frow[:], rsum[:], start=True, stop=True))

        @pr.block.scalar
        def _(s, tmj=tmj, scj=scj):
            for p in range(16):
                s.wait_ge(pr.tm.h, tmj + 2 * p + 1)
                pr.sc.inc(s.activation(Ebuf[:, (2 * p % 4) * 512:(2 * p % 4) * 512 + 1024],
                                       pr.PA[:], AF.Exp, scale=SCALE))
            s.wait_ge(pr.tm.h, tmj + TPI)
            pr.sc.inc(s.copy(rb32[:], pr.PD[:]))

        @pr.block.vector
        def _(v, h=h, qt=qt, tmj=tmj, scj=scj):
            v.wait_ge(pr.tm.h, tmj + 32)
            pr.ve.inc(v.reciprocal(rsum[:], pr.PC[0:1, 0:512]))
            v.wait_ge(pr.sc.h, scj + SPI)
            pr.ve.inc(v.tensor_tensor(oT[:, h * T + qt * 512: h * T + qt * 512 + 512],
                                      pr.PB[:], rb32[:], ALU.mult))

    attn_done = pr.ve.v
    assert attn_done == ve_b + VPI * NIT and pr.tm.v == tm_b + TPI * NIT and pr.sc.v == sc_b + SPI * NIT

    if upto <= 4:
        return pr.ve.v
    bob = phQ.enter_context(nc.sbuf_tensor("b0_bob", [128, DCH], F32))

    @pr.block.sync
    def _(sy):
        sy.wait_ge(pr.dma.h, pr.dma.v)
        pr.dma.inc(sy.dma_start(out=bob[:].unsqueeze(-1),
                                in_=io["b0_bo"].ap().rearrange("(c p) -> p c", p=128).unsqueeze(-1),
                                allow_slow_non_contiguous=True), 16)
    bob_loaded = pr.dma.v

    @pr.block.gpsimd
    def _(g):
        g.wait_ge(pr.tm.h, tm_b)
        g.wait_ge(pr.dmw.h, pr.dmw.v)
        pr.dmw.inc(g.dma_start(out=wA[:].rearrange("p (c m) -> p c m", c=DCH),
                               in_=io["b0_wo"].ap().rearrange("(c p) m -> p c m", p=128)), 16)
    wo_loaded = pr.dmw.v

    tm4, ve4 = pr.tm.v, pr.ve.v

    @pr.block.tensor
    def _(t):
        t.wait_ge(pr.dmw.h, wo_loaded)
        t.wait_ge(pr.ve.h, attn_done)
        for idx in range(DCH * NTT):
            dc, tt = divmod(idx, NTT)
            if idx >= 2:
                t.wait_ge(pr.ve.h, ve4 + idx)
            for k in range(DCH):
                i = t.matmul(pr.PA[:, 512 * (idx % 2):512 * (idx % 2) + 512],
                             wA[:, k * 1024 + dc * 128: k * 1024 + dc * 128 + 128],
                             oT[:, k * T + tt * 512: k * T + tt * 512 + 512],
                             start=(k == 0), stop=(k == DCH - 1))
            pr.tm.inc(i)

    @pr.block.vector
    def _(v):
        v.wait_ge(pr.dma.h, bob_loaded)
        for idx in range(DCH * NTT):
            dc, tt = divmod(idx, NTT)
            v.wait_ge(pr.tm.h, tm4 + idx + 1)
            sl = slice(dc * T + tt * 512, dc * T + tt * 512 + 512)
            pr.ve.inc(v.scalar_tensor_tensor(
                pr.xT[:, sl], pr.PA[:, 512 * (idx % 2):512 * (idx % 2) + 512],
                bob[:, dc:dc + 1], pr.xT[:, sl], ALU.add, ALU.add))
    phQ.close()
    return pr.ve.v




def emit_load_xt(pr, xt_in):
    @pr.block.sync
    def _(sy):
        sy.wait_ge(pr.dma.h, pr.dma.v)
        xi = xt_in.ap().rearrange("(c p) t -> p c t", p=128)
        pr.dma.inc(sy.dma_start(out=pr.xT[:].rearrange("p (c t) -> p c t", c=DCH), in_=xi), 16)
    x_loaded = pr.dma.v

    @pr.block.vector
    def _(v):
        v.wait_ge(pr.dma.h, x_loaded)
        pr.ve.inc(v.engine_nop())
    return pr.ve.v


def columnize_vec(pr, rows_and_slices, dst, dst_col0=0):
    cols = []
    col = dst_col0
    for row in rows_and_slices:
        m = row.shape[-1]
        nch = m // 128
        cols.append((row, col, nch))
        col += nch
    ncol_total = col - dst_col0

    @pr.block.tensor
    def _(t):
        t.wait_ge(pr.dma.h, pr.dma.v)
        t.wait_ge(pr.dmg.h, pr.dmg.v)
        t.wait_ge(pr.sc.h, pr.sc.v)
        t.wait_ge(pr.ve.h, pr.ve.v)
        for row, c0, nch in cols:
            for c in range(nch):
                i = t.matmul(pr.PD[0:128, c0 - dst_col0 + c:c0 - dst_col0 + c + 1],
                             row[:, c * 128:(c + 1) * 128], pr.ones_f11[:],
                             start=True, stop=True)
        pr.tm.inc(i)
    mm = pr.tm.v

    @pr.block.vector
    def _(v):
        v.wait_ge(pr.tm.h, mm)
        pr.ve.inc(v.tensor_copy(dst[:, dst_col0:dst_col0 + ncol_total],
                                pr.PD[:, 0:ncol_total]))
    return pr.ve.v



B1_PREP_SPECS = {"b1_w1t": (8, 32), "b1_w2t": (8, 32),
                 "b1_h1t": (32, 32), "b1_h2t": (32, 32), "b1_w3t": (32, 8)}


def make_b1_scratch(nc):
    return {name: nc.dram_tensor(name + "_tl", [oc, 128, kc * 128], F32, kind="Internal")
            for name, (kc, oc) in B1_PREP_SPECS.items()}


def emit_b1_prep(pr, io, scr, cnvf, names):

    blk = 0

    @pr.block.sync
    def _(sy, names=tuple(names)):
        nonlocal blk
        for name in names:
            KC, OC = B1_PREP_SPECS[name]
            s_ = scr[name]
            for oc in range(OC):
                for k0 in range(0, KC, 7):
                    kw = min(7, KC - k0)
                    buf = cnvf[blk % 2]
                    blk += 1
                    sy.wait_ge(pr.dmc.h, pr.dmc.v)
                    if pr.dmc2.v >= 32:
                        sy.wait_ge(pr.dmc2.h, pr.dmc2.v - 16)
                    pr.dmc.inc(sy.dma_start(
                        out=buf[:, 0:kw * 128].rearrange("p (kc j) -> p kc j", kc=kw),
                        in_=io[name].ap()[k0 * 128:(k0 + kw) * 128,
                                          oc * 128:(oc + 1) * 128]
                            .rearrange("(kc p) j -> p kc j", p=128)), 16)
                    sy.wait_ge(pr.dmc.h, pr.dmc.v)
                    sy.wait_ge(pr.dmc2.h, pr.dmc2.v)
                    pr.dmc2.inc(sy.dma_start(
                        out=s_.ap()[oc, :, k0 * 128:(k0 + kw) * 128],
                        in_=buf[:, 0:kw * 128]), 16)


def emit_block1(pr, io, xt_ready, upto=99, scr=None):
    nc = pr.nc
    ph = ExitStack()
    A = ph.enter_context(nc.sbuf_tensor("b1_A", [128, 32 * 512], BF16))
    B = ph.enter_context(nc.sbuf_tensor("b1_B", [128, 32 * 512], BF16))
    w12 = [ph.enter_context(nc.sbuf_tensor(f"b1_w12_{i}", [128, 2 * 8 * 128], BF16)) for i in range(2)]
    wh = [ph.enter_context(nc.sbuf_tensor(f"b1_wh_{i}", [128, 32 * 128], BF16)) for i in range(2)]
    scl = ph.enter_context(nc.sbuf_tensor("b1_scl", [128, 4 * 32 + 8], F32))
    t1f = ph.enter_context(nc.sbuf_tensor("b1_t1", [128, 1024], F32))
    xlq = pr.xln[:, 0:DCH * 512]
    sqq = pr.xln[:, DCH * 512:DCH * 1024]

    scrow = pr.srow512
    svecs = [("b1_s1", 4096), ("b1_s2", 4096), ("b1_sh1", 4096), ("b1_sh2", 4096), ("b1_s3", 1024)]
    col0 = 0
    for name, m in svecs:
        for part in range(0, m, 512):
            w_ = min(512, m - part)
            @pr.block.sync
            def _(sy, name=name, part=part, w_=w_):
                sy.wait_ge(pr.dma.h, pr.dma.v)
                sy.wait_ge(pr.ve.h, pr.ve.v)
                pr.dma.inc(sy.dma_start(out=scrow[:, 0:w_],
                                        in_=io[name].ap().rearrange("(a d) -> a d", a=1)[:, part:part + w_]), 16)
            columnize_vec(pr, [scrow[:, 0:w_]], scl, col0)
            col0 += w_ // 128
    scl_done = pr.ve.v
    S1C, S2C, SH1C, SH2C, S3C = 0, 32, 64, 96, 128

    NQ = 4
    prev_q_done = xt_ready
    for q in range(NQ):
        ln_done = pr.ln(pr.xT, xlq, sqq, [(pr.ve, prev_q_done)], tts=[q], dst_stride=512)

        dmw0 = pr.dmw.v
        tm0 = pr.tm.v
        sc0 = pr.sc.v
        ve0 = pr.ve.v

        @pr.block.gpsimd
        def _(g, q=q, tm0=tm0):
            for oc in range(32):
                g.wait_ge(pr.dmw.h, pr.dmw.v)
                if oc >= 2:
                    g.wait_ge(pr.tm.h, tm0 + oc - 1)
                buf = w12[oc % 2]
                if scr is None:
                    w1v = io["b1_w1t"].ap()[:, oc * 128:(oc + 1) * 128].rearrange("(kc p) j -> p kc j", p=128)
                    w2v = io["b1_w2t"].ap()[:, oc * 128:(oc + 1) * 128].rearrange("(kc p) j -> p kc j", p=128)
                    pr.dmw.inc(g.dma_start(out=buf[:, 0:1024].rearrange("p (kc j) -> p kc j", kc=8), in_=w1v), 16)
                    g.wait_ge(pr.dmw.h, pr.dmw.v)
                    pr.dmw.inc(g.dma_start(out=buf[:, 1024:2048].rearrange("p (kc j) -> p kc j", kc=8), in_=w2v), 16)
                else:
                    pr.dmw.inc(g.dma_start(out=buf[:, 0:1024], in_=scr["b1_w1t"].ap()[oc]), 16)
                    g.wait_ge(pr.dmw.h, pr.dmw.v)
                    pr.dmw.inc(g.dma_start(out=buf[:, 1024:2048], in_=scr["b1_w2t"].ap()[oc]), 16)

        @pr.block.tensor
        def _(t, q=q, tm0=tm0, sc0=sc0, ve0=ve0, dmw0=dmw0):
            t.wait_ge(pr.ve.h, ln_done)
            t.wait_ge(pr.ve.h, scl_done)
            t.wait_ge(pr.ve.h, ve0)
            for oc in range(32):
                t.wait_ge(pr.dmw.h, dmw0 + 32 * (oc + 1))
                if oc >= 2:
                    t.wait_ge(pr.sc.h, sc0 + oc - 1)
                    t.wait_ge(pr.ve.h, ve0 + oc - 1)
                buf = w12[oc % 2]
                psA = pr.PA[:, 512 * (oc % 2):512 * (oc % 2) + 512]
                psB = pr.PB[:] if oc % 2 == 0 else pr.PC[:]
                for k in range(8):
                    t.matmul(psA, buf[:, k * 128:(k + 1) * 128],
                             xlq[:, k * 512:(k + 1) * 512],
                             start=(k == 0), stop=(k == 7))
                for k in range(8):
                    i = t.matmul(psB, buf[:, 1024 + k * 128:1024 + k * 128 + 128],
                                 xlq[:, k * 512:(k + 1) * 512],
                                 start=(k == 0), stop=(k == 7))
                pr.tm.inc(i)

        @pr.block.scalar
        def _(s, tm0=tm0, ve0=ve0):
            for oc in range(32):
                s.wait_ge(pr.tm.h, tm0 + oc + 1)
                if oc >= 2:
                    s.wait_ge(pr.ve.h, ve0 + oc - 1)
                psA = pr.PA[:, 512 * (oc % 2):512 * (oc % 2) + 512]
                psB = pr.PB[:] if oc % 2 == 0 else pr.PC[:]
                h_ = slice(512 * (oc % 2), 512 * (oc % 2) + 512)
                pr.sc.inc(s.activation(t1f[:, h_], psA, AF.Silu, scale=scl[:, S1C + oc:S1C + oc + 1]))

        @pr.block.vector
        def _(v, sc0=sc0, ve0=ve0, q=q):
            for oc in range(32):
                v.wait_ge(pr.sc.h, sc0 + oc + 1)
                h_ = slice(512 * (oc % 2), 512 * (oc % 2) + 512)
                psB = pr.PB[:] if oc % 2 == 0 else pr.PC[:]
                pr.ve.inc(v.scalar_tensor_tensor(A[:, oc * 512:(oc + 1) * 512], psB,
                                                 scl[:, S2C + oc:S2C + oc + 1], t1f[:, h_],
                                                 ALU.mult, ALU.mult))
        st1_ve = pr.ve.v

        prev_guard = (pr.ve, st1_ve)
        for stage, (wname, scol, src, dstb) in enumerate(
                [("b1_h1t", SH1C, A, B), ("b1_h2t", SH2C, B, A)]):
            dmw1 = pr.dmw.v
            tm1 = pr.tm.v
            sc1 = pr.sc.v

            @pr.block.gpsimd
            def _(g, wname=wname, tm1=tm1):
                for oc in range(32):
                    wv = None if scr is not None else io[wname].ap()[:, oc * 128:(oc + 1) * 128]
                    buf = wh[oc % 2]
                    if oc >= 2:
                        g.wait_ge(pr.tm.h, tm1 + oc - 1)
                    else:
                        g.wait_ge(pr.tm.h, tm1)
                    for half in range(2):
                        g.wait_ge(pr.dmw.h, pr.dmw.v)
                        if scr is None:
                            pr.dmw.inc(g.dma_start(
                                out=buf[:, half * 2048:(half + 1) * 2048].rearrange("p (kc j) -> p kc j", kc=16),
                                in_=wv[half * 2048:(half + 1) * 2048, :].rearrange("(kc p) j -> p kc j", p=128)), 16)
                        else:
                            pr.dmw.inc(g.dma_start(
                                out=buf[:, half * 2048:(half + 1) * 2048],
                                in_=scr[wname].ap()[oc, :, half * 2048:(half + 1) * 2048]), 16)

            @pr.block.tensor
            def _(t, tm1=tm1, sc1=sc1, dmw1=dmw1, src=src, prev_guard=prev_guard, stage=stage):
                t.wait_ge(prev_guard[0].h, prev_guard[1])
                for oc in range(32):
                    t.wait_ge(pr.dmw.h, dmw1 + 32 * (oc + 1))
                    if oc >= 2:
                        t.wait_ge(pr.sc.h, sc1 + oc - 1)
                    buf = wh[oc % 2]
                    ps = pr.PA[:, 512 * (oc % 2):512 * (oc % 2) + 512]
                    for k in range(32):
                        i = t.matmul(ps, buf[:, k * 128:(k + 1) * 128],
                                     src[:, k * 512:(k + 1) * 512],
                                     start=(k == 0), stop=(k == 31))
                    pr.tm.inc(i)

            @pr.block.scalar
            def _(s, tm1=tm1, scol=scol, dstb=dstb):
                for oc in range(32):
                    s.wait_ge(pr.tm.h, tm1 + oc + 1)
                    ps = pr.PA[:, 512 * (oc % 2):512 * (oc % 2) + 512]
                    pr.sc.inc(s.activation(dstb[:, oc * 512:(oc + 1) * 512], ps, AF.Gelu,
                                           scale=scl[:, scol + oc:scol + oc + 1]))
            prev_guard = (pr.sc, pr.sc.v)

        st3_sc = pr.sc.v

        dmw2 = pr.dmw.v
        tm2g = pr.tm.v
        @pr.block.gpsimd
        def _(g, tm2g=tm2g):
            for oc in range(8):
                wv = None if scr is not None else io["b1_w3t"].ap()[:, oc * 128:(oc + 1) * 128]
                buf = wh[oc % 2]
                if oc >= 2:
                    g.wait_ge(pr.tm.h, tm2g + oc - 1)
                else:
                    g.wait_ge(pr.tm.h, tm2g)
                for half in range(2):
                    g.wait_ge(pr.dmw.h, pr.dmw.v)
                    if scr is None:
                        pr.dmw.inc(g.dma_start(
                            out=buf[:, half * 2048:(half + 1) * 2048].rearrange("p (kc j) -> p kc j", kc=16),
                            in_=wv[half * 2048:(half + 1) * 2048, :].rearrange("(kc p) j -> p kc j", p=128)), 16)
                    else:
                        pr.dmw.inc(g.dma_start(
                            out=buf[:, half * 2048:(half + 1) * 2048],
                            in_=scr["b1_w3t"].ap()[oc, :, half * 2048:(half + 1) * 2048]), 16)

        tm2 = pr.tm.v
        ve2 = pr.ve.v

        @pr.block.tensor
        def _(t, tm2=tm2, dmw2=dmw2, ve2=ve2, st3_sc=st3_sc):
            t.wait_ge(pr.sc.h, st3_sc)
            for oc in range(8):
                t.wait_ge(pr.dmw.h, dmw2 + 32 * (oc + 1))
                if oc >= 2:
                    t.wait_ge(pr.ve.h, ve2 + oc - 1)
                buf = wh[oc % 2]
                ps = pr.PA[:, 512 * (oc % 2):512 * (oc % 2) + 512]
                for k in range(32):
                    i = t.matmul(ps, buf[:, k * 128:(k + 1) * 128],
                                 A[:, k * 512:(k + 1) * 512],
                                 start=(k == 0), stop=(k == 31))
                pr.tm.inc(i)

        @pr.block.vector
        def _(v, tm2=tm2, q=q):
            for oc in range(8):
                v.wait_ge(pr.tm.h, tm2 + oc + 1)
                ps = pr.PA[:, 512 * (oc % 2):512 * (oc % 2) + 512]
                sl = slice(oc * T + q * 512, oc * T + q * 512 + 512)
                pr.ve.inc(v.scalar_tensor_tensor(pr.xT[:, sl], ps,
                                                 scl[:, S3C + oc:S3C + oc + 1],
                                                 pr.xT[:, sl], ALU.mult, ALU.add))
        prev_q_done = pr.ve.v
    ph.close()
    return pr.ve.v


def build_b1():
    nc = bass.Bass("TRN2", target_bir_lowering=False, debug=False)
    io = {
        "xin": nc.dram_tensor("xin", [D, T], F32, kind="ExternalInput"),
        "b1_w1t": nc.dram_tensor("b1_w1t", [1024, 4096], F32, kind="ExternalInput"),
        "b1_w2t": nc.dram_tensor("b1_w2t", [1024, 4096], F32, kind="ExternalInput"),
        "b1_h1t": nc.dram_tensor("b1_h1t", [4096, 4096], F32, kind="ExternalInput"),
        "b1_h2t": nc.dram_tensor("b1_h2t", [4096, 4096], F32, kind="ExternalInput"),
        "b1_w3t": nc.dram_tensor("b1_w3t", [4096, 1024], F32, kind="ExternalInput"),
        "b1_s1": nc.dram_tensor("b1_s1", [4096], F32, kind="ExternalInput"),
        "b1_s2": nc.dram_tensor("b1_s2", [4096], F32, kind="ExternalInput"),
        "b1_sh1": nc.dram_tensor("b1_sh1", [4096], F32, kind="ExternalInput"),
        "b1_sh2": nc.dram_tensor("b1_sh2", [4096], F32, kind="ExternalInput"),
        "b1_s3": nc.dram_tensor("b1_s3", [1024], F32, kind="ExternalInput"),
    }
    xt_out = nc.dram_tensor("xt_out", [D, T], F32, kind="ExternalOutput")
    with ExitStack() as ctx:
        pr = Prog(nc, ctx, xln_cols=DCH * 1024)
        pr.emit_consts()
        xt_ready = emit_load_xt(pr, io["xin"])
        emit_block1(pr, io, xt_ready)
        emit_store_xt(pr, xt_out)
    return nc







def emit_quant_tiles(pr, tiles, scale_ap, inv_ap, zt, it_, dt_, s1t, s2t,
                     width, pre_wait_v=None):
    ve0 = pr.ve.v
    sc0 = pr.sc.v
    n = len(tiles)

    @pr.block.vector
    def _(v):
        if pre_wait_v:
            for sem, val in pre_wait_v:
                v.wait_ge(sem.h, val)
        for j, (src, dst, waits) in enumerate(tiles):
            for sem, val in waits:
                v.wait_ge(sem.h, val)
            v.tensor_scalar(zt[:, 0:width], src, scale_ap, None, ALU.mult)
            v.drain()
            v.tensor_copy(it_[:, 0:width], zt[:, 0:width])
            v.drain()
            pr.ve.inc(v.tensor_tensor(dt_[:, 0:width], zt[:, 0:width], it_[:, 0:width], ALU.subtract))
            v.wait_ge(pr.sc.h, sc0 + j + 1)
            v.scalar_tensor_tensor(zt[:, 0:width], s1t[:, 0:width], 0.5, it_[:, 0:width], ALU.mult, ALU.add)
            v.drain()
            v.scalar_tensor_tensor(zt[:, 0:width], s2t[:, 0:width], -0.5, zt[:, 0:width], ALU.mult, ALU.add)
            v.drain()
            pr.ve.inc(v.tensor_scalar(dst, zt[:, 0:width], inv_ap, None, ALU.mult))
            v.drain()

    @pr.block.scalar
    def _(s):
        for j in range(n):
            s.wait_ge(pr.ve.h, ve0 + 2 * j + 1)
            s.activation(s1t[:, 0:width], dt_[:, 0:width], AF.Sign, bias=pr.neg_half[:])
            s.drain()
            pr.sc.inc(s.activation(s2t[:, 0:width], dt_[:, 0:width], AF.Sign, scale=-1.0, bias=pr.neg_half[:]))


def emit_wq_stream(pr, pr_io_name, wst, dst_writer, zt, it_, dt_, s1t, s2t, qsc,
                   gp_guard, io, wqb=None, transpose_dst=None):
    dmw0 = pr.dmw.v
    ve0 = pr.ve.v
    sc0 = pr.sc.v
    tmT0 = pr.tm.v
    TR = transpose_dst is not None
    VPC = 10 if TR else 2

    @pr.block.gpsimd
    def _(g):
        for kc in range(32):
            g.wait_ge(pr.dmw.h, pr.dmw.v)
            for gg in gp_guard(kc) or []:
                g.wait_ge(gg[0].h, gg[1])
            if TR and kc >= 2:
                g.wait_ge(pr.tm.h, tmT0 + 8 * (kc - 1))
            pr.dmw.inc(g.dma_start(out=wst[kc % 2][:],
                                   in_=io[pr_io_name].ap()[kc * 128:(kc + 1) * 128, :]), 16)

    def quant_d(kc):
        if not TR or kc == 0:
            return ve0 + VPC * kc + 1 if not TR else ve0 + 1
        return ve0 + 10 * kc - 7

    def quant_f(kc):
        if not TR or kc == 0:
            return ve0 + VPC * kc + 2 if not TR else ve0 + 2
        return ve0 + 10 * kc - 6

    def evac_val(j, dc):
        if j <= 30:
            return ve0 + 10 * j + 5 + dc
        return ve0 + 313 + dc

    @pr.block.vector
    def _(v):
        for kc in range(32):
            v.wait_ge(pr.dmw.h, dmw0 + 16 * (kc + 1))
            w = wst[kc % 2]
            v.tensor_reduce(qsc[:, 0:1], w[:], AX.X, ALU.max, apply_absolute_value=True)
            v.drain()
            v.tensor_scalar(qsc[:, 0:1], qsc[:, 0:1], 1e-8, None, ALU.max)
            v.drain()
            v.reciprocal(qsc[:, 1:2], qsc[:, 0:1])
            v.drain()
            v.tensor_scalar(qsc[:, 2:3], qsc[:, 1:2], 127.0, None, ALU.mult)
            v.drain()
            v.reciprocal(qsc[:, 3:4], qsc[:, 2:3])
            v.drain()
            v.tensor_scalar(zt[:], w[:], qsc[:, 2:3], None, ALU.mult)
            v.drain()
            v.tensor_copy(it_[:], zt[:])
            v.drain()
            pr.ve.inc(v.tensor_tensor(dt_[:], zt[:], it_[:], ALU.subtract))
            v.wait_ge(pr.sc.h, sc0 + kc + 1)
            v.scalar_tensor_tensor(zt[:], s1t[:], 0.5, it_[:], ALU.mult, ALU.add)
            v.drain()
            v.scalar_tensor_tensor(zt[:], s2t[:], -0.5, zt[:], ALU.mult, ALU.add)
            v.drain()
            dst = wqb[kc % 2][:] if TR else dst_writer(kc)
            pr.ve.inc(v.tensor_scalar(dst, zt[:], qsc[:, 3:4], None, ALU.mult))
            v.drain()
            if TR and kc >= 1:
                for dc in range(8):
                    v.wait_ge(pr.tm.h, tmT0 + (kc - 1) * 8 + dc + 1)
                    pt = pr.PT1 if ((kc - 1) * 8 + dc) % 2 == 0 else pr.PT2
                    pr.ve.inc(v.tensor_copy(transpose_dst(kc - 1, dc), pt[:, 0:128]))
        if TR:
            for dc in range(8):
                v.wait_ge(pr.tm.h, tmT0 + 31 * 8 + dc + 1)
                pt = pr.PT1 if (31 * 8 + dc) % 2 == 0 else pr.PT2
                pr.ve.inc(v.tensor_copy(transpose_dst(31, dc), pt[:, 0:128]))

    @pr.block.scalar
    def _(s):
        for kc in range(32):
            s.wait_ge(pr.ve.h, quant_d(kc))
            s.activation(s1t[:], dt_[:], AF.Sign, bias=pr.neg_half[:])
            s.drain()
            pr.sc.inc(s.activation(s2t[:], dt_[:], AF.Sign, scale=-1.0, bias=pr.neg_half[:]))

    if TR:
        @pr.block.tensor
        def _(t):
            for kc in range(32):
                t.wait_ge(pr.ve.h, quant_f(kc))
                for dc in range(8):
                    idx = kc * 8 + dc
                    if idx >= 2:
                        j2, dc2 = divmod(idx - 2, 8)
                        t.wait_ge(pr.ve.h, evac_val(j2, dc2))
                    pt = pr.PT1 if idx % 2 == 0 else pr.PT2
                    pr.tm.inc(t.transpose(pt[:, 0:128],
                                          wqb[kc % 2][:, dc * 128:(dc + 1) * 128],
                                          pr.ident_b[:]))
    return pr.ve.v


def emit_global_amax(pr, tile, cc_in, cc_out, dst_col, dst_inv_col, qlevels, bscr):
    ve0 = pr.ve.v

    @pr.block.vector
    def _(v):
        pr.ve.inc(v.engine_nop())
    tree_done = pr.ve.v

    @pr.block.sync
    def _(sy):
        sy.wait_ge(pr.ve.h, tree_done)
        sy.wait_ge(pr.dma.h, pr.dma.v)
        pr.dma.inc(sy.dma_start(out=bscr.ap().rearrange("(a b) -> a b", a=128)[:, 0:1],
                                in_=tile[:, 0:1]), 16)
        sy.wait_ge(pr.dma.h, pr.dma.v)
        pr.dma.inc(sy.dma_start(out=pr.srow32[:],
                                in_=bscr.ap().rearrange("(a b) -> a b", a=1)[:, 0:128]), 16)
    ld1 = pr.dma.v

    @pr.block.vector
    def _(v):
        v.wait_ge(pr.dma.h, ld1)
        v.tensor_reduce(pr.s11[:], pr.srow32[:], AX.X, ALU.max)
        v.drain()
        pr.ve.inc(v.tensor_scalar(pr.s11[:], pr.s11[:], 1e-8, None, ALU.max))
    amax_local = pr.ve.v

    @pr.block.gpsimd
    def _(g):
        pr.gp.inc(g.memset(pr.zrow8[:], 0.0))
    zr = pr.gp.v

    @pr.block.sync
    def _(sy):
        sy.wait_ge(pr.ve.h, amax_local)
        sy.wait_ge(pr.gp.h, zr)
        sy.wait_ge(pr.dma.h, pr.dma.v)
        pr.dma.inc(sy.dma_start(out=cc_in.ap(), in_=pr.zrow8[:]), 16)
        sy.wait_ge(pr.dma.h, pr.dma.v)
        pr.dma.inc(sy.dma_start(out=cc_in.ap()[0:1, 0:1], in_=pr.s11[:]), 16)
    ccst = pr.dma.v

    @pr.block.gpsimd
    def _(g):
        g.wait_ge(pr.dma.h, ccst)
        pr.cc.inc(g.collective_compute(
            "AllReduce", ALU.max, replica_groups=[list(range(NCORE))],
            ins=[cc_in.ap().opt()], outs=[cc_out.ap().opt()]))
    ccd = pr.cc.v

    @pr.block.sync
    def _(sy):
        sy.wait_ge(pr.cc.h, ccd)
        sy.wait_ge(pr.dma.h, pr.dma.v)
        pr.dma.inc(sy.dma_start(out=pr.s11[:], in_=cc_out.ap()[0:1, 0:1]), 16)
    ld2 = pr.dma.v

    @pr.block.vector
    def _(v):
        v.wait_ge(pr.dma.h, ld2)
        v.reciprocal(pr.s12[:], pr.s11[:])
        v.drain()
        v.tensor_scalar(pr.s12[:], pr.s12[:], qlevels, None, ALU.mult)
        v.drain()
        v.reciprocal(pr.s13[:], pr.s12[:])
        v.drain()
        v.tensor_copy(pr.s14[:, 0:1], pr.s12[:])
        v.drain()
        pr.ve.inc(v.tensor_copy(pr.s14[:, 1:2], pr.s13[:]))
    row2 = pr.ve.v

    @pr.block.tensor
    def _(t):
        t.wait_ge(pr.ve.h, row2)
        pr.tm.inc(t.matmul(pr.PD[0:128, 0:2], pr.ones_frow[:], pr.s14[:, 0:2],
                           start=True, stop=True))
    bmm = pr.tm.v

    @pr.block.vector
    def _(v):
        v.wait_ge(pr.tm.h, bmm)
        v.tensor_copy(dst_col, pr.PD[:, 0:1])
        v.drain()
        pr.ve.inc(v.tensor_copy(dst_inv_col, pr.PD[:, 1:2]))
    return pr.ve.v


def emit_block3(pr, io, xt_ready):
    nc = pr.nc
    ph = ExitStack()
    W_q = ph.enter_context(nc.sbuf_tensor("b3_wq", [128, 32 * 1024], BF16))
    zt = ph.enter_context(nc.sbuf_tensor("b3_zt", [128, 1024], F32))
    it_ = ph.enter_context(nc.sbuf_tensor("b3_it", [128, 1024], I32))
    dt_ = zt
    s1t = pr.xln[:, 12288:13312]
    s2t = pr.xln[:, 13312:14336]
    binc = ph.enter_context(nc.sbuf_tensor("b3_binc", [128, 44], F32))
    qsc = ph.enter_context(nc.sbuf_tensor("b3_qsc", [128, 4], F32))
    xmax = ph.enter_context(nc.sbuf_tensor("b3_xmax", [128, 4], F32))
    hmax = ph.enter_context(nc.sbuf_tensor("b3_hmax", [128, 4], F32))

    hid = nc.dram_tensor("b3_hid", [32, 128, T], F32, kind="Internal")
    bscr = nc.dram_tensor("b3_bscr", [128], F32, kind="Internal")
    ccxi = nc.dram_tensor("b3_ccxi", [1, 8], F32, kind="Internal")
    ccxo = nc.dram_tensor("b3_ccxo", [1, 8], F32, kind="Internal")
    cchi = nc.dram_tensor("b3_cchi", [1, 8], F32, kind="Internal")
    ccho = nc.dram_tensor("b3_ccho", [1, 8], F32, kind="Internal")
    SXC, SIXC, SHC, SIHC = 40, 41, 42, 43

    @pr.block.gpsimd
    def _(g):
        g.memset(xmax[:], 0.0)
        pr.gp.inc(g.memset(hmax[:], 0.0))
    mm0 = pr.gp.v

    phW1 = ExitStack()
    wst = [phW1.enter_context(nc.sbuf_tensor(f"b3_wst{i}", [128, 1024], F32)) for i in range(2)]
    wqb = [phW1.enter_context(nc.sbuf_tensor(f"b3_wqb{i}", [128, 1024], BF16)) for i in range(2)]

    winq_done = emit_wq_stream(
        pr, "b3_wint", wst, None, zt, it_, dt_, s1t, s2t, qsc,
        lambda kc: [(pr.ve, xt_ready)] if kc == 0 else None, io,
        wqb=wqb,
        transpose_dst=lambda kc, dc: W_q[:, dc * 4096 + kc * 128: dc * 4096 + kc * 128 + 128])
    phW1.close()
    phI = ExitStack()
    xf32 = phI.enter_context(nc.sbuf_tensor("b3_xf32", [128, DCH * 512], F32))
    xqb = pr.xln[:, 8192:12288]
    stg = phI.enter_context(nc.sbuf_tensor("b3_stg", [128, 2 * 512], F32))

    for nm, c0, m in [("b3_bin", 0, 4096), ("b3_bout", 32, 1024)]:
        for part in range(0, m, 512):
            @pr.block.sync
            def _(sy, nm=nm, part=part):
                sy.wait_ge(pr.dma.h, pr.dma.v)
                sy.wait_ge(pr.ve.h, pr.ve.v)
                pr.dma.inc(sy.dma_start(out=pr.srow512[:],
                                        in_=io[nm].ap().rearrange("(a d) -> a d", a=1)[:, part:part + 512]), 16)
            columnize_vec(pr, [pr.srow512[:]], binc, c0 + part // 128)

    for q in range(NTT):
        lnd = pr.ln(pr.xT, pr.xln[:, 0:DCH * 512], pr.xln[:, DCH * 512:DCH * 1024],
                    [(pr.ve, xt_ready if q == 0 else pr.ve.v)], tts=[q], dst_stride=512,
                    dst_f32=xf32)

        @pr.block.vector
        def _(v, lnd=lnd):
            v.wait_ge(pr.ve.h, lnd)
            v.wait_ge(pr.gp.h, mm0)
            v.tensor_reduce(xmax[:, 1:2], xf32[:], AX.X, ALU.max, apply_absolute_value=True)
            v.drain()
            pr.ve.inc(v.tensor_tensor(xmax[:, 0:1], xmax[:, 0:1], xmax[:, 1:2], ALU.max))
            v.drain()

    emit_global_amax(pr, xmax[:, 0:4], ccxi, ccxo, binc[:, SXC:SXC + 1],
                     binc[:, SIXC:SIXC + 1], 32767.0, bscr)
    sx_ready = pr.ve.v

    for q in range(NTT):
        lnd = pr.ln(pr.xT, pr.xln[:, 0:DCH * 512], pr.xln[:, DCH * 512:DCH * 1024],
                    [(pr.ve, pr.ve.v)], tts=[q], dst_stride=512, dst_f32=xf32)
        tiles = [(xf32[:, sl * 512:(sl + 1) * 512], xqb[:, sl * 512:(sl + 1) * 512], [])
                 for sl in range(8)]
        emit_quant_tiles(pr, tiles, binc[:, SXC:SXC + 1], binc[:, SIXC:SIXC + 1],
                         zt, it_, dt_, s1t, s2t, 512,
                         pre_wait_v=[(pr.ve, lnd), (pr.ve, sx_ready)])
        xq_done = pr.ve.v

        tmS = pr.tm.v
        scS = pr.sc.v
        veS = pr.ve.v
        dmaS = pr.dma.v

        @pr.block.tensor
        def _(t, tmS=tmS, scS=scS, xq_done=xq_done):
            t.wait_ge(pr.ve.h, xq_done)
            t.wait_ge(pr.ve.h, winq_done)
            for oc in range(32):
                if oc >= 2:
                    t.wait_ge(pr.sc.h, scS + oc - 1)
                ps = pr.PA[:, 512 * (oc % 2):512 * (oc % 2) + 512]
                for k in range(8):
                    i = t.matmul(ps, W_q[:, k * 4096 + oc * 128: k * 4096 + oc * 128 + 128],
                                 xqb[:, k * 512:(k + 1) * 512], start=(k == 0), stop=(k == 7))
                pr.tm.inc(i)

        @pr.block.scalar
        def _(s, tmS=tmS, veS=veS, dmaS=dmaS):
            for oc in range(32):
                s.wait_ge(pr.tm.h, tmS + oc + 1)
                if oc >= 2:
                    s.wait_ge(pr.ve.h, veS + 2 * (oc - 1))
                    s.wait_ge(pr.dma.h, dmaS + 16 * (oc - 1))
                else:
                    s.wait_ge(pr.dma.h, dmaS)
                ps = pr.PA[:, 512 * (oc % 2):512 * (oc % 2) + 512]
                pr.sc.inc(s.activation(stg[:, 512 * (oc % 2):512 * (oc % 2) + 512], ps,
                                       AF.Gelu, bias=binc[:, oc:oc + 1]))

        @pr.block.vector
        def _(v, scS=scS):
            for oc in range(32):
                v.wait_ge(pr.sc.h, scS + oc + 1)
                sgl = stg[:, 512 * (oc % 2):512 * (oc % 2) + 512]
                v.tensor_reduce(hmax[:, 1:2], sgl, AX.X, ALU.max, apply_absolute_value=True)
                v.drain()
                v.tensor_tensor(hmax[:, 0:1], hmax[:, 0:1], hmax[:, 1:2], ALU.max)
                v.drain()
                pr.ve.inc(v.engine_nop(), 2)

        @pr.block.sync
        def _(sy, q=q, veS=veS, dmaS=dmaS):
            for oc in range(32):
                sy.wait_ge(pr.ve.h, veS + 2 * (oc + 1))
                sy.wait_ge(pr.dma.h, pr.dma.v)
                pr.dma.inc(sy.dma_start(out=hid.ap()[oc, :, q * 512:(q + 1) * 512],
                                        in_=stg[:, 512 * (oc % 2):512 * (oc % 2) + 512]), 16)
    stageI_tm = pr.tm.v
    stageI_dma = pr.dma.v
    phI.close()
    phW2 = ExitStack()
    wst = [phW2.enter_context(nc.sbuf_tensor(f"b3_wst2{i}", [128, 1024], F32)) for i in range(2)]

    emit_global_amax(pr, hmax[:, 0:4], cchi, ccho, binc[:, SHC:SHC + 1],
                     binc[:, SIHC:SIHC + 1], 32767.0, bscr)
    sh_ready = pr.ve.v

    ve_w1 = pr.ve.v
    emit_wq_stream(
        pr, "b3_wout", wst, lambda kc: W_q[:, kc * 1024:(kc + 1) * 1024],
        zt, it_, dt_, s1t, s2t, qsc,
        lambda kc: ([(pr.tm, stageI_tm), (pr.dma, stageI_dma)] if kc < 2
                    else [(pr.ve, ve_w1 + 2 * (kc - 1))]), io)
    woutq_done = pr.ve.v
    phW2.close()

    phII = ExitStack()
    hld = [phII.enter_context(nc.sbuf_tensor(f"b3_hld{i}", [128, 512], F32)) for i in range(2)]
    hqb = [phII.enter_context(nc.sbuf_tensor(f"b3_hqb{i}", [128, 512], BF16)) for i in range(2)]

    for q in range(NTT):
        dmaQ = pr.dma.v
        veQ = pr.ve.v
        tmQ = pr.tm.v

        @pr.block.sync
        def _(sy, q=q, veQ=veQ):
            if q == 0:
                sy.wait_ge(pr.ve.h, woutq_done)
            else:
                sy.wait_ge(pr.ve.h, veQ)
            for k in range(32):
                sy.wait_ge(pr.dma.h, pr.dma.v)
                if k >= 2:
                    sy.wait_ge(pr.ve.h, veQ + 2 * (k - 1))
                pr.dma.inc(sy.dma_start(out=hld[k % 2][:],
                                        in_=hid.ap()[k, :, q * 512:(q + 1) * 512]), 16)

        tiles = [(hld[k % 2][:], hqb[k % 2][:],
                  [(pr.dma, dmaQ + 16 * (k + 1))] + ([(pr.tm, tmQ + k - 1)] if k >= 2 else []))
                 for k in range(32)]
        emit_quant_tiles(pr, tiles, binc[:, SHC:SHC + 1], binc[:, SIHC:SIHC + 1],
                         zt, it_, dt_, s1t, s2t, 512,
                         pre_wait_v=([(pr.ve, sh_ready), (pr.ve, woutq_done)] if q == 0 else None))

        banks = [pr.PA[:, 0:512], pr.PA[:, 512:1024], pr.PB[:], pr.PC[:], pr.PD[:],
                 pr.PF[:], pr.PT1.bitcast(F32)[:, 0:512], pr.PT2.bitcast(F32)[:, 0:512]]

        @pr.block.tensor
        def _(t, q=q, veQ=veQ, banks=banks):
            if q == 0:
                t.wait_ge(pr.ve.h, woutq_done)
            else:
                t.wait_ge(pr.ve.h, veQ)
            for k in range(32):
                t.wait_ge(pr.ve.h, veQ + 2 * (k + 1))
                for dc in range(8):
                    i = t.matmul(banks[dc], W_q[:, k * 1024 + dc * 128: k * 1024 + dc * 128 + 128],
                                 hqb[k % 2][:], start=(k == 0), stop=(k == 31))
                pr.tm.inc(i)

        tmE = pr.tm.v

        @pr.block.vector
        def _(v, q=q, tmE=tmE, banks=banks):
            v.wait_ge(pr.tm.h, tmE)
            for dc in range(8):
                sl = slice(dc * T + q * 512, dc * T + q * 512 + 512)
                i = v.scalar_tensor_tensor(pr.xT[:, sl], banks[dc], binc[:, 32 + dc:32 + dc + 1],
                                           pr.xT[:, sl], ALU.add, ALU.add)
                v.drain()
            pr.ve.inc(i)
    phII.close()
    ph.close()
    return pr.ve.v


def build_b3():
    nc = bass.Bass("TRN2", target_bir_lowering=False, debug=False)
    io = {
        "xin": nc.dram_tensor("xin", [D, T], F32, kind="ExternalInput"),
        "b3_wint": nc.dram_tensor("b3_wint", [4096, 1024], F32, kind="ExternalInput"),
        "b3_wout": nc.dram_tensor("b3_wout", [4096, 1024], F32, kind="ExternalInput"),
        "b3_bin": nc.dram_tensor("b3_bin", [4096], F32, kind="ExternalInput"),
        "b3_bout": nc.dram_tensor("b3_bout", [1024], F32, kind="ExternalInput"),
    }
    xt_out = nc.dram_tensor("xt_out", [D, T], F32, kind="ExternalOutput")
    with ExitStack() as ctx:
        pr = Prog(nc, ctx, xln_cols=DCH * 1024, with_trimask=False)
        pr.emit_consts()
        xt_ready = emit_load_xt(pr, io["xin"])
        emit_block3(pr, io, xt_ready)
        emit_store_xt(pr, xt_out)
    return nc





def emit_block2(pr, io, xt_ready):
    nc = pr.nc
    NIT_BIS = 33
    ph = ExitStack()
    qn = pr.xln[:, 8192:12288]
    kn = pr.xln[:, 12288:16384]
    vs = ph.enter_context(nc.sbuf_tensor("b2_vs", [128, 4096], BF16))
    rS2 = ph.enter_context(nc.sbuf_tensor("b2_rs2", [128, 4096], F32))
    wbuf = [ph.enter_context(nc.sbuf_tensor("b2_w0", [128, 8192], BF16))] * 2
    oT = ph.enter_context(nc.sbuf_tensor("b2_oT", [128, 4096], BF16))
    sqs = ph.enter_context(nc.sbuf_tensor("b2_sqs", [128, 4096], BF16))
    msk = ph.enter_context(nc.sbuf_tensor("b2_msk", [128, 1024], BF16))
    mmt = ph.enter_context(nc.sbuf_tensor("b2_mmt", [128, 4096], BF16))
    sgm = mmt
    bis = ph.enter_context(nc.sbuf_tensor("b2_bis", [128, 256], F32))
    selu = ph.enter_context(nc.sbuf_tensor("b2_selu", [128, 64], mybir.dt.uint8))
    S32 = ph.enter_context(nc.sbuf_tensor("b2_S32", [128, 1024], F32))
    Sbf = ph.enter_context(nc.sbuf_tensor("b2_Sbf", [128, 1024], BF16))
    fh8 = ph.enter_context(nc.sbuf_tensor("b2_fh8", [8, 512], F32))
    rows = ph.enter_context(nc.sbuf_tensor("b2_rows", [1, 1024], F32))
    wab = ph.enter_context(nc.sbuf_tensor("b2_wab", [128, 64], BF16))
    nrm = ph.enter_context(nc.sbuf_tensor("b2_nrm", [128, 10], F32))
    ab8 = ph.enter_context(nc.sbuf_tensor("b2_ab8", [1, 16], F32))
    dtb = ph.enter_context(nc.sbuf_tensor("b2_dtbc", [128, 2], F32))
    chs = ph.enter_context(nc.sbuf_tensor("b2_chs", [128, 1024], F32))
    chb = ph.enter_context(nc.sbuf_tensor("b2_chb", [128, 2048], BF16))
    fts = ph.enter_context(nc.sbuf_tensor("b2_fts", [128, 512], F32))
    rbc = ph.enter_context(nc.sbuf_tensor("b2_rbc", [128, 512], F32))
    vtt = ph.enter_context(nc.sbuf_tensor("b2_vtt", [128, 128], BF16))
    fscr = nc.dram_tensor("b2_fscr", [8, 512], F32, kind="Internal")

    LO, HI, MID, CNT, T1 = 0, 32, 64, 96, 128

    @pr.block.gpsimd
    def _(g):
        g.wait_ge(pr.dmw.h, pr.dmw.v)
        pr.dmw.inc(g.dma_start(out=wab[:].rearrange("p (c m) -> p c m", c=8),
                               in_=io["b2_wa"].ap().rearrange("(c p) m -> p c m", p=128)), 16)
    wa_ld = pr.dmw.v

    @pr.block.sync
    def _(sy):
        sy.wait_ge(pr.dma.h, pr.dma.v)
        pr.dma.inc(sy.dma_start(out=ab8[:, 0:8], in_=io["b2_alog"].ap().rearrange("(a d) -> a d", a=1)), 16)
        sy.wait_ge(pr.dma.h, pr.dma.v)
        pr.dma.inc(sy.dma_start(out=ab8[:, 8:16], in_=io["b2_dtb"].ap().rearrange("(a d) -> a d", a=1)), 16)
        sy.wait_ge(pr.dma.h, pr.dma.v)
        pr.dma.inc(sy.dma_start(out=pr.srow512[:, 0:128], in_=io["b2_qnw"].ap().rearrange("(a d) -> a d", a=1)), 16)
        sy.wait_ge(pr.dma.h, pr.dma.v)
        pr.dma.inc(sy.dma_start(out=pr.srow512[:, 128:256], in_=io["b2_knw"].ap().rearrange("(a d) -> a d", a=1)), 16)
    nw_ld = pr.dma.v

    @pr.block.scalar
    def _(s):
        s.wait_ge(pr.dma.h, nw_ld)
        pr.sc.inc(s.activation(ab8[:, 0:8], ab8[:, 0:8], AF.Exp, scale=1.0))
    negea = pr.sc.v

    @pr.block.tensor
    def _(t):
        t.wait_ge(pr.sc.h, negea)
        t.wait_ge(pr.ve.h, pr.ve.v)
        t.wait_ge(pr.tm.h, pr.tm.v)
        t.matmul(pr.PD[0:128, 0:1], pr.srow512[:, 0:128], pr.ones_f11[:], start=True, stop=True)
        t.matmul(pr.PD[0:128, 1:2], pr.srow512[:, 128:256], pr.ones_f11[:], start=True, stop=True)
        t.matmul(pr.PD[0:8, 2:3], ab8[:, 8:16], pr.ones_f11[:], start=True, stop=True)
        pr.tm.inc(t.matmul(pr.PD[0:8, 3:4], ab8[:, 0:8], pr.ones_f11[:], start=True, stop=True))
    cmm = pr.tm.v

    @pr.block.vector
    def _(v):
        v.wait_ge(pr.tm.h, cmm)
        v.tensor_copy(nrm[:, 0:2], pr.PD[:, 0:2])
        v.drain()
        v.tensor_copy(dtb[0:8, 0:2], pr.PD[0:8, 2:4])
        v.drain()
        pr.ve.inc(v.tensor_scalar(dtb[0:8, 1:2], dtb[0:8, 1:2], -1.0, None, ALU.mult))
    small_done = pr.ve.v

    for part in range(0, 1024, 512):
        @pr.block.sync
        def _(sy, part=part):
            sy.wait_ge(pr.dma.h, pr.dma.v)
            sy.wait_ge(pr.ve.h, pr.ve.v)
            pr.dma.inc(sy.dma_start(out=pr.srow512[:],
                                    in_=io["b2_gnw"].ap().rearrange("(a d) -> a d", a=1)[:, part:part + 512]), 16)
        columnize_vec(pr, [pr.srow512[:]], nrm, 2 + part // 128)
    gnw_done = pr.ve.v

    @pr.block.gpsimd
    def _(g):
        pr.gp.inc(g.memset(S32[:], 0.0))
        pr.gp.inc(g.memset(Sbf[:], 0.0))
    s0done = pr.gp.v

    prev_tt = xt_ready
    for tt in range(NTT):
        lnd = pr.ln(pr.xT, pr.xln[:, 0:4096], pr.xln[:, 4096:8192],
                    [(pr.ve, prev_tt)], tts=[tt], dst_stride=512)
        xlq = pr.xln[:, 0:4096]

        def proj(wname, evac_fn, n_evac_inc, prev_marker):
            pass

        projs = [("b2_wq", "q"), ("b2_wk", "k"), ("b2_wv", "v"), ("b2_wr", "r")]
        ev_markers = {}
        for pi, (wname, kind) in enumerate(projs):
            dmw0 = pr.dmw.v

            @pr.block.gpsimd
            def _(g, wname=wname, pi=pi, tt=tt):
                g.wait_ge(pr.dmw.h, pr.dmw.v)
                if not (tt == 0 and pi == 0):
                    g.wait_ge(pr.tm.h, pr.tm.v)
                pr.dmw.inc(g.dma_start(out=wbuf[pi % 2][:].rearrange("p (c m) -> p c m", c=8),
                                       in_=io[wname].ap().rearrange("(c p) m -> p c m", p=128)), 16)
            wld = pr.dmw.v

            tmP = pr.tm.v
            scP = pr.sc.v
            veP = pr.ve.v

            if kind != "r":
                @pr.block.tensor
                def _(t, wld=wld, lnd=lnd, tmP=tmP, scP=scP, pi=pi, tt=tt):
                    t.wait_ge(pr.dmw.h, wld)
                    t.wait_ge(pr.ve.h, lnd)
                    t.wait_ge(pr.sc.h, scP)
                    if tt == 0 and pi == 0:
                        t.wait_ge(pr.gp.h, s0done)
                        t.wait_ge(pr.ve.h, gnw_done)
                    for dc in range(8):
                        if dc >= 2:
                            t.wait_ge(pr.sc.h, scP + dc - 1)
                        ps = pr.PA[:, 512 * (dc % 2):512 * (dc % 2) + 512]
                        for k in range(8):
                            i = t.matmul(ps, wbuf[pi % 2][:, k * 1024 + dc * 128: k * 1024 + dc * 128 + 128],
                                         xlq[:, k * 512:(k + 1) * 512], start=(k == 0), stop=(k == 7))
                        pr.tm.inc(i)

            if kind in ("q", "k", "v"):
                dstb = {"q": qn, "k": kn, "v": vs}[kind]

                @pr.block.scalar
                def _(s, tmP=tmP, dstb=dstb):
                    for dc in range(8):
                        s.wait_ge(pr.tm.h, tmP + dc + 1)
                        ps = pr.PA[:, 512 * (dc % 2):512 * (dc % 2) + 512]
                        pr.sc.inc(s.activation(dstb[:, dc * 512:(dc + 1) * 512], ps, AF.Silu))
                ev_markers[kind] = pr.sc.v
            else:
                veR = pr.ve.v

                def tm_mm(dc):
                    return tmP + dc + 1 + 4 * max(0, dc - 1)

                def tm_tr(j, ts):
                    if j <= 6:
                        return tmP + (j + 2) + 4 * j + ts + 1
                    return tmP + 36 + ts + 1

                @pr.block.tensor
                def _(t, wld=wld, lnd=lnd, tmP=tmP, scP=scP, pi=pi, veR=veR):
                    t.wait_ge(pr.dmw.h, wld)
                    t.wait_ge(pr.ve.h, lnd)
                    t.wait_ge(pr.sc.h, scP)
                    def emit_tr(j):
                        t.wait_ge(pr.sc.h, scP + j + 1)
                        for ts in range(4):
                            idx = j * 4 + ts
                            if idx >= 2:
                                t.wait_ge(pr.ve.h, veR + idx - 1)
                            pt = pr.PT1 if idx % 2 == 0 else pr.PT2
                            pr.tm.inc(t.transpose(pt.bitcast(F32)[:, 0:128],
                                                  chs[:, 512 * (j % 2) + ts * 128: 512 * (j % 2) + ts * 128 + 128],
                                                  pr.ident_f[:]))
                    for dc in range(8):
                        if dc >= 2:
                            t.wait_ge(pr.sc.h, scP + dc - 1)
                        ps = pr.PA[:, 512 * (dc % 2):512 * (dc % 2) + 512]
                        for k in range(8):
                            i = t.matmul(ps, wbuf[pi % 2][:, k * 1024 + dc * 128: k * 1024 + dc * 128 + 128],
                                         xlq[:, k * 512:(k + 1) * 512], start=(k == 0), stop=(k == 7))
                        pr.tm.inc(i)
                        if dc >= 1:
                            emit_tr(dc - 1)
                    emit_tr(7)

                @pr.block.scalar
                def _(s, tmP=tmP):
                    for dc in range(8):
                        s.wait_ge(pr.tm.h, tm_mm(dc))
                        ps = pr.PA[:, 512 * (dc % 2):512 * (dc % 2) + 512]
                        pr.sc.inc(s.copy(chs[:, 512 * (dc % 2):512 * (dc % 2) + 512], ps))

                @pr.block.vector
                def _(v):
                    for dc in range(8):
                        for ts in range(4):
                            idx = dc * 4 + ts
                            v.wait_ge(pr.tm.h, tm_tr(dc, ts))
                            pt = pr.PT1 if idx % 2 == 0 else pr.PT2
                            pr.ve.inc(v.tensor_copy(rS2[:, ts * 1024 + dc * 128: ts * 1024 + dc * 128 + 128],
                                                    pt.bitcast(F32)[:, 0:128]))
                ev_markers["r"] = pr.ve.v

        dmw0 = pr.dmw.v
        tmA = pr.tm.v

        @pr.block.tensor
        def _(t, tmA=tmA, wa_ld=wa_ld, lnd=lnd):
            t.wait_ge(pr.dmw.h, wa_ld)
            t.wait_ge(pr.ve.h, lnd)
            t.wait_ge(pr.ve.h, pr.ve.v)
            for k in range(8):
                i = t.matmul(pr.PD[0:8, 0:512], wab[:, k * 8:(k + 1) * 8],
                             xlq[:, k * 512:(k + 1) * 512], start=(k == 0), stop=(k == 7))
            pr.tm.inc(i)
        fhmm = pr.tm.v

        @pr.block.scalar
        def _(s, fhmm=fhmm, dmav=pr.dma.v):
            s.wait_ge(pr.tm.h, fhmm)
            s.wait_ge(pr.ve.h, small_done)
            s.wait_ge(pr.dma.h, dmav)
            s.activation(fh8[:, 0:512], pr.PD[0:8, 0:512], AF.Exp, bias=dtb[0:8, 0:1])
            s.drain()
            pr.sc.inc(s.activation(fh8[:, 0:512], fh8[:, 0:512], AF.Ln, bias=pr.onecol[0:8, 0:1]))
        sp_done = pr.sc.v

        @pr.block.vector
        def _(v, sp_done=sp_done):
            v.wait_ge(pr.sc.h, sp_done)
            pr.ve.inc(v.tensor_scalar(fh8[:, 0:512], fh8[:, 0:512], dtb[0:8, 1:2], None, ALU.mult))
        fh_done = pr.ve.v

        @pr.block.sync
        def _(sy, fh_done=fh_done):
            sy.wait_ge(pr.ve.h, fh_done)
            sy.wait_ge(pr.dma.h, pr.dma.v)
            pr.dma.inc(sy.dma_start(out=fscr.ap(), in_=fh8[:, 0:512]), 16)
        fhr_ld = pr.dma.v


        for kind, buf, wcol in [("q", qn, 0), ("k", kn, 1)]:
            sqm = pr.sc.v

            @pr.block.scalar
            def _(s, buf=buf, kind=kind, tmguard=pr.tm.v, evm=ev_markers[kind]):
                s.wait_ge(pr.sc.h, evm)
                s.wait_ge(pr.tm.h, tmguard)
                pr.sc.inc(s.activation(sqs[:], buf[:], AF.Square))
            sq_d = pr.sc.v

            for h in range(8):
                tmN = pr.tm.v

                @pr.block.tensor
                def _(t, h=h, sq_d=sq_d, ve_pf=pr.ve.v):
                    t.wait_ge(pr.sc.h, sq_d)
                    t.wait_ge(pr.ve.h, ve_pf)
                    pr.tm.inc(t.matmul(pr.PF[0:1, 0:512], pr.ones_bcol[:],
                                       sqs[:, h * 512:(h + 1) * 512], start=True, stop=True))
                nmm = pr.tm.v

                @pr.block.scalar
                def _(s, nmm=nmm):
                    s.wait_ge(pr.tm.h, nmm)
                    s.activation(rows[:, 0:512], pr.PF[0:1, 0:512], AF.Ln, scale=1.0 / 128.0,
                                 bias=pr.eps_t[:])
                    s.drain()
                    pr.sc.inc(s.activation(rows[:, 0:512], rows[:, 0:512], AF.Exp, scale=-0.5))
                nr_d = pr.sc.v

                @pr.block.tensor
                def _(t, nr_d=nr_d):
                    t.wait_ge(pr.sc.h, nr_d)
                    pr.tm.inc(t.matmul(pr.PF[:, 0:512], pr.ones_frow[:], rows[:, 0:512],
                                       start=True, stop=True))
                nb_mm = pr.tm.v

                @pr.block.vector
                def _(v, nb_mm=nb_mm, h=h, buf=buf, wcol=wcol):
                    v.wait_ge(pr.tm.h, nb_mm)
                    v.tensor_copy(rbc[:], pr.PF[:, 0:512])
                    v.drain()
                    pr.ve.inc(v.scalar_tensor_tensor(buf[:, h * 512:(h + 1) * 512],
                                                     buf[:, h * 512:(h + 1) * 512],
                                                     nrm[:, wcol:wcol + 1], rbc[:],
                                                     ALU.mult, ALU.mult))
                    v.drain()
        qknorm_done = pr.ve.v

        veB = pr.ve.v

        @pr.block.vector
        def _(v, veB=veB, tmguard=pr.tm.v, rmark=ev_markers["r"]):
            v.wait_ge(pr.ve.h, rmark)
            v.wait_ge(pr.tm.h, tmguard)
            v.drain()
            v.memset(bis[:, LO:LO + 32], -40.0)
            v.memset(bis[:, HI:HI + 32], 40.0)
            v.memset(bis[:, MID:MID + 32], 0.0)
            v.drain()
            for it in range(NIT_BIS):
                for u in range(32):
                    ts, h = divmod(u, 8)
                    v.tensor_scalar(mmt[:, u * 128:(u + 1) * 128],
                                    rS2[:, ts * 1024 + h * 128: ts * 1024 + h * 128 + 128],
                                    bis[:, MID + u:MID + u + 1], 0.0, ALU.is_ge, ALU.add,
                                    accum_out=bis[:, CNT + u:CNT + u + 1])
                v.drain()
                v.tensor_scalar(selu[:, 0:32], bis[:, CNT:CNT + 32], 31.5, None, ALU.is_ge)
                v.tensor_scalar(selu[:, 32:64], bis[:, CNT:CNT + 32], 31.5, None, ALU.is_lt)
                v.drain()
                v.select(bis[:, LO:LO + 32], selu[:, 0:32], bis[:, MID:MID + 32], bis[:, LO:LO + 32], add_drain=True)
                v.select(bis[:, HI:HI + 32], selu[:, 32:64], bis[:, MID:MID + 32], bis[:, HI:HI + 32], add_drain=True)
                v.drain()
                v.tensor_tensor(bis[:, T1:T1 + 32], bis[:, LO:LO + 32], bis[:, HI:HI + 32], ALU.add)
                v.drain()
                v.tensor_scalar(bis[:, MID:MID + 32], bis[:, T1:T1 + 32], 0.5, None, ALU.mult)
                v.drain()
            for u in range(32):
                ts, h = divmod(u, 8)
                i = v.tensor_scalar(sqs[:, ts * 1024 + h * 128: ts * 1024 + h * 128 + 128],
                                    rS2[:, ts * 1024 + h * 128: ts * 1024 + h * 128 + 128],
                                    bis[:, LO + u:LO + u + 1], None, ALU.is_ge)
            pr.ve.inc(i)
        bis_done = pr.ve.v

        @pr.block.scalar
        def _(s, bis_done=bis_done):
            s.wait_ge(pr.ve.h, bis_done)
            pr.sc.inc(s.activation(sgm[:], rS2[:, 0:4096], AF.Sigmoid))
        sgm_done = pr.sc.v

        @pr.block.vector
        def _(v, sgm_done=sgm_done, bis_done=bis_done):
            v.wait_ge(pr.sc.h, sgm_done)
            v.wait_ge(pr.ve.h, bis_done)
            pr.ve.inc(v.tensor_tensor(mmt[:], sgm[:], sqs[:], ALU.mult))
        mskd_done = pr.ve.v

        tmM = pr.tm.v
        veM = pr.ve.v

        @pr.block.tensor
        def _(t, tmM=tmM, veM=veM, mskd_done=mskd_done):
            t.wait_ge(pr.ve.h, mskd_done)
            for h in range(8):
                for ts in range(4):
                    idx = h * 4 + ts
                    if idx >= 2:
                        t.wait_ge(pr.ve.h, veM + idx - 1)
                    pt = pr.PT1 if idx % 2 == 0 else pr.PT2
                    pr.tm.inc(t.transpose(pt[:, 0:128],
                                          mmt[:, ts * 1024 + h * 128: ts * 1024 + h * 128 + 128],
                                          pr.ident_b[:]))

        @pr.block.vector
        def _(v, tmM=tmM):
            for h in range(8):
                for ts in range(4):
                    idx = h * 4 + ts
                    v.wait_ge(pr.tm.h, tmM + idx + 1)
                    pt = pr.PT1 if idx % 2 == 0 else pr.PT2
                    pr.ve.inc(v.tensor_copy(sqs[:, h * 512 + ts * 128: h * 512 + ts * 128 + 128],
                                            pt[:, 0:128]))
        mmt_done = pr.ve.v

        for h in range(8):
            tmH = pr.tm.v
            scH = pr.sc.v
            veH = pr.ve.v

            @pr.block.tensor
            def _(t, h=h, mmt_done=mmt_done, veH=veH, tmH=tmH):
                t.wait_ge(pr.ve.h, mmt_done)
                if h >= 1:
                    t.wait_ge(pr.ve.h, veH)
                pr.tm.inc(t.matmul(pr.PF[0:1, 0:512], pr.ones_bcol[:],
                                   sqs[:, h * 512:(h + 1) * 512], start=True, stop=True))
            ts_mm = pr.tm.v

            @pr.block.scalar
            def _(s, ts_mm=ts_mm, h=h):
                s.wait_ge(pr.tm.h, ts_mm)
                s.activation(rows[:, 0:512], pr.PF[0:1, 0:512], AF.Ln, bias=pr.eps9[:])
                s.drain()
                pr.sc.inc(s.activation(rows[:, 0:512], rows[:, 0:512], AF.Exp, scale=-1.0))
            rt_done = pr.sc.v

            @pr.block.sync
            def _(sy, h=h, fhr_ld=fhr_ld):
                sy.wait_ge(pr.dma.h, pr.dma.v)
                sy.wait_ge(pr.tm.h, pr.tm.v)
                pr.dma.inc(sy.dma_start(out=rows[:, 512:1024], in_=fscr.ap()[h, :]), 16)
            fh_row = pr.dma.v

            @pr.block.tensor
            def _(t, rt_done=rt_done, h=h, fh_row=fh_row):
                t.wait_ge(pr.sc.h, rt_done)
                t.wait_ge(pr.dma.h, fh_row)
                t.matmul(pr.PF[:, 0:512], pr.ones_frow[:], rows[:, 0:512], start=True, stop=True)
                pr.tm.inc(t.matmul(pr.PB[:, 0:512], pr.ones_frow[:], rows[:, 512:1024],
                                   start=True, stop=True))
            bc_mm = pr.tm.v

            @pr.block.vector
            def _(v, bc_mm=bc_mm, h=h):
                v.wait_ge(pr.tm.h, bc_mm)
                v.tensor_copy(rbc[:], pr.PF[:, 0:512])
                v.drain()
                v.tensor_tensor(fts[:], sqs[:, h * 512:(h + 1) * 512], rbc[:], ALU.mult)
                v.drain()
                pr.ve.inc(v.tensor_tensor(fts[:], fts[:], pr.PB[:, 0:512], ALU.mult))
                v.drain()
            f_done = pr.ve.v

            prev_o_ev = pr.sc.v
            for ck in range(4):
                csl = slice(ck * 128, ck * 128 + 128)
                first = (tt == 0 and ck == 0)
                tmC = pr.tm.v
                scC = pr.sc.v
                veC = pr.ve.v

                @pr.block.vector
                def _(v, h=h, ck=ck, csl=csl, f_done=f_done, veC=veC):
                    v.wait_ge(pr.ve.h, f_done)
                    pr.ve.inc(v.tensor_tensor_scan(chs[:, 0:128], fts[:, csl], fts[:, csl],
                                                   0.0, ALU.add, ALU.bypass))
                    v.drain()
                L_done = pr.ve.v

                @pr.block.scalar
                def _(s, L_done=L_done, csl=csl):
                    s.wait_ge(pr.ve.h, L_done)
                    s.activation(chb[:, 0:128], chs[:, 0:128], AF.Exp)
                    s.drain()
                    s.activation(chb[:, 128:256], chs[:, 0:128], AF.Exp, scale=-1.0)
                    s.drain()
                    s.activation(chs[:, 128:256], fts[:, csl], AF.Exp)
                    s.drain()
                    pr.sc.inc(s.copy(chs[:, 384:385], chs[:, 127:128]))
                exp_done = pr.sc.v

                @pr.block.scalar
                def _(s, exp_done=exp_done):
                    s.wait_ge(pr.sc.h, exp_done)
                    pr.sc.inc(s.activation(chs[:, 385:386], chs[:, 384:385], AF.Exp))
                eL_done = pr.sc.v

                @pr.block.vector
                def _(v, h=h, csl=csl, eL_done=eL_done):
                    v.wait_ge(pr.sc.h, eL_done)
                    v.tensor_scalar(chs[:, 256:384], chs[:, 128:256], -1.0, 1.0, ALU.mult, ALU.add)
                    v.tensor_tensor(chb[:, 256:384], qn[:, h * 512:h * 512 + 512][:, csl],
                                    chb[:, 0:128], ALU.mult)
                    v.drain()
                    v.tensor_tensor(chb[:, 384:512], kn[:, h * 512:h * 512 + 512][:, csl],
                                    chs[:, 256:384], ALU.mult)
                    v.drain()
                    v.tensor_tensor(chb[:, 512:640], chb[:, 384:512], chb[:, 128:256], ALU.mult)
                    v.drain()
                    pr.ve.inc(v.tensor_scalar(chb[:, 640:768], chb[:, 512:640],
                                              chs[:, 385:386], None, ALU.mult))
                    v.drain()
                qk_done = pr.ve.v

                tmC2 = pr.tm.v

                @pr.block.tensor
                def _(t, h=h, ck=ck, qk_done=qk_done, tmC2=tmC2, first=first):
                    t.wait_ge(pr.ve.h, qk_done)
                    pr.tm.inc(t.matmul(pr.PB[:, 0:128], chb[:, 512:640], chb[:, 256:384],
                                       start=True, stop=True))
                    pr.tm.inc(t.transpose(pr.PT1[:, 0:128],
                                          vs[:, h * 512:h * 512 + 512][:, ck * 128:ck * 128 + 128],
                                          pr.ident_b[:]))
                    pr.tm.inc(t.transpose(pr.PT2[:, 0:128], chb[:, 640:768], pr.ident_b[:]))
                pt_mm = pr.tm.v

                @pr.block.vector
                def _(v, pt_mm=pt_mm):
                    v.wait_ge(pr.tm.h, pt_mm)
                    v.tensor_tensor(chb[:, 768:896], pr.PB[:, 0:128], pr.trimask[:], ALU.mult)
                    v.tensor_copy(vtt[:, 0:128], pr.PT1[:, 0:128])
                    v.tensor_copy(chb[:, 896:1024], pr.PT2[:, 0:128])
                    pr.ve.inc(v.drain())
                pm_done = pr.ve.v

                @pr.block.tensor
                def _(t, h=h, pm_done=pm_done, first=first, prev_o_ev=prev_o_ev):
                    t.wait_ge(pr.ve.h, pm_done)
                    t.wait_ge(pr.sc.h, prev_o_ev)
                    t.matmul(pr.PC[:, 0:128], vtt[:, 0:128], chb[:, 768:896],
                             start=True, stop=first)
                    if not first:
                        t.matmul(pr.PC[:, 0:128], Sbf[:, h * 128:(h + 1) * 128],
                                 chb[:, 256:384], start=False, stop=True)
                    i = t.matmul(pr.PD[:, 0:128], chb[:, 896:1024], vtt[:, 0:128],
                                 start=True, stop=True)
                    pr.tm.inc(i)
                o_mm = pr.tm.v

                @pr.block.scalar
                def _(s, h=h, ck=ck, o_mm=o_mm, tt=tt):
                    s.wait_ge(pr.tm.h, o_mm)
                    pr.sc.inc(s.copy(oT[:, h * 512 + ck * 128: h * 512 + ck * 128 + 128],
                                     pr.PC[:, 0:128]))
                o_ev = pr.sc.v
                prev_o_ev = o_ev

                @pr.block.vector
                def _(v, h=h, o_mm=o_mm, o_ev=o_ev):
                    v.wait_ge(pr.tm.h, o_mm)
                    v.scalar_tensor_tensor(S32[:, h * 128:(h + 1) * 128], S32[:, h * 128:(h + 1) * 128],
                                           chs[:, 385:386], pr.PD[:, 0:128], ALU.mult, ALU.add)
                    v.drain()
                    pr.ve.inc(v.tensor_copy(Sbf[:, h * 128:(h + 1) * 128], S32[:, h * 128:(h + 1) * 128]))
                    v.drain()

        last_o_sc = pr.sc.v
        last_ve = pr.ve.v

        @pr.block.scalar
        def _(s, last_o_sc=last_o_sc, last_ve=last_ve, tmg=pr.tm.v):
            s.wait_ge(pr.sc.h, last_o_sc)
            s.wait_ge(pr.ve.h, last_ve)
            s.wait_ge(pr.tm.h, tmg)
            pr.sc.inc(s.activation(sqs[:], oT[:], AF.Square))
        sq_done = pr.sc.v

        tmG = pr.tm.v

        @pr.block.tensor
        def _(t, sq_done=sq_done, tmG=tmG):
            t.wait_ge(pr.sc.h, sq_done)
            for h in range(8):
                i = t.matmul(pr.PF[0:1, 0:512], pr.ones_bcol[:], sqs[:, h * 512:(h + 1) * 512],
                             start=(h == 0), stop=(h == 7))
            pr.tm.inc(i)
        gs_mm = pr.tm.v

        @pr.block.scalar
        def _(s, gs_mm=gs_mm):
            s.wait_ge(pr.tm.h, gs_mm)
            s.activation(rows[:, 0:512], pr.PF[0:1, 0:512], AF.Ln, scale=1.0 / 1024.0,
                         bias=pr.eps_t[:])
            s.drain()
            pr.sc.inc(s.activation(rows[:, 0:512], rows[:, 0:512], AF.Exp, scale=-0.5))
        gr_done = pr.sc.v

        @pr.block.tensor
        def _(t, gr_done=gr_done):
            t.wait_ge(pr.sc.h, gr_done)
            pr.tm.inc(t.matmul(pr.PF[:, 0:512], pr.ones_frow[:], rows[:, 0:512],
                               start=True, stop=True))
        grb_mm = pr.tm.v

        @pr.block.vector
        def _(v, grb_mm=grb_mm):
            v.wait_ge(pr.tm.h, grb_mm)
            v.tensor_copy(rbc[:], pr.PF[:, 0:512])
            v.drain()
            for h in range(8):
                v.scalar_tensor_tensor(oT[:, h * 512:(h + 1) * 512], oT[:, h * 512:(h + 1) * 512],
                                       nrm[:, 2 + h:3 + h], rbc[:], ALU.mult, ALU.mult)
            pr.ve.inc(v.drain())
        on_done = pr.ve.v

        dmwO = pr.dmw.v

        @pr.block.gpsimd
        def _(g, tt=tt):
            g.wait_ge(pr.dmw.h, pr.dmw.v)
            g.wait_ge(pr.tm.h, pr.tm.v)
            pr.dmw.inc(g.dma_start(out=wbuf[0][:].rearrange("p (c m) -> p c m", c=8),
                                   in_=io["b2_wo"].ap().rearrange("(c p) m -> p c m", p=128)), 16)
        wo_ld = pr.dmw.v

        tmW = pr.tm.v
        veW = pr.ve.v

        @pr.block.tensor
        def _(t, wo_ld=wo_ld, on_done=on_done, tmW=tmW, veW=veW):
            t.wait_ge(pr.dmw.h, wo_ld)
            t.wait_ge(pr.ve.h, on_done)
            for dc in range(8):
                if dc >= 2:
                    t.wait_ge(pr.ve.h, veW + dc - 1)
                ps = pr.PA[:, 512 * (dc % 2):512 * (dc % 2) + 512]
                for k in range(8):
                    i = t.matmul(ps, wbuf[0][:, k * 1024 + dc * 128: k * 1024 + dc * 128 + 128],
                                 oT[:, k * 512:(k + 1) * 512], start=(k == 0), stop=(k == 7))
                pr.tm.inc(i)

        @pr.block.vector
        def _(v, tmW=tmW, tt=tt):
            for dc in range(8):
                v.wait_ge(pr.tm.h, tmW + dc + 1)
                sl = slice(dc * T + tt * 512, dc * T + tt * 512 + 512)
                ps = pr.PA[:, 512 * (dc % 2):512 * (dc % 2) + 512]
                pr.ve.inc(v.tensor_tensor(pr.xT[:, sl], pr.xT[:, sl], ps, ALU.add))
                v.drain()
        prev_tt = pr.ve.v
    ph.close()
    return pr.ve.v


def build_b2():
    nc = bass.Bass("TRN2", target_bir_lowering=False, debug=False)
    io = {
        "xin": nc.dram_tensor("xin", [D, T], F32, kind="ExternalInput"),
        "b2_wq": nc.dram_tensor("b2_wq", [D, D], F32, kind="ExternalInput"),
        "b2_wk": nc.dram_tensor("b2_wk", [D, D], F32, kind="ExternalInput"),
        "b2_wv": nc.dram_tensor("b2_wv", [D, D], F32, kind="ExternalInput"),
        "b2_wr": nc.dram_tensor("b2_wr", [D, D], F32, kind="ExternalInput"),
        "b2_wo": nc.dram_tensor("b2_wo", [D, D], F32, kind="ExternalInput"),
        "b2_wa": nc.dram_tensor("b2_wa", [D, 8], F32, kind="ExternalInput"),
        "b2_alog": nc.dram_tensor("b2_alog", [8], F32, kind="ExternalInput"),
        "b2_dtb": nc.dram_tensor("b2_dtb", [8], F32, kind="ExternalInput"),
        "b2_qnw": nc.dram_tensor("b2_qnw", [128], F32, kind="ExternalInput"),
        "b2_knw": nc.dram_tensor("b2_knw", [128], F32, kind="ExternalInput"),
        "b2_gnw": nc.dram_tensor("b2_gnw", [1024], F32, kind="ExternalInput"),
    }
    xt_out = nc.dram_tensor("xt_out", [D, T], F32, kind="ExternalOutput")
    with ExitStack() as ctx:
        pr = Prog(nc, ctx, xln_cols=DCH * 1024)
        pr.emit_consts()
        xt_ready = emit_load_xt(pr, io["xin"])
        emit_block2(pr, io, xt_ready)
        emit_store_xt(pr, xt_out)
    return nc


def emit_store_xt(pr, xt_out):
    @pr.block.sync
    def _(sy):
        sy.wait_ge(pr.ve.h, pr.ve.v)
        o = xt_out.ap().rearrange("(c p) t -> p c t", p=128)
        pr.od.inc(sy.dma_start(out=o, in_=pr.xT[:].rearrange("p (c t) -> p c t", c=DCH)), 16)
        sy.wait_ge(pr.od.h, pr.od.v)


def build_b0(x_natural=True, upto=99):
    nc = bass.Bass("TRN2", target_bir_lowering=False, debug=False)
    io = {
        "xin": nc.dram_tensor("xin", [T, D] if x_natural else [D, T], F32, kind="ExternalInput"),
        "b0_ctx": nc.dram_tensor("b0_ctx", [512, CD], F32, kind="ExternalInput"),
        "b0_wq": nc.dram_tensor("b0_wq", [D, D], F32, kind="ExternalInput"),
        "b0_wk": nc.dram_tensor("b0_wk", [CD, D], F32, kind="ExternalInput"),
        "b0_wv": nc.dram_tensor("b0_wv", [CD, D], F32, kind="ExternalInput"),
        "b0_wo": nc.dram_tensor("b0_wo", [D, D], F32, kind="ExternalInput"),
        "b0_bo": nc.dram_tensor("b0_bo", [D], F32, kind="ExternalInput"),
    }
    xt_out = nc.dram_tensor("xt_out", [D, T], F32, kind="ExternalOutput")
    with ExitStack() as ctx:
        pr = Prog(nc, ctx)
        pr.emit_consts()
        emit_block0(pr, io, have_x_natural=x_natural, upto=upto)
        emit_store_xt(pr, xt_out)
    return nc


def emit_barrier(pr):

    @pr.block.tensor
    def _(t):
        pr.tm.inc(t.drain())

    @pr.block.vector
    def _(v):
        pr.ve.inc(v.drain())

    @pr.block.scalar
    def _(s):
        pr.sc.inc(s.drain())

    @pr.block.gpsimd
    def _(g):
        pr.gp.inc(g.drain())

    vals = [(s, s.v) for s in (pr.dma, pr.dmg, pr.dmw, pr.dmk, pr.cc,
                               pr.tm, pr.sc, pr.ve, pr.gp, pr.od,
                               pr.dmc, pr.dmc2)]
    for sec in (pr.block.tensor, pr.block.vector, pr.block.scalar,
                pr.block.gpsimd, pr.block.sync):
        @sec
        def _(e, vals=vals):
            for s, v in vals:
                e.wait_ge(s.h, v)
    return pr.ve.v


def build_all():
    nc = bass.Bass("TRN2", target_bir_lowering=False, debug=False)
    io = {
        "xin": nc.dram_tensor("xin", [T, D], F32, kind="ExternalInput"),
        "b0_ctx": nc.dram_tensor("b0_ctx", [512, CD], F32, kind="ExternalInput"),
        "b0_wq": nc.dram_tensor("b0_wq", [D, D], F32, kind="ExternalInput"),
        "b0_wk": nc.dram_tensor("b0_wk", [CD, D], F32, kind="ExternalInput"),
        "b0_wv": nc.dram_tensor("b0_wv", [CD, D], F32, kind="ExternalInput"),
        "b0_wo": nc.dram_tensor("b0_wo", [D, D], F32, kind="ExternalInput"),
        "b0_bo": nc.dram_tensor("b0_bo", [D], F32, kind="ExternalInput"),
        "b1_w1t": nc.dram_tensor("b1_w1t", [1024, 4096], F32, kind="ExternalInput"),
        "b1_w2t": nc.dram_tensor("b1_w2t", [1024, 4096], F32, kind="ExternalInput"),
        "b1_h1t": nc.dram_tensor("b1_h1t", [4096, 4096], F32, kind="ExternalInput"),
        "b1_h2t": nc.dram_tensor("b1_h2t", [4096, 4096], F32, kind="ExternalInput"),
        "b1_w3t": nc.dram_tensor("b1_w3t", [4096, 1024], F32, kind="ExternalInput"),
        "b1_s1": nc.dram_tensor("b1_s1", [4096], F32, kind="ExternalInput"),
        "b1_s2": nc.dram_tensor("b1_s2", [4096], F32, kind="ExternalInput"),
        "b1_sh1": nc.dram_tensor("b1_sh1", [4096], F32, kind="ExternalInput"),
        "b1_sh2": nc.dram_tensor("b1_sh2", [4096], F32, kind="ExternalInput"),
        "b1_s3": nc.dram_tensor("b1_s3", [1024], F32, kind="ExternalInput"),
        "b2_wq": nc.dram_tensor("b2_wq", [D, D], F32, kind="ExternalInput"),
        "b2_wk": nc.dram_tensor("b2_wk", [D, D], F32, kind="ExternalInput"),
        "b2_wv": nc.dram_tensor("b2_wv", [D, D], F32, kind="ExternalInput"),
        "b2_wr": nc.dram_tensor("b2_wr", [D, D], F32, kind="ExternalInput"),
        "b2_wo": nc.dram_tensor("b2_wo", [D, D], F32, kind="ExternalInput"),
        "b2_wa": nc.dram_tensor("b2_wa", [D, 8], F32, kind="ExternalInput"),
        "b2_alog": nc.dram_tensor("b2_alog", [8], F32, kind="ExternalInput"),
        "b2_dtb": nc.dram_tensor("b2_dtb", [8], F32, kind="ExternalInput"),
        "b2_qnw": nc.dram_tensor("b2_qnw", [128], F32, kind="ExternalInput"),
        "b2_knw": nc.dram_tensor("b2_knw", [128], F32, kind="ExternalInput"),
        "b2_gnw": nc.dram_tensor("b2_gnw", [1024], F32, kind="ExternalInput"),
        "b3_wint": nc.dram_tensor("b3_wint", [4096, 1024], F32, kind="ExternalInput"),
        "b3_wout": nc.dram_tensor("b3_wout", [4096, 1024], F32, kind="ExternalInput"),
        "b3_bin": nc.dram_tensor("b3_bin", [4096], F32, kind="ExternalInput"),
        "b3_bout": nc.dram_tensor("b3_bout", [1024], F32, kind="ExternalInput"),
    }
    xt_out = nc.dram_tensor("xt_out", [D, T], F32, kind="ExternalOutput")
    with ExitStack() as ctx:
        pr = Prog(nc, ctx, xln_cols=DCH * T)
        pr.emit_consts()
        scr = make_b1_scratch(nc)
        cnv_stack = ExitStack()
        cnvf = [cnv_stack.enter_context(nc.sbuf_tensor(f"b1cnv{i}", [128, 896], F32))
                for i in range(2)]
        emit_b1_prep(pr, io, scr, cnvf, ["b1_w1t", "b1_w2t"])
        emit_block0(pr, io, have_x_natural=True)
        emit_b1_prep(pr, io, scr, cnvf, ["b1_h1t", "b1_h2t", "b1_w3t"])
        emit_barrier(pr)
        cnv_stack.close()
        emit_block1(pr, io, pr.ve.v, scr=scr)
        emit_barrier(pr)
        emit_block2(pr, io, pr.ve.v)
        emit_barrier(pr)
        emit_block3(pr, io, pr.ve.v)
        emit_store_xt(pr, xt_out)
    return nc


def kernel(**inputs):
    from concourse.bass_utils import run_bass_kernel_spmd
    f32 = np.float32
    g = lambda k: np.ascontiguousarray(np.asarray(inputs[k], dtype=f32))
    cores = list(range(NCORE))

    x = g("L0_attn_x")
    ctxf = np.ascontiguousarray(g("L0_attn_context").reshape(4096, CD))
    nc = build_all()
    w1t = np.ascontiguousarray(g("L0_ffn_w1_w").T)
    w2t = np.ascontiguousarray(g("L0_ffn_w2_w").T)
    h1t = np.ascontiguousarray(g("L0_ffn_h1_w").T)
    h2t = np.ascontiguousarray(g("L0_ffn_h2_w").T)
    w3t = np.ascontiguousarray(g("L0_ffn_w3_w").T)
    wint = np.ascontiguousarray(g("L1_ffn_W_in").T)
    in_maps = [dict(
        xin=np.ascontiguousarray(x[c]),
        b0_ctx=np.ascontiguousarray(ctxf[c * 512:(c + 1) * 512]),
        b0_wq=g("L0_attn_Wq"), b0_wk=g("L0_attn_Wk"), b0_wv=g("L0_attn_Wv"),
        b0_wo=g("L0_attn_Wout"), b0_bo=g("L0_attn_bout"),
        b1_w1t=w1t, b1_w2t=w2t, b1_h1t=h1t, b1_h2t=h2t, b1_w3t=w3t,
        b1_s1=g("L0_ffn_w1_s"), b1_s2=g("L0_ffn_w2_s"),
        b1_sh1=g("L0_ffn_h1_s"), b1_sh2=g("L0_ffn_h2_s"), b1_s3=g("L0_ffn_w3_s"),
        b2_wq=g("L1_attn_Wq"), b2_wk=g("L1_attn_Wk"), b2_wv=g("L1_attn_Wv"),
        b2_wr=g("L1_attn_Wr"), b2_wo=g("L1_attn_Wo"), b2_wa=g("L1_attn_Wa"),
        b2_alog=g("L1_attn_A_log"), b2_dtb=g("L1_attn_dt_bias"),
        b2_qnw=g("L1_attn_q_norm_w"), b2_knw=g("L1_attn_k_norm_w"),
        b2_gnw=g("L1_attn_g_norm_w"),
        b3_wint=wint, b3_wout=g("L1_ffn_W_out"),
        b3_bin=g("L1_ffn_b_in"), b3_bout=g("L1_ffn_b_out")) for c in cores]
    res = run_bass_kernel_spmd(nc, in_maps, core_ids=cores)
    return np.stack([res.results[c]["xt_out"].T for c in cores]).astype(f32)


def kernel_unfused(**inputs):
    from concourse.bass_utils import run_bass_kernel_spmd
    f32 = np.float32
    g = lambda k: np.ascontiguousarray(np.asarray(inputs[k], dtype=f32))
    cores = list(range(NCORE))

    x = g("L0_attn_x")
    ctxf = np.ascontiguousarray(g("L0_attn_context").reshape(4096, CD))

    nc0 = build_b0(x_natural=True)
    in0 = [dict(xin=np.ascontiguousarray(x[c]),
                b0_ctx=np.ascontiguousarray(ctxf[c * 512:(c + 1) * 512]),
                b0_wq=g("L0_attn_Wq"), b0_wk=g("L0_attn_Wk"),
                b0_wv=g("L0_attn_Wv"), b0_wo=g("L0_attn_Wout"),
                b0_bo=g("L0_attn_bout")) for c in cores]
    r0 = run_bass_kernel_spmd(nc0, in0, core_ids=cores)
    xt = [np.ascontiguousarray(r0.results[c]["xt_out"]) for c in cores]

    nc1 = build_b1()
    w1t = np.ascontiguousarray(g("L0_ffn_w1_w").T)
    w2t = np.ascontiguousarray(g("L0_ffn_w2_w").T)
    h1t = np.ascontiguousarray(g("L0_ffn_h1_w").T)
    h2t = np.ascontiguousarray(g("L0_ffn_h2_w").T)
    w3t = np.ascontiguousarray(g("L0_ffn_w3_w").T)
    in1 = [dict(xin=xt[c], b1_w1t=w1t, b1_w2t=w2t, b1_h1t=h1t, b1_h2t=h2t,
                b1_w3t=w3t, b1_s1=g("L0_ffn_w1_s"), b1_s2=g("L0_ffn_w2_s"),
                b1_sh1=g("L0_ffn_h1_s"), b1_sh2=g("L0_ffn_h2_s"),
                b1_s3=g("L0_ffn_w3_s")) for c in cores]
    r1 = run_bass_kernel_spmd(nc1, in1, core_ids=cores)
    xt = [np.ascontiguousarray(r1.results[c]["xt_out"]) for c in cores]

    nc2 = build_b2()
    in2 = [dict(xin=xt[c], b2_wq=g("L1_attn_Wq"), b2_wk=g("L1_attn_Wk"),
                b2_wv=g("L1_attn_Wv"), b2_wr=g("L1_attn_Wr"),
                b2_wo=g("L1_attn_Wo"), b2_wa=g("L1_attn_Wa"),
                b2_alog=g("L1_attn_A_log"), b2_dtb=g("L1_attn_dt_bias"),
                b2_qnw=g("L1_attn_q_norm_w"), b2_knw=g("L1_attn_k_norm_w"),
                b2_gnw=g("L1_attn_g_norm_w")) for c in cores]
    r2 = run_bass_kernel_spmd(nc2, in2, core_ids=cores)
    xt = [np.ascontiguousarray(r2.results[c]["xt_out"]) for c in cores]

    nc3 = build_b3()
    wint = np.ascontiguousarray(g("L1_ffn_W_in").T)
    in3 = [dict(xin=xt[c], b3_wint=wint, b3_wout=g("L1_ffn_W_out"),
                b3_bin=g("L1_ffn_b_in"), b3_bout=g("L1_ffn_b_out"))
           for c in cores]
    r3 = run_bass_kernel_spmd(nc3, in3, core_ids=cores)

    return np.stack([r3.results[c]["xt_out"].T for c in cores]).astype(f32)
